# version 05
# speedup vs baseline: 1.2849x; 1.0104x over previous
import contextlib
import numpy as np
import concourse.bass as bass
import concourse.mybir as mybir
from concourse.bass_utils import run_bass_kernel_spmd

F32 = mybir.dt.float32
BF16 = mybir.dt.bfloat16
AF = mybir.ActivationFunctionType
ALU = mybir.AluOpType
AX = mybir.AxisListType

L = 4096
DM = 1024
NCORE = 8
NEG = -1e30


def _ceil(a, b):
    return -(-a // b)


class Tok:
    def __init__(self, prog, name):
        self.sem = prog.stack.enter_context(prog.nc.semaphore(name))
        self.n = 0

    def inc(self, instr):
        instr.then_inc(self.sem, 1)
        self.n += 1
        return self.n

    def wait(self, eng, n=None):
        eng.wait_ge(self.sem, self.n if n is None else n)


class Prog:
    def __init__(self):
        self.nc = bass.Bass("TRN2", target_bir_lowering=False)
        self.stack = contextlib.ExitStack()
        nc = self.nc
        self.q = {
            "sync": [nc.sync, self.stack.enter_context(nc.semaphore("q_sync")), 0],
            "gp": [nc.gpsimd, self.stack.enter_context(nc.semaphore("q_gp")), 0],
            "act": [nc.scalar, self.stack.enter_context(nc.semaphore("q_act")), 0],
            "st": [nc.sync, self.stack.enter_context(nc.semaphore("q_st")), 0],
            "ld0": [nc.sync, self.stack.enter_context(nc.semaphore("q_ld0")), 0],
            "ld1": [nc.sync, self.stack.enter_context(nc.semaphore("q_ld1")), 0],
            "gp0": [nc.gpsimd, self.stack.enter_context(nc.semaphore("q_gp0")), 0],
            "fr": [nc.sync, self.stack.enter_context(nc.semaphore("q_fr")), 0],
            "st0": [nc.sync, self.stack.enter_context(nc.semaphore("q_st0")), 0],
            "st1": [nc.sync, self.stack.enter_context(nc.semaphore("q_st1")), 0],
            "st2": [nc.sync, self.stack.enter_context(nc.semaphore("q_st2")), 0],
            "st3": [nc.sync, self.stack.enter_context(nc.semaphore("q_st3")), 0],
            "gp1": [nc.gpsimd, self.stack.enter_context(nc.semaphore("q_gp1")), 0],
        }
        self.tv = Tok(self, "tok_v")
        self.ts = Tok(self, "tok_s")
        self.tt = Tok(self, "tok_t")
        self.tg = Tok(self, "tok_g")
        self.psum = [nc.alloc_psum_tensor(f"pb{i}", [128, 512], F32) for i in range(6)]
        self.psum_bf = [nc.alloc_psum_tensor(f"pbb{i}", [128, 1024], BF16) for i in range(2)]
        self._consts = {}
        self._names = {}
        self._fconsts = {}

    def eng(self, qn):
        return self.q[qn][0]

    def dma(self, qn, out, in_):
        eng, sem, _ = self.q[qn]
        r = eng.dma_start(out=out, in_=in_).then_inc(sem, 16)
        self.q[qn][2] += 1
        return self.q[qn][2]

    def qcnt(self, qn):
        return self.q[qn][2]

    def qwait(self, eng, qn, cnt=None):
        _, sem, n = self.q[qn]
        eng.wait_ge(sem, 16 * (n if cnt is None else cnt))

    def flush(self):
        for qn, (eng, sem, cnt) in self.q.items():
            if cnt:
                eng.wait_ge(sem, 16 * cnt)

    def barrier(self):
        self.flush()
        self.nc.all_engine_barrier()

    def const(self, name, arr):
        if name not in self._consts:
            self._consts[name] = self.nc.inline_tensor(np.ascontiguousarray(arr), name=name)
        return self._consts[name]

    def fconst(self, val):
        if val not in self._fconsts:
            t = self.nc.alloc_sbuf_tensor(self.uname("fc"), [128, 1], F32)
            self.nc.gpsimd.memset(t.ap(), float(val))
            self._fconsts[val] = t
        return self._fconsts[val].ap()

    @contextlib.contextmanager
    def scope(self):
        stg = contextlib.ExitStack()
        saved = self.sb

        def sb2(name, shape, dtype=F32):
            return stg.enter_context(
                self.nc.sbuf_tensor(self.uname(name), list(shape), dtype))
        self.sb = sb2
        try:
            yield
        finally:
            self.barrier()
            self.sb = saved
            stg.close()

    def uname(self, base):
        k = self._names.get(base, 0)
        self._names[base] = k + 1
        return f"{base}{k}"

    def sb(self, name, shape, dtype=F32):
        return self.nc.alloc_sbuf_tensor(self.uname(name), list(shape), dtype)

    def dram(self, name, shape, dtype=F32, kind="Internal"):
        return self.nc.dram_tensor(name, list(shape), dtype, kind=kind)

    def close(self):
        self.stack.close()


def fm_matmul(P, X, W, out, K, N, Lw=None, bias=None, act=None, scale=1.0,
              out_dtype=F32, TTT=512, gate_row=None, accum=False):
    Lw = L if Lw is None else Lw
    assert not (accum and (bias is not None or act is not None or scale != 1.0))
    nc = P.nc
    if accum:
        ones_row(P)
    kcs = K // 128
    nch = _ceil(N, 128)
    ntt = Lw // TTT
    _stg = contextlib.ExitStack()
    def _sb(name, shape, dtype=F32):
        return _stg.enter_context(nc.sbuf_tensor(P.uname(name), list(shape), dtype))
    P_sb_saved = P.sb
    P.sb = _sb
    wsb = P.sb("mmw", [128, kcs, N], BF16)
    P.dma("gp", wsb.ap(), W.ap().rearrange("(kc p) n -> p kc n", p=128))
    bsb = None
    if bias is not None:
        bsb = P.sb("mmbias", [128, nch], F32)
        for i in range(nch):
            rows = min(128, N - i * 128)
            P.dma("sync", bsb.ap()[:rows, i:i + 1], bias.ap()[i * 128:i * 128 + rows].unsqueeze(1))
    P.barrier()

    xsb = [P.sb("mmx", [128, kcs, TTT], BF16) for _ in range(2)]
    osb = [P.sb("mmo", [128, TTT], out_dtype) for _ in range(4)]
    if accum:
        fb_sb = P.sb("mmfb", [128, TTT], F32)
        tmp_sb = P.sb("mmtmp", [128, TTT], F32)
        rsb = [P.sb("mmr", [128, TTT], F32) for _ in range(2)]
        racc_free = [0, 0]
    xa = X.ap().rearrange("(kc p) l -> p kc l", p=128)
    use_scalar = (act is not None) or (bias is not None) or (scale != 1.0)
    etok = P.ts if use_scalar else P.tv
    eeng = nc.scalar if use_scalar else nc.vector

    pe_done_at_x = [0, 0]
    bank_free = [0] * 2

    frow_ld = 0
    for tt in range(ntt):
        xb = tt % 2
        gp = nc.gpsimd
        P.tt.wait(gp, pe_done_at_x[xb])
        P.dma(f"gp{xb}", xsb[xb].ap(), xa[:, :, tt * TTT:(tt + 1) * TTT])
        if accum and gate_row is not None:
            P.tv.wait(nc.sync)
            P.dma("fr", fb_sb.ap()[0:1, :],
                  gate_row.ap()[:, tt * TTT:(tt + 1) * TTT])
            t = nc.tensor
            P.qwait(t, "fr")
            P.tv.wait(t)
            fmm = P.tt.inc(t.matmul(P.psum[2].ap()[:, 0:TTT], ones_row(P).ap(),
                                    fb_sb.ap()[0:1, :], start=True, stop=True))
            v0 = nc.vector
            P.tt.wait(v0, fmm)
            fb_n = P.tv.inc(v0.tensor_copy(fb_sb.ap(), P.psum[2].ap()[:, 0:TTT]))
            v0.drain()
            P.tv.wait(v0, fb_n)
        for ncI in range(nch):
            bk = ncI % 2
            pb = P.psum[bk]
            rows = min(128, N - ncI * 128)
            t = nc.tensor
            P.qwait(t, f"gp{xb}")
            etok.wait(t, bank_free[bk])
            for kc in range(kcs):
                ins = t.matmul(pb.ap()[:rows, :],
                               wsb.ap()[:, kc, ncI * 128:ncI * 128 + rows],
                               xsb[xb].ap()[:, kc, :],
                               start=(kc == 0), stop=(kc == kcs - 1))
            mm_n = P.tt.inc(ins)
            j = ncI % 4
            ob = osb[j]
            if accum:
                P.tv.wait(nc.sync, racc_free[ncI % 2])
                P.dma(f"ld{xb}", rsb[ncI % 2].ap()[:rows, :],
                      out.ap()[ncI * 128:ncI * 128 + rows, tt * TTT:(tt + 1) * TTT])
            P.tt.wait(eeng, mm_n)
            P.qwait(eeng, f"st{j}")
            if accum:
                v0 = nc.vector
                P.qwait(v0, f"ld{xb}")
                if gate_row is not None:
                    v0.scalar_tensor_tensor(out=tmp_sb.ap()[:rows, :],
                                            in0=fb_sb.ap()[:rows, :], scalar=1.0,
                                            in1=pb.ap()[:rows, :], op0=ALU.mult,
                                            op1=ALU.mult)
                else:
                    v0.tensor_copy(tmp_sb.ap()[:rows, :], pb.ap()[:rows, :])
                v0.drain()
                ev = v0.tensor_tensor(ob.ap()[:rows, :], tmp_sb.ap()[:rows, :],
                                      rsb[ncI % 2].ap()[:rows, :], ALU.add)
            elif use_scalar:
                kw = dict(scale=scale)
                if bias is not None:
                    kw["bias"] = bsb.ap()[:rows, ncI:ncI + 1]
                ev = eeng.activation(ob.ap()[:rows, :], pb.ap()[:rows, :],
                                     act if act is not None else AF.Identity, **kw)
            else:
                ev = eeng.tensor_copy(ob.ap()[:rows, :], pb.ap()[:rows, :])
            bank_free[bk] = etok.inc(ev)
            if accum:
                racc_free[ncI % 2] = bank_free[bk]
            sy = nc.sync
            etok.wait(sy)
            P.dma(f"st{j}", out.ap()[ncI * 128:ncI * 128 + rows, tt * TTT:(tt + 1) * TTT],
                  ob.ap()[:rows, :])
        pe_done_at_x[xb] = P.tt.n
    P.barrier()
    P.sb = P_sb_saved
    _stg.close()


def tok_matmul(P, X, W, out, K, N, Lw=None, out_dtype=F32):
    Lw = L if Lw is None else Lw
    nc = P.nc
    kcs = K // 128
    nns = _ceil(N, 512)
    _stg = contextlib.ExitStack()
    def _sb(name, shape, dtype=F32):
        return _stg.enter_context(nc.sbuf_tensor(P.uname(name), list(shape), dtype))
    P_sb_saved = P.sb
    P.sb = _sb
    wsb = P.sb("wt", [128, kcs, N], BF16)
    P.dma("gp", wsb.ap(), W.ap().rearrange("(kc p) n -> p kc n", p=128))
    P.barrier()
    xsb = [P.sb("xt", [128, kcs, 128], BF16) for _ in range(2)]
    osb = [P.sb("ot", [128, 512], out_dtype) for _ in range(4)]
    xa = X.ap().rearrange("(kc p) l -> p kc l", p=128)
    pe_done_at_x = [0, 0]
    bank_free = [0] * 2
    for ti in range(Lw // 128):
        xb = ti % 2
        gp = nc.gpsimd
        P.tt.wait(gp, pe_done_at_x[xb])
        P.dma(f"gp{xb}", xsb[xb].ap(), xa[:, :, ti * 128:(ti + 1) * 128])
        for ns in range(nns):
            cols = min(512, N - ns * 512)
            bk = ns % 2
            pb = P.psum[bk]
            t = nc.tensor
            P.qwait(t, f"gp{xb}")
            P.tv.wait(t, bank_free[bk])
            for kc in range(kcs):
                ins = t.matmul(pb.ap()[:, :cols],
                               xsb[xb].ap()[:, kc, :],
                               wsb.ap()[:, kc, ns * 512:ns * 512 + cols],
                               start=(kc == 0), stop=(kc == kcs - 1))
            mm_n = P.tt.inc(ins)
            j = (ti * nns + ns) % 4
            ob = osb[j]
            v = nc.vector
            P.tt.wait(v, mm_n)
            P.qwait(v, f"st{j}")
            ev = v.tensor_copy(ob.ap()[:, :cols], pb.ap()[:, :cols])
            bank_free[bk] = P.tv.inc(ev)
            sy = nc.sync
            P.tv.wait(sy)
            P.dma(f"st{j}", out.ap()[ti * 128:(ti + 1) * 128, ns * 512:ns * 512 + cols],
                  ob.ap()[:, :cols])
        pe_done_at_x[xb] = P.tt.n
    P.barrier()
    P.sb = P_sb_saved
    _stg.close()


def fm_elemwise(P, ins, outs, R, emit, Lw=None, TTT=2048, in_dtypes=None, out_dtypes=None):
    Lw = L if Lw is None else Lw
    TTT = min(TTT, Lw)
    nc = P.nc
    rch = _ceil(R, 128)
    ntt = Lw // TTT
    if in_dtypes is None:
        in_dtypes = [F32] * len(ins)
    if out_dtypes is None:
        out_dtypes = [F32] * len(outs)
    _stg = contextlib.ExitStack()
    def _sb(name, shape, dtype=F32):
        return _stg.enter_context(nc.sbuf_tensor(P.uname(name), list(shape), dtype))
    isb = [[_sb("ew_i", [128, TTT], dt) for dt in in_dtypes] for _ in range(2)]
    osb = [[_sb("ew_o", [128, TTT], dt) for dt in out_dtypes] for _ in range(2)]
    done = [0, 0]
    for idx in range(rch * ntt):
        rc, tt = divmod(idx, ntt)
        rows = min(128, R - rc * 128)
        bb = idx % 2
        gp = nc.gpsimd
        P.tv.wait(gp, done[bb])
        for k, x in enumerate(ins):
            P.dma(f"gp{bb}", isb[bb][k].ap()[:rows, :],
                  x.ap()[rc * 128:rc * 128 + rows, tt * TTT:(tt + 1) * TTT])
        v = nc.vector
        P.qwait(v, f"gp{bb}")
        P.qwait(v, f"st{bb}")
        P.qwait(nc.scalar, f"gp{bb}")
        P.qwait(nc.scalar, f"st{bb}")
        last = emit(v, [o.ap()[:rows, :] for o in osb[bb]], [i.ap()[:rows, :] for i in isb[bb]])
        done[bb] = P.tv.inc(last)
        sy = nc.sync
        P.tv.wait(sy)
        for k, o in enumerate(outs):
            P.dma(f"st{bb}", o.ap()[rc * 128:rc * 128 + rows, tt * TTT:(tt + 1) * TTT],
                  osb[bb][k].ap()[:rows, :])
    P.barrier()
    _stg.close()


def build_p1():
    P = Prog()
    x = P.dram("x", [L, DM], F32, kind="ExternalInput")
    xfm = P.dram("xfm", [DM, L], F32, kind="ExternalOutput")
    xnfm = P.dram("xnfm", [DM, L], F32, kind="ExternalOutput")
    emit_p1(P, x, xfm, xnfm)
    return P


def emit_p1(P, x, xfm, xnfm):
    nc = P.nc
    idf = P.const("identf", np.eye(128, dtype=np.float32))
    id_s = P.sb("id_s", [128, 128], F32)
    P.dma("sync", id_s.ap(), idf.ap())
    eps_c = P.fconst(1e-5)

    xt = [P.sb("xt", [128, DM], F32) for _ in range(2)]
    xnb = [P.sb("xnb", [128, DM], F32) for _ in range(2)]
    stat = P.sb("stat", [128, 8], F32)
    ofm = [P.sb("ofm", [128, DM], F32) for _ in range(2)]
    onb = [P.sb("onb", [128, DM], F32) for _ in range(2)]
    P.barrier()

    cons = [0, 0]
    prev_ev = 0
    prev_xn = 0
    for ti in range(L // 128):
        bb = ti % 2
        gp = nc.gpsimd
        P.tt.wait(gp, cons[bb])
        P.dma(f"gp{bb}", xt[bb].ap(), x.ap()[ti * 128:(ti + 1) * 128, :])
        v, s, t = nc.vector, nc.scalar, nc.tensor
        P.qwait(v, f"gp{bb}")
        P.ts.wait(v, prev_xn)
        v.reduce_sum(stat.ap()[:, 0:1], xt[bb].ap(), axis=AX.X)
        v.drain()
        P.qwait(s, f"gp{bb}")
        P.tt.wait(s, cons[bb])
        sq_n = P.ts.inc(s.activation(xnb[bb].ap(), xt[bb].ap(), AF.Square))
        P.ts.wait(v, sq_n)
        v.reduce_sum(stat.ap()[:, 1:2], xnb[bb].ap(), axis=AX.X)
        v.drain()
        v.tensor_scalar(out=stat.ap()[:, 0:1], in0=stat.ap()[:, 0:1], scalar1=1.0 / DM,
                        scalar2=None, op0=ALU.mult)
        v.drain()
        v.tensor_tensor(stat.ap()[:, 2:3], stat.ap()[:, 0:1], stat.ap()[:, 0:1], ALU.mult)
        v.drain()
        var_n = P.tv.inc(v.scalar_tensor_tensor(out=stat.ap()[:, 2:3], in0=stat.ap()[:, 1:2],
                               scalar=1.0 / DM, in1=stat.ap()[:, 2:3],
                               op0=ALU.mult, op1=ALU.subtract))
        v.drain()
        P.tv.wait(s, var_n)
        sd_n = P.ts.inc(s.activation(stat.ap()[:, 3:4], stat.ap()[:, 2:3], AF.Sqrt, bias=eps_c))
        P.ts.wait(v, sd_n)
        v.reciprocal(stat.ap()[:, 4:5], stat.ap()[:, 3:4])
        v.drain()
        r_n = P.tv.inc(v.scalar_tensor_tensor(out=stat.ap()[:, 5:6], in0=stat.ap()[:, 0:1],
                               scalar=-1.0, in1=stat.ap()[:, 4:5],
                               op0=ALU.mult, op1=ALU.mult))
        v.drain()
        P.tv.wait(s, r_n)
        xn_n = P.ts.inc(s.activation(xnb[bb].ap(), xt[bb].ap(), AF.Identity,
                     scale=stat.ap()[:, 4:5], bias=stat.ap()[:, 5:6]))
        prev_xn = xn_n
        P.ts.wait(t, xn_n)
        P.tv.wait(t, prev_ev)
        tr_n = 0
        for half in range(2):
            for c in range(4):
                cc = half * 4 + c
                pbn = P.psum[half]
                pbx = P.psum[2 + half]
                i1 = t.transpose(pbn.ap()[:, c * 128:(c + 1) * 128],
                                 xnb[bb].ap()[:, cc * 128:(cc + 1) * 128], id_s.ap())
                i2 = t.transpose(pbx.ap()[:, c * 128:(c + 1) * 128],
                                 xt[bb].ap()[:, cc * 128:(cc + 1) * 128], id_s.ap())
        cons[bb] = P.tt.inc(i2)
        P.tt.wait(v)
        P.qwait(v, f"st{bb}")
        v.tensor_copy(onb[bb].ap()[:, 0:512], P.psum[0].ap())
        v.tensor_copy(onb[bb].ap()[:, 512:1024], P.psum[1].ap())
        v.tensor_copy(ofm[bb].ap()[:, 0:512], P.psum[2].ap())
        ev_n = P.tv.inc(v.tensor_copy(ofm[bb].ap()[:, 512:1024], P.psum[3].ap()))
        prev_ev = ev_n
        sy = nc.sync
        P.tv.wait(sy)
        P.dma("st", xnfm.ap().rearrange("(c p) l -> p c l", p=128)[:, :, ti * 128:(ti + 1) * 128],
              onb[bb].ap().rearrange("p (c t) -> p c t", t=128))
        P.dma("st", xfm.ap().rearrange("(c p) l -> p c l", p=128)[:, :, ti * 128:(ti + 1) * 128],
              ofm[bb].ap().rearrange("p (c t) -> p c t", t=128))
    P.barrier()


def build_p2():
    P = Prog()
    xnfm = P.dram("xnfm", [DM, L], F32, kind="ExternalInput")
    w_in = P.dram("w_in", [DM, 4480], F32, kind="ExternalInput")
    projfm = P.dram("projfm", [4480, L], F32, kind="ExternalOutput")
    xs_tok = P.dram("xs_tok", [L, 2048], BF16, kind="ExternalOutput")
    emit_p2(P, xnfm, w_in, projfm, xs_tok)
    return P


def emit_p2(P, xnfm, w_in, projfm, xs_tok):
    nc = P.nc
    fm_matmul(P, xnfm, w_in, projfm, K=DM, N=4480)
    kcs = 8
    wsb = P.sb("wsm", [128, kcs, 128], F32)
    P.dma("sync", wsb.ap(), w_in.ap().rearrange("(kc p) n -> p kc n", p=128)[:, :, 4352:4480])
    P.barrier()
    xsb = [P.sb("xsm", [128, kcs, 512], F32) for _ in range(2)]
    osb = [P.sb("osm", [128, 512], F32) for _ in range(2)]
    pe_done = [0, 0]; bank_free = [0, 0]
    xa = xnfm.ap().rearrange("(kc p) l -> p kc l", p=128)
    for tt in range(L // 512):
        bb = tt % 2
        gp = nc.gpsimd
        P.tt.wait(gp, pe_done[bb])
        P.dma(f"gp{bb}", xsb[bb].ap(), xa[:, :, tt * 512:(tt + 1) * 512])
        t = nc.tensor
        P.qwait(t, f"gp{bb}")
        P.tv.wait(t, bank_free[bb])
        pb = P.psum[bb]
        for kc in range(kcs):
            ins = t.matmul(pb.ap(), wsb.ap()[:, kc, :], xsb[bb].ap()[:, kc, :],
                           start=(kc == 0), stop=(kc == kcs - 1))
        mm_n = P.tt.inc(ins)
        v = nc.vector
        P.tt.wait(v, mm_n)
        P.qwait(v, f"st{bb}")
        bank_free[bb] = P.tv.inc(v.tensor_copy(osb[bb].ap(), pb.ap()))
        sy = nc.sync
        P.tv.wait(sy)
        P.dma(f"st{bb}", projfm.ap()[4352:4480, tt * 512:(tt + 1) * 512], osb[bb].ap())
        pe_done[bb] = P.tt.n
    P.barrier()

    class _View:
        def __init__(self, ap, name):
            self._ap = ap; self.name = name
        def ap(self):
            return self._ap
    wv = _View(w_in.ap()[:, 2048:4096], "w_xs")
    tok_matmul(P, xnfm, wv, xs_tok, K=DM, N=2048, out_dtype=BF16)


def kernel(**inputs):
    raise NotImplementedError("under construction")


NH = 32
HD = 64
NST = 128
CH = 128
NCH = L // CH
PI = float(np.pi)


def build_p3(dbg=(), serial=0):
    P = Prog()
    projfm = P.dram("projfm", [4480, L], F32, kind="ExternalInput")
    xs_tok = P.dram("xs_tok", [L, 2048], BF16, kind="ExternalInput")
    dt_bias = P.dram("dt_bias", [32], F32, kind="ExternalInput")
    b_biasT = P.dram("b_biasT", [128, 32], F32, kind="ExternalInput")
    c_biasT = P.dram("c_biasT", [128, 32], F32, kind="ExternalInput")
    b_norm_w = P.dram("b_norm_w", [128], F32, kind="ExternalInput")
    c_norm_w = P.dram("c_norm_w", [128], F32, kind="ExternalInput")
    d_rep = P.dram("d_rep", [128, 16], F32, kind="ExternalInput")
    ygfm = P.dram("ygfm", [2048, L], BF16, kind="ExternalOutput")
    P.fconst(1e-5)
    with P.scope():
        emit_p3(P, projfm, xs_tok, dt_bias, b_biasT, c_biasT, b_norm_w, c_norm_w,
                d_rep, ygfm, dbg=dbg, serial=serial)
    return P


def emit_p3(P, projfm, xs_tok, dt_bias, b_biasT, c_biasT, b_norm_w, c_norm_w,
            d_rep, ygfm, dbg=(), serial=0):
    nc = P.nc
    v, s, t, g = nc.vector, nc.scalar, nc.tensor, nc.gpsimd

    def dk(name):
        return "ExternalOutput" if name in dbg else "Internal"

    ell_d = P.dram("ell_d", [32, L], F32, kind=dk("ell_d"))
    dt_d = P.dram("dt_d", [32, L], F32, kind=dk("dt_d"))
    c_d = P.dram("c_d", [NH * 32, L], F32, kind=dk("c_d"))
    s_d = P.dram("s_d", [NH * 32, L], F32, kind=dk("s_d"))
    K_d = P.dram("K_d", [128, NH, L + 1], BF16, kind=dk("K_d"))
    Q_d = P.dram("Q_d", [128, NH, L], BF16, kind=dk("Q_d"))
    Qe_d = P.dram("Qe_d", [128, NH, L], BF16, kind=dk("Qe_d"))

    oh = np.zeros((32, 32 * 128), dtype=np.float32)
    for h in range(32):
        oh[h, h * 128:(h + 1) * 128] = 1.0
    ohc = P.const("ohc", oh)
    rep4 = np.zeros((32, 128), dtype=np.float32)
    for r in range(128):
        rep4[r % 32, r] = 1.0
    rep4c = P.const("rep4c", rep4)
    rdt = np.zeros((32, 8 * 128), dtype=np.float32)
    for b in range(8):
        for r in range(128):
            rdt[4 * b + r // 32, b * 128 + r] = 1.0
    rdtc = P.const("rdtc", rdt)
    msk = np.where(np.arange(128)[:, None] <= np.arange(128)[None, :], 0.0, NEG
                   ).astype(np.float32)
    mskc = P.const("mskc", msk)
    onesr = P.const("onesr128", np.ones((1, 128), dtype=np.float32))

    ell_sb = P.sb("ell_sb", [32, L], F32)
    elltok = P.sb("elltok", [128, NCH * 32], F32)
    g1tok = P.sb("g1tok", [128, NCH * 32], F32)
    g2tok = P.sb("g2tok", [128, NCH * 32], F32)
    w1full = P.sb("w1full", [128, NCH * 32], F32)
    w2full = P.sb("w2full", [128, NCH * 32], F32)
    cdb = P.sb("cdb", [128, NCH * 32], F32)
    ones_sb = P.sb("ones_sb", [1, 128], F32)
    id_sb = P.sb("id_sb", [128, 128], F32)
    idb_sb = P.sb("idb_sb", [128, 128], BF16)
    drep_sb = P.sb("drep_sb", [128, 16], F32)
    P.dma("sync", ones_sb.ap(), onesr.ap())
    P.dma("sync", id_sb.ap(), P.const("identf", np.eye(128, dtype=np.float32)).ap())
    P.dma("gp", idb_sb.ap(), P.const("identf", np.eye(128, dtype=np.float32)).ap())
    P.dma("sync", drep_sb.ap(), d_rep.ap())
    eps_c3 = P.fconst(1e-5)
    st_dt = contextlib.ExitStack()
    st_bn = contextlib.ExitStack()

    dt_sb = st_dt.enter_context(nc.sbuf_tensor("dt_sb", [32, L + 128], F32))
    with contextlib.ExitStack() as stg:
        def ssb(name, shape, dtype=F32):
            return stg.enter_context(nc.sbuf_tensor(P.uname(name), list(shape), dtype))
        dd_dt = ssb("dd_dt", [32, L]); dd_A = ssb("dd_A", [32, L])
        trap = ssb("trap", [32, L])
        tmp1 = ssb("tmp1", [32, L]); tmp2 = ssb("tmp2", [32, L])
        g1f = dd_dt; g2f = tmp1; adt = dd_A; beta = trap
        dtb_sb = ssb("dtb_sb", [32, 1])
        P.dma("sync", dd_dt.ap(), projfm.ap()[4352:4384, :])
        P.dma("sync", dd_A.ap(), projfm.ap()[4384:4416, :])
        P.dma("sync", trap.ap(), projfm.ap()[4416:4448, :])
        P.dma("sync", dtb_sb.ap(), dt_bias.ap().unsqueeze(1))
        g.memset(dt_sb.ap()[:, 0:128], 0.0)
        segm = ssb("segm", [32, L])
        P.tg.inc(g.memset(segm.ap(), 1.0))
        P.tg.wait(g)
        for c in range(NCH):
            g.memset(segm.ap()[:, c * CH:c * CH + 1], 0.0)
        P.barrier()

        v.tensor_scalar(out=dd_dt.ap(), in0=dd_dt.ap(), scalar1=dtb_sb.ap(), scalar2=None,
                        op0=ALU.add)
        u_n = P.tv.inc(v.drain())
        P.tv.wait(s, u_n)
        P.ts.wait(s, P.ts.inc(s.activation(tmp1.ap(), dd_dt.ap(), AF.Abs)))
        P.ts.wait(s, P.ts.inc(s.activation(tmp1.ap(), tmp1.ap(), AF.Exp, scale=-1.0)))
        ln_n = P.ts.inc(s.activation(tmp1.ap(), tmp1.ap(), AF.Ln, bias=1.0))
        v.tensor_scalar(out=tmp2.ap(), in0=dd_dt.ap(), scalar1=0.0, scalar2=None, op0=ALU.max)
        v.drain()
        P.ts.wait(v, ln_n)
        v.tensor_tensor(dt_sb.ap()[:, 128:], tmp2.ap(), tmp1.ap(), ALU.add)
        dt_n = P.tv.inc(v.drain())
        P.tv.wait(s, dt_n)
        P.ts.wait(s, P.ts.inc(s.activation(tmp1.ap(), dd_A.ap(), AF.Abs)))
        P.ts.wait(s, P.ts.inc(s.activation(tmp1.ap(), tmp1.ap(), AF.Exp, scale=-1.0)))
        ln2_n = P.ts.inc(s.activation(tmp1.ap(), tmp1.ap(), AF.Ln, bias=1.0))
        v.tensor_scalar(out=tmp2.ap(), in0=dd_A.ap(), scalar1=0.0, scalar2=None, op0=ALU.max)
        v.drain()
        P.ts.wait(v, ln2_n)
        v.tensor_tensor(tmp2.ap(), tmp2.ap(), tmp1.ap(), ALU.add)
        v.drain()
        v.tensor_scalar(out=tmp2.ap(), in0=tmp2.ap(), scalar1=1e-4, scalar2=None, op0=ALU.max)
        v.drain()
        v.scalar_tensor_tensor(out=adt.ap(), in0=tmp2.ap(), scalar=-1.0,
                               in1=dt_sb.ap()[:, 128:], op0=ALU.mult, op1=ALU.mult)
        adt_n = P.tv.inc(v.drain())
        bt_n = P.ts.inc(s.activation(beta.ap(), trap.ap(), AF.Sigmoid))
        P.tv.wait(s, adt_n)
        ea_n = P.ts.inc(s.activation(tmp1.ap(), adt.ap(), AF.Exp))
        P.ts.wait(v, bt_n)
        v.tensor_tensor(g1f.ap(), beta.ap(), dt_sb.ap()[:, 128:], ALU.mult)
        v.tensor_scalar(out=tmp2.ap(), in0=beta.ap(), scalar1=-1.0, scalar2=1.0,
                        op0=ALU.mult, op1=ALU.add)
        v.drain()
        P.ts.wait(v, ea_n)
        v.tensor_tensor(tmp2.ap(), tmp2.ap(), tmp1.ap(), ALU.mult)
        v.drain()
        v.tensor_tensor(g2f.ap(), tmp2.ap(), dt_sb.ap()[:, 127:127 + L], ALU.mult)
        g2_n = P.tv.inc(v.drain())
        v.tensor_tensor_scan(ell_sb.ap(), segm.ap(), adt.ap(), 0.0, ALU.mult, ALU.add)
        ell_n = P.tv.inc(v.drain())
        P.tv.wait(nc.sync, ell_n)
        P.dma("sync", ell_d.ap(), ell_sb.ap())
        P.dma("sync", dt_d.ap(), dt_sb.ap()[:, 128:])

        for c in range(NCH):
            bk = c % 2
            pb = P.psum[bk]
            P.tv.wait(t)
            i1 = t.transpose(pb.ap()[:, 0:32], ell_sb.ap()[:, c * CH:(c + 1) * CH],
                             id_sb.ap()[:32, :32])
            i2 = t.transpose(pb.ap()[:, 32:64], g1f.ap()[:, c * CH:(c + 1) * CH],
                             id_sb.ap()[:32, :32])
            i3 = t.transpose(pb.ap()[:, 64:96], g2f.ap()[:, c * CH:(c + 1) * CH],
                             id_sb.ap()[:32, :32])
            tr_n = P.tt.inc(i3)
            P.tt.wait(v, tr_n)
            v.tensor_copy(elltok.ap()[:, c * 32:(c + 1) * 32], pb.ap()[:, 0:32])
            v.tensor_copy(g1tok.ap()[:, c * 32:(c + 1) * 32], pb.ap()[:, 32:64])
            P.tv.inc(v.tensor_copy(g2tok.ap()[:, c * 32:(c + 1) * 32], pb.ap()[:, 64:96]))
        P.barrier()

        lastell_sb = ssb("lastell", [32, NCH])
        v.tensor_copy(lastell_sb.ap().unsqueeze(2),
                      ell_sb.ap().rearrange("h (c t) -> h c t", t=CH)[:, :, CH - 1:CH])
        le_n = P.tv.inc(v.drain())
        lastellT = ssb("lastellT", [NCH, 32])
        P.tv.wait(t, le_n)
        ltr_n = P.tt.inc(t.transpose(P.psum[0].ap()[:NCH, 0:32], lastell_sb.ap(),
                                     id_sb.ap()[:32, :32]))
        P.tt.wait(v, ltr_n)
        le2_n = P.tv.inc(v.tensor_copy(lastellT.ap(), P.psum[0].ap()[:NCH, 0:32]))
        lastell_d = P.dram("lastell_d", [NCH, 32], F32)
        P.tv.wait(nc.sync, le2_n)
        P.dma("sync", lastell_d.ap(), lastellT.ap())
        P.barrier()
        lerow = ssb("lerow", [1, NCH * 32])
        P.dma("sync", lerow.ap(), lastell_d.ap().rearrange("c h -> (c h)").unsqueeze(0))
        P.barrier()
        for sl in range(_ceil(NCH * 32, 512)):
            c0 = sl * 512
            cols = min(512, NCH * 32 - c0)
            pb = P.psum[sl % 2]
            P.tv.wait(t)
            mm_n = P.tt.inc(t.matmul(pb.ap()[:, :cols], ones_sb.ap(),
                                     lerow.ap()[:, c0:c0 + cols], start=True, stop=True))
            P.tt.wait(v, mm_n)
            v.scalar_tensor_tensor(out=w1full.ap()[:, c0:c0 + cols],
                                   in0=elltok.ap()[:, c0:c0 + cols], scalar=-1.0,
                                   in1=pb.ap()[:, :cols], op0=ALU.mult, op1=ALU.add)
            ee_n = P.tv.inc(v.tensor_copy(cdb.ap()[:, c0:c0 + cols], pb.ap()[:, :cols]))
            P.tv.wait(s, ee_n)
            s.activation(w1full.ap()[:, c0:c0 + cols], w1full.ap()[:, c0:c0 + cols], AF.Exp)
            P.ts.inc(s.activation(cdb.ap()[:, c0:c0 + cols], cdb.ap()[:, c0:c0 + cols], AF.Exp))
        P.barrier()
        v.tensor_tensor(w2full.ap(), g2tok.ap(), w1full.ap(), ALU.mult)
        v.drain()
        v.tensor_tensor(w1full.ap(), g1tok.ap(), w1full.ap(), ALU.mult)
        v.drain()
        P.barrier()
        if "dbg3_d" in dbg:
            dbg3_d = P.dram("dbg3_d", [128, 6 * NCH * 32], F32, kind="ExternalOutput")
            for i, tsr in enumerate([cdb, w1full, w2full, elltok, g1tok, g2tok]):
                P.dma("sync", dbg3_d.ap()[:, i * NCH * 32:(i + 1) * NCH * 32], tsr.ap())
            P.barrier()

    with contextlib.ExitStack() as stg:
        def ssb(name, shape, dtype=F32):
            return stg.enter_context(nc.sbuf_tensor(P.uname(name), list(shape), dtype))
        rep4_sb = ssb("rep4_sb", [32, 128])
        rdt_sb = ssb("rdt_sb", [32, 8 * 128])
        ang4 = ssb("ang4", [128, L])
        angb = ssb("angb", [128, L])
        thb = ssb("thb", [128, L])
        ones128 = ssb("ones128", [128, L])
        wq = ssb("wq", [128, L])
        wqi = ssb("wqi", [128, L], mybir.dt.int32)
        P.dma("sync", rep4_sb.ap(), rep4c.ap())
        P.dma("sync", rdt_sb.ap(), rdtc.ap())
        g.memset(ones128.ap(), 1.0)
        P.barrier()
        ang_sb = ssb("ang_keep", [32, L])
        P.dma("sync", ang_sb.ap(), projfm.ap()[4448:4480, :])
        P.barrier()
        for sl in range(L // 512):
            pb = P.psum[sl % 2]
            P.tv.wait(t)
            mm_n = P.tt.inc(t.matmul(pb.ap(), rep4_sb.ap(),
                                     ang_sb.ap()[:, sl * 512:(sl + 1) * 512],
                                     start=True, stop=True))
            P.tt.wait(v, mm_n)
            P.tv.inc(v.tensor_copy(ang4.ap()[:, sl * 512:(sl + 1) * 512], pb.ap()))
        P.barrier()
        for b in range(8):
            for sl in range(L // 512):
                pb = P.psum[sl % 2]
                P.tv.wait(t)
                mm_n = P.tt.inc(t.matmul(pb.ap(), rdt_sb.ap()[:, b * 128:(b + 1) * 128],
                                         dt_sb.ap()[:, 128 + sl * 512:128 + (sl + 1) * 512],
                                         start=True, stop=True))
                P.tt.wait(v, mm_n)
                P.tv.inc(v.tensor_tensor(angb.ap()[:, sl * 512:(sl + 1) * 512],
                                         ang4.ap()[:, sl * 512:(sl + 1) * 512],
                                         pb.ap(), ALU.mult))
            v.drain()
            v.tensor_tensor_scan(thb.ap(), ones128.ap(), angb.ap(), 0.0, ALU.mult, ALU.add)
            v.drain()
            for trig in range(2):
                off = PI / 2 if trig == 0 else 0.0
                c1 = off + 2 * PI * 64 - PI
                v.tensor_scalar(out=wq.ap(), in0=thb.ap(), scalar1=c1,
                                scalar2=1.0 / (2 * PI), op0=ALU.add, op1=ALU.mult)
                v.drain()
                v.tensor_copy(wqi.ap(), wq.ap())
                v.drain()
                v.scalar_tensor_tensor(out=wq.ap(), in0=wqi.ap(), scalar=-1.0,
                                       in1=wq.ap(), op0=ALU.mult, op1=ALU.add)
                v.drain()
                v.tensor_scalar(out=wq.ap(), in0=wq.ap(), scalar1=-0.499999,
                                scalar2=0.499999, op0=ALU.max, op1=ALU.min)
                rr_n = P.tv.inc(v.drain())
                P.tv.wait(s, rr_n)
                tn = P.ts.inc(s.activation(angb.ap(), wq.ap(), AF.Sin, scale=2 * PI))
                P.ts.wait(nc.sync, tn)
                dd = c_d if trig == 0 else s_d
                P.dma("sync", dd.ap()[b * 128:(b + 1) * 128, :], angb.ap())
                P.qwait(v, "sync")
        P.barrier()
    st_dt.close()

    def _bn_sb(name, shape, dtype=F32):
        return st_bn.enter_context(nc.sbuf_tensor(P.uname(name), list(shape), dtype))
    Bn = _bn_sb("Bn", [128, L])
    Cn = _bn_sb("Cn", [128, L])
    B12n = _bn_sb("B12n", [64, L])
    Bswn = _bn_sb("Bswn", [128, L])
    Cswn = _bn_sb("Cswn", [128, L])
    bb_sb = _bn_sb("bb_sb", [128, 32])
    cb_sb = _bn_sb("cb_sb", [128, 32])
    bbn_sb = _bn_sb("bbn_sb", [128, 32])
    cbn_sb = _bn_sb("cbn_sb", [128, 32])
    bbsw_sb = _bn_sb("bbsw_sb", [64, 32])
    cbsw_sb = _bn_sb("cbsw_sb", [64, 32])
    with contextlib.ExitStack() as stg:
        def ssb(name, shape, dtype=F32):
            return stg.enter_context(nc.sbuf_tensor(P.uname(name), list(shape), dtype))
        Bg = ssb("Bg", [128, L]); sq = ssb("sq", [128, L])
        nw_sb = ssb("nw_sb", [128, 2])
        rrow = ssb("rrow", [1, L])
        onescol = ssb("onescol", [128, 1])
        g.memset(onescol.ap(), 1.0)
        eps_b = eps_c3
        P.dma("sync", Bg.ap(), projfm.ap()[4096:4224, :])
        P.dma("sync", nw_sb.ap()[:, 0:1], b_norm_w.ap().unsqueeze(1))
        P.dma("sync", nw_sb.ap()[:, 1:2], c_norm_w.ap().unsqueeze(1))
        P.dma("sync", bb_sb.ap(), b_biasT.ap())
        P.dma("sync", cb_sb.ap(), c_biasT.ap())
        P.barrier()
        for which in range(2):
            dst = Bn if which == 0 else Cn
            sq_n = P.ts.inc(s.activation(sq.ap(), Bg.ap(), AF.Square))
            P.ts.wait(t, sq_n)
            for sl in range(L // 512):
                pb = P.psum[sl % 2]
                P.ts.wait(t)
                P.tv.wait(t)
                mm_n = P.tt.inc(t.matmul(pb.ap()[:1, :], onescol.ap(),
                                         sq.ap()[:, sl * 512:(sl + 1) * 512],
                                         start=True, stop=True))
                P.tt.wait(s, mm_n)
                sd_n = P.ts.inc(s.activation(rrow.ap()[:, sl * 512:(sl + 1) * 512],
                                             pb.ap()[:1, :], AF.Sqrt,
                                             scale=1.0 / 128, bias=eps_b[0:1, :]))
                P.ts.wait(v, sd_n)
                v.reciprocal(rrow.ap()[:, sl * 512:(sl + 1) * 512],
                             rrow.ap()[:, sl * 512:(sl + 1) * 512])
                rc_n = P.tv.inc(v.drain())
                P.tv.wait(t, rc_n)
                mm2_n = P.tt.inc(t.matmul(pb.ap()[:, :], ones_sb.ap(),
                                          rrow.ap()[:, sl * 512:(sl + 1) * 512],
                                          start=True, stop=True))
                P.tt.wait(v, mm2_n)
                P.tv.inc(v.scalar_tensor_tensor(
                    out=dst.ap()[:, sl * 512:(sl + 1) * 512],
                    in0=Bg.ap()[:, sl * 512:(sl + 1) * 512],
                    scalar=nw_sb.ap()[:, which:which + 1],
                    in1=pb.ap()[:, :], op0=ALU.mult, op1=ALU.mult))
                P.tv.inc(v.drain())
            P.barrier()
            if which == 0:
                P.dma("sync", Bg.ap(), projfm.ap()[4224:4352, :])
                P.barrier()
        v.tensor_scalar(out=B12n.ap(), in0=Bn.ap()[0:64, :], scalar1=-1.0,
                        scalar2=None, op0=ALU.mult)
        v.tensor_copy(Bswn.ap()[0:32, :], Bn.ap()[32:64, :])
        v.tensor_scalar(out=Bswn.ap()[32:64, :], in0=Bn.ap()[0:32, :], scalar1=-1.0,
                        scalar2=None, op0=ALU.mult)
        v.tensor_copy(Cswn.ap()[0:32, :], Cn.ap()[32:64, :])
        v.tensor_scalar(out=Cswn.ap()[32:64, :], in0=Cn.ap()[0:32, :], scalar1=-1.0,
                        scalar2=None, op0=ALU.mult)
        v.drain()
        v.tensor_scalar(out=Bn.ap()[0:64, :], in0=Cn.ap()[0:64, :], scalar1=-1.0,
                        scalar2=None, op0=ALU.mult)
        v.tensor_scalar(out=bbn_sb.ap(), in0=bb_sb.ap(), scalar1=-1.0, scalar2=None, op0=ALU.mult)
        v.tensor_scalar(out=cbn_sb.ap(), in0=cb_sb.ap(), scalar1=-1.0, scalar2=None, op0=ALU.mult)
        v.tensor_copy(bbsw_sb.ap()[0:32, :], bb_sb.ap()[32:64, :])
        v.tensor_scalar(out=bbsw_sb.ap()[32:64, :], in0=bb_sb.ap()[0:32, :], scalar1=-1.0,
                        scalar2=None, op0=ALU.mult)
        v.tensor_copy(cbsw_sb.ap()[0:32, :], cb_sb.ap()[32:64, :])
        v.tensor_scalar(out=cbsw_sb.ap()[32:64, :], in0=cb_sb.ap()[0:32, :], scalar1=-1.0,
                        scalar2=None, op0=ALU.mult)
        v.drain()
        P.barrier()

    with contextlib.ExitStack() as stg:
        def ssb(name, shape, dtype=F32):
            return stg.enter_context(nc.sbuf_tensor(P.uname(name), list(shape), dtype))
        cdup = ssb("cdup", [64, L]); sdup = ssb("sdup", [64, L])
        Kf = ssb("Kf", [128, L]); Qf = ssb("Qf", [128, L])
        lrow = [ssb("lrow", [1, 512]) for _ in range(2)]
        qe_bf = ssb("qe_bf", [128, L], BF16)
        zcol_sb = ssb("zcol", [128, 1], BF16)
        P.tg.inc(g.memset(zcol_sb.ap(), 0.0))
        P.tg.wait(nc.scalar)
        cons_v = 0
        st_gp = 0
        st_sync = 0
        for h in range(NH):
            P.tv.wait(nc.sync, cons_v)
            P.dma("sync", cdup.ap()[0:32, :], c_d.ap()[h * 32:(h + 1) * 32, :])
            P.dma("sync", cdup.ap()[32:64, :], c_d.ap()[h * 32:(h + 1) * 32, :])
            P.dma("sync", sdup.ap()[0:32, :], s_d.ap()[h * 32:(h + 1) * 32, :])
            P.dma("sync", sdup.ap()[32:64, :], s_d.ap()[h * 32:(h + 1) * 32, :])

            P.qwait(v, "sync")
            P.qwait(v, "st")
            P.qwait(v, "gp")
            v.scalar_tensor_tensor(out=Kf.ap()[0:64, :], in0=B12n.ap(),
                                   scalar=bbn_sb.ap()[0:64, h:h + 1], in1=cdup.ap(),
                                   op0=ALU.add, op1=ALU.mult)
            v.scalar_tensor_tensor(out=Qf.ap()[0:64, :], in0=Bswn.ap()[0:64, :],
                                   scalar=bbsw_sb.ap()[:, h:h + 1], in1=sdup.ap(),
                                   op0=ALU.add, op1=ALU.mult)
            v.drain()
            v.tensor_tensor(Kf.ap()[0:64, :], Kf.ap()[0:64, :], Qf.ap()[0:64, :], ALU.add)
            v.tensor_scalar(out=Kf.ap()[64:128, :], in0=Bn.ap()[64:128, :],
                            scalar1=bb_sb.ap()[64:128, h:h + 1], scalar2=None, op0=ALU.add)
            k_n = P.tv.inc(v.drain())
            P.tv.wait(g, k_n)
            kst = P.dma("gp", K_d.ap()[:, h:h + 1, 1:L + 1], Kf.ap().unsqueeze(1))
            nc.scalar.dma_start(out=K_d.ap()[:, h:h + 1, 0:1], in_=zcol_sb.ap().unsqueeze(1),
                                allow_slow_non_contiguous=True).then_inc(P.q["act"][1], 16)
            P.q["act"][2] += 1
            v.scalar_tensor_tensor(out=Qf.ap()[0:64, :], in0=Bn.ap()[0:64, :],
                                   scalar=cbn_sb.ap()[0:64, h:h + 1], in1=cdup.ap(),
                                   op0=ALU.add, op1=ALU.mult)
            v.drain()
            v.scalar_tensor_tensor(out=cdup.ap(), in0=Cswn.ap()[0:64, :],
                                   scalar=cbsw_sb.ap()[:, h:h + 1], in1=sdup.ap(),
                                   op0=ALU.add, op1=ALU.mult)
            v.drain()
            v.tensor_tensor(Qf.ap()[0:64, :], Qf.ap()[0:64, :], cdup.ap(), ALU.add)
            v.tensor_scalar(out=Qf.ap()[64:128, :], in0=Cn.ap()[64:128, :],
                            scalar1=cb_sb.ap()[64:128, h:h + 1], scalar2=None, op0=ALU.add)
            q_n = P.tv.inc(v.drain())
            P.tv.wait(g, q_n)
            st_gp = P.dma("gp", Q_d.ap()[:, h:h + 1, :], Qf.ap().unsqueeze(1))
            for sl in range(L // 512):
                pb = P.psum[sl % 2]
                P.tt.wait(nc.sync)
                P.dma(f"ld{sl % 2}", lrow[sl % 2].ap(),
                      ell_d.ap()[h:h + 1, sl * 512:(sl + 1) * 512])
                P.qwait(t, f"ld{sl % 2}")
                P.tv.wait(t)
                mm_n = P.tt.inc(t.matmul(pb.ap(), ones_sb.ap(), lrow[sl % 2].ap(),
                                         start=True, stop=True))
                P.tt.wait(s, mm_n)
                ex_n = P.ts.inc(s.activation(pb.ap(), pb.ap(), AF.Exp))
                P.ts.wait(v, ex_n)
                P.tv.inc(v.tensor_tensor(qe_bf.ap()[:, sl * 512:(sl + 1) * 512],
                                         Qf.ap()[:, sl * 512:(sl + 1) * 512],
                                         pb.ap(), ALU.mult))
            qe_n = P.tv.n
            P.tv.wait(nc.sync, qe_n)
            st_sync = P.dma("st", Qe_d.ap()[:, h:h + 1, :], qe_bf.ap().unsqueeze(1))
            cons_v = qe_n
        P.barrier()
    st_bn.close()

    oh_sb = P.sb("oh_sb", [32, 32 * 128], F32)
    msk_sb = P.sb("msk_sb", [128, 128], F32)
    P.dma("sync", oh_sb.ap(), ohc.ap())
    P.dma("sync", msk_sb.ap(), mskc.ap())
    H = P.sb("H", [128, NH * HD], F32)
    Hb = P.sb("Hb", [128, NH * HD], BF16)
    g.memset(H.ap(), 0.0)
    g.memset(Hb.ap(), 0.0)
    Esb = P.sb("Esb", [128, 128], F32)
    Dsb = P.sb("Dsb", [128, 128], F32)
    Ssb1 = P.sb("Ssb1", [128, 128], BF16)
    Ssb2 = P.sb("Ssb2", [128, 128], BF16)
    Ktsb = P.sb("Ktsb", [128, 128], BF16)
    Kptsb = P.sb("Kptsb", [128, 128], BF16)
    xw1 = P.sb("xw1", [128, HD], BF16)
    xw2 = P.sb("xw2", [128, HD], BF16)
    yg1 = P.sb("yg1", [64, 128], F32)
    zsil = P.sb("zsil", [64, 128], F32)
    Kc = [P.sb("Kc", [128, NH, 128], BF16) for _ in range(2)]
    Kpc = [P.sb("Kpc", [128, NH, 128], BF16) for _ in range(2)]
    Qc = [P.sb("Qc", [128, NH, 128], BF16) for _ in range(2)]
    Qec = [P.sb("Qec", [128, NH, 128], BF16) for _ in range(2)]
    xsc = [P.sb("xsc", [128, 2048], BF16) for _ in range(2)]
    xpc = [P.sb("xpc", [128, 2048], BF16) for _ in range(2)]
    zc = [P.sb("zc", [128, 16, 128], BF16) for _ in range(2)]
    xfc = [P.sb("xfc", [128, 16, 128], BF16) for _ in range(2)]
    ygc = [P.sb("ygc", [128, 16, 128], BF16) for _ in range(2)]
    g.memset(xpc[0].ap()[0:1, :], 0.0)
    P.barrier()

    buf_done = [0, 0]
    yg_done = [0, 0]
    bank_free = {0: 0, 1: 0, 2: 0, 3: 0, 4: 0}
    bankbf_free = [0, 0]
    e_free = 0
    d_free = 0
    s_free = 0
    kt_free = 0
    xw_free = 0
    hb_last = 0
    yg1_free = 0
    zs_free = 0

    projz = projfm.ap().rearrange("(rc p) l -> p rc l", p=128)
    DBG2 = "dbg2_d" in dbg
    if DBG2:
        dbg2_d = P.dram("dbg2_d", [128, 16 * 128], F32, kind="ExternalOutput")
        dbgslot = [0]
        dbg_sb = P.sb("dbg_sb", [128, 16 * 128], F32)

    def dbg_cap(src_ap, rows=128, cols=128):
        if not DBG2 or dbgslot[0] >= 16:
            return
        sl = dbgslot[0]; dbgslot[0] += 1
        cp_n = P.tv.inc(v.tensor_copy(dbg_sb.ap()[:rows, sl * 128:sl * 128 + cols], src_ap))
        P.tv.wait(nc.sync, cp_n)
        P.dma("sync", dbg2_d.ap()[:rows, sl * 128:sl * 128 + cols],
              dbg_sb.ap()[:rows, sl * 128:sl * 128 + cols])

    def load_chunk(c, bi):
        ch_ld = f"ld{bi}"
        for hs in range(0, NH, 8):
            P.dma(ch_ld, Kc[bi].ap()[:, hs:hs + 8, :],
                  K_d.ap()[:, hs:hs + 8, 1 + c * CH:1 + (c + 1) * CH])
            P.dma(ch_ld, Qc[bi].ap()[:, hs:hs + 8, :],
                  Q_d.ap()[:, hs:hs + 8, c * CH:(c + 1) * CH])
            P.dma(ch_ld, Qec[bi].ap()[:, hs:hs + 8, :],
                  Qe_d.ap()[:, hs:hs + 8, c * CH:(c + 1) * CH])
            P.dma(ch_ld, Kpc[bi].ap()[:, hs:hs + 8, :],
                  K_d.ap()[:, hs:hs + 8, c * CH:(c + 1) * CH])
        if c == 0:
            P.dma(ch_ld, xpc[bi].ap()[1:128, :], xs_tok.ap()[0:CH - 1, :])
        else:
            P.dma(ch_ld, xpc[bi].ap(), xs_tok.ap()[c * CH - 1:(c + 1) * CH - 1, :])
        P.dma(ch_ld, xsc[bi].ap(), xs_tok.ap()[c * CH:(c + 1) * CH, :])
        P.dma(f"gp{bi}", zc[bi].ap(), projz[:, 0:16, c * CH:(c + 1) * CH])
        P.dma(f"gp{bi}", xfc[bi].ap(), projz[:, 16:32, c * CH:(c + 1) * CH])

    load_chunk(0, 0)
    for c in range(NCH):
        bi = c % 2
        if c + 1 < NCH:
            P.tt.wait(nc.sync, buf_done[1 - bi])
            P.tt.wait(g, buf_done[1 - bi])
            P.tv.wait(nc.sync, yg_done[1 - bi])
            P.tv.wait(g, yg_done[1 - bi])
            load_chunk(c + 1, 1 - bi)
        if serial == 2:
            P.barrier()
        for h in range(NH):
            if serial == 1:
                P.barrier()
            col = c * 32 + h
            hc, off = h // 2, (h % 2) * 64
            P.qwait(t, f"ld{bi}")
            P.tv.wait(t, max(bank_free[0], bank_free[1], bank_free[2]))
            t.matmul(P.psum[0].ap()[:, 0:128], Kc[bi].ap()[:, h, :], Qc[bi].ap()[:, h, :],
                     start=True, stop=True)
            t.matmul(P.psum[1].ap()[:, 0:128], Kpc[bi].ap()[:, h, :], Qc[bi].ap()[:, h, :],
                     start=True, stop=True)
            a_n = P.tt.inc(t.matmul(P.psum[2].ap()[:, 0:128],
                                    oh_sb.ap()[:, h * 128:(h + 1) * 128],
                                    ell_sb.ap()[:, c * CH:(c + 1) * CH],
                                    start=True, stop=True))
            P.tt.wait(v, a_n)
            P.ts.wait(v, e_free)
            e_n = P.tv.inc(v.scalar_tensor_tensor(
                out=Esb.ap(), in0=P.psum[2].ap()[:, 0:128], scalar=elltok.ap()[:, col:col + 1],
                op0=ALU.subtract, in1=msk_sb.ap(), op1=ALU.add))

            P.tv.wait(s, max(e_n, d_free))
            d_n = P.ts.inc(s.activation(Dsb.ap(), Esb.ap(), AF.Exp))
            if (c, h) == (0, 0):
                P.ts.wait(v, d_n)
                dbg_cap(Dsb.ap())
            P.ts.wait(t, max(bankbf_free[0], bankbf_free[1]))
            t.transpose(P.psum_bf[0].ap()[:, 0:128], Kc[bi].ap()[:, h, :], idb_sb.ap())
            tr_n = P.tt.inc(t.transpose(P.psum_bf[1].ap()[:, 0:128], Kpc[bi].ap()[:, h, :],
                                        idb_sb.ap()))
            P.tt.wait(s, max(tr_n, kt_free))
            s.copy(Ktsb.ap(), P.psum_bf[0].ap()[:, 0:128])
            kte_n = P.ts.inc(s.copy(Kptsb.ap(), P.psum_bf[1].ap()[:, 0:128]))
            e_free = kte_n
            P.ts.wait(v, d_n)
            P.tt.wait(v, s_free)
            v.scalar_tensor_tensor(out=Ssb1.ap(), in0=Dsb.ap(), scalar=g1tok.ap()[:, col:col + 1],
                                   in1=P.psum[0].ap()[:, 0:128], op0=ALU.mult, op1=ALU.mult)
            ss_n = P.tv.inc(v.scalar_tensor_tensor(
                out=Ssb2.ap(), in0=Dsb.ap(), scalar=g2tok.ap()[:, col:col + 1],
                in1=P.psum[1].ap()[:, 0:128], op0=ALU.mult, op1=ALU.mult))
            bank_free[2] = ss_n
            P.qwait(v, f"ld{bi}")
            P.tt.wait(v, xw_free)
            v.tensor_scalar(out=xw1.ap(), in0=xsc[bi].ap()[:, h * HD:(h + 1) * HD],
                            scalar1=w1full.ap()[:, col:col + 1], scalar2=None, op0=ALU.mult)
            xw_n = P.tv.inc(v.tensor_scalar(
                out=xw2.ap(), in0=xpc[bi].ap()[:, h * HD:(h + 1) * HD],
                scalar1=w2full.ap()[:, col:col + 1], scalar2=None, op0=ALU.mult))
            bank_free[0] = xw_n
            bank_free[1] = xw_n
            d_free = xw_n
            P.ts.wait(t, kte_n)
            P.tv.wait(t, max(xw_n, bank_free[3]))
            t.matmul(P.psum[3].ap()[:, 0:HD], Ktsb.ap(), xw1.ap(), start=True, stop=False)
            ws_n = P.tt.inc(t.matmul(P.psum[3].ap()[:, 0:HD], Kptsb.ap(), xw2.ap(),
                                     start=False, stop=True))
            kt_free = ws_n
            xw_free = ws_n
            P.tv.wait(t, max(ss_n, bank_free[4], hb_last))
            t.matmul(P.psum[4].ap()[:HD, 0:128], xsc[bi].ap()[:, h * HD:(h + 1) * HD],
                     Ssb1.ap(), start=True, stop=False)
            t.matmul(P.psum[4].ap()[:HD, 0:128], xpc[bi].ap()[:, h * HD:(h + 1) * HD],
                     Ssb2.ap(), start=False, stop=False)
            y_n = P.tt.inc(t.matmul(P.psum[4].ap()[:HD, 0:128], Hb.ap()[:, h * HD:(h + 1) * HD],
                                    Qec[bi].ap()[:, h, :], start=False, stop=True))
            s_free = y_n
            if (c, h) == (0, 0):
                P.tt.wait(v, y_n)
                dbg_cap(P.psum[4].ap()[:HD, 0:128], HD, 128)
                dbg_cap(Ssb1.ap())
                dbg_cap(Ssb2.ap())
                dbg_cap(Qec[bi].ap()[:, 0, :])
                dbg_cap(xsc[bi].ap()[:, 0:128])
            P.tt.wait(v, y_n)
            v.scalar_tensor_tensor(out=Hb.ap()[:, h * HD:(h + 1) * HD],
                                   in0=H.ap()[:, h * HD:(h + 1) * HD],
                                   scalar=cdb.ap()[:, col:col + 1],
                                   in1=P.psum[3].ap()[:, 0:HD], op0=ALU.mult, op1=ALU.add)
            v.drain()
            hb_last = P.tv.n
            hf_n = P.tv.inc(v.scalar_tensor_tensor(
                out=H.ap()[:, h * HD:(h + 1) * HD], in0=H.ap()[:, h * HD:(h + 1) * HD],
                scalar=cdb.ap()[:, col:col + 1], in1=P.psum[3].ap()[:, 0:HD],
                op0=ALU.mult, op1=ALU.add))
            P.qwait(s, f"gp{bi}")
            P.tv.wait(s, zs_free)
            zs_n = P.ts.inc(s.activation(zsil.ap(), zc[bi].ap()[off:off + 64, hc, :], AF.Silu))
            bankbf_free[0] = zs_n
            bankbf_free[1] = zs_n
            P.tt.wait(v, y_n)
            P.qwait(v, f"gp{bi}")
            yg1_n = P.tv.inc(v.scalar_tensor_tensor(
                out=yg1.ap(), in0=xfc[bi].ap()[off:off + 64, hc, :],
                scalar=drep_sb.ap()[off:off + 64, hc:hc + 1],
                in1=P.psum[4].ap()[:HD, 0:128], op0=ALU.mult, op1=ALU.add))
            bank_free[3] = yg1_n
            v.drain()
            P.ts.wait(v, zs_n)
            P.qwait(v, f"st{bi}")
            ygtt_n = P.tv.inc(v.tensor_tensor(ygc[bi].ap()[off:off + 64, hc, :],
                                              yg1.ap(), zsil.ap(), ALU.mult))
            bank_free[4] = ygtt_n
            yg1_free = ygtt_n
            zs_free = P.tv.inc(v.drain())
        buf_done[bi] = P.tt.n
        yg_done[bi] = P.tv.n

        P.tv.wait(nc.sync, yg_done[bi])
        P.dma(f"st{bi}", ygfm.ap().rearrange("(rc p) l -> p rc l", p=128)[:, :, c * CH:(c + 1) * CH],
              ygc[bi].ap())
    P.barrier()


def build_p4():
    P = Prog()
    ygfm = P.dram("ygfm", [2048, L], BF16, kind="ExternalInput")
    w_out = P.dram("w_out", [2048, DM], F32, kind="ExternalInput")
    sfm = P.dram("sfm", [DM, L], F32, kind="ExternalOutput")
    fm_matmul(P, ygfm, w_out, sfm, K=2048, N=DM)
    return P


class _View:
    def __init__(self, ap, name):
        self._ap = ap
        self.name = name

    def ap(self):
        return self._ap


def fm_layernorm(P, X, out, lg=None, lb=None, eps=1e-5, Lw=None, R=DM, out_dtype=BF16):
    Lw = L if Lw is None else Lw
    nc = P.nc
    v, s, t = nc.vector, nc.scalar, nc.tensor
    kcs = R // 128
    eps_c = P.fconst(eps)
    ones_row(P)
    _stg = contextlib.ExitStack()
    def ssb(name, shape, dtype=F32):
        return _stg.enter_context(nc.sbuf_tensor(P.uname(name), list(shape), dtype))
    onescol = ssb("ln_one", [128, 1])
    P.tg.inc(nc.gpsimd.memset(onescol.ap(), 1.0))
    gcol = lcol = None
    if lg is not None:
        gcol = ssb("ln_g", [128, kcs])
        for i in range(kcs):
            P.dma("sync", gcol.ap()[:, i:i + 1], lg.ap()[i * 128:(i + 1) * 128].unsqueeze(1))
    if lb is not None:
        lcol = ssb("ln_b", [128, kcs])
        for i in range(kcs):
            P.dma("sync", lcol.ap()[:, i:i + 1], lb.ap()[i * 128:(i + 1) * 128].unsqueeze(1))
    xsb = [ssb("ln_x", [128, kcs, 512]) for _ in range(2)]
    sq = ssb("ln_sq", [128, kcs, 512])
    row = ssb("ln_row", [1, 512])
    row2 = ssb("ln_row2", [1, 512])
    usb = [ssb("ln_u", [128, 512]) for _ in range(2)]
    osb = [ssb("ln_o", [128, 512], out_dtype) for _ in range(4)]
    xa = X.ap().rearrange("(kc p) l -> p kc l", p=128)
    P.barrier()
    done = [0, 0]
    for tt in range(Lw // 512):
        bb = tt % 2
        gp = nc.gpsimd
        P.tv.wait(gp, done[bb])
        P.dma(f"gp{bb}", xsb[bb].ap(), xa[:, :, tt * 512:(tt + 1) * 512])
        P.qwait(t, f"gp{bb}")
        P.ts.wait(t)
        P.tv.wait(t)
        for kc in range(kcs):
            i1 = t.matmul(P.psum[0].ap()[:1, :], onescol.ap(), xsb[bb].ap()[:, kc, :],
                          start=(kc == 0), stop=(kc == kcs - 1))
        mm1 = P.tt.inc(i1)
        P.qwait(s, f"gp{bb}")
        P.tt.wait(s)
        sq_n = P.ts.inc(s.activation(sq.ap(), xsb[bb].ap(), AF.Square))
        P.ts.wait(t, sq_n)
        for kc in range(kcs):
            i2 = t.matmul(P.psum[1].ap()[:1, :], onescol.ap(), sq.ap()[:, kc, :],
                          start=(kc == 0), stop=(kc == kcs - 1))
        mm2 = P.tt.inc(i2)
        P.tt.wait(v, mm2)
        v.tensor_scalar(out=row.ap(), in0=P.psum[0].ap()[:1, :], scalar1=1.0 / R,
                        scalar2=None, op0=ALU.mult)
        v.drain()
        v.tensor_tensor(row2.ap(), row.ap(), row.ap(), ALU.mult)
        v.drain()
        v.scalar_tensor_tensor(out=row2.ap(), in0=P.psum[1].ap()[:1, :], scalar=1.0 / R,
                               in1=row2.ap(), op0=ALU.mult, op1=ALU.subtract)
        rv_n = P.tv.inc(v.drain())
        P.tv.wait(s, rv_n)
        sd_n = P.ts.inc(s.activation(row2.ap(), row2.ap(), AF.Sqrt, bias=eps_c[0:1, :]))
        P.ts.wait(v, sd_n)
        v.reciprocal(row2.ap(), row2.ap())
        v.drain()
        v.scalar_tensor_tensor(out=row.ap(), in0=row.ap(), scalar=-1.0, in1=row2.ap(),
                               op0=ALU.mult, op1=ALU.mult)
        nm_n = P.tv.inc(v.drain())
        P.tv.wait(t, nm_n)
        t.matmul(P.psum[2].ap(), ones_row(P).ap(), row2.ap(), start=True, stop=True)
        b2 = t.matmul(P.psum[3].ap(), ones_row(P).ap(), row.ap(), start=True, stop=True)
        bc_n = P.tt.inc(b2)
        P.tt.wait(v, bc_n)
        for kc in range(kcs):
            j = kc % 4
            P.qwait(v, f"st{j}")
            v.drain()
            v.scalar_tensor_tensor(out=usb[0].ap(), in0=xsb[bb].ap()[:, kc, :],
                                   scalar=(gcol.ap()[:, kc:kc + 1] if lg is not None else 1.0),
                                   in1=P.psum[2].ap(), op0=ALU.mult, op1=ALU.mult)
            if lb is not None:
                v.scalar_tensor_tensor(out=usb[1].ap(), in0=P.psum[3].ap(),
                                       scalar=(gcol.ap()[:, kc:kc + 1] if lg is not None else 1.0),
                                       in1=lcol.ap()[:, kc:kc + 1].broadcast_to([128, 512]),
                                       op0=ALU.mult, op1=ALU.add)
            elif lg is not None:
                v.tensor_scalar(out=usb[1].ap(), in0=P.psum[3].ap(),
                                scalar1=gcol.ap()[:, kc:kc + 1], scalar2=None, op0=ALU.mult)
            else:
                v.tensor_copy(usb[1].ap(), P.psum[3].ap())
            v.drain()
            ev = v.tensor_tensor(osb[j].ap(), usb[0].ap(), usb[1].ap(), ALU.add)
            ev_n = P.tv.inc(ev)
            P.tv.wait(nc.sync, ev_n)
            P.dma(f"st{j}", out.ap()[kc * 128:(kc + 1) * 128, tt * 512:(tt + 1) * 512],
                  osb[j].ap())
        done[bb] = P.tv.n
    P.barrier()
    _stg.close()


def ones_row(P):
    if not hasattr(P, "_ones_row"):
        t = P.nc.alloc_sbuf_tensor(P.uname("ones_row"), [1, 128], F32)
        P.tg.inc(P.nc.gpsimd.memset(t.ap(), 1.0))
        P._ones_row = t
    return P._ones_row


def ew_add(v, outs, ins):
    return v.tensor_tensor(outs[0], ins[0], ins[1], ALU.add)


def ew_mult(v, outs, ins):
    return v.tensor_tensor(outs[0], ins[0], ins[1], ALU.mult)


def shift_fm(P, X, out, half=512):
    R = 2 * half
    zc = P.const("zcol_h", np.zeros((half, 1), dtype=np.float32))
    P.dma("gp", out.ap()[0:half, :], X.ap()[0:half, :])
    P.dma("gp", out.ap()[half:R, 1:L], X.ap()[half:R, 0:L - 1])
    P.nc.gpsimd.dma_start(out=out.ap()[half:R, 0:1], in_=zc.ap(),
                          allow_slow_non_contiguous=True).then_inc(P.q["gp"][1], 16)
    P.q["gp"][2] += 1
    P.barrier()


def _rope_tables():
    DH = 64
    hi = np.arange(0, DH, 2, dtype=np.float64)
    inv = 1.0 / 10000.0 ** (hi / DH)
    tt = np.arange(L, dtype=np.float64)
    fr = np.concatenate([tt[:, None] * inv, tt[:, None] * inv], axis=-1)
    base = (hi + 0.4 * DH) / (1.4 * DH)
    power = (tt - L // 2) / 256.0
    sc = base[None, :] ** power[:, None]
    sc = np.concatenate([sc, sc], axis=-1)
    Cq = (np.cos(fr) * sc).T.astype(np.float32)
    Sq = (np.sin(fr) * sc).T.astype(np.float32)
    Ck = (np.cos(fr) / sc).T.astype(np.float32)
    Sk = (np.sin(fr) / sc).T.astype(np.float32)
    def two(x):
        return np.concatenate([x, x], axis=0)
    return two(Cq), two(Sq), two(Ck), two(Sk)


def rope_pass(P, qk_d, out_d, qs_col, ks_col):
    nc = P.nc
    v, s, t = nc.vector, nc.scalar, nc.tensor
    Cq, Sq, Ck, Sk = _rope_tables()
    rr = np.zeros((128, 128), dtype=np.float32)
    for blk in range(2):
        for d in range(32):
            rr[blk * 64 + 32 + d, blk * 64 + d] = -1.0
            rr[blk * 64 + d, blk * 64 + 32 + d] = 1.0
    ind2 = np.zeros((128, 2), dtype=np.float32)
    ind2[0:64, 0] = 1.0
    ind2[64:128, 1] = 1.0
    indT = np.zeros((2, 128), dtype=np.float32)
    indT[0, 0:64] = 1.0
    indT[1, 64:128] = 1.0
    _stg = contextlib.ExitStack()
    def ssb(name, shape, dtype=F32):
        return _stg.enter_context(nc.sbuf_tensor(P.uname(name), list(shape), dtype))
    tabs = []
    for nm, arr in [("Cq", Cq), ("Sq", Sq), ("Ck", Ck), ("Sk", Sk)]:
        tsb = ssb("tab" + nm, [128, L])
        P.dma("sync", tsb.ap(), P.const("rope" + nm, arr).ap())
        tabs.append(tsb)
    rr_sb = ssb("rr_sb", [128, 128], BF16)
    P.dma("gp", rr_sb.ap(), P.const("ropeRR", rr).ap())
    ind2_sb = ssb("ind2", [128, 2])
    P.dma("sync", ind2_sb.ap(), P.const("ropeI2", ind2).ap())
    indT_sb = ssb("indT", [2, 128])
    P.dma("sync", indT_sb.ap(), P.const("ropeIT", indT).ap())
    sc_sb = ssb("sc_sb", [128, 2])
    P.dma("sync", sc_sb.ap()[:, 0:1], qs_col.ap())
    P.dma("sync", sc_sb.ap()[:, 1:2], ks_col.ap())
    q_bf = [ssb("rp_qbf", [128, 512], BF16) for _ in range(2)]
    q_f = ssb("rp_qf", [128, 512])
    sq = ssb("rp_sq", [128, 512])
    nrm = ssb("rp_nrm", [2, 512])
    qhs = ssb("rp_qhs", [128, 512])
    qhs_bf = ssb("rp_qhsb", [128, 512], BF16)
    t1 = ssb("rp_t1", [128, 512])
    t2 = ssb("rp_t2", [128, 512])
    osb = [ssb("rp_o", [128, 512], BF16) for _ in range(2)]
    P.barrier()
    done = [0, 0]
    for qk in range(2):
        tabC, tabS = tabs[2 * qk], tabs[2 * qk + 1]
        for rc in range(4):
            rows = qk * 512 + rc * 128
            for tt in range(L // 512):
                bb = tt % 2
                sl = slice(tt * 512, (tt + 1) * 512)
                gp = nc.sync
                P.ts.wait(gp, done[bb])
                P.dma(f"ld{bb}", q_bf[bb].ap(), qk_d.ap()[rows:rows + 128, sl])
                P.qwait(s, f"ld{bb}")
                P.tt.wait(s)
                cp_n = P.ts.inc(s.copy(q_f.ap(), q_bf[bb].ap()))
                sq_n = P.ts.inc(s.activation(sq.ap(), q_bf[bb].ap(), AF.Square))
                P.ts.wait(t, sq_n)
                P.tv.wait(t)
                mm1 = P.tt.inc(t.matmul(P.psum[0].ap()[:2, :], ind2_sb.ap(), sq.ap(),
                                        start=True, stop=True))
                P.tt.wait(s, mm1)
                P.tv.wait(s)
                nr_n = P.ts.inc(s.activation(nrm.ap(), P.psum[0].ap()[:2, :], AF.Sqrt))
                P.ts.wait(v, nr_n)
                v.tensor_scalar(out=nrm.ap(), in0=nrm.ap(), scalar1=1e-12, scalar2=None,
                                op0=ALU.max)
                v.drain()
                v.reciprocal(nrm.ap(), nrm.ap())
                ni_n = P.tv.inc(v.drain())
                P.tv.wait(t, ni_n)
                mm2 = P.tt.inc(t.matmul(P.psum[1].ap(), indT_sb.ap(), nrm.ap(),
                                        start=True, stop=True))
                P.tt.wait(v, mm2)
                P.ts.wait(v, cp_n)
                v.scalar_tensor_tensor(out=qhs.ap(), in0=q_f.ap(),
                                       scalar=sc_sb.ap()[:, qk:qk + 1],
                                       in1=P.psum[1].ap(), op0=ALU.mult, op1=ALU.mult)
                v.drain()
                qb_n = P.tv.inc(v.tensor_copy(qhs_bf.ap(), qhs.ap()))
                P.tv.wait(t, qb_n)
                mm3 = P.tt.inc(t.matmul(P.psum[2].ap(), rr_sb.ap(), qhs_bf.ap(),
                                        start=True, stop=True))
                v.tensor_tensor(t1.ap(), qhs.ap(), tabC.ap()[:, sl], ALU.mult)
                P.tt.wait(v, mm3)
                v.scalar_tensor_tensor(out=t2.ap(), in0=tabS.ap()[:, sl], scalar=1.0,
                                       in1=P.psum[2].ap(), op0=ALU.mult, op1=ALU.mult)
                v.drain()
                P.qwait(v, f"st{bb}")
                ev_n = P.tv.inc(v.tensor_tensor(osb[bb].ap(), t1.ap(), t2.ap(), ALU.add))
                P.tv.wait(nc.sync, ev_n)
                P.dma(f"st{bb}", out_d.ap()[rows:rows + 128, sl], osb[bb].ap())
                done[bb] = P.ts.n
    P.barrier()
    _stg.close()


def local_attn(P, qk_d, v_d, o_d):
    nc = P.nc
    v, s, t = nc.vector, nc.scalar, nc.tensor
    mA = np.where(np.arange(128)[None, :] >= np.arange(128)[:, None], 0.0, NEG
                  ).astype(np.float32)
    mB = np.where(np.arange(128)[None, :] <= np.arange(128)[:, None], 0.0, NEG
                  ).astype(np.float32)
    _stg = contextlib.ExitStack()
    def ssb(name, shape, dtype=F32):
        return _stg.enter_context(nc.sbuf_tensor(P.uname(name), list(shape), dtype))
    mA_sb = ssb("mA", [128, 128])
    mB_sb = ssb("mB", [128, 128])
    P.dma("sync", mA_sb.ap(), P.const("lamA", mA).ap())
    P.dma("sync", mB_sb.ap(), P.const("lamB", mB).ap())
    idb = ssb("la_idb", [128, 128], BF16)
    P.dma("gp", idb.ap(), P.const("identf", np.eye(128, dtype=np.float32)).ap())
    qt = [ssb("la_q", [128, 4, 128], BF16) for _ in range(2)]
    kt = [ssb("la_k", [128, 4, 640], BF16) for _ in range(2)]
    vw = [ssb("la_v", [128, 5, 512], BF16) for _ in range(2)]
    Ssb = ssb("la_S", [128, 640])
    rmax = ssb("la_m", [128, 2])
    Pf = ssb("la_P", [128, 640])
    Pbf = ssb("la_Pb", [128, 640], BF16)
    Pt = ssb("la_Pt", [128, 640], BF16)
    ob = [ssb("la_o", [64, 128], BF16) for _ in range(4)]
    P.barrier()
    done = [0, 0]
    for qi in range(L // 128):
        bb = qi % 2
        kw0 = max(0, qi * 128 - 512)
        nkt = (qi * 128 + 128 - kw0) // 128
        sy = nc.sync
        P.ts.wait(sy, done[bb])
        P.dma(f"ld{bb}", qt[bb].ap(),
              qk_d.ap().rearrange("(rc p) l -> p rc l", p=128)[:, 0:4, qi * 128:(qi + 1) * 128])
        P.dma(f"ld{bb}", kt[bb].ap()[:, :, 0:nkt * 128],
              qk_d.ap().rearrange("(rc p) l -> p rc l", p=128)[:, 4:8, kw0:kw0 + nkt * 128])
        P.dma(f"ld{bb}", vw[bb].ap()[:, 0:nkt, :],
              v_d.ap().rearrange("(a p) c -> p a c", p=128)[:, kw0 // 128:kw0 // 128 + nkt, :])
        for h in range(8):
            rc, off = h // 2, (h % 2) * 64
            P.qwait(t, f"ld{bb}")
            P.ts.wait(t)
            n1 = min(nkt * 128, 512)
            ins = t.matmul(P.psum[0].ap()[:, 0:n1], qt[bb].ap()[off:off + 64, rc, :],
                           kt[bb].ap()[off:off + 64, rc, 0:n1], start=True, stop=True)
            if nkt == 5:
                ins = t.matmul(P.psum[1].ap()[:, 0:128], qt[bb].ap()[off:off + 64, rc, :],
                               kt[bb].ap()[off:off + 64, rc, 512:640], start=True, stop=True)
            smm = P.tt.inc(ins)
            P.tt.wait(s, smm)
            P.tv.wait(s)
            s.activation(Ssb.ap()[:, 0:n1], P.psum[0].ap()[:, 0:n1], AF.Identity, scale=8.0)
            if nkt == 5:
                s.activation(Ssb.ap()[:, 512:640], P.psum[1].ap()[:, 0:128], AF.Identity,
                             scale=8.0)
            ev_n = P.ts.inc(s.drain())
            P.ts.wait(v, ev_n)
            if qi >= 4:
                v.tensor_tensor(Ssb.ap()[:, 0:128], Ssb.ap()[:, 0:128], mA_sb.ap(), ALU.add)
            dg0 = (nkt - 1) * 128
            v.tensor_tensor(Ssb.ap()[:, dg0:dg0 + 128], Ssb.ap()[:, dg0:dg0 + 128],
                            mB_sb.ap(), ALU.add)
            v.drain()
            v.reduce_max(rmax.ap()[:, 0:1], Ssb.ap()[:, 0:nkt * 128], axis=AX.X)
            v.drain()
            v.tensor_scalar(out=rmax.ap()[:, 1:2], in0=rmax.ap()[:, 0:1], scalar1=-1.0,
                            scalar2=None, op0=ALU.mult)
            nm_n = P.tv.inc(v.drain())
            P.tv.wait(s, nm_n)
            ex_n = P.ts.inc(s.activation(Pf.ap()[:, 0:nkt * 128], Ssb.ap()[:, 0:nkt * 128],
                                         AF.Exp, bias=rmax.ap()[:, 1:2],
                                         accum_out=rmax.ap()[:, 0:1]))
            P.ts.wait(v, ex_n)
            v.reciprocal(rmax.ap()[:, 0:1], rmax.ap()[:, 0:1])
            v.drain()
            v.tensor_scalar(out=Pbf.ap()[:, 0:nkt * 128], in0=Pf.ap()[:, 0:nkt * 128],
                            scalar1=rmax.ap()[:, 0:1], scalar2=None, op0=ALU.mult)
            pb_n = P.tv.inc(v.drain())
            P.tv.wait(t, pb_n)
            P.ts.wait(t)
            for kti in range(nkt):
                itr = t.transpose(P.psum_bf[0].ap()[:, kti * 128:(kti + 1) * 128],
                                  Pbf.ap()[:, kti * 128:(kti + 1) * 128], idb.ap())
            tr_n = P.tt.inc(itr)
            P.tt.wait(s, tr_n)
            kte = P.ts.inc(s.copy(Pt.ap()[:, 0:nkt * 128], P.psum_bf[0].ap()[:, 0:nkt * 128]))
            P.ts.wait(t, kte)
            P.tv.wait(t)
            for kti in range(nkt):
                io = t.matmul(P.psum[2].ap()[:64, 0:128],
                              vw[bb].ap()[:, kti, h * 64:(h + 1) * 64],
                              Pt.ap()[:, kti * 128:(kti + 1) * 128],
                              start=(kti == 0), stop=(kti == nkt - 1))
            om = P.tt.inc(io)
            j = h % 4
            P.tt.wait(v, om)
            P.qwait(v, f"st{j}")
            o_n = P.tv.inc(v.tensor_copy(ob[j].ap(), P.psum[2].ap()[:64, 0:128]))
            P.tv.wait(sy, o_n)
            P.dma(f"st{j}", o_d.ap()[h * 64:(h + 1) * 64, qi * 128:(qi + 1) * 128], ob[j].ap())
        done[bb] = P.ts.n
    P.barrier()
    _stg.close()


def p5_scratch(P, tag=""):
    return dict(
        h_d=P.dram(f"h_d{tag}", [DM, L], BF16),
        qk_d=P.dram(f"qk_d{tag}", [DM, L], BF16),
        rope_d=P.dram(f"rope_d{tag}", [DM, L], BF16),
        v_d=P.dram(f"v_d{tag}", [L, 512], BF16),
        o_d=P.dram(f"o_d{tag}", [512, L], BF16),
        t_d=P.dram(f"t_d{tag}", [DM, L], F32),
        r2_d=P.dram(f"r2_d{tag}", [DM, L], F32),
        h2_d=P.dram(f"h2_d{tag}", [DM, L], F32),
        h2n_d=P.dram(f"h2n_d{tag}", [DM, L], BF16),
        ua_d=P.dram(f"ua_d{tag}", [2730, L], BF16),
        ug_d=P.dram(f"ug_d{tag}", [2730, L], BF16),
        act_d=P.dram(f"act_d{tag}", [2816, L], BF16),
        w2p_d=P.dram(f"w2p_d{tag}", [2816, DM], F32),
        t2_d=P.dram(f"t2_d{tag}", [DM, L], F32))


def build_p5(dbg=()):
    P = Prog()
    rin = P.dram("rin", [DM, L], F32, kind="ExternalInput")
    wqkv = P.dram("wqkv", [DM, 1536], F32, kind="ExternalInput")
    qs_col = P.dram("qs_col", [128, 1], F32, kind="ExternalInput")
    ks_col = P.dram("ks_col", [128, 1], F32, kind="ExternalInput")
    wo = P.dram("wo", [512, DM], F32, kind="ExternalInput")
    lg = P.dram("lg", [DM], F32, kind="ExternalInput")
    lbv = P.dram("lbv", [DM], F32, kind="ExternalInput")
    w1a = P.dram("w1a", [DM, 2730], F32, kind="ExternalInput")
    b1a = P.dram("b1a", [2730], F32, kind="ExternalInput")
    w1g = P.dram("w1g", [DM, 2730], F32, kind="ExternalInput")
    b1g = P.dram("b1g", [2730], F32, kind="ExternalInput")
    w2 = P.dram("w2", [2730, DM], F32, kind="ExternalInput")
    b2v = P.dram("b2v", [DM], F32, kind="ExternalInput")
    rout = P.dram("rout", [DM, L], F32, kind="ExternalOutput")
    P.fconst(1e-5)
    ones_row(P)
    with P.scope():
        emit_p5(P, rin, rout, wqkv, qs_col, ks_col, wo, lg, lbv, w1a, b1a, w1g,
                b1g, w2, b2v, p5_scratch(P))
    return P


def emit_p5(P, rin, rout, wqkv, qs_col, ks_col, wo, lg, lbv, w1a, b1a, w1g, b1g,
            w2, b2v, sc):
    nc = P.nc
    h_d = sc["h_d"]; qk_d = sc["qk_d"]; rope_d = sc["rope_d"]; v_d = sc["v_d"]
    o_d = sc["o_d"]; t_d = sc["t_d"]; r2_d = sc["r2_d"]; h2_d = sc["h2_d"]
    h2n_d = sc["h2n_d"]; ua_d = sc["ua_d"]; ug_d = sc["ug_d"]; act_d = sc["act_d"]
    w2p_d = sc["w2p_d"]; t2_d = sc["t2_d"]

    zpad = P.sb("zpad", [86, L], BF16)
    P.tg.inc(nc.gpsimd.memset(zpad.ap(), 0.0))
    P.tg.wait(nc.sync)
    P.dma("sync", act_d.ap()[2730:2816, :], zpad.ap())
    zpad2 = P.sb("zpad2", [86, 1024], F32)
    P.tg.inc(nc.gpsimd.memset(zpad2.ap(), 0.0))
    P.tg.wait(nc.sync)
    P.dma("sync", w2p_d.ap()[2730:2816, :], zpad2.ap())
    P.dma("sync", w2p_d.ap()[0:2730, :], w2.ap())
    P.barrier()

    shift_fm(P, rin, h_d)
    fm_matmul(P, h_d, _View(wqkv.ap()[:, 0:1024], "wqk"), qk_d, K=DM, N=1024,
              out_dtype=BF16)
    tok_matmul(P, h_d, _View(wqkv.ap()[:, 1024:1536], "wv"), v_d, K=DM, N=512,
               out_dtype=BF16)
    rope_pass(P, qk_d, rope_d, qs_col, ks_col)
    local_attn(P, rope_d, v_d, o_d)
    fm_matmul(P, o_d, wo, t_d, K=512, N=DM)
    fm_elemwise(P, [rin, t_d], [r2_d], R=DM, emit=ew_add)
    shift_fm(P, r2_d, h2_d)
    fm_layernorm(P, h2_d, h2n_d, lg, lbv)
    emit_p5_ffn(P, h2n_d, w1a, b1a, w1g, b1g, w2p_d, b2v, r2_d, rout, act_d)


def emit_p5_ffn(P, h2n_d, w1a, b1a, w1g, b1g, w2p_d, b2v, r2_d, rout, act_d):
    nc = P.nc
    v, s, t = nc.vector, nc.scalar, nc.tensor
    TT = 512
    N1, NCH1 = 2730, 22
    xa = h2n_d.ap().rearrange("(kc p) l -> p kc l", p=128)
    ra = r2_d.ap().rearrange("(kc p) l -> p kc l", p=128)
    oa = rout.ap().rearrange("(kc p) l -> p kc l", p=128)
    ca = act_d.ap().rearrange("(kc p) l -> p kc l", p=128)

    _stg = contextlib.ExitStack()
    def ssb(name, shape, dtype=F32):
        return _stg.enter_context(nc.sbuf_tensor(P.uname(name), list(shape), dtype))
    wa_sb = ssb("f_wa", [128, 8, N1], BF16)
    wg_sb = ssb("f_wg", [128, 8, N1], BF16)
    x_sb = [ssb("f_x", [128, 8, TT], BF16) for _ in range(2)]
    ua_sb = [ssb("f_ua", [128, TT]) for _ in range(2)]
    ug_sb = [ssb("f_ug", [128, TT]) for _ in range(2)]
    at_sb = [ssb("f_at", [128, TT], BF16) for _ in range(4)]
    ba_sb = ssb("f_ba", [128, NCH1])
    bg_sb = ssb("f_bg", [128, NCH1])
    P.dma("gp0", wa_sb.ap(), w1a.ap().rearrange("(kc p) n -> p kc n", p=128))
    P.dma("gp0", wg_sb.ap(), w1g.ap().rearrange("(kc p) n -> p kc n", p=128))
    for i in range(NCH1):
        rows = min(128, N1 - i * 128)
        P.dma("sync", ba_sb.ap()[:rows, i:i + 1],
              b1a.ap()[i * 128:i * 128 + rows].unsqueeze(1))
        P.dma("sync", bg_sb.ap()[:rows, i:i + 1],
              b1g.ap()[i * 128:i * 128 + rows].unsqueeze(1))
    P.barrier()
    bank_ts = [0, 0, 0, 0]
    pe_done_x = [0, 0]
    gm_tv = [0, 0]
    at_tv = [0, 0, 0, 0]
    for tt in range(L // TT):
        xb = tt % 2
        P.tt.wait(nc.sync, pe_done_x[xb])
        P.dma(f"ld{xb}", x_sb[xb].ap(), xa[:, :, tt * TT:(tt + 1) * TT])
        P.qwait(t, f"ld{xb}")
        for nch in range(NCH1):
            rows = min(128, N1 - nch * 128)
            bka = nch % 2
            bkg = 2 + nch % 2
            P.ts.wait(t, max(bank_ts[bka], bank_ts[bkg]))
            for kc in range(8):
                t.matmul(P.psum[bka].ap()[:rows, :],
                         wa_sb.ap()[:, kc, nch * 128:nch * 128 + rows],
                         x_sb[xb].ap()[:, kc, :],
                         start=(kc == 0), stop=(kc == 7))
            for kc in range(8):
                ins = t.matmul(P.psum[bkg].ap()[:rows, :],
                               wg_sb.ap()[:, kc, nch * 128:nch * 128 + rows],
                               x_sb[xb].ap()[:, kc, :],
                               start=(kc == 0), stop=(kc == 7))
            n = P.tt.inc(ins)
            P.tt.wait(s, n)
            jb = nch % 2
            P.tv.wait(s, gm_tv[jb])
            eva = s.activation(ua_sb[jb].ap()[:rows, :], P.psum[bka].ap()[:rows, :],
                               AF.Identity, bias=ba_sb.ap()[:rows, nch:nch + 1])
            bank_ts[bka] = P.ts.inc(eva)
            evg = s.activation(ug_sb[jb].ap()[:rows, :], P.psum[bkg].ap()[:rows, :],
                               AF.Gelu, bias=bg_sb.ap()[:rows, nch:nch + 1])
            bank_ts[bkg] = P.ts.inc(evg)
            j = nch % 4
            P.ts.wait(v, bank_ts[bkg])
            P.qwait(v, f"st{j}")
            gm = v.tensor_tensor(at_sb[j].ap()[:rows, :], ua_sb[jb].ap()[:rows, :],
                                 ug_sb[jb].ap()[:rows, :], ALU.mult)
            gm_tv[jb] = P.tv.inc(gm)
            P.tv.wait(nc.sync, gm_tv[jb])
            P.dma(f"st{j}", act_d.ap()[nch * 128:nch * 128 + rows,
                                       tt * TT:(tt + 1) * TT], at_sb[j].ap()[:rows, :])
        pe_done_x[xb] = P.tt.n
    P.barrier()
    _stg.close()

    _st2 = contextlib.ExitStack()
    def ssb2(name, shape, dtype=F32):
        return _st2.enter_context(nc.sbuf_tensor(P.uname(name), list(shape), dtype))
    w2_sb = ssb2("f_w2", [128, NCH1, DM], BF16)
    c_sb = [ssb2("f_c", [128, NCH1, TT], BF16) for _ in range(2)]
    r2_sb = [ssb2("f_r2", [128, 8, TT]) for _ in range(2)]
    o_sb = [ssb2("f_o", [128, TT]) for _ in range(4)]
    b2_sb = ssb2("f_b2", [128, 8])
    P.dma("gp1", w2_sb.ap(), w2p_d.ap().rearrange("(kc p) n -> p kc n", p=128))
    for i in range(8):
        P.dma("sync", b2_sb.ap()[:, i:i + 1],
              b2v.ap()[i * 128:(i + 1) * 128].unsqueeze(1))
    P.barrier()
    bank2_ts = [0, 0]
    pe_done_c = [0, 0]
    fin_tv = [0, 0, 0, 0]
    r2_done = [0, 0]
    for tt in range(L // TT):
        xb = tt % 2
        P.tt.wait(nc.sync, pe_done_c[xb])
        P.tv.wait(nc.sync, r2_done[xb])
        P.dma(f"ld{xb}", c_sb[xb].ap(), ca[:, :, tt * TT:(tt + 1) * TT])
        P.dma(f"ld{xb}", r2_sb[xb].ap(), ra[:, :, tt * TT:(tt + 1) * TT])
        P.qwait(t, f"ld{xb}")
        for nch2 in range(8):
            bk = 4 + nch2 % 2
            P.ts.wait(t, bank2_ts[nch2 % 2])
            for kc2 in range(NCH1):
                ins = t.matmul(P.psum[bk].ap()[:, 0:TT],
                               w2_sb.ap()[:, kc2, nch2 * 128:(nch2 + 1) * 128],
                               c_sb[xb].ap()[:, kc2, :],
                               start=(kc2 == 0), stop=(kc2 == NCH1 - 1))
            n = P.tt.inc(ins)
            j = nch2 % 4
            P.tt.wait(s, n)
            P.tv.wait(s, fin_tv[j])
            P.qwait(s, f"st{j}")
            ev = s.activation(o_sb[j].ap(), P.psum[bk].ap()[:, 0:TT], AF.Identity,
                              bias=b2_sb.ap()[:, nch2:nch2 + 1])
            bank2_ts[nch2 % 2] = P.ts.inc(ev)
            P.ts.wait(v, bank2_ts[nch2 % 2])
            fin = v.tensor_tensor(o_sb[j].ap(), o_sb[j].ap(),
                                  r2_sb[xb].ap()[:, nch2, :], ALU.add)
            fin_tv[j] = P.tv.inc(fin)
            P.tv.wait(nc.sync, fin_tv[j])
            P.dma(f"st{j}", oa[:, nch2, tt * TT:(tt + 1) * TT], o_sb[j].ap())
        pe_done_c[xb] = P.tt.n
        r2_done[xb] = P.tv.n
    P.barrier()
    _st2.close()


def build_p6():
    P = Prog()
    xfm = P.dram("xfm", [DM, L], F32, kind="ExternalInput")
    sfm = P.dram("sfm", [DM, L], F32, kind="ExternalInput")
    gfm = P.dram("gfm", [DM, L], F32, kind="ExternalInput")
    x1fm = P.dram("x1fm", [DM, L], F32, kind="ExternalOutput")
    xn1fm = P.dram("xn1fm", [DM, L], BF16, kind="ExternalOutput")
    xn1f32 = P.dram("xn1f32", [DM, L], F32, kind="ExternalOutput")
    P.fconst(1e-5)
    ones_row(P)
    with P.scope():
        emit_p6(P, xfm, sfm, gfm, x1fm, xn1fm, xn1f32)
    return P


def emit_p6(P, xfm, sfm, gfm, x1fm, xn1fm, xn1f32):
    nc = P.nc
    v, s = nc.vector, nc.scalar

    def emit(vv, outs, ins):
        zs_n = P.ts.inc(s.activation(outs[0], ins[2], AF.Silu))
        P.ts.wait(vv, zs_n)
        vv.tensor_tensor(outs[0], ins[1], outs[0], ALU.mult)
        vv.drain()
        return vv.tensor_tensor(outs[0], ins[0], outs[0], ALU.add)
    fm_elemwise(P, [xfm, sfm, gfm], [x1fm], R=DM, emit=emit)
    fm_layernorm(P, x1fm, xn1fm, out_dtype=BF16)
    fm_layernorm(P, x1fm, xn1f32, out_dtype=F32)


def build_p7(dbg=()):
    P = Prog()
    xn1f32 = P.dram("xn1f32", [DM, L], F32, kind="ExternalInput")
    g_ln = P.dram("g_ln", [DM], F32, kind="ExternalInput")
    wq = P.dram("wq", [DM, DM], F32, kind="ExternalInput")
    wkv = P.dram("wkv", [DM, 2 * DM], F32, kind="ExternalInput")
    wo_l1 = P.dram("wo_l1", [DM, DM], F32, kind="ExternalInput")
    s1fm = P.dram("s1fm", [DM, L], F32, kind="ExternalOutput")
    P.fconst(1e-5)
    ones_row(P)
    with P.scope():
        emit_p7(P, xn1f32, g_ln, wq, wkv, wo_l1, s1fm, dbg=dbg)
    return P


def emit_p7(P, xn1f32, g_ln, wq, wkv, wo_l1, s1fm, dbg=()):
    nc = P.nc
    v, s, t = nc.vector, nc.scalar, nc.tensor

    def dk(name):
        return "ExternalOutput" if name in dbg else "Internal"

    xln_d = P.dram("l1xln_d", [DM, L], BF16, kind=dk("xln_d"))
    q_d = P.dram("l1q_d", [DM, L], BF16, kind=dk("q_d"))
    k_d = P.dram("l1k_d", [DM, L], BF16, kind=dk("k_d"))
    v_d = P.dram("l1v_d", [L, DM], BF16, kind=dk("v_d"))
    ofm_d = P.dram("l1ofm_d", [DM, L], BF16, kind=dk("ofm_d"))

    fm_layernorm(P, xn1f32, xln_d, lg=g_ln)
    fm_matmul(P, xln_d, wq, q_d, K=DM, N=DM, out_dtype=BF16)
    fm_matmul(P, xln_d, _View(wkv.ap()[:, 0:DM], "wk"), k_d, K=DM, N=DM, out_dtype=BF16)
    tok_matmul(P, xln_d, _View(wkv.ap()[:, DM:2 * DM], "wv1"), v_d, K=DM, N=DM,
               out_dtype=BF16)

    mB = np.where(np.arange(128)[None, :] <= np.arange(128)[:, None], 0.0, NEG
                  ).astype(np.float32)
    _stg = contextlib.ExitStack()
    def ssb(name, shape, dtype=F32):
        return _stg.enter_context(nc.sbuf_tensor(P.uname(name), list(shape), dtype))
    mB_sb = ssb("l1mB", [128, 128])
    P.dma("sync", mB_sb.ap(), P.const("lamB", mB).ap())
    idb = ssb("l1idb", [128, 128], BF16)
    P.dma("gp", idb.ap(), P.const("identf", np.eye(128, dtype=np.float32)).ap())
    k_sb = ssb("l1k", [128, 8, L], BF16)
    v_sb = ssb("l1v", [128, L // 128, DM], BF16)
    P.dma("sync", k_sb.ap(), k_d.ap().rearrange("(rc p) l -> p rc l", p=128))
    for i in range(0, L // 128, 8):
        nbl = min(8, L // 128 - i)
        P.dma("sync", v_sb.ap()[:, i:i + nbl, :],
              v_d.ap().rearrange("(a p) c -> p a c", p=128)[:, i:i + nbl, :])
    qt = [ssb("l1q", [128, 8, 128], BF16) for _ in range(2)]
    Ssb = ssb("l1S", [128, L])
    rmax = ssb("l1m", [128, 2])
    Pf = ssb("l1P", [128, L])
    Pbf = ssb("l1Pb", [128, L], BF16)
    Pt = ssb("l1Pt", [128, L], BF16)
    ob = [ssb("l1o", [64, 128], BF16) for _ in range(4)]
    P.barrier()
    done = [0, 0]
    o_free = 0
    sevac_guard = [0, 0]
    for qi in range(L // 128):
        bb = qi % 2
        nkt = qi + 1
        sy = nc.sync
        P.ts.wait(sy, done[bb])
        P.dma(f"ld{bb}", qt[bb].ap(),
              q_d.ap().rearrange("(rc p) l -> p rc l", p=128)[:, :, qi * 128:(qi + 1) * 128])
        for h in range(16):
            rc, off = h // 2, (h % 2) * 64
            P.qwait(t, f"ld{bb}")
            for g0 in range(0, nkt * 128, 512):
                cols = min(512, nkt * 128 - g0)
                bk = (g0 // 512) % 2
                P.ts.wait(t, sevac_guard[bk])
                ins = t.matmul(P.psum[bk].ap()[:, 0:cols], qt[bb].ap()[off:off + 64, rc, :],
                               k_sb.ap()[off:off + 64, rc, g0:g0 + cols],
                               start=True, stop=True)
                smm = P.tt.inc(ins)
                P.tt.wait(s, smm)
                P.tv.wait(s)
                se_n = P.ts.inc(s.activation(Ssb.ap()[:, g0:g0 + cols],
                                             P.psum[bk].ap()[:, 0:cols],
                                             AF.Identity, scale=0.125))
                sevac_guard[bk] = se_n + 1
            ev_n = P.ts.inc(s.drain())
            P.ts.wait(v, ev_n)
            dg0 = (nkt - 1) * 128
            v.tensor_tensor(Ssb.ap()[:, dg0:dg0 + 128], Ssb.ap()[:, dg0:dg0 + 128],
                            mB_sb.ap(), ALU.add)
            v.drain()
            v.reduce_max(rmax.ap()[:, 0:1], Ssb.ap()[:, 0:nkt * 128], axis=AX.X)
            v.drain()
            v.tensor_scalar(out=rmax.ap()[:, 1:2], in0=rmax.ap()[:, 0:1], scalar1=-1.0,
                            scalar2=None, op0=ALU.mult)
            nm_n = P.tv.inc(v.drain())
            P.tv.wait(s, nm_n)
            ex_n = P.ts.inc(s.activation(Pf.ap()[:, 0:nkt * 128], Ssb.ap()[:, 0:nkt * 128],
                                         AF.Exp, bias=rmax.ap()[:, 1:2],
                                         accum_out=rmax.ap()[:, 0:1]))
            P.ts.wait(v, ex_n)
            v.reciprocal(rmax.ap()[:, 0:1], rmax.ap()[:, 0:1])
            v.drain()
            v.tensor_scalar(out=Pbf.ap()[:, 0:nkt * 128], in0=Pf.ap()[:, 0:nkt * 128],
                            scalar1=rmax.ap()[:, 0:1], scalar2=None, op0=ALU.mult)
            pb_n = P.tv.inc(v.drain())
            P.tv.wait(t, max(pb_n, o_free))
            kte_at = [0, 0]
            kte = 0
            for kg in range(0, nkt, 8):
                kn = min(8, nkt - kg)
                bfb = (kg // 8) % 2
                P.ts.wait(t, kte_at[bfb])
                for kk in range(kn):
                    itr = t.transpose(
                        P.psum_bf[bfb].ap()[:, kk * 128:(kk + 1) * 128],
                        Pbf.ap()[:, (kg + kk) * 128:(kg + kk + 1) * 128], idb.ap())
                tr_n = P.tt.inc(itr)
                P.tt.wait(s, tr_n)
                kte = P.ts.inc(s.copy(Pt.ap()[:, kg * 128:(kg + kn) * 128],
                                      P.psum_bf[bfb].ap()[:, 0:kn * 128]))
                kte_at[bfb] = kte + 1
            P.ts.wait(t, kte)
            for kk in range(nkt):
                io = t.matmul(P.psum[2].ap()[:64, 0:128],
                              v_sb.ap()[:, kk, h * 64:(h + 1) * 64],
                              Pt.ap()[:, kk * 128:(kk + 1) * 128],
                              start=(kk == 0), stop=(kk == nkt - 1))
            om = P.tt.inc(io)
            j = h % 4
            P.tt.wait(v, om)
            P.qwait(v, f"st{j}")
            o_n = P.tv.inc(v.tensor_copy(ob[j].ap(), P.psum[2].ap()[:64, 0:128]))
            o_free = o_n + 1
            P.tv.wait(sy, o_n)
            P.dma(f"st{j}", ofm_d.ap()[h * 64:(h + 1) * 64, qi * 128:(qi + 1) * 128], ob[j].ap())
        done[bb] = P.ts.n
    P.barrier()
    _stg.close()
    fm_matmul(P, ofm_d, wo_l1, s1fm, K=DM, N=DM)


def emit_p8_experts(P, xn1fm, Fp_d, b2p_d, w1, b1, w2, acc_d):
    nc = P.nc
    v, s, t = nc.vector, nc.scalar, nc.tensor
    NQ, LQ, TT = 4, L // 4, 512
    NTT = LQ // TT
    _stg = contextlib.ExitStack()
    def ssb(name, shape, dtype=F32):
        return _stg.enter_context(nc.sbuf_tensor(P.uname(name), list(shape), dtype))
    xq = ssb("e_x", [128, 8, LQ], BF16)
    accq = ssb("e_acc", [128, 8, LQ], F32)
    w1sb = ssb("e_w1", [128, 8, 2048], BF16)
    w2sb = ssb("e_w2", [128, 16, 1024], BF16)
    hsb = [ssb("e_h", [128, 16, TT], BF16) for _ in range(NTT)]
    fqb = ssb("e_fq", [128, LQ], BF16)
    fqr = ssb("e_fqr", [1, 16, LQ], BF16)
    b1sb = ssb("e_b1", [128, 256], F32)
    b2pb = ssb("e_b2", [128, DM], BF16)
    gsb = [ssb("e_g", [128, TT], BF16) for _ in range(NTT)]
    ones_bf = ssb("e_1", [1, 128], BF16)
    P.tg.inc(nc.gpsimd.memset(ones_bf.ap(), 1.0))
    P.dma("gp", b2pb.ap(), b2p_d.ap())
    P.dma("sync", b1sb.ap(), b1.ap())
    P.barrier()
    xa = xn1fm.ap().rearrange("(kc p) l -> p kc l", p=128)
    aa = acc_d.ap().rearrange("(rc p) l -> p rc l", p=128)
    bank_tv = [0, 0]
    bank_ts = [0, 0]
    bank2_tv = [0, 0]
    bank45_ts = [0, 0]
    q_end_tt = 0
    for q in range(NQ):
        q0 = q * LQ
        P.tt.wait(nc.sync, q_end_tt)
        P.dma("ld0", xq.ap(), xa[:, :, q0:q0 + LQ])
        P.tt.wait(nc.gpsimd, q_end_tt)
        P.dma("gp", fqb.ap(), Fp_d.ap()[:, q0:q0 + LQ])
        P.dma("gp", fqr.ap(), Fp_d.ap()[0:16, q0:q0 + LQ].unsqueeze(0))
        P.dma("gp0", w1sb.ap(), w1.ap()[0].rearrange("(kc p) n -> p kc n", p=128))
        P.dma("gp1", w2sb.ap(), w2.ap()[0].rearrange("(kc p) n -> p kc n", p=128))
        w1_cnt = P.qcnt("gp0")
        w2_cnt = P.qcnt("gp1")
        P.qwait(t, "ld0")
        P.qwait(t, "gp")
        for tt2 in range(NTT):
            for ncI in range(8):
                bk = ncI % 2
                P.tv.wait(t, bank_tv[bk])
                P.ts.wait(t, bank_ts[bk])
                mm = t.matmul(P.psum[bk].ap()[:, 0:TT],
                              b2pb.ap()[:, ncI * 128:(ncI + 1) * 128],
                              fqb.ap()[:, tt2 * TT:(tt2 + 1) * TT],
                              start=True, stop=True)
                n = P.tt.inc(mm)
                P.tt.wait(v, n)
                if tt2 == 0 and ncI == 0:
                    P.qwait(v, "st0")
                ev = v.tensor_copy(accq.ap()[:, ncI, tt2 * TT:(tt2 + 1) * TT],
                                   P.psum[bk].ap()[:, 0:TT])
                bank_tv[bk] = P.tv.inc(ev)
        for e in range(16):
            sc_n = [0, 0]
            for tt2 in range(NTT):
                bk45 = 4 + tt2
                P.ts.wait(t, bank45_ts[tt2])
                bmm = t.matmul(P.psum[bk45].ap()[:, 0:TT], ones_bf.ap(),
                               fqr.ap()[0:1, e, tt2 * TT:(tt2 + 1) * TT],
                               start=True, stop=True)
                bn = P.tt.inc(bmm)
                P.tt.wait(s, bn)
                sc = s.copy(gsb[tt2].ap(), P.psum[bk45].ap()[:, 0:TT])
                sc_n[tt2] = P.ts.inc(sc)
                bank45_ts[tt2] = sc_n[tt2]
            gm_n = [0, 0]
            for tt2 in range(NTT):
                for nch in range(16):
                    bk = nch % 2
                    if nch < 2:
                        P.qwait(t, "gp0", w1_cnt)
                        P.tv.wait(t, bank_tv[bk])
                    P.ts.wait(t, bank_ts[bk])
                    for kc in range(8):
                        ins = t.matmul(P.psum[bk].ap()[:, 0:TT],
                                       w1sb.ap()[:, kc, nch * 128:(nch + 1) * 128],
                                       xq.ap()[:, kc, tt2 * TT:(tt2 + 1) * TT],
                                       start=(kc == 0), stop=(kc == 7))
                    n = P.tt.inc(ins)
                    P.tt.wait(s, n)
                    ev = s.activation(hsb[tt2].ap()[:, nch, :],
                                      P.psum[bk].ap()[:, 0:TT], AF.Gelu,
                                      bias=b1sb.ap()[:, e * 16 + nch:e * 16 + nch + 1])
                    bank_ts[bk] = P.ts.inc(ev)
                    P.ts.wait(v, max(bank_ts[bk], sc_n[tt2]))
                    gm = v.tensor_tensor(hsb[tt2].ap()[:, nch, :],
                                         hsb[tt2].ap()[:, nch, :],
                                         gsb[tt2].ap(), ALU.mult)
                    gm_n[tt2] = P.tv.inc(gm)
            mm1_end = P.tt.n
            if e < 15:
                P.tt.wait(nc.gpsimd, mm1_end)
                P.dma("gp0", w1sb.ap(),
                      w1.ap()[e + 1].rearrange("(kc p) n -> p kc n", p=128))
                w1_cnt = P.qcnt("gp0")
            for tt2 in range(NTT):
                P.tv.wait(t, gm_n[tt2])
                for ncI2 in range(8):
                    bk = 2 + ncI2 % 2
                    if ncI2 < 2:
                        P.qwait(t, "gp1", w2_cnt)
                    P.tv.wait(t, bank2_tv[ncI2 % 2])
                    for kc2 in range(16):
                        ins = t.matmul(P.psum[bk].ap()[:, 0:TT],
                                       w2sb.ap()[:, kc2, ncI2 * 128:(ncI2 + 1) * 128],
                                       hsb[tt2].ap()[:, kc2, :],
                                       start=(kc2 == 0), stop=(kc2 == 15))
                    n = P.tt.inc(ins)
                    P.tt.wait(v, n)
                    ev = v.tensor_tensor(accq.ap()[:, ncI2, tt2 * TT:(tt2 + 1) * TT],
                                         accq.ap()[:, ncI2, tt2 * TT:(tt2 + 1) * TT],
                                         P.psum[bk].ap()[:, 0:TT], ALU.add)
                    bank2_tv[ncI2 % 2] = P.tv.inc(ev)
            mm2_end = P.tt.n
            if e < 15:
                P.tt.wait(nc.gpsimd, mm2_end)
                P.dma("gp1", w2sb.ap(),
                      w2.ap()[e + 1].rearrange("(kc p) n -> p kc n", p=128))
                w2_cnt = P.qcnt("gp1")
        P.tv.wait(nc.sync, P.tv.n)
        P.dma("st0", aa[:, :, q0:q0 + LQ], accq.ap())
        q_end_tt = P.tt.n
    P.barrier()
    _stg.close()


def build_p8(dbg=()):
    P = Prog()
    xn1fm = P.dram("xn1fm", [DM, L], BF16, kind="ExternalInput")
    x1fm = P.dram("x1fm", [DM, L], F32, kind="ExternalInput")
    s1fm = P.dram("s1fm", [DM, L], F32, kind="ExternalInput")
    router_w = P.dram("router_w", [DM, 16], F32, kind="ExternalInput")
    w1 = P.dram("w1", [16, DM, 2048], F32, kind="ExternalInput")
    b1 = P.dram("b1", [128, 256], F32, kind="ExternalInput")
    w2 = P.dram("w2moe", [16, 2048, DM], F32, kind="ExternalInput")
    b2all = P.dram("b2all", [16, DM], F32, kind="ExternalInput")
    out = P.dram("out", [L, DM], F32, kind="ExternalOutput")
    P.fconst(1e-5)
    ones_row(P)
    with P.scope():
        emit_p8(P, xn1fm, x1fm, s1fm, router_w, w1, b1, w2, b2all, out, dbg=dbg)
    return P


def emit_p8(P, xn1fm, x1fm, s1fm, router_w, w1, b1, w2, b2all, out, dbg=()):
    nc = P.nc
    v, s, t = nc.vector, nc.scalar, nc.tensor

    def dk(name):
        return "ExternalOutput" if name in dbg else "Internal"

    logits_d = P.dram("logits_d", [16, L], F32, kind=dk("logits_d"))
    Fp_d = P.dram("Fp_d", [128, L], F32, kind=dk("Fp_d"))
    w_d = P.dram("w_d", [1, L], F32, kind=dk("w_d"))
    b2p_d = P.dram("b2p_d", [128, DM], F32)
    he_d = P.dram("he_d", [2048, L], BF16)
    acc_d = P.dram("acc_d", [DM, L], F32, kind=dk("acc_d"))

    fm_matmul(P, xn1fm, router_w, logits_d, K=DM, N=16)

    iota16 = np.broadcast_to(np.arange(16, dtype=np.float32)[None, :], (128, 16)).copy()
    _stg = contextlib.ExitStack()
    def ssb(name, shape, dtype=F32):
        return _stg.enter_context(nc.sbuf_tensor(P.uname(name), list(shape), dtype))
    io_sb = ssb("io16", [128, 16])
    io2_sb = ssb("io16b", [128, 16])
    negio_sb = ssb("io16n", [128, 16])
    P.dma("sync", io_sb.ap(), P.const("iota16", iota16).ap())
    P.dma("sync", io2_sb.ap(), P.const("iota16c", -(iota16 + 1e4)).ap())
    P.dma("sync", negio_sb.ap(), P.const("iota16n", -iota16).ap())
    idf_sb = ssb("g_id", [128, 128])
    P.dma("sync", idf_sb.ap(), P.const("identf", np.eye(128, dtype=np.float32)).ap())
    zp = ssb("g_zp", [112, L // 8])
    P.tg.inc(nc.gpsimd.memset(zp.ap(), 0.0))
    P.tg.wait(nc.sync)
    for i in range(8):
        P.dma("sync", Fp_d.ap()[16:128, i * (L // 8):(i + 1) * (L // 8)], zp.ap())
    zp2 = ssb("g_zp2", [112, DM])
    P.tg.inc(nc.gpsimd.memset(zp2.ap(), 0.0))
    P.tg.wait(nc.sync)
    P.dma("sync", b2p_d.ap()[16:128, :], zp2.ap())
    P.dma("sync", b2p_d.ap()[0:16, :], b2all.ap())
    lg_sb = [ssb("g_l", [16, 128]) for _ in range(2)]
    lt = ssb("g_lt", [128, 16])
    Mc = ssb("g_M", [128, 4])
    eq = ssb("g_eq", [128, 16])
    fh = ssb("g_fh", [128, 16], BF16)
    fT = [ssb("g_fT", [16, 128]) for _ in range(2)]
    wrow = [ssb("g_wr", [1, 128]) for _ in range(2)]
    idb16 = ssb("g_idb", [128, 128], BF16)
    P.dma("gp", idb16.ap(), P.const("identf", np.eye(128, dtype=np.float32)).ap())
    P.barrier()
    done = [0, 0]
    for ti in range(L // 128):
        bb = ti % 2
        sl = slice(ti * 128, (ti + 1) * 128)
        P.tv.wait(nc.sync, done[bb])
        P.dma(f"ld{bb}", lg_sb[bb].ap(), logits_d.ap()[:, sl])
        P.qwait(t, f"ld{bb}")
        P.tv.wait(t)
        tr1 = P.tt.inc(t.transpose(P.psum[0].ap()[:, 0:16], lg_sb[bb].ap(),
                                   idf_sb.ap()[:16, :16]))
        P.tt.wait(v, tr1)
        v.tensor_copy(lt.ap(), P.psum[0].ap()[:, 0:16])
        v.drain()
        v.reduce_max(Mc.ap()[:, 0:1], lt.ap(), axis=AX.X)
        M_n = P.tv.inc(v.drain())
        P.tv.wait(s, M_n)
        sg_n = P.ts.inc(s.activation(Mc.ap()[:, 1:2], Mc.ap()[:, 0:1], AF.Sigmoid))
        v.tensor_scalar(out=eq.ap(), in0=lt.ap(), scalar1=Mc.ap()[:, 0:1], scalar2=None,
                        op0=ALU.is_ge)
        v.drain()
        v.scalar_tensor_tensor(out=eq.ap(), in0=eq.ap(), scalar=1e4, in1=io2_sb.ap(),
                               op0=ALU.mult, op1=ALU.add)
        v.drain()
        v.reduce_max(Mc.ap()[:, 2:3], eq.ap(), axis=AX.X)
        v.drain()
        v.tensor_scalar(out=fh.ap(), in0=negio_sb.ap(), scalar1=Mc.ap()[:, 2:3], scalar2=None,
                        op0=ALU.is_equal)
        fh_n = P.tv.inc(v.drain())
        P.tv.wait(t, fh_n)
        P.ts.wait(t, sg_n)
        tr2 = t.transpose(P.psum_bf[0].ap()[:16, 0:128], fh.ap(), idb16.ap())
        tr3 = t.transpose(P.psum[1].ap()[:1, 0:128], Mc.ap()[:, 1:2], idf_sb.ap())
        tr_n = P.tt.inc(tr3)
        P.tt.wait(v, tr_n)
        P.qwait(v, f"st{bb}")
        v.tensor_copy(fT[bb].ap(), P.psum_bf[0].ap()[:16, 0:128])
        ev_n = P.tv.inc(v.tensor_copy(wrow[bb].ap(), P.psum[1].ap()[:1, 0:128]))
        P.tv.wait(nc.sync, ev_n)
        P.dma(f"st{bb}", Fp_d.ap()[0:16, sl], fT[bb].ap())
        P.dma(f"st{bb}", w_d.ap()[:, sl], wrow[bb].ap())
        done[bb] = P.tv.n
    P.barrier()
    _stg.close()

    emit_p8_experts(P, xn1fm, Fp_d, b2p_d, w1, b1, w2, acc_d)

    _st2 = contextlib.ExitStack()
    def ssb2(name, shape, dtype=F32):
        return _st2.enter_context(nc.sbuf_tensor(P.uname(name), list(shape), dtype))
    ones_row(P)
    acc_sb = [ssb2("f_acc", [128, 8, 128]) for _ in range(2)]
    s1_sb = [ssb2("f_s1", [128, 8, 128]) for _ in range(2)]
    x1_sb = [ssb2("f_x1", [128, 8, 128]) for _ in range(2)]
    wr_sb = ssb2("f_wr", [1, 128])
    gs_sb = ssb2("f_gs", [128, 8, 128])
    t_sb = ssb2("f_t", [128, 8, 128])
    o_sb = [ssb2("f_o", [128, 1024]) for _ in range(2)]
    idf2 = ssb2("f_id", [128, 128])
    P.dma("sync", idf2.ap(), P.const("identf", np.eye(128, dtype=np.float32)).ap())
    P.barrier()
    done2 = [0, 0]
    for ti in range(L // 128):
        bb = ti % 2
        sl = slice(ti * 128, (ti + 1) * 128)
        sy = nc.sync
        P.tv.wait(sy, done2[bb])
        P.dma(f"ld{bb}", acc_sb[bb].ap(),
              acc_d.ap().rearrange("(rc p) l -> p rc l", p=128)[:, :, sl])
        P.dma(f"ld{bb}", s1_sb[bb].ap(),
              s1fm.ap().rearrange("(rc p) l -> p rc l", p=128)[:, :, sl])
        P.dma(f"ld{bb}", x1_sb[bb].ap(),
              x1fm.ap().rearrange("(rc p) l -> p rc l", p=128)[:, :, sl])
        P.dma(f"ld{bb}", wr_sb.ap(), w_d.ap()[:, sl])
        P.qwait(t, f"ld{bb}")
        P.tv.wait(t)
        wb = P.tt.inc(t.matmul(P.psum[2].ap()[:, 0:128], ones_row(P).ap(), wr_sb.ap(),
                               start=True, stop=True))
        P.tt.wait(v, wb)
        for rc in range(8):
            v.scalar_tensor_tensor(out=t_sb.ap()[:, rc, :], in0=acc_sb[bb].ap()[:, rc, :],
                                   scalar=1.0, in1=P.psum[2].ap()[:, 0:128],
                                   op0=ALU.mult, op1=ALU.mult)
        g1_n = P.tv.inc(v.drain())
        P.tv.wait(s, g1_n)
        zs_n = P.ts.inc(s.activation(gs_sb.ap(), t_sb.ap(), AF.Silu))
        P.ts.wait(v, zs_n)
        v.tensor_tensor(t_sb.ap(), s1_sb[bb].ap(), gs_sb.ap(), ALU.mult)
        v.drain()
        v.tensor_tensor(t_sb.ap(), x1_sb[bb].ap(), t_sb.ap(), ALU.add)
        tt_n = P.tv.inc(v.drain())
        P.tv.wait(t, tt_n)
        for rc in range(8):
            half, col = rc // 4, (rc % 4) * 128
            itr = t.transpose(P.psum[3 + half].ap()[:, col:col + 128],
                              t_sb.ap()[:, rc, :], idf2.ap())
        tro = P.tt.inc(itr)
        P.tt.wait(v, tro)
        P.qwait(v, f"st{bb}")
        v.tensor_copy(o_sb[bb].ap()[:, 0:512], P.psum[3].ap())
        oe_n = P.tv.inc(v.tensor_copy(o_sb[bb].ap()[:, 512:1024], P.psum[4].ap()))
        P.tv.wait(sy, oe_n)
        P.dma(f"st{bb}", out.ap()[sl, :], o_sb[bb].ap())
        done2[bb] = P.tv.n
    P.barrier()
    _st2.close()


_PROGS = {}


def _prog(name, builder):
    if name not in _PROGS:
        _PROGS[name] = builder()
    return _PROGS[name]


def kernel_multi(**ins):
    x = np.ascontiguousarray(ins["L0_attn_u"], dtype=np.float32)
    ncore = x.shape[0]
    cores = list(range(ncore))

    def run(name, builder, per_core):
        P = _prog(name, builder)
        res = run_bass_kernel_spmd(P.nc, per_core, core_ids=cores)
        return res.results

    def rep(arrs):
        return [arrs for _ in cores] if isinstance(arrs, dict) else arrs

    r1 = run("p1", build_p1, [dict(x=np.ascontiguousarray(x[i])) for i in cores])
    w_in = ins["L0_attn_W_in"]
    r2 = run("p2", build_p2, [dict(xnfm=r1[i]["xnfm"], w_in=w_in) for i in cores])
    D = ins["L0_attn_D"]
    drep = np.ascontiguousarray(np.repeat(D, 64).reshape(16, 128).T)
    bbT = np.ascontiguousarray(ins["L0_attn_B_bias"].T)
    cbT = np.ascontiguousarray(ins["L0_attn_C_bias"].T)
    r3 = run("p3", build_p3, [dict(
        projfm=r2[i]["projfm"], xs_tok=r2[i]["xs_tok"],
        dt_bias=ins["L0_attn_dt_bias"], b_biasT=bbT, c_biasT=cbT,
        b_norm_w=ins["L0_attn_B_norm_w"], c_norm_w=ins["L0_attn_C_norm_w"],
        d_rep=drep) for i in cores])
    r4 = run("p4", build_p4, [dict(ygfm=r3[i]["ygfm"], w_out=ins["L0_attn_W_out"])
                              for i in cores])
    cur = [r1[i]["xnfm"] for i in cores]
    for l in range(2):
        im = []
        for i in cores:
            im.append(dict(
                rin=cur[i],
                wqkv=np.ascontiguousarray(ins["L0_ffn_wqkv"][l]),
                qs_col=np.ascontiguousarray(np.tile(ins["L0_ffn_q_scale"][l], 2)[:, None]),
                ks_col=np.ascontiguousarray(np.tile(ins["L0_ffn_k_scale"][l], 2)[:, None]),
                wo=np.ascontiguousarray(ins["L0_ffn_wo"][l]),
                lg=np.ascontiguousarray(ins["L0_ffn_ln_g"][l]),
                lbv=np.ascontiguousarray(ins["L0_ffn_ln_b"][l]),
                w1a=np.ascontiguousarray(ins["L0_ffn_w1"][l][:, 0:2730]),
                b1a=np.ascontiguousarray(ins["L0_ffn_b1"][l][0:2730]),
                w1g=np.ascontiguousarray(ins["L0_ffn_w1"][l][:, 2730:5460]),
                b1g=np.ascontiguousarray(ins["L0_ffn_b1"][l][2730:5460]),
                w2=np.ascontiguousarray(ins["L0_ffn_w2"][l]),
                b2v=np.ascontiguousarray(ins["L0_ffn_b2"][l])))
        r5 = run("p5", build_p5, im)
        cur = [r5[i]["rout"] for i in cores]
    r6 = run("p6", build_p6, [dict(xfm=r1[i]["xfm"], sfm=r4[i]["sfm"], gfm=cur[i])
                              for i in cores])
    r7 = run("p7", build_p7, [dict(
        xn1f32=r6[i]["xn1f32"], g_ln=ins["L1_attn_g_ln"], wq=ins["L1_attn_Wq"],
        wkv=ins["L1_attn_Wkv"], wo_l1=ins["L1_attn_Wo"]) for i in cores])
    r8 = run("p8", build_p8, [dict(
        xn1fm=r6[i]["xn1fm"], x1fm=r6[i]["x1fm"], s1fm=r7[i]["s1fm"],
        router_w=ins["L1_ffn_router_w"], w1=ins["L1_ffn_w1"],
        b1=np.ascontiguousarray(
            ins["L1_ffn_b1"].reshape(16, 16, 128).transpose(2, 0, 1).reshape(128, 256)),
        w2moe=ins["L1_ffn_w2"], b2all=ins["L1_ffn_b2"]) for i in cores])
    out = np.stack([np.asarray(r8[i]["out"], dtype=np.float32) for i in cores], axis=0)
    return out


def build_full():
    P = Prog()
    P.fconst(1e-5)
    ones_row(P)
    D = lambda *a, **k: P.dram(*a, **k)
    x = D("x", [L, DM], F32, kind="ExternalInput")
    w_in = D("w_in", [DM, 4480], F32, kind="ExternalInput")
    dt_bias = D("dt_bias", [32], F32, kind="ExternalInput")
    b_biasT = D("b_biasT", [128, 32], F32, kind="ExternalInput")
    c_biasT = D("c_biasT", [128, 32], F32, kind="ExternalInput")
    b_norm_w = D("b_norm_w", [128], F32, kind="ExternalInput")
    c_norm_w = D("c_norm_w", [128], F32, kind="ExternalInput")
    d_rep = D("d_rep", [128, 16], F32, kind="ExternalInput")
    w_out = D("w_out", [2048, DM], F32, kind="ExternalInput")
    wqkv2 = D("wqkv2", [2, DM, 1536], F32, kind="ExternalInput")
    qs2 = D("qs2", [2, 128, 1], F32, kind="ExternalInput")
    ks2 = D("ks2", [2, 128, 1], F32, kind="ExternalInput")
    wo2 = D("wo2", [2, 512, DM], F32, kind="ExternalInput")
    lg2 = D("lg2", [2, DM], F32, kind="ExternalInput")
    lb2 = D("lb2", [2, DM], F32, kind="ExternalInput")
    w1f = D("w1f", [2, DM, 5460], F32, kind="ExternalInput")
    b1f = D("b1f", [2, 5460], F32, kind="ExternalInput")
    w2f = D("w2f", [2, 2730, DM], F32, kind="ExternalInput")
    b2f = D("b2f", [2, DM], F32, kind="ExternalInput")
    g_ln = D("g_ln", [DM], F32, kind="ExternalInput")
    wq = D("wq", [DM, DM], F32, kind="ExternalInput")
    wkv = D("wkv", [DM, 2 * DM], F32, kind="ExternalInput")
    wo_l1 = D("wo_l1", [DM, DM], F32, kind="ExternalInput")
    router_w = D("router_w", [DM, 16], F32, kind="ExternalInput")
    mw1 = D("mw1", [16, DM, 2048], F32, kind="ExternalInput")
    mb1 = D("mb1", [128, 256], F32, kind="ExternalInput")
    mw2 = D("mw2", [16, 2048, DM], F32, kind="ExternalInput")
    mb2 = D("mb2", [16, DM], F32, kind="ExternalInput")
    out = D("out", [L, DM], F32, kind="ExternalOutput")

    xfm = D("xfm", [DM, L], F32)
    xnfm = D("xnfm", [DM, L], F32)
    projfm = D("projfm", [4480, L], F32)
    xs_tok = D("xs_tok", [L, 2048], BF16)
    ygfm = D("ygfm", [2048, L], BF16)
    sfm = D("sfm", [DM, L], F32)
    r5a = D("r5a", [DM, L], F32)
    r5b = D("r5b", [DM, L], F32)
    x1fm = D("x1fm", [DM, L], F32)
    xn1fm = D("xn1fm", [DM, L], BF16)
    xn1f32 = D("xn1f32", [DM, L], F32)
    s1fm = D("s1fm", [DM, L], F32)
    sc5 = p5_scratch(P)

    with P.scope():
        emit_p1(P, x, xfm, xnfm)
    with P.scope():
        emit_p2(P, xnfm, w_in, projfm, xs_tok)
    with P.scope():
        emit_p3(P, projfm, xs_tok, dt_bias, b_biasT, c_biasT, b_norm_w, c_norm_w,
                d_rep, ygfm)
    fm_matmul(P, ygfm, w_out, sfm, K=2048, N=DM)
    rin = xnfm
    for l, rout_l in ((0, r5a), (1, r5b)):
        with P.scope():
            emit_p5(P, rin, rout_l,
                    _View(wqkv2.ap()[l], f"wqkv{l}"),
                    _View(qs2.ap()[l], f"qs{l}"), _View(ks2.ap()[l], f"ks{l}"),
                    _View(wo2.ap()[l], f"wo{l}"),
                    _View(lg2.ap()[l], f"lg{l}"), _View(lb2.ap()[l], f"lb{l}"),
                    _View(w1f.ap()[l][:, 0:2730], f"w1a{l}"),
                    _View(b1f.ap()[l][0:2730], f"b1a{l}"),
                    _View(w1f.ap()[l][:, 2730:5460], f"w1g{l}"),
                    _View(b1f.ap()[l][2730:5460], f"b1g{l}"),
                    _View(w2f.ap()[l], f"w2l{l}"), _View(b2f.ap()[l], f"b2l{l}"),
                    sc5)
        rin = rout_l
    with P.scope():
        emit_p6(P, xfm, sfm, r5b, x1fm, xn1fm, xn1f32)
    with P.scope():
        emit_p7(P, xn1f32, g_ln, wq, wkv, wo_l1, s1fm)
    with P.scope():
        emit_p8(P, xn1fm, x1fm, s1fm, router_w, mw1, mb1, mw2, mb2, out)
    P.barrier()
    return P


def kernel(**ins):
    return kernel_fused(**ins)


def kernel_fused(**ins):
    x = np.ascontiguousarray(ins["L0_attn_u"], dtype=np.float32)
    ncore = x.shape[0]
    cores = list(range(ncore))
    P = _prog("full", build_full)
    D = ins["L0_attn_D"]
    common = dict(
        w_in=ins["L0_attn_W_in"],
        dt_bias=ins["L0_attn_dt_bias"],
        b_biasT=np.ascontiguousarray(ins["L0_attn_B_bias"].T),
        c_biasT=np.ascontiguousarray(ins["L0_attn_C_bias"].T),
        b_norm_w=ins["L0_attn_B_norm_w"], c_norm_w=ins["L0_attn_C_norm_w"],
        d_rep=np.ascontiguousarray(np.repeat(D, 64).reshape(16, 128).T),
        w_out=ins["L0_attn_W_out"],
        wqkv2=ins["L0_ffn_wqkv"],
        qs2=np.ascontiguousarray(np.tile(ins["L0_ffn_q_scale"], (1, 2))[:, :, None]),
        ks2=np.ascontiguousarray(np.tile(ins["L0_ffn_k_scale"], (1, 2))[:, :, None]),
        wo2=ins["L0_ffn_wo"], lg2=ins["L0_ffn_ln_g"], lb2=ins["L0_ffn_ln_b"],
        w1f=ins["L0_ffn_w1"], b1f=ins["L0_ffn_b1"], w2f=ins["L0_ffn_w2"],
        b2f=ins["L0_ffn_b2"],
        g_ln=ins["L1_attn_g_ln"], wq=ins["L1_attn_Wq"], wkv=ins["L1_attn_Wkv"],
        wo_l1=ins["L1_attn_Wo"],
        router_w=ins["L1_ffn_router_w"], mw1=ins["L1_ffn_w1"],
        mb1=np.ascontiguousarray(
            ins["L1_ffn_b1"].reshape(16, 16, 128).transpose(2, 0, 1).reshape(128, 256)),
        mw2=ins["L1_ffn_w2"], mb2=ins["L1_ffn_b2"])
    im = [dict(common, x=np.ascontiguousarray(x[i])) for i in cores]
    res = run_bass_kernel_spmd(P.nc, im, core_ids=cores)
    return np.stack([np.asarray(res.results[i]["out"], dtype=np.float32)
                     for i in cores], axis=0)


# version 06
# speedup vs baseline: 1.3710x; 1.0670x over previous
import contextlib
import numpy as np
import concourse.bass as bass
import concourse.mybir as mybir
from concourse.bass_utils import run_bass_kernel_spmd

F32 = mybir.dt.float32
BF16 = mybir.dt.bfloat16
AF = mybir.ActivationFunctionType
ALU = mybir.AluOpType
AX = mybir.AxisListType

L = 4096
DM = 1024
NCORE = 8
NEG = -1e30


def _ceil(a, b):
    return -(-a // b)


class Tok:
    def __init__(self, prog, name):
        self.sem = prog.stack.enter_context(prog.nc.semaphore(name))
        self.n = 0

    def inc(self, instr):
        instr.then_inc(self.sem, 1)
        self.n += 1
        return self.n

    def wait(self, eng, n=None):
        eng.wait_ge(self.sem, self.n if n is None else n)


class Prog:
    def __init__(self):
        self.nc = bass.Bass("TRN2", target_bir_lowering=False)
        self.stack = contextlib.ExitStack()
        nc = self.nc
        self.q = {
            "sync": [nc.sync, self.stack.enter_context(nc.semaphore("q_sync")), 0],
            "gp": [nc.gpsimd, self.stack.enter_context(nc.semaphore("q_gp")), 0],
            "act": [nc.scalar, self.stack.enter_context(nc.semaphore("q_act")), 0],
            "st": [nc.sync, self.stack.enter_context(nc.semaphore("q_st")), 0],
            "ld0": [nc.sync, self.stack.enter_context(nc.semaphore("q_ld0")), 0],
            "ld1": [nc.sync, self.stack.enter_context(nc.semaphore("q_ld1")), 0],
            "gp0": [nc.gpsimd, self.stack.enter_context(nc.semaphore("q_gp0")), 0],
            "fr": [nc.sync, self.stack.enter_context(nc.semaphore("q_fr")), 0],
            "st0": [nc.sync, self.stack.enter_context(nc.semaphore("q_st0")), 0],
            "st1": [nc.sync, self.stack.enter_context(nc.semaphore("q_st1")), 0],
            "st2": [nc.sync, self.stack.enter_context(nc.semaphore("q_st2")), 0],
            "st3": [nc.sync, self.stack.enter_context(nc.semaphore("q_st3")), 0],
            "gp1": [nc.gpsimd, self.stack.enter_context(nc.semaphore("q_gp1")), 0],
        }
        self.tv = Tok(self, "tok_v")
        self.ts = Tok(self, "tok_s")
        self.tt = Tok(self, "tok_t")
        self.tg = Tok(self, "tok_g")
        self.psum = [nc.alloc_psum_tensor(f"pb{i}", [128, 512], F32) for i in range(6)]
        self.psum_bf = [nc.alloc_psum_tensor(f"pbb{i}", [128, 1024], BF16) for i in range(2)]
        self._consts = {}
        self._names = {}
        self._fconsts = {}

    def eng(self, qn):
        return self.q[qn][0]

    def dma(self, qn, out, in_):
        eng, sem, _ = self.q[qn]
        r = eng.dma_start(out=out, in_=in_).then_inc(sem, 16)
        self.q[qn][2] += 1
        return self.q[qn][2]

    def qcnt(self, qn):
        return self.q[qn][2]

    def qwait(self, eng, qn, cnt=None):
        _, sem, n = self.q[qn]
        eng.wait_ge(sem, 16 * (n if cnt is None else cnt))

    def flush(self):
        for qn, (eng, sem, cnt) in self.q.items():
            if cnt:
                eng.wait_ge(sem, 16 * cnt)

    def barrier(self):
        self.flush()
        self.nc.all_engine_barrier()

    def const(self, name, arr):
        if name not in self._consts:
            self._consts[name] = self.nc.inline_tensor(np.ascontiguousarray(arr), name=name)
        return self._consts[name]

    def fconst(self, val):
        if val not in self._fconsts:
            t = self.nc.alloc_sbuf_tensor(self.uname("fc"), [128, 1], F32)
            self.nc.gpsimd.memset(t.ap(), float(val))
            self._fconsts[val] = t
        return self._fconsts[val].ap()

    @contextlib.contextmanager
    def scope(self):
        stg = contextlib.ExitStack()
        saved = self.sb

        def sb2(name, shape, dtype=F32):
            return stg.enter_context(
                self.nc.sbuf_tensor(self.uname(name), list(shape), dtype))
        self.sb = sb2
        try:
            yield
        finally:
            self.barrier()
            self.sb = saved
            stg.close()

    def uname(self, base):
        k = self._names.get(base, 0)
        self._names[base] = k + 1
        return f"{base}{k}"

    def sb(self, name, shape, dtype=F32):
        return self.nc.alloc_sbuf_tensor(self.uname(name), list(shape), dtype)

    def dram(self, name, shape, dtype=F32, kind="Internal"):
        return self.nc.dram_tensor(name, list(shape), dtype, kind=kind)

    def close(self):
        self.stack.close()


def fm_matmul(P, X, W, out, K, N, Lw=None, bias=None, act=None, scale=1.0,
              out_dtype=F32, TTT=512, gate_row=None, accum=False):
    Lw = L if Lw is None else Lw
    assert not (accum and (bias is not None or act is not None or scale != 1.0))
    nc = P.nc
    if accum:
        ones_row(P)
    kcs = K // 128
    nch = _ceil(N, 128)
    ntt = Lw // TTT
    _stg = contextlib.ExitStack()
    def _sb(name, shape, dtype=F32):
        return _stg.enter_context(nc.sbuf_tensor(P.uname(name), list(shape), dtype))
    P_sb_saved = P.sb
    P.sb = _sb
    wsb = P.sb("mmw", [128, kcs, N], BF16)
    P.dma("gp", wsb.ap(), W.ap().rearrange("(kc p) n -> p kc n", p=128))
    bsb = None
    if bias is not None:
        bsb = P.sb("mmbias", [128, nch], F32)
        for i in range(nch):
            rows = min(128, N - i * 128)
            P.dma("sync", bsb.ap()[:rows, i:i + 1], bias.ap()[i * 128:i * 128 + rows].unsqueeze(1))
    P.barrier()

    xsb = [P.sb("mmx", [128, kcs, TTT], BF16) for _ in range(2)]
    osb = [P.sb("mmo", [128, TTT], out_dtype) for _ in range(4)]
    if accum:
        fb_sb = P.sb("mmfb", [128, TTT], F32)
        tmp_sb = P.sb("mmtmp", [128, TTT], F32)
        rsb = [P.sb("mmr", [128, TTT], F32) for _ in range(2)]
        racc_free = [0, 0]
    xa = X.ap().rearrange("(kc p) l -> p kc l", p=128)
    use_scalar = (act is not None) or (bias is not None) or (scale != 1.0)
    etok = P.ts if use_scalar else P.tv
    eeng = nc.scalar if use_scalar else nc.vector

    pe_done_at_x = [0, 0]
    bank_free = [0] * 2

    frow_ld = 0
    for tt in range(ntt):
        xb = tt % 2
        gp = nc.gpsimd
        P.tt.wait(gp, pe_done_at_x[xb])
        P.dma(f"gp{xb}", xsb[xb].ap(), xa[:, :, tt * TTT:(tt + 1) * TTT])
        if accum and gate_row is not None:
            P.tv.wait(nc.sync)
            P.dma("fr", fb_sb.ap()[0:1, :],
                  gate_row.ap()[:, tt * TTT:(tt + 1) * TTT])
            t = nc.tensor
            P.qwait(t, "fr")
            P.tv.wait(t)
            fmm = P.tt.inc(t.matmul(P.psum[2].ap()[:, 0:TTT], ones_row(P).ap(),
                                    fb_sb.ap()[0:1, :], start=True, stop=True))
            v0 = nc.vector
            P.tt.wait(v0, fmm)
            fb_n = P.tv.inc(v0.tensor_copy(fb_sb.ap(), P.psum[2].ap()[:, 0:TTT]))
            v0.drain()
            P.tv.wait(v0, fb_n)
        for ncI in range(nch):
            bk = ncI % 2
            pb = P.psum[bk]
            rows = min(128, N - ncI * 128)
            t = nc.tensor
            P.qwait(t, f"gp{xb}")
            etok.wait(t, bank_free[bk])
            for kc in range(kcs):
                ins = t.matmul(pb.ap()[:rows, :],
                               wsb.ap()[:, kc, ncI * 128:ncI * 128 + rows],
                               xsb[xb].ap()[:, kc, :],
                               start=(kc == 0), stop=(kc == kcs - 1))
            mm_n = P.tt.inc(ins)
            j = ncI % 4
            ob = osb[j]
            if accum:
                P.tv.wait(nc.sync, racc_free[ncI % 2])
                P.dma(f"ld{xb}", rsb[ncI % 2].ap()[:rows, :],
                      out.ap()[ncI * 128:ncI * 128 + rows, tt * TTT:(tt + 1) * TTT])
            P.tt.wait(eeng, mm_n)
            P.qwait(eeng, f"st{j}")
            if accum:
                v0 = nc.vector
                P.qwait(v0, f"ld{xb}")
                if gate_row is not None:
                    v0.scalar_tensor_tensor(out=tmp_sb.ap()[:rows, :],
                                            in0=fb_sb.ap()[:rows, :], scalar=1.0,
                                            in1=pb.ap()[:rows, :], op0=ALU.mult,
                                            op1=ALU.mult)
                else:
                    v0.tensor_copy(tmp_sb.ap()[:rows, :], pb.ap()[:rows, :])
                v0.drain()
                ev = v0.tensor_tensor(ob.ap()[:rows, :], tmp_sb.ap()[:rows, :],
                                      rsb[ncI % 2].ap()[:rows, :], ALU.add)
            elif use_scalar:
                kw = dict(scale=scale)
                if bias is not None:
                    kw["bias"] = bsb.ap()[:rows, ncI:ncI + 1]
                ev = eeng.activation(ob.ap()[:rows, :], pb.ap()[:rows, :],
                                     act if act is not None else AF.Identity, **kw)
            else:
                ev = eeng.tensor_copy(ob.ap()[:rows, :], pb.ap()[:rows, :])
            bank_free[bk] = etok.inc(ev)
            if accum:
                racc_free[ncI % 2] = bank_free[bk]
            sy = nc.sync
            etok.wait(sy)
            P.dma(f"st{j}", out.ap()[ncI * 128:ncI * 128 + rows, tt * TTT:(tt + 1) * TTT],
                  ob.ap()[:rows, :])
        pe_done_at_x[xb] = P.tt.n
    P.barrier()
    P.sb = P_sb_saved
    _stg.close()


def tok_matmul(P, X, W, out, K, N, Lw=None, out_dtype=F32):
    Lw = L if Lw is None else Lw
    nc = P.nc
    kcs = K // 128
    nns = _ceil(N, 512)
    _stg = contextlib.ExitStack()
    def _sb(name, shape, dtype=F32):
        return _stg.enter_context(nc.sbuf_tensor(P.uname(name), list(shape), dtype))
    P_sb_saved = P.sb
    P.sb = _sb
    wsb = P.sb("wt", [128, kcs, N], BF16)
    P.dma("gp", wsb.ap(), W.ap().rearrange("(kc p) n -> p kc n", p=128))
    P.barrier()
    xsb = [P.sb("xt", [128, kcs, 128], BF16) for _ in range(2)]
    osb = [P.sb("ot", [128, 512], out_dtype) for _ in range(4)]
    xa = X.ap().rearrange("(kc p) l -> p kc l", p=128)
    pe_done_at_x = [0, 0]
    bank_free = [0] * 2
    for ti in range(Lw // 128):
        xb = ti % 2
        gp = nc.gpsimd
        P.tt.wait(gp, pe_done_at_x[xb])
        P.dma(f"gp{xb}", xsb[xb].ap(), xa[:, :, ti * 128:(ti + 1) * 128])
        for ns in range(nns):
            cols = min(512, N - ns * 512)
            bk = ns % 2
            pb = P.psum[bk]
            t = nc.tensor
            P.qwait(t, f"gp{xb}")
            P.tv.wait(t, bank_free[bk])
            for kc in range(kcs):
                ins = t.matmul(pb.ap()[:, :cols],
                               xsb[xb].ap()[:, kc, :],
                               wsb.ap()[:, kc, ns * 512:ns * 512 + cols],
                               start=(kc == 0), stop=(kc == kcs - 1))
            mm_n = P.tt.inc(ins)
            j = (ti * nns + ns) % 4
            ob = osb[j]
            v = nc.vector
            P.tt.wait(v, mm_n)
            P.qwait(v, f"st{j}")
            ev = v.tensor_copy(ob.ap()[:, :cols], pb.ap()[:, :cols])
            bank_free[bk] = P.tv.inc(ev)
            sy = nc.sync
            P.tv.wait(sy)
            P.dma(f"st{j}", out.ap()[ti * 128:(ti + 1) * 128, ns * 512:ns * 512 + cols],
                  ob.ap()[:, :cols])
        pe_done_at_x[xb] = P.tt.n
    P.barrier()
    P.sb = P_sb_saved
    _stg.close()


def fm_elemwise(P, ins, outs, R, emit, Lw=None, TTT=2048, in_dtypes=None, out_dtypes=None):
    Lw = L if Lw is None else Lw
    TTT = min(TTT, Lw)
    nc = P.nc
    rch = _ceil(R, 128)
    ntt = Lw // TTT
    if in_dtypes is None:
        in_dtypes = [F32] * len(ins)
    if out_dtypes is None:
        out_dtypes = [F32] * len(outs)
    _stg = contextlib.ExitStack()
    def _sb(name, shape, dtype=F32):
        return _stg.enter_context(nc.sbuf_tensor(P.uname(name), list(shape), dtype))
    isb = [[_sb("ew_i", [128, TTT], dt) for dt in in_dtypes] for _ in range(2)]
    osb = [[_sb("ew_o", [128, TTT], dt) for dt in out_dtypes] for _ in range(2)]
    done = [0, 0]
    for idx in range(rch * ntt):
        rc, tt = divmod(idx, ntt)
        rows = min(128, R - rc * 128)
        bb = idx % 2
        gp = nc.gpsimd
        P.tv.wait(gp, done[bb])
        for k, x in enumerate(ins):
            P.dma(f"gp{bb}", isb[bb][k].ap()[:rows, :],
                  x.ap()[rc * 128:rc * 128 + rows, tt * TTT:(tt + 1) * TTT])
        v = nc.vector
        P.qwait(v, f"gp{bb}")
        P.qwait(v, f"st{bb}")
        P.qwait(nc.scalar, f"gp{bb}")
        P.qwait(nc.scalar, f"st{bb}")
        last = emit(v, [o.ap()[:rows, :] for o in osb[bb]], [i.ap()[:rows, :] for i in isb[bb]])
        done[bb] = P.tv.inc(last)
        sy = nc.sync
        P.tv.wait(sy)
        for k, o in enumerate(outs):
            P.dma(f"st{bb}", o.ap()[rc * 128:rc * 128 + rows, tt * TTT:(tt + 1) * TTT],
                  osb[bb][k].ap()[:rows, :])
    P.barrier()
    _stg.close()


def build_p1():
    P = Prog()
    x = P.dram("x", [L, DM], F32, kind="ExternalInput")
    xfm = P.dram("xfm", [DM, L], F32, kind="ExternalOutput")
    xnfm = P.dram("xnfm", [DM, L], F32, kind="ExternalOutput")
    emit_p1(P, x, xfm, xnfm)
    return P


def emit_p1(P, x, xfm, xnfm):
    nc = P.nc
    idf = P.const("identf", np.eye(128, dtype=np.float32))
    id_s = P.sb("id_s", [128, 128], F32)
    P.dma("sync", id_s.ap(), idf.ap())
    eps_c = P.fconst(1e-5)

    xt = [P.sb("xt", [128, DM], F32) for _ in range(2)]
    xnb = [P.sb("xnb", [128, DM], F32) for _ in range(2)]
    stat = P.sb("stat", [128, 8], F32)
    ofm = [P.sb("ofm", [128, DM], F32) for _ in range(2)]
    onb = [P.sb("onb", [128, DM], F32) for _ in range(2)]
    P.barrier()

    cons = [0, 0]
    prev_ev = 0
    prev_xn = 0
    for ti in range(L // 128):
        bb = ti % 2
        gp = nc.gpsimd
        P.tt.wait(gp, cons[bb])
        P.dma(f"gp{bb}", xt[bb].ap(), x.ap()[ti * 128:(ti + 1) * 128, :])
        v, s, t = nc.vector, nc.scalar, nc.tensor
        P.qwait(v, f"gp{bb}")
        P.ts.wait(v, prev_xn)
        v.reduce_sum(stat.ap()[:, 0:1], xt[bb].ap(), axis=AX.X)
        v.drain()
        P.qwait(s, f"gp{bb}")
        P.tt.wait(s, cons[bb])
        sq_n = P.ts.inc(s.activation(xnb[bb].ap(), xt[bb].ap(), AF.Square))
        P.ts.wait(v, sq_n)
        v.reduce_sum(stat.ap()[:, 1:2], xnb[bb].ap(), axis=AX.X)
        v.drain()
        v.tensor_scalar(out=stat.ap()[:, 0:1], in0=stat.ap()[:, 0:1], scalar1=1.0 / DM,
                        scalar2=None, op0=ALU.mult)
        v.drain()
        v.tensor_tensor(stat.ap()[:, 2:3], stat.ap()[:, 0:1], stat.ap()[:, 0:1], ALU.mult)
        v.drain()
        var_n = P.tv.inc(v.scalar_tensor_tensor(out=stat.ap()[:, 2:3], in0=stat.ap()[:, 1:2],
                               scalar=1.0 / DM, in1=stat.ap()[:, 2:3],
                               op0=ALU.mult, op1=ALU.subtract))
        v.drain()
        P.tv.wait(s, var_n)
        sd_n = P.ts.inc(s.activation(stat.ap()[:, 3:4], stat.ap()[:, 2:3], AF.Sqrt, bias=eps_c))
        P.ts.wait(v, sd_n)
        v.reciprocal(stat.ap()[:, 4:5], stat.ap()[:, 3:4])
        v.drain()
        r_n = P.tv.inc(v.scalar_tensor_tensor(out=stat.ap()[:, 5:6], in0=stat.ap()[:, 0:1],
                               scalar=-1.0, in1=stat.ap()[:, 4:5],
                               op0=ALU.mult, op1=ALU.mult))
        v.drain()
        P.tv.wait(s, r_n)
        xn_n = P.ts.inc(s.activation(xnb[bb].ap(), xt[bb].ap(), AF.Identity,
                     scale=stat.ap()[:, 4:5], bias=stat.ap()[:, 5:6]))
        prev_xn = xn_n
        P.ts.wait(t, xn_n)
        P.tv.wait(t, prev_ev)
        tr_n = 0
        for half in range(2):
            for c in range(4):
                cc = half * 4 + c
                pbn = P.psum[half]
                pbx = P.psum[2 + half]
                i1 = t.transpose(pbn.ap()[:, c * 128:(c + 1) * 128],
                                 xnb[bb].ap()[:, cc * 128:(cc + 1) * 128], id_s.ap())
                i2 = t.transpose(pbx.ap()[:, c * 128:(c + 1) * 128],
                                 xt[bb].ap()[:, cc * 128:(cc + 1) * 128], id_s.ap())
        cons[bb] = P.tt.inc(i2)
        P.tt.wait(v)
        P.qwait(v, f"st{bb}")
        v.tensor_copy(onb[bb].ap()[:, 0:512], P.psum[0].ap())
        v.tensor_copy(onb[bb].ap()[:, 512:1024], P.psum[1].ap())
        v.tensor_copy(ofm[bb].ap()[:, 0:512], P.psum[2].ap())
        ev_n = P.tv.inc(v.tensor_copy(ofm[bb].ap()[:, 512:1024], P.psum[3].ap()))
        prev_ev = ev_n
        sy = nc.sync
        P.tv.wait(sy)
        P.dma("st", xnfm.ap().rearrange("(c p) l -> p c l", p=128)[:, :, ti * 128:(ti + 1) * 128],
              onb[bb].ap().rearrange("p (c t) -> p c t", t=128))
        P.dma("st", xfm.ap().rearrange("(c p) l -> p c l", p=128)[:, :, ti * 128:(ti + 1) * 128],
              ofm[bb].ap().rearrange("p (c t) -> p c t", t=128))
    P.barrier()


def build_p2():
    P = Prog()
    xnfm = P.dram("xnfm", [DM, L], F32, kind="ExternalInput")
    w_in = P.dram("w_in", [DM, 4480], F32, kind="ExternalInput")
    projfm = P.dram("projfm", [4480, L], F32, kind="ExternalOutput")
    xs_tok = P.dram("xs_tok", [L, 2048], BF16, kind="ExternalOutput")
    emit_p2(P, xnfm, w_in, projfm, xs_tok)
    return P


def emit_p2(P, xnfm, w_in, projfm, xs_tok):
    nc = P.nc
    fm_matmul(P, xnfm, w_in, projfm, K=DM, N=4480)
    kcs = 8
    wsb = P.sb("wsm", [128, kcs, 128], F32)
    P.dma("sync", wsb.ap(), w_in.ap().rearrange("(kc p) n -> p kc n", p=128)[:, :, 4352:4480])
    P.barrier()
    xsb = [P.sb("xsm", [128, kcs, 512], F32) for _ in range(2)]
    osb = [P.sb("osm", [128, 512], F32) for _ in range(2)]
    pe_done = [0, 0]; bank_free = [0, 0]
    xa = xnfm.ap().rearrange("(kc p) l -> p kc l", p=128)
    for tt in range(L // 512):
        bb = tt % 2
        gp = nc.gpsimd
        P.tt.wait(gp, pe_done[bb])
        P.dma(f"gp{bb}", xsb[bb].ap(), xa[:, :, tt * 512:(tt + 1) * 512])
        t = nc.tensor
        P.qwait(t, f"gp{bb}")
        P.tv.wait(t, bank_free[bb])
        pb = P.psum[bb]
        for kc in range(kcs):
            ins = t.matmul(pb.ap(), wsb.ap()[:, kc, :], xsb[bb].ap()[:, kc, :],
                           start=(kc == 0), stop=(kc == kcs - 1))
        mm_n = P.tt.inc(ins)
        v = nc.vector
        P.tt.wait(v, mm_n)
        P.qwait(v, f"st{bb}")
        bank_free[bb] = P.tv.inc(v.tensor_copy(osb[bb].ap(), pb.ap()))
        sy = nc.sync
        P.tv.wait(sy)
        P.dma(f"st{bb}", projfm.ap()[4352:4480, tt * 512:(tt + 1) * 512], osb[bb].ap())
        pe_done[bb] = P.tt.n
    P.barrier()

    class _View:
        def __init__(self, ap, name):
            self._ap = ap; self.name = name
        def ap(self):
            return self._ap
    wv = _View(w_in.ap()[:, 2048:4096], "w_xs")
    tok_matmul(P, xnfm, wv, xs_tok, K=DM, N=2048, out_dtype=BF16)


def kernel(**inputs):
    raise NotImplementedError("under construction")


NH = 32
HD = 64
NST = 128
CH = 128
NCH = L // CH
PI = float(np.pi)


def build_p3(dbg=(), serial=0):
    P = Prog()
    projfm = P.dram("projfm", [4480, L], F32, kind="ExternalInput")
    xs_tok = P.dram("xs_tok", [L, 2048], BF16, kind="ExternalInput")
    dt_bias = P.dram("dt_bias", [32], F32, kind="ExternalInput")
    b_biasT = P.dram("b_biasT", [128, 32], F32, kind="ExternalInput")
    c_biasT = P.dram("c_biasT", [128, 32], F32, kind="ExternalInput")
    b_norm_w = P.dram("b_norm_w", [128], F32, kind="ExternalInput")
    c_norm_w = P.dram("c_norm_w", [128], F32, kind="ExternalInput")
    d_rep = P.dram("d_rep", [128, 16], F32, kind="ExternalInput")
    ygfm = P.dram("ygfm", [2048, L], BF16, kind="ExternalOutput")
    P.fconst(1e-5)
    with P.scope():
        emit_p3(P, projfm, xs_tok, dt_bias, b_biasT, c_biasT, b_norm_w, c_norm_w,
                d_rep, ygfm, dbg=dbg, serial=serial)
    return P


def emit_p3(P, projfm, xs_tok, dt_bias, b_biasT, c_biasT, b_norm_w, c_norm_w,
            d_rep, ygfm, dbg=(), serial=0):
    nc = P.nc
    v, s, t, g = nc.vector, nc.scalar, nc.tensor, nc.gpsimd

    def dk(name):
        return "ExternalOutput" if name in dbg else "Internal"

    ell_d = P.dram("ell_d", [32, L], F32, kind=dk("ell_d"))
    dt_d = P.dram("dt_d", [32, L], F32, kind=dk("dt_d"))
    c_d = P.dram("c_d", [NH * 32, L], F32, kind=dk("c_d"))
    s_d = P.dram("s_d", [NH * 32, L], F32, kind=dk("s_d"))
    K_d = P.dram("K_d", [128, NH, L + 1], BF16, kind=dk("K_d"))
    Q_d = P.dram("Q_d", [128, NH, L], BF16, kind=dk("Q_d"))
    Qe_d = P.dram("Qe_d", [128, NH, L], BF16, kind=dk("Qe_d"))

    oh = np.zeros((32, 32 * 128), dtype=np.float32)
    for h in range(32):
        oh[h, h * 128:(h + 1) * 128] = 1.0
    ohc = P.const("ohc", oh)
    rep4 = np.zeros((32, 128), dtype=np.float32)
    for r in range(128):
        rep4[r % 32, r] = 1.0
    rep4c = P.const("rep4c", rep4)
    rdt = np.zeros((32, 8 * 128), dtype=np.float32)
    for b in range(8):
        for r in range(128):
            rdt[4 * b + r // 32, b * 128 + r] = 1.0
    rdtc = P.const("rdtc", rdt)
    msk = np.where(np.arange(128)[:, None] <= np.arange(128)[None, :], 0.0, NEG
                   ).astype(np.float32)
    mskc = P.const("mskc", msk)
    onesr = P.const("onesr128", np.ones((1, 128), dtype=np.float32))

    ell_sb = P.sb("ell_sb", [32, L], F32)
    elltok = P.sb("elltok", [128, NCH * 32], F32)
    g1tok = P.sb("g1tok", [128, NCH * 32], F32)
    g2tok = P.sb("g2tok", [128, NCH * 32], F32)
    w1full = P.sb("w1full", [128, NCH * 32], F32)
    w2full = P.sb("w2full", [128, NCH * 32], F32)
    cdb = P.sb("cdb", [128, NCH * 32], F32)
    ones_sb = P.sb("ones_sb", [1, 128], F32)
    id_sb = P.sb("id_sb", [128, 128], F32)
    idb_sb = P.sb("idb_sb", [128, 128], BF16)
    drep_sb = P.sb("drep_sb", [128, 16], F32)
    P.dma("sync", ones_sb.ap(), onesr.ap())
    P.dma("sync", id_sb.ap(), P.const("identf", np.eye(128, dtype=np.float32)).ap())
    P.dma("gp", idb_sb.ap(), P.const("identf", np.eye(128, dtype=np.float32)).ap())
    P.dma("sync", drep_sb.ap(), d_rep.ap())
    eps_c3 = P.fconst(1e-5)
    st_dt = contextlib.ExitStack()
    st_bn = contextlib.ExitStack()

    dt_sb = st_dt.enter_context(nc.sbuf_tensor("dt_sb", [32, L + 128], F32))
    with contextlib.ExitStack() as stg:
        def ssb(name, shape, dtype=F32):
            return stg.enter_context(nc.sbuf_tensor(P.uname(name), list(shape), dtype))
        dd_dt = ssb("dd_dt", [32, L]); dd_A = ssb("dd_A", [32, L])
        trap = ssb("trap", [32, L])
        tmp1 = ssb("tmp1", [32, L]); tmp2 = ssb("tmp2", [32, L])
        g1f = dd_dt; g2f = tmp1; adt = dd_A; beta = trap
        dtb_sb = ssb("dtb_sb", [32, 1])
        P.dma("sync", dd_dt.ap(), projfm.ap()[4352:4384, :])
        P.dma("sync", dd_A.ap(), projfm.ap()[4384:4416, :])
        P.dma("sync", trap.ap(), projfm.ap()[4416:4448, :])
        P.dma("sync", dtb_sb.ap(), dt_bias.ap().unsqueeze(1))
        g.memset(dt_sb.ap()[:, 0:128], 0.0)
        segm = ssb("segm", [32, L])
        P.tg.inc(g.memset(segm.ap(), 1.0))
        P.tg.wait(g)
        for c in range(NCH):
            g.memset(segm.ap()[:, c * CH:c * CH + 1], 0.0)
        P.barrier()

        v.tensor_scalar(out=dd_dt.ap(), in0=dd_dt.ap(), scalar1=dtb_sb.ap(), scalar2=None,
                        op0=ALU.add)
        u_n = P.tv.inc(v.drain())
        P.tv.wait(s, u_n)
        P.ts.wait(s, P.ts.inc(s.activation(tmp1.ap(), dd_dt.ap(), AF.Abs)))
        P.ts.wait(s, P.ts.inc(s.activation(tmp1.ap(), tmp1.ap(), AF.Exp, scale=-1.0)))
        ln_n = P.ts.inc(s.activation(tmp1.ap(), tmp1.ap(), AF.Ln, bias=1.0))
        v.tensor_scalar(out=tmp2.ap(), in0=dd_dt.ap(), scalar1=0.0, scalar2=None, op0=ALU.max)
        v.drain()
        P.ts.wait(v, ln_n)
        v.tensor_tensor(dt_sb.ap()[:, 128:], tmp2.ap(), tmp1.ap(), ALU.add)
        dt_n = P.tv.inc(v.drain())
        P.tv.wait(s, dt_n)
        P.ts.wait(s, P.ts.inc(s.activation(tmp1.ap(), dd_A.ap(), AF.Abs)))
        P.ts.wait(s, P.ts.inc(s.activation(tmp1.ap(), tmp1.ap(), AF.Exp, scale=-1.0)))
        ln2_n = P.ts.inc(s.activation(tmp1.ap(), tmp1.ap(), AF.Ln, bias=1.0))
        v.tensor_scalar(out=tmp2.ap(), in0=dd_A.ap(), scalar1=0.0, scalar2=None, op0=ALU.max)
        v.drain()
        P.ts.wait(v, ln2_n)
        v.tensor_tensor(tmp2.ap(), tmp2.ap(), tmp1.ap(), ALU.add)
        v.drain()
        v.tensor_scalar(out=tmp2.ap(), in0=tmp2.ap(), scalar1=1e-4, scalar2=None, op0=ALU.max)
        v.drain()
        v.scalar_tensor_tensor(out=adt.ap(), in0=tmp2.ap(), scalar=-1.0,
                               in1=dt_sb.ap()[:, 128:], op0=ALU.mult, op1=ALU.mult)
        adt_n = P.tv.inc(v.drain())
        bt_n = P.ts.inc(s.activation(beta.ap(), trap.ap(), AF.Sigmoid))
        P.tv.wait(s, adt_n)
        ea_n = P.ts.inc(s.activation(tmp1.ap(), adt.ap(), AF.Exp))
        P.ts.wait(v, bt_n)
        v.tensor_tensor(g1f.ap(), beta.ap(), dt_sb.ap()[:, 128:], ALU.mult)
        v.tensor_scalar(out=tmp2.ap(), in0=beta.ap(), scalar1=-1.0, scalar2=1.0,
                        op0=ALU.mult, op1=ALU.add)
        v.drain()
        P.ts.wait(v, ea_n)
        v.tensor_tensor(tmp2.ap(), tmp2.ap(), tmp1.ap(), ALU.mult)
        v.drain()
        v.tensor_tensor(g2f.ap(), tmp2.ap(), dt_sb.ap()[:, 127:127 + L], ALU.mult)
        g2_n = P.tv.inc(v.drain())
        v.tensor_tensor_scan(ell_sb.ap(), segm.ap(), adt.ap(), 0.0, ALU.mult, ALU.add)
        ell_n = P.tv.inc(v.drain())
        P.tv.wait(nc.sync, ell_n)
        P.dma("sync", ell_d.ap(), ell_sb.ap())
        P.dma("sync", dt_d.ap(), dt_sb.ap()[:, 128:])

        for c in range(NCH):
            bk = c % 2
            pb = P.psum[bk]
            P.tv.wait(t)
            i1 = t.transpose(pb.ap()[:, 0:32], ell_sb.ap()[:, c * CH:(c + 1) * CH],
                             id_sb.ap()[:32, :32])
            i2 = t.transpose(pb.ap()[:, 32:64], g1f.ap()[:, c * CH:(c + 1) * CH],
                             id_sb.ap()[:32, :32])
            i3 = t.transpose(pb.ap()[:, 64:96], g2f.ap()[:, c * CH:(c + 1) * CH],
                             id_sb.ap()[:32, :32])
            tr_n = P.tt.inc(i3)
            P.tt.wait(v, tr_n)
            v.tensor_copy(elltok.ap()[:, c * 32:(c + 1) * 32], pb.ap()[:, 0:32])
            v.tensor_copy(g1tok.ap()[:, c * 32:(c + 1) * 32], pb.ap()[:, 32:64])
            P.tv.inc(v.tensor_copy(g2tok.ap()[:, c * 32:(c + 1) * 32], pb.ap()[:, 64:96]))
        P.barrier()

        lastell_sb = ssb("lastell", [32, NCH])
        v.tensor_copy(lastell_sb.ap().unsqueeze(2),
                      ell_sb.ap().rearrange("h (c t) -> h c t", t=CH)[:, :, CH - 1:CH])
        le_n = P.tv.inc(v.drain())
        lastellT = ssb("lastellT", [NCH, 32])
        P.tv.wait(t, le_n)
        ltr_n = P.tt.inc(t.transpose(P.psum[0].ap()[:NCH, 0:32], lastell_sb.ap(),
                                     id_sb.ap()[:32, :32]))
        P.tt.wait(v, ltr_n)
        le2_n = P.tv.inc(v.tensor_copy(lastellT.ap(), P.psum[0].ap()[:NCH, 0:32]))
        lastell_d = P.dram("lastell_d", [NCH, 32], F32)
        P.tv.wait(nc.sync, le2_n)
        P.dma("sync", lastell_d.ap(), lastellT.ap())
        P.barrier()
        lerow = ssb("lerow", [1, NCH * 32])
        P.dma("sync", lerow.ap(), lastell_d.ap().rearrange("c h -> (c h)").unsqueeze(0))
        P.barrier()
        for sl in range(_ceil(NCH * 32, 512)):
            c0 = sl * 512
            cols = min(512, NCH * 32 - c0)
            pb = P.psum[sl % 2]
            P.tv.wait(t)
            mm_n = P.tt.inc(t.matmul(pb.ap()[:, :cols], ones_sb.ap(),
                                     lerow.ap()[:, c0:c0 + cols], start=True, stop=True))
            P.tt.wait(v, mm_n)
            v.scalar_tensor_tensor(out=w1full.ap()[:, c0:c0 + cols],
                                   in0=elltok.ap()[:, c0:c0 + cols], scalar=-1.0,
                                   in1=pb.ap()[:, :cols], op0=ALU.mult, op1=ALU.add)
            ee_n = P.tv.inc(v.tensor_copy(cdb.ap()[:, c0:c0 + cols], pb.ap()[:, :cols]))
            P.tv.wait(s, ee_n)
            s.activation(w1full.ap()[:, c0:c0 + cols], w1full.ap()[:, c0:c0 + cols], AF.Exp)
            P.ts.inc(s.activation(cdb.ap()[:, c0:c0 + cols], cdb.ap()[:, c0:c0 + cols], AF.Exp))
        P.barrier()
        v.tensor_tensor(w2full.ap(), g2tok.ap(), w1full.ap(), ALU.mult)
        v.drain()
        v.tensor_tensor(w1full.ap(), g1tok.ap(), w1full.ap(), ALU.mult)
        v.drain()
        P.barrier()
        if "dbg3_d" in dbg:
            dbg3_d = P.dram("dbg3_d", [128, 6 * NCH * 32], F32, kind="ExternalOutput")
            for i, tsr in enumerate([cdb, w1full, w2full, elltok, g1tok, g2tok]):
                P.dma("sync", dbg3_d.ap()[:, i * NCH * 32:(i + 1) * NCH * 32], tsr.ap())
            P.barrier()

    with contextlib.ExitStack() as stg:
        def ssb(name, shape, dtype=F32):
            return stg.enter_context(nc.sbuf_tensor(P.uname(name), list(shape), dtype))
        rep4_sb = ssb("rep4_sb", [32, 128])
        rdt_sb = ssb("rdt_sb", [32, 8 * 128])
        ang4 = ssb("ang4", [128, L])
        angb = ssb("angb", [128, L])
        thb = ssb("thb", [128, L])
        ones128 = ssb("ones128", [128, L])
        wq = ssb("wq", [128, L])
        wqi = ssb("wqi", [128, L], mybir.dt.int32)
        P.dma("sync", rep4_sb.ap(), rep4c.ap())
        P.dma("sync", rdt_sb.ap(), rdtc.ap())
        g.memset(ones128.ap(), 1.0)
        P.barrier()
        ang_sb = ssb("ang_keep", [32, L])
        P.dma("sync", ang_sb.ap(), projfm.ap()[4448:4480, :])
        P.barrier()
        for sl in range(L // 512):
            pb = P.psum[sl % 2]
            P.tv.wait(t)
            mm_n = P.tt.inc(t.matmul(pb.ap(), rep4_sb.ap(),
                                     ang_sb.ap()[:, sl * 512:(sl + 1) * 512],
                                     start=True, stop=True))
            P.tt.wait(v, mm_n)
            P.tv.inc(v.tensor_copy(ang4.ap()[:, sl * 512:(sl + 1) * 512], pb.ap()))
        P.barrier()
        for b in range(8):
            for sl in range(L // 512):
                pb = P.psum[sl % 2]
                P.tv.wait(t)
                mm_n = P.tt.inc(t.matmul(pb.ap(), rdt_sb.ap()[:, b * 128:(b + 1) * 128],
                                         dt_sb.ap()[:, 128 + sl * 512:128 + (sl + 1) * 512],
                                         start=True, stop=True))
                P.tt.wait(v, mm_n)
                P.tv.inc(v.tensor_tensor(angb.ap()[:, sl * 512:(sl + 1) * 512],
                                         ang4.ap()[:, sl * 512:(sl + 1) * 512],
                                         pb.ap(), ALU.mult))
            v.drain()
            v.tensor_tensor_scan(thb.ap(), ones128.ap(), angb.ap(), 0.0, ALU.mult, ALU.add)
            v.drain()
            for trig in range(2):
                off = PI / 2 if trig == 0 else 0.0
                c1 = off + 2 * PI * 64 - PI
                v.tensor_scalar(out=wq.ap(), in0=thb.ap(), scalar1=c1,
                                scalar2=1.0 / (2 * PI), op0=ALU.add, op1=ALU.mult)
                v.drain()
                v.tensor_copy(wqi.ap(), wq.ap())
                v.drain()
                v.scalar_tensor_tensor(out=wq.ap(), in0=wqi.ap(), scalar=-1.0,
                                       in1=wq.ap(), op0=ALU.mult, op1=ALU.add)
                v.drain()
                v.tensor_scalar(out=wq.ap(), in0=wq.ap(), scalar1=-0.499999,
                                scalar2=0.499999, op0=ALU.max, op1=ALU.min)
                rr_n = P.tv.inc(v.drain())
                P.tv.wait(s, rr_n)
                tn = P.ts.inc(s.activation(angb.ap(), wq.ap(), AF.Sin, scale=2 * PI))
                P.ts.wait(nc.sync, tn)
                dd = c_d if trig == 0 else s_d
                P.dma("sync", dd.ap()[b * 128:(b + 1) * 128, :], angb.ap())
                P.qwait(v, "sync")
        P.barrier()
    st_dt.close()

    def _bn_sb(name, shape, dtype=F32):
        return st_bn.enter_context(nc.sbuf_tensor(P.uname(name), list(shape), dtype))
    Bn = _bn_sb("Bn", [128, L])
    Cn = _bn_sb("Cn", [128, L])
    B12n = _bn_sb("B12n", [64, L])
    Bswn = _bn_sb("Bswn", [128, L])
    Cswn = _bn_sb("Cswn", [128, L])
    bb_sb = _bn_sb("bb_sb", [128, 32])
    cb_sb = _bn_sb("cb_sb", [128, 32])
    bbn_sb = _bn_sb("bbn_sb", [128, 32])
    cbn_sb = _bn_sb("cbn_sb", [128, 32])
    bbsw_sb = _bn_sb("bbsw_sb", [64, 32])
    cbsw_sb = _bn_sb("cbsw_sb", [64, 32])
    with contextlib.ExitStack() as stg:
        def ssb(name, shape, dtype=F32):
            return stg.enter_context(nc.sbuf_tensor(P.uname(name), list(shape), dtype))
        Bg = ssb("Bg", [128, L]); sq = ssb("sq", [128, L])
        nw_sb = ssb("nw_sb", [128, 2])
        rrow = ssb("rrow", [1, L])
        onescol = ssb("onescol", [128, 1])
        g.memset(onescol.ap(), 1.0)
        eps_b = eps_c3
        P.dma("sync", Bg.ap(), projfm.ap()[4096:4224, :])
        P.dma("sync", nw_sb.ap()[:, 0:1], b_norm_w.ap().unsqueeze(1))
        P.dma("sync", nw_sb.ap()[:, 1:2], c_norm_w.ap().unsqueeze(1))
        P.dma("sync", bb_sb.ap(), b_biasT.ap())
        P.dma("sync", cb_sb.ap(), c_biasT.ap())
        P.barrier()
        for which in range(2):
            dst = Bn if which == 0 else Cn
            sq_n = P.ts.inc(s.activation(sq.ap(), Bg.ap(), AF.Square))
            P.ts.wait(t, sq_n)
            for sl in range(L // 512):
                pb = P.psum[sl % 2]
                P.ts.wait(t)
                P.tv.wait(t)
                mm_n = P.tt.inc(t.matmul(pb.ap()[:1, :], onescol.ap(),
                                         sq.ap()[:, sl * 512:(sl + 1) * 512],
                                         start=True, stop=True))
                P.tt.wait(s, mm_n)
                sd_n = P.ts.inc(s.activation(rrow.ap()[:, sl * 512:(sl + 1) * 512],
                                             pb.ap()[:1, :], AF.Sqrt,
                                             scale=1.0 / 128, bias=eps_b[0:1, :]))
                P.ts.wait(v, sd_n)
                v.reciprocal(rrow.ap()[:, sl * 512:(sl + 1) * 512],
                             rrow.ap()[:, sl * 512:(sl + 1) * 512])
                rc_n = P.tv.inc(v.drain())
                P.tv.wait(t, rc_n)
                mm2_n = P.tt.inc(t.matmul(pb.ap()[:, :], ones_sb.ap(),
                                          rrow.ap()[:, sl * 512:(sl + 1) * 512],
                                          start=True, stop=True))
                P.tt.wait(v, mm2_n)
                P.tv.inc(v.scalar_tensor_tensor(
                    out=dst.ap()[:, sl * 512:(sl + 1) * 512],
                    in0=Bg.ap()[:, sl * 512:(sl + 1) * 512],
                    scalar=nw_sb.ap()[:, which:which + 1],
                    in1=pb.ap()[:, :], op0=ALU.mult, op1=ALU.mult))
                P.tv.inc(v.drain())
            P.barrier()
            if which == 0:
                P.dma("sync", Bg.ap(), projfm.ap()[4224:4352, :])
                P.barrier()
        v.tensor_scalar(out=B12n.ap(), in0=Bn.ap()[0:64, :], scalar1=-1.0,
                        scalar2=None, op0=ALU.mult)
        v.tensor_copy(Bswn.ap()[0:32, :], Bn.ap()[32:64, :])
        v.tensor_scalar(out=Bswn.ap()[32:64, :], in0=Bn.ap()[0:32, :], scalar1=-1.0,
                        scalar2=None, op0=ALU.mult)
        v.tensor_copy(Cswn.ap()[0:32, :], Cn.ap()[32:64, :])
        v.tensor_scalar(out=Cswn.ap()[32:64, :], in0=Cn.ap()[0:32, :], scalar1=-1.0,
                        scalar2=None, op0=ALU.mult)
        v.drain()
        v.tensor_scalar(out=Bn.ap()[0:64, :], in0=Cn.ap()[0:64, :], scalar1=-1.0,
                        scalar2=None, op0=ALU.mult)
        v.tensor_scalar(out=bbn_sb.ap(), in0=bb_sb.ap(), scalar1=-1.0, scalar2=None, op0=ALU.mult)
        v.tensor_scalar(out=cbn_sb.ap(), in0=cb_sb.ap(), scalar1=-1.0, scalar2=None, op0=ALU.mult)
        v.tensor_copy(bbsw_sb.ap()[0:32, :], bb_sb.ap()[32:64, :])
        v.tensor_scalar(out=bbsw_sb.ap()[32:64, :], in0=bb_sb.ap()[0:32, :], scalar1=-1.0,
                        scalar2=None, op0=ALU.mult)
        v.tensor_copy(cbsw_sb.ap()[0:32, :], cb_sb.ap()[32:64, :])
        v.tensor_scalar(out=cbsw_sb.ap()[32:64, :], in0=cb_sb.ap()[0:32, :], scalar1=-1.0,
                        scalar2=None, op0=ALU.mult)
        v.drain()
        P.barrier()

    with contextlib.ExitStack() as stg:
        def ssb(name, shape, dtype=F32):
            return stg.enter_context(nc.sbuf_tensor(P.uname(name), list(shape), dtype))
        cdup = ssb("cdup", [64, L]); sdup = ssb("sdup", [64, L])
        Kf = ssb("Kf", [128, L]); Qf = ssb("Qf", [128, L])
        lrow = [ssb("lrow", [1, 512]) for _ in range(2)]
        qe_bf = ssb("qe_bf", [128, L], BF16)
        zcol_sb = ssb("zcol", [128, 1], BF16)
        P.tg.inc(g.memset(zcol_sb.ap(), 0.0))
        P.tg.wait(nc.scalar)
        cons_v = 0
        st_gp = 0
        st_sync = 0
        for h in range(NH):
            P.tv.wait(nc.sync, cons_v)
            P.dma("sync", cdup.ap()[0:32, :], c_d.ap()[h * 32:(h + 1) * 32, :])
            P.dma("sync", cdup.ap()[32:64, :], c_d.ap()[h * 32:(h + 1) * 32, :])
            P.dma("sync", sdup.ap()[0:32, :], s_d.ap()[h * 32:(h + 1) * 32, :])
            P.dma("sync", sdup.ap()[32:64, :], s_d.ap()[h * 32:(h + 1) * 32, :])

            P.qwait(v, "sync")
            P.qwait(v, "st")
            P.qwait(v, "gp")
            v.scalar_tensor_tensor(out=Kf.ap()[0:64, :], in0=B12n.ap(),
                                   scalar=bbn_sb.ap()[0:64, h:h + 1], in1=cdup.ap(),
                                   op0=ALU.add, op1=ALU.mult)
            v.scalar_tensor_tensor(out=Qf.ap()[0:64, :], in0=Bswn.ap()[0:64, :],
                                   scalar=bbsw_sb.ap()[:, h:h + 1], in1=sdup.ap(),
                                   op0=ALU.add, op1=ALU.mult)
            v.drain()
            v.tensor_tensor(Kf.ap()[0:64, :], Kf.ap()[0:64, :], Qf.ap()[0:64, :], ALU.add)
            v.tensor_scalar(out=Kf.ap()[64:128, :], in0=Bn.ap()[64:128, :],
                            scalar1=bb_sb.ap()[64:128, h:h + 1], scalar2=None, op0=ALU.add)
            k_n = P.tv.inc(v.drain())
            P.tv.wait(g, k_n)
            kst = P.dma("gp", K_d.ap()[:, h:h + 1, 1:L + 1], Kf.ap().unsqueeze(1))
            nc.scalar.dma_start(out=K_d.ap()[:, h:h + 1, 0:1], in_=zcol_sb.ap().unsqueeze(1),
                                allow_slow_non_contiguous=True).then_inc(P.q["act"][1], 16)
            P.q["act"][2] += 1
            v.scalar_tensor_tensor(out=Qf.ap()[0:64, :], in0=Bn.ap()[0:64, :],
                                   scalar=cbn_sb.ap()[0:64, h:h + 1], in1=cdup.ap(),
                                   op0=ALU.add, op1=ALU.mult)
            v.drain()
            v.scalar_tensor_tensor(out=cdup.ap(), in0=Cswn.ap()[0:64, :],
                                   scalar=cbsw_sb.ap()[:, h:h + 1], in1=sdup.ap(),
                                   op0=ALU.add, op1=ALU.mult)
            v.drain()
            v.tensor_tensor(Qf.ap()[0:64, :], Qf.ap()[0:64, :], cdup.ap(), ALU.add)
            v.tensor_scalar(out=Qf.ap()[64:128, :], in0=Cn.ap()[64:128, :],
                            scalar1=cb_sb.ap()[64:128, h:h + 1], scalar2=None, op0=ALU.add)
            q_n = P.tv.inc(v.drain())
            P.tv.wait(g, q_n)
            st_gp = P.dma("gp", Q_d.ap()[:, h:h + 1, :], Qf.ap().unsqueeze(1))
            for sl in range(L // 512):
                pb = P.psum[sl % 2]
                P.tt.wait(nc.sync)
                P.dma(f"ld{sl % 2}", lrow[sl % 2].ap(),
                      ell_d.ap()[h:h + 1, sl * 512:(sl + 1) * 512])
                P.qwait(t, f"ld{sl % 2}")
                P.tv.wait(t)
                mm_n = P.tt.inc(t.matmul(pb.ap(), ones_sb.ap(), lrow[sl % 2].ap(),
                                         start=True, stop=True))
                P.tt.wait(s, mm_n)
                ex_n = P.ts.inc(s.activation(pb.ap(), pb.ap(), AF.Exp))
                P.ts.wait(v, ex_n)
                P.tv.inc(v.tensor_tensor(qe_bf.ap()[:, sl * 512:(sl + 1) * 512],
                                         Qf.ap()[:, sl * 512:(sl + 1) * 512],
                                         pb.ap(), ALU.mult))
            qe_n = P.tv.n
            P.tv.wait(nc.sync, qe_n)
            st_sync = P.dma("st", Qe_d.ap()[:, h:h + 1, :], qe_bf.ap().unsqueeze(1))
            cons_v = qe_n
        P.barrier()
    st_bn.close()

    oh_sb = P.sb("oh_sb", [32, 32 * 128], F32)
    msk_sb = P.sb("msk_sb", [128, 128], F32)
    P.dma("sync", oh_sb.ap(), ohc.ap())
    P.dma("sync", msk_sb.ap(), mskc.ap())
    H = P.sb("H", [128, NH * HD], F32)
    Hb = P.sb("Hb", [128, NH * HD], BF16)
    g.memset(H.ap(), 0.0)
    g.memset(Hb.ap(), 0.0)
    Esb = P.sb("Esb", [128, 128], F32)
    Dsb = P.sb("Dsb", [128, 128], F32)
    Ssb1 = P.sb("Ssb1", [128, 128], BF16)
    Ssb2 = P.sb("Ssb2", [128, 128], BF16)
    Ktsb = P.sb("Ktsb", [128, 128], BF16)
    Kptsb = P.sb("Kptsb", [128, 128], BF16)
    xw1 = P.sb("xw1", [128, HD], BF16)
    xw2 = P.sb("xw2", [128, HD], BF16)
    yg1 = P.sb("yg1", [64, 128], F32)
    zsil = P.sb("zsil", [64, 128], F32)
    Kc = [P.sb("Kc", [128, NH, 128], BF16) for _ in range(2)]
    Kpc = [P.sb("Kpc", [128, NH, 128], BF16) for _ in range(2)]
    Qc = [P.sb("Qc", [128, NH, 128], BF16) for _ in range(2)]
    Qec = [P.sb("Qec", [128, NH, 128], BF16) for _ in range(2)]
    xsc = [P.sb("xsc", [128, 2048], BF16) for _ in range(2)]
    xpc = [P.sb("xpc", [128, 2048], BF16) for _ in range(2)]
    zc = [P.sb("zc", [128, 16, 128], BF16) for _ in range(2)]
    xfc = [P.sb("xfc", [128, 16, 128], BF16) for _ in range(2)]
    ygc = [P.sb("ygc", [128, 16, 128], BF16) for _ in range(2)]
    g.memset(xpc[0].ap()[0:1, :], 0.0)
    P.barrier()

    buf_done = [0, 0]
    yg_done = [0, 0]
    bank_free = {0: 0, 1: 0, 2: 0, 3: 0, 4: 0}
    bankbf_free = [0, 0]
    e_free = 0
    d_free = 0
    s_free = 0
    kt_free = 0
    xw_free = 0
    hb_last = 0
    yg1_free = 0
    zs_free = 0

    projz = projfm.ap().rearrange("(rc p) l -> p rc l", p=128)
    DBG2 = "dbg2_d" in dbg
    if DBG2:
        dbg2_d = P.dram("dbg2_d", [128, 16 * 128], F32, kind="ExternalOutput")
        dbgslot = [0]
        dbg_sb = P.sb("dbg_sb", [128, 16 * 128], F32)

    def dbg_cap(src_ap, rows=128, cols=128):
        if not DBG2 or dbgslot[0] >= 16:
            return
        sl = dbgslot[0]; dbgslot[0] += 1
        cp_n = P.tv.inc(v.tensor_copy(dbg_sb.ap()[:rows, sl * 128:sl * 128 + cols], src_ap))
        P.tv.wait(nc.sync, cp_n)
        P.dma("sync", dbg2_d.ap()[:rows, sl * 128:sl * 128 + cols],
              dbg_sb.ap()[:rows, sl * 128:sl * 128 + cols])

    def load_chunk(c, bi):
        ch_ld = f"ld{bi}"
        for hs in range(0, NH, 8):
            P.dma(ch_ld, Kc[bi].ap()[:, hs:hs + 8, :],
                  K_d.ap()[:, hs:hs + 8, 1 + c * CH:1 + (c + 1) * CH])
            P.dma(ch_ld, Qc[bi].ap()[:, hs:hs + 8, :],
                  Q_d.ap()[:, hs:hs + 8, c * CH:(c + 1) * CH])
            P.dma(ch_ld, Qec[bi].ap()[:, hs:hs + 8, :],
                  Qe_d.ap()[:, hs:hs + 8, c * CH:(c + 1) * CH])
            P.dma(ch_ld, Kpc[bi].ap()[:, hs:hs + 8, :],
                  K_d.ap()[:, hs:hs + 8, c * CH:(c + 1) * CH])
        if c == 0:
            P.dma(ch_ld, xpc[bi].ap()[1:128, :], xs_tok.ap()[0:CH - 1, :])
        else:
            P.dma(ch_ld, xpc[bi].ap(), xs_tok.ap()[c * CH - 1:(c + 1) * CH - 1, :])
        P.dma(ch_ld, xsc[bi].ap(), xs_tok.ap()[c * CH:(c + 1) * CH, :])
        P.dma(f"gp{bi}", zc[bi].ap(), projz[:, 0:16, c * CH:(c + 1) * CH])
        P.dma(f"gp{bi}", xfc[bi].ap(), projz[:, 16:32, c * CH:(c + 1) * CH])

    load_chunk(0, 0)
    for c in range(NCH):
        bi = c % 2
        if c + 1 < NCH:
            P.tt.wait(nc.sync, buf_done[1 - bi])
            P.tt.wait(g, buf_done[1 - bi])
            P.tv.wait(nc.sync, yg_done[1 - bi])
            P.tv.wait(g, yg_done[1 - bi])
            load_chunk(c + 1, 1 - bi)
        if serial == 2:
            P.barrier()
        for h in range(NH):
            if serial == 1:
                P.barrier()
            col = c * 32 + h
            hc, off = h // 2, (h % 2) * 64
            P.qwait(t, f"ld{bi}")
            P.tv.wait(t, max(bank_free[0], bank_free[1], bank_free[2]))
            t.matmul(P.psum[0].ap()[:, 0:128], Kc[bi].ap()[:, h, :], Qc[bi].ap()[:, h, :],
                     start=True, stop=True)
            t.matmul(P.psum[1].ap()[:, 0:128], Kpc[bi].ap()[:, h, :], Qc[bi].ap()[:, h, :],
                     start=True, stop=True)
            a_n = P.tt.inc(t.matmul(P.psum[2].ap()[:, 0:128],
                                    oh_sb.ap()[:, h * 128:(h + 1) * 128],
                                    ell_sb.ap()[:, c * CH:(c + 1) * CH],
                                    start=True, stop=True))
            P.tt.wait(v, a_n)
            P.ts.wait(v, e_free)
            e_n = P.tv.inc(v.scalar_tensor_tensor(
                out=Esb.ap(), in0=P.psum[2].ap()[:, 0:128], scalar=elltok.ap()[:, col:col + 1],
                op0=ALU.subtract, in1=msk_sb.ap(), op1=ALU.add))

            P.tv.wait(s, max(e_n, d_free))
            d_n = P.ts.inc(s.activation(Dsb.ap(), Esb.ap(), AF.Exp))
            if (c, h) == (0, 0):
                P.ts.wait(v, d_n)
                dbg_cap(Dsb.ap())
            P.ts.wait(t, max(bankbf_free[0], bankbf_free[1]))
            t.transpose(P.psum_bf[0].ap()[:, 0:128], Kc[bi].ap()[:, h, :], idb_sb.ap())
            tr_n = P.tt.inc(t.transpose(P.psum_bf[1].ap()[:, 0:128], Kpc[bi].ap()[:, h, :],
                                        idb_sb.ap()))
            P.tt.wait(s, max(tr_n, kt_free))
            s.copy(Ktsb.ap(), P.psum_bf[0].ap()[:, 0:128])
            kte_n = P.ts.inc(s.copy(Kptsb.ap(), P.psum_bf[1].ap()[:, 0:128]))
            e_free = kte_n
            P.ts.wait(v, d_n)
            P.tt.wait(v, s_free)
            v.scalar_tensor_tensor(out=Ssb1.ap(), in0=Dsb.ap(), scalar=g1tok.ap()[:, col:col + 1],
                                   in1=P.psum[0].ap()[:, 0:128], op0=ALU.mult, op1=ALU.mult)
            ss_n = P.tv.inc(v.scalar_tensor_tensor(
                out=Ssb2.ap(), in0=Dsb.ap(), scalar=g2tok.ap()[:, col:col + 1],
                in1=P.psum[1].ap()[:, 0:128], op0=ALU.mult, op1=ALU.mult))
            bank_free[2] = ss_n
            P.qwait(v, f"ld{bi}")
            P.tt.wait(v, xw_free)
            v.tensor_scalar(out=xw1.ap(), in0=xsc[bi].ap()[:, h * HD:(h + 1) * HD],
                            scalar1=w1full.ap()[:, col:col + 1], scalar2=None, op0=ALU.mult)
            xw_n = P.tv.inc(v.tensor_scalar(
                out=xw2.ap(), in0=xpc[bi].ap()[:, h * HD:(h + 1) * HD],
                scalar1=w2full.ap()[:, col:col + 1], scalar2=None, op0=ALU.mult))
            bank_free[0] = xw_n
            bank_free[1] = xw_n
            d_free = xw_n
            P.ts.wait(t, kte_n)
            P.tv.wait(t, max(xw_n, bank_free[3]))
            t.matmul(P.psum[3].ap()[:, 0:HD], Ktsb.ap(), xw1.ap(), start=True, stop=False)
            ws_n = P.tt.inc(t.matmul(P.psum[3].ap()[:, 0:HD], Kptsb.ap(), xw2.ap(),
                                     start=False, stop=True))
            kt_free = ws_n
            xw_free = ws_n
            P.tv.wait(t, max(ss_n, bank_free[4], hb_last))
            t.matmul(P.psum[4].ap()[:HD, 0:128], xsc[bi].ap()[:, h * HD:(h + 1) * HD],
                     Ssb1.ap(), start=True, stop=False)
            t.matmul(P.psum[4].ap()[:HD, 0:128], xpc[bi].ap()[:, h * HD:(h + 1) * HD],
                     Ssb2.ap(), start=False, stop=False)
            y_n = P.tt.inc(t.matmul(P.psum[4].ap()[:HD, 0:128], Hb.ap()[:, h * HD:(h + 1) * HD],
                                    Qec[bi].ap()[:, h, :], start=False, stop=True))
            s_free = y_n
            if (c, h) == (0, 0):
                P.tt.wait(v, y_n)
                dbg_cap(P.psum[4].ap()[:HD, 0:128], HD, 128)
                dbg_cap(Ssb1.ap())
                dbg_cap(Ssb2.ap())
                dbg_cap(Qec[bi].ap()[:, 0, :])
                dbg_cap(xsc[bi].ap()[:, 0:128])
            P.tt.wait(v, y_n)
            v.scalar_tensor_tensor(out=Hb.ap()[:, h * HD:(h + 1) * HD],
                                   in0=H.ap()[:, h * HD:(h + 1) * HD],
                                   scalar=cdb.ap()[:, col:col + 1],
                                   in1=P.psum[3].ap()[:, 0:HD], op0=ALU.mult, op1=ALU.add)
            v.drain()
            hb_last = P.tv.n
            hf_n = P.tv.inc(v.scalar_tensor_tensor(
                out=H.ap()[:, h * HD:(h + 1) * HD], in0=H.ap()[:, h * HD:(h + 1) * HD],
                scalar=cdb.ap()[:, col:col + 1], in1=P.psum[3].ap()[:, 0:HD],
                op0=ALU.mult, op1=ALU.add))
            P.qwait(s, f"gp{bi}")
            P.tv.wait(s, zs_free)
            zs_n = P.ts.inc(s.activation(zsil.ap(), zc[bi].ap()[off:off + 64, hc, :], AF.Silu))
            bankbf_free[0] = zs_n
            bankbf_free[1] = zs_n
            P.tt.wait(v, y_n)
            P.qwait(v, f"gp{bi}")
            yg1_n = P.tv.inc(v.scalar_tensor_tensor(
                out=yg1.ap(), in0=xfc[bi].ap()[off:off + 64, hc, :],
                scalar=drep_sb.ap()[off:off + 64, hc:hc + 1],
                in1=P.psum[4].ap()[:HD, 0:128], op0=ALU.mult, op1=ALU.add))
            bank_free[3] = yg1_n
            v.drain()
            P.ts.wait(v, zs_n)
            P.qwait(v, f"st{bi}")
            ygtt_n = P.tv.inc(v.tensor_tensor(ygc[bi].ap()[off:off + 64, hc, :],
                                              yg1.ap(), zsil.ap(), ALU.mult))
            bank_free[4] = ygtt_n
            yg1_free = ygtt_n
            zs_free = P.tv.inc(v.drain())
        buf_done[bi] = P.tt.n
        yg_done[bi] = P.tv.n

        P.tv.wait(nc.sync, yg_done[bi])
        P.dma(f"st{bi}", ygfm.ap().rearrange("(rc p) l -> p rc l", p=128)[:, :, c * CH:(c + 1) * CH],
              ygc[bi].ap())
    P.barrier()


def build_p4():
    P = Prog()
    ygfm = P.dram("ygfm", [2048, L], BF16, kind="ExternalInput")
    w_out = P.dram("w_out", [2048, DM], F32, kind="ExternalInput")
    sfm = P.dram("sfm", [DM, L], F32, kind="ExternalOutput")
    fm_matmul(P, ygfm, w_out, sfm, K=2048, N=DM)
    return P


class _View:
    def __init__(self, ap, name):
        self._ap = ap
        self.name = name

    def ap(self):
        return self._ap


def fm_layernorm(P, X, out, lg=None, lb=None, eps=1e-5, Lw=None, R=DM, out_dtype=BF16):
    Lw = L if Lw is None else Lw
    nc = P.nc
    v, s, t = nc.vector, nc.scalar, nc.tensor
    kcs = R // 128
    eps_c = P.fconst(eps)
    ones_row(P)
    _stg = contextlib.ExitStack()
    def ssb(name, shape, dtype=F32):
        return _stg.enter_context(nc.sbuf_tensor(P.uname(name), list(shape), dtype))
    onescol = ssb("ln_one", [128, 1])
    P.tg.inc(nc.gpsimd.memset(onescol.ap(), 1.0))
    gcol = lcol = None
    if lg is not None:
        gcol = ssb("ln_g", [128, kcs])
        for i in range(kcs):
            P.dma("sync", gcol.ap()[:, i:i + 1], lg.ap()[i * 128:(i + 1) * 128].unsqueeze(1))
    if lb is not None:
        lcol = ssb("ln_b", [128, kcs])
        for i in range(kcs):
            P.dma("sync", lcol.ap()[:, i:i + 1], lb.ap()[i * 128:(i + 1) * 128].unsqueeze(1))
    xsb = [ssb("ln_x", [128, kcs, 512]) for _ in range(2)]
    sq = ssb("ln_sq", [128, kcs, 512])
    row = ssb("ln_row", [1, 512])
    row2 = ssb("ln_row2", [1, 512])
    usb = [ssb("ln_u", [128, 512]) for _ in range(2)]
    osb = [ssb("ln_o", [128, 512], out_dtype) for _ in range(4)]
    xa = X.ap().rearrange("(kc p) l -> p kc l", p=128)
    P.barrier()
    done = [0, 0]
    for tt in range(Lw // 512):
        bb = tt % 2
        gp = nc.gpsimd
        P.tv.wait(gp, done[bb])
        P.dma(f"gp{bb}", xsb[bb].ap(), xa[:, :, tt * 512:(tt + 1) * 512])
        P.qwait(t, f"gp{bb}")
        P.ts.wait(t)
        P.tv.wait(t)
        for kc in range(kcs):
            i1 = t.matmul(P.psum[0].ap()[:1, :], onescol.ap(), xsb[bb].ap()[:, kc, :],
                          start=(kc == 0), stop=(kc == kcs - 1))
        mm1 = P.tt.inc(i1)
        P.qwait(s, f"gp{bb}")
        P.tt.wait(s)
        sq_n = P.ts.inc(s.activation(sq.ap(), xsb[bb].ap(), AF.Square))
        P.ts.wait(t, sq_n)
        for kc in range(kcs):
            i2 = t.matmul(P.psum[1].ap()[:1, :], onescol.ap(), sq.ap()[:, kc, :],
                          start=(kc == 0), stop=(kc == kcs - 1))
        mm2 = P.tt.inc(i2)
        P.tt.wait(v, mm2)
        v.tensor_scalar(out=row.ap(), in0=P.psum[0].ap()[:1, :], scalar1=1.0 / R,
                        scalar2=None, op0=ALU.mult)
        v.drain()
        v.tensor_tensor(row2.ap(), row.ap(), row.ap(), ALU.mult)
        v.drain()
        v.scalar_tensor_tensor(out=row2.ap(), in0=P.psum[1].ap()[:1, :], scalar=1.0 / R,
                               in1=row2.ap(), op0=ALU.mult, op1=ALU.subtract)
        rv_n = P.tv.inc(v.drain())
        P.tv.wait(s, rv_n)
        sd_n = P.ts.inc(s.activation(row2.ap(), row2.ap(), AF.Sqrt, bias=eps_c[0:1, :]))
        P.ts.wait(v, sd_n)
        v.reciprocal(row2.ap(), row2.ap())
        v.drain()
        v.scalar_tensor_tensor(out=row.ap(), in0=row.ap(), scalar=-1.0, in1=row2.ap(),
                               op0=ALU.mult, op1=ALU.mult)
        nm_n = P.tv.inc(v.drain())
        P.tv.wait(t, nm_n)
        t.matmul(P.psum[2].ap(), ones_row(P).ap(), row2.ap(), start=True, stop=True)
        b2 = t.matmul(P.psum[3].ap(), ones_row(P).ap(), row.ap(), start=True, stop=True)
        bc_n = P.tt.inc(b2)
        P.tt.wait(v, bc_n)
        for kc in range(kcs):
            j = kc % 4
            P.qwait(v, f"st{j}")
            v.drain()
            v.scalar_tensor_tensor(out=usb[0].ap(), in0=xsb[bb].ap()[:, kc, :],
                                   scalar=(gcol.ap()[:, kc:kc + 1] if lg is not None else 1.0),
                                   in1=P.psum[2].ap(), op0=ALU.mult, op1=ALU.mult)
            if lb is not None:
                v.scalar_tensor_tensor(out=usb[1].ap(), in0=P.psum[3].ap(),
                                       scalar=(gcol.ap()[:, kc:kc + 1] if lg is not None else 1.0),
                                       in1=lcol.ap()[:, kc:kc + 1].broadcast_to([128, 512]),
                                       op0=ALU.mult, op1=ALU.add)
            elif lg is not None:
                v.tensor_scalar(out=usb[1].ap(), in0=P.psum[3].ap(),
                                scalar1=gcol.ap()[:, kc:kc + 1], scalar2=None, op0=ALU.mult)
            else:
                v.tensor_copy(usb[1].ap(), P.psum[3].ap())
            v.drain()
            ev = v.tensor_tensor(osb[j].ap(), usb[0].ap(), usb[1].ap(), ALU.add)
            ev_n = P.tv.inc(ev)
            P.tv.wait(nc.sync, ev_n)
            P.dma(f"st{j}", out.ap()[kc * 128:(kc + 1) * 128, tt * 512:(tt + 1) * 512],
                  osb[j].ap())
        done[bb] = P.tv.n
    P.barrier()
    _stg.close()


def ones_row(P):
    if not hasattr(P, "_ones_row"):
        t = P.nc.alloc_sbuf_tensor(P.uname("ones_row"), [1, 128], F32)
        P.tg.inc(P.nc.gpsimd.memset(t.ap(), 1.0))
        P._ones_row = t
    return P._ones_row


def ew_add(v, outs, ins):
    return v.tensor_tensor(outs[0], ins[0], ins[1], ALU.add)


def ew_mult(v, outs, ins):
    return v.tensor_tensor(outs[0], ins[0], ins[1], ALU.mult)


def shift_fm(P, X, out, half=512):
    R = 2 * half
    zc = P.const("zcol_h", np.zeros((half, 1), dtype=np.float32))
    P.dma("gp", out.ap()[0:half, :], X.ap()[0:half, :])
    P.dma("gp", out.ap()[half:R, 1:L], X.ap()[half:R, 0:L - 1])
    P.nc.gpsimd.dma_start(out=out.ap()[half:R, 0:1], in_=zc.ap(),
                          allow_slow_non_contiguous=True).then_inc(P.q["gp"][1], 16)
    P.q["gp"][2] += 1
    P.barrier()


def _rope_tables():
    DH = 64
    hi = np.arange(0, DH, 2, dtype=np.float64)
    inv = 1.0 / 10000.0 ** (hi / DH)
    tt = np.arange(L, dtype=np.float64)
    fr = np.concatenate([tt[:, None] * inv, tt[:, None] * inv], axis=-1)
    base = (hi + 0.4 * DH) / (1.4 * DH)
    power = (tt - L // 2) / 256.0
    sc = base[None, :] ** power[:, None]
    sc = np.concatenate([sc, sc], axis=-1)
    Cq = (np.cos(fr) * sc).T.astype(np.float32)
    Sq = (np.sin(fr) * sc).T.astype(np.float32)
    Ck = (np.cos(fr) / sc).T.astype(np.float32)
    Sk = (np.sin(fr) / sc).T.astype(np.float32)
    def two(x):
        return np.concatenate([x, x], axis=0)
    return two(Cq), two(Sq), two(Ck), two(Sk)


def rope_pass(P, qk_d, out_d, qs_col, ks_col):
    nc = P.nc
    v, s, t = nc.vector, nc.scalar, nc.tensor
    Cq, Sq, Ck, Sk = _rope_tables()
    rr = np.zeros((128, 128), dtype=np.float32)
    for blk in range(2):
        for d in range(32):
            rr[blk * 64 + 32 + d, blk * 64 + d] = -1.0
            rr[blk * 64 + d, blk * 64 + 32 + d] = 1.0
    ind2 = np.zeros((128, 2), dtype=np.float32)
    ind2[0:64, 0] = 1.0
    ind2[64:128, 1] = 1.0
    indT = np.zeros((2, 128), dtype=np.float32)
    indT[0, 0:64] = 1.0
    indT[1, 64:128] = 1.0
    _stg = contextlib.ExitStack()
    def ssb(name, shape, dtype=F32):
        return _stg.enter_context(nc.sbuf_tensor(P.uname(name), list(shape), dtype))
    tabs = []
    for nm, arr in [("Cq", Cq), ("Sq", Sq), ("Ck", Ck), ("Sk", Sk)]:
        tsb = ssb("tab" + nm, [128, L])
        P.dma("sync", tsb.ap(), P.const("rope" + nm, arr).ap())
        tabs.append(tsb)
    rr_sb = ssb("rr_sb", [128, 128], BF16)
    P.dma("gp", rr_sb.ap(), P.const("ropeRR", rr).ap())
    ind2_sb = ssb("ind2", [128, 2])
    P.dma("sync", ind2_sb.ap(), P.const("ropeI2", ind2).ap())
    indT_sb = ssb("indT", [2, 128])
    P.dma("sync", indT_sb.ap(), P.const("ropeIT", indT).ap())
    sc_sb = ssb("sc_sb", [128, 2])
    P.dma("sync", sc_sb.ap()[:, 0:1], qs_col.ap())
    P.dma("sync", sc_sb.ap()[:, 1:2], ks_col.ap())
    q_bf = [ssb("rp_qbf", [128, 512], BF16) for _ in range(2)]
    q_f = ssb("rp_qf", [128, 512])
    sq = ssb("rp_sq", [128, 512])
    nrm = ssb("rp_nrm", [2, 512])
    qhs = ssb("rp_qhs", [128, 512])
    qhs_bf = ssb("rp_qhsb", [128, 512], BF16)
    t1 = ssb("rp_t1", [128, 512])
    t2 = ssb("rp_t2", [128, 512])
    osb = [ssb("rp_o", [128, 512], BF16) for _ in range(2)]
    P.barrier()
    done = [0, 0]
    for qk in range(2):
        tabC, tabS = tabs[2 * qk], tabs[2 * qk + 1]
        for rc in range(4):
            rows = qk * 512 + rc * 128
            for tt in range(L // 512):
                bb = tt % 2
                sl = slice(tt * 512, (tt + 1) * 512)
                gp = nc.sync
                P.ts.wait(gp, done[bb])
                P.dma(f"ld{bb}", q_bf[bb].ap(), qk_d.ap()[rows:rows + 128, sl])
                P.qwait(s, f"ld{bb}")
                P.tt.wait(s)
                cp_n = P.ts.inc(s.copy(q_f.ap(), q_bf[bb].ap()))
                sq_n = P.ts.inc(s.activation(sq.ap(), q_bf[bb].ap(), AF.Square))
                P.ts.wait(t, sq_n)
                P.tv.wait(t)
                mm1 = P.tt.inc(t.matmul(P.psum[0].ap()[:2, :], ind2_sb.ap(), sq.ap(),
                                        start=True, stop=True))
                P.tt.wait(s, mm1)
                P.tv.wait(s)
                nr_n = P.ts.inc(s.activation(nrm.ap(), P.psum[0].ap()[:2, :], AF.Sqrt))
                P.ts.wait(v, nr_n)
                v.tensor_scalar(out=nrm.ap(), in0=nrm.ap(), scalar1=1e-12, scalar2=None,
                                op0=ALU.max)
                v.drain()
                v.reciprocal(nrm.ap(), nrm.ap())
                ni_n = P.tv.inc(v.drain())
                P.tv.wait(t, ni_n)
                mm2 = P.tt.inc(t.matmul(P.psum[1].ap(), indT_sb.ap(), nrm.ap(),
                                        start=True, stop=True))
                P.tt.wait(v, mm2)
                P.ts.wait(v, cp_n)
                v.scalar_tensor_tensor(out=qhs.ap(), in0=q_f.ap(),
                                       scalar=sc_sb.ap()[:, qk:qk + 1],
                                       in1=P.psum[1].ap(), op0=ALU.mult, op1=ALU.mult)
                v.drain()
                qb_n = P.tv.inc(v.tensor_copy(qhs_bf.ap(), qhs.ap()))
                P.tv.wait(t, qb_n)
                mm3 = P.tt.inc(t.matmul(P.psum[2].ap(), rr_sb.ap(), qhs_bf.ap(),
                                        start=True, stop=True))
                v.tensor_tensor(t1.ap(), qhs.ap(), tabC.ap()[:, sl], ALU.mult)
                P.tt.wait(v, mm3)
                v.scalar_tensor_tensor(out=t2.ap(), in0=tabS.ap()[:, sl], scalar=1.0,
                                       in1=P.psum[2].ap(), op0=ALU.mult, op1=ALU.mult)
                v.drain()
                P.qwait(v, f"st{bb}")
                ev_n = P.tv.inc(v.tensor_tensor(osb[bb].ap(), t1.ap(), t2.ap(), ALU.add))
                P.tv.wait(nc.sync, ev_n)
                P.dma(f"st{bb}", out_d.ap()[rows:rows + 128, sl], osb[bb].ap())
                done[bb] = P.ts.n
    P.barrier()
    _stg.close()


def local_attn(P, qk_d, v_d, o_d):
    nc = P.nc
    v, s, t = nc.vector, nc.scalar, nc.tensor
    mA = np.where(np.arange(128)[None, :] >= np.arange(128)[:, None], 0.0, NEG
                  ).astype(np.float32)
    mB = np.where(np.arange(128)[None, :] <= np.arange(128)[:, None], 0.0, NEG
                  ).astype(np.float32)
    _stg = contextlib.ExitStack()
    def ssb(name, shape, dtype=F32):
        return _stg.enter_context(nc.sbuf_tensor(P.uname(name), list(shape), dtype))
    mA_sb = ssb("mA", [128, 128])
    mB_sb = ssb("mB", [128, 128])
    P.dma("sync", mA_sb.ap(), P.const("lamA", mA).ap())
    P.dma("sync", mB_sb.ap(), P.const("lamB", mB).ap())
    idb = ssb("la_idb", [128, 128], BF16)
    P.dma("gp", idb.ap(), P.const("identf", np.eye(128, dtype=np.float32)).ap())
    qt = [ssb("la_q", [128, 4, 128], BF16) for _ in range(2)]
    kt = [ssb("la_k", [128, 4, 640], BF16) for _ in range(2)]
    vw = [ssb("la_v", [128, 5, 512], BF16) for _ in range(2)]
    Ssb = ssb("la_S", [128, 640])
    rmax = ssb("la_m", [128, 2])
    Pf = ssb("la_P", [128, 640])
    Pbf = ssb("la_Pb", [128, 640], BF16)
    Pt = ssb("la_Pt", [128, 640], BF16)
    ob = [ssb("la_o", [64, 128], BF16) for _ in range(4)]
    P.barrier()
    done = [0, 0]
    for qi in range(L // 128):
        bb = qi % 2
        kw0 = max(0, qi * 128 - 512)
        nkt = (qi * 128 + 128 - kw0) // 128
        sy = nc.sync
        P.ts.wait(sy, done[bb])
        P.dma(f"ld{bb}", qt[bb].ap(),
              qk_d.ap().rearrange("(rc p) l -> p rc l", p=128)[:, 0:4, qi * 128:(qi + 1) * 128])
        P.dma(f"ld{bb}", kt[bb].ap()[:, :, 0:nkt * 128],
              qk_d.ap().rearrange("(rc p) l -> p rc l", p=128)[:, 4:8, kw0:kw0 + nkt * 128])
        P.dma(f"ld{bb}", vw[bb].ap()[:, 0:nkt, :],
              v_d.ap().rearrange("(a p) c -> p a c", p=128)[:, kw0 // 128:kw0 // 128 + nkt, :])
        for h in range(8):
            rc, off = h // 2, (h % 2) * 64
            P.qwait(t, f"ld{bb}")
            P.ts.wait(t)
            n1 = min(nkt * 128, 512)
            ins = t.matmul(P.psum[0].ap()[:, 0:n1], qt[bb].ap()[off:off + 64, rc, :],
                           kt[bb].ap()[off:off + 64, rc, 0:n1], start=True, stop=True)
            if nkt == 5:
                ins = t.matmul(P.psum[1].ap()[:, 0:128], qt[bb].ap()[off:off + 64, rc, :],
                               kt[bb].ap()[off:off + 64, rc, 512:640], start=True, stop=True)
            smm = P.tt.inc(ins)
            P.tt.wait(s, smm)
            P.tv.wait(s)
            s.activation(Ssb.ap()[:, 0:n1], P.psum[0].ap()[:, 0:n1], AF.Identity, scale=8.0)
            if nkt == 5:
                s.activation(Ssb.ap()[:, 512:640], P.psum[1].ap()[:, 0:128], AF.Identity,
                             scale=8.0)
            ev_n = P.ts.inc(s.drain())
            P.ts.wait(v, ev_n)
            if qi >= 4:
                v.tensor_tensor(Ssb.ap()[:, 0:128], Ssb.ap()[:, 0:128], mA_sb.ap(), ALU.add)
            dg0 = (nkt - 1) * 128
            v.tensor_tensor(Ssb.ap()[:, dg0:dg0 + 128], Ssb.ap()[:, dg0:dg0 + 128],
                            mB_sb.ap(), ALU.add)
            v.drain()
            v.reduce_max(rmax.ap()[:, 0:1], Ssb.ap()[:, 0:nkt * 128], axis=AX.X)
            v.drain()
            v.tensor_scalar(out=rmax.ap()[:, 1:2], in0=rmax.ap()[:, 0:1], scalar1=-1.0,
                            scalar2=None, op0=ALU.mult)
            nm_n = P.tv.inc(v.drain())
            P.tv.wait(s, nm_n)
            ex_n = P.ts.inc(s.activation(Pf.ap()[:, 0:nkt * 128], Ssb.ap()[:, 0:nkt * 128],
                                         AF.Exp, bias=rmax.ap()[:, 1:2],
                                         accum_out=rmax.ap()[:, 0:1]))
            P.ts.wait(v, ex_n)
            v.reciprocal(rmax.ap()[:, 0:1], rmax.ap()[:, 0:1])
            v.drain()
            v.tensor_scalar(out=Pbf.ap()[:, 0:nkt * 128], in0=Pf.ap()[:, 0:nkt * 128],
                            scalar1=rmax.ap()[:, 0:1], scalar2=None, op0=ALU.mult)
            pb_n = P.tv.inc(v.drain())
            P.tv.wait(t, pb_n)
            P.ts.wait(t)
            for kti in range(nkt):
                itr = t.transpose(P.psum_bf[0].ap()[:, kti * 128:(kti + 1) * 128],
                                  Pbf.ap()[:, kti * 128:(kti + 1) * 128], idb.ap())
            tr_n = P.tt.inc(itr)
            P.tt.wait(s, tr_n)
            kte = P.ts.inc(s.copy(Pt.ap()[:, 0:nkt * 128], P.psum_bf[0].ap()[:, 0:nkt * 128]))
            P.ts.wait(t, kte)
            P.tv.wait(t)
            for kti in range(nkt):
                io = t.matmul(P.psum[2].ap()[:64, 0:128],
                              vw[bb].ap()[:, kti, h * 64:(h + 1) * 64],
                              Pt.ap()[:, kti * 128:(kti + 1) * 128],
                              start=(kti == 0), stop=(kti == nkt - 1))
            om = P.tt.inc(io)
            j = h % 4
            P.tt.wait(v, om)
            P.qwait(v, f"st{j}")
            o_n = P.tv.inc(v.tensor_copy(ob[j].ap(), P.psum[2].ap()[:64, 0:128]))
            P.tv.wait(sy, o_n)
            P.dma(f"st{j}", o_d.ap()[h * 64:(h + 1) * 64, qi * 128:(qi + 1) * 128], ob[j].ap())
        done[bb] = P.ts.n
    P.barrier()
    _stg.close()


def p5_scratch(P, tag=""):
    return dict(
        h_d=P.dram(f"h_d{tag}", [DM, L], BF16),
        qk_d=P.dram(f"qk_d{tag}", [DM, L], BF16),
        rope_d=P.dram(f"rope_d{tag}", [DM, L], BF16),
        v_d=P.dram(f"v_d{tag}", [L, 512], BF16),
        o_d=P.dram(f"o_d{tag}", [512, L], BF16),
        t_d=P.dram(f"t_d{tag}", [DM, L], F32),
        r2_d=P.dram(f"r2_d{tag}", [DM, L], F32),
        h2_d=P.dram(f"h2_d{tag}", [DM, L], F32),
        h2n_d=P.dram(f"h2n_d{tag}", [DM, L], BF16),
        ua_d=P.dram(f"ua_d{tag}", [2730, L], BF16),
        ug_d=P.dram(f"ug_d{tag}", [2730, L], BF16),
        act_d=P.dram(f"act_d{tag}", [2816, L], BF16),
        w2p_d=P.dram(f"w2p_d{tag}", [2816, DM], F32),
        t2_d=P.dram(f"t2_d{tag}", [DM, L], F32))


def build_p5(dbg=()):
    P = Prog()
    rin = P.dram("rin", [DM, L], F32, kind="ExternalInput")
    wqkv = P.dram("wqkv", [DM, 1536], F32, kind="ExternalInput")
    qs_col = P.dram("qs_col", [128, 1], F32, kind="ExternalInput")
    ks_col = P.dram("ks_col", [128, 1], F32, kind="ExternalInput")
    wo = P.dram("wo", [512, DM], F32, kind="ExternalInput")
    lg = P.dram("lg", [DM], F32, kind="ExternalInput")
    lbv = P.dram("lbv", [DM], F32, kind="ExternalInput")
    w1a = P.dram("w1a", [DM, 2730], F32, kind="ExternalInput")
    b1a = P.dram("b1a", [2730], F32, kind="ExternalInput")
    w1g = P.dram("w1g", [DM, 2730], F32, kind="ExternalInput")
    b1g = P.dram("b1g", [2730], F32, kind="ExternalInput")
    w2 = P.dram("w2", [2730, DM], F32, kind="ExternalInput")
    b2v = P.dram("b2v", [DM], F32, kind="ExternalInput")
    rout = P.dram("rout", [DM, L], F32, kind="ExternalOutput")
    P.fconst(1e-5)
    ones_row(P)
    with P.scope():
        emit_p5(P, rin, rout, wqkv, qs_col, ks_col, wo, lg, lbv, w1a, b1a, w1g,
                b1g, w2, b2v, p5_scratch(P))
    return P


def emit_p5(P, rin, rout, wqkv, qs_col, ks_col, wo, lg, lbv, w1a, b1a, w1g, b1g,
            w2, b2v, sc):
    nc = P.nc
    h_d = sc["h_d"]; qk_d = sc["qk_d"]; rope_d = sc["rope_d"]; v_d = sc["v_d"]
    o_d = sc["o_d"]; t_d = sc["t_d"]; r2_d = sc["r2_d"]; h2_d = sc["h2_d"]
    h2n_d = sc["h2n_d"]; ua_d = sc["ua_d"]; ug_d = sc["ug_d"]; act_d = sc["act_d"]
    w2p_d = sc["w2p_d"]; t2_d = sc["t2_d"]

    zpad = P.sb("zpad", [86, L], BF16)
    P.tg.inc(nc.gpsimd.memset(zpad.ap(), 0.0))
    P.tg.wait(nc.sync)
    P.dma("sync", act_d.ap()[2730:2816, :], zpad.ap())
    zpad2 = P.sb("zpad2", [86, 1024], F32)
    P.tg.inc(nc.gpsimd.memset(zpad2.ap(), 0.0))
    P.tg.wait(nc.sync)
    P.dma("sync", w2p_d.ap()[2730:2816, :], zpad2.ap())
    P.dma("sync", w2p_d.ap()[0:2730, :], w2.ap())
    P.barrier()

    shift_fm(P, rin, h_d)
    fm_matmul(P, h_d, _View(wqkv.ap()[:, 0:1024], "wqk"), qk_d, K=DM, N=1024,
              out_dtype=BF16)
    tok_matmul(P, h_d, _View(wqkv.ap()[:, 1024:1536], "wv"), v_d, K=DM, N=512,
               out_dtype=BF16)
    rope_pass(P, qk_d, rope_d, qs_col, ks_col)
    local_attn(P, rope_d, v_d, o_d)
    fm_matmul(P, o_d, wo, t_d, K=512, N=DM)
    fm_elemwise(P, [rin, t_d], [r2_d], R=DM, emit=ew_add)
    shift_fm(P, r2_d, h2_d)
    fm_layernorm(P, h2_d, h2n_d, lg, lbv)
    emit_p5_ffn(P, h2n_d, w1a, b1a, w1g, b1g, w2p_d, b2v, r2_d, rout, act_d)


def emit_p5_ffn(P, h2n_d, w1a, b1a, w1g, b1g, w2p_d, b2v, r2_d, rout, act_d):
    nc = P.nc
    v, s, t = nc.vector, nc.scalar, nc.tensor
    TT = 512
    N1, NCH1 = 2730, 22
    xa = h2n_d.ap().rearrange("(kc p) l -> p kc l", p=128)
    ra = r2_d.ap().rearrange("(kc p) l -> p kc l", p=128)
    oa = rout.ap().rearrange("(kc p) l -> p kc l", p=128)
    ca = act_d.ap().rearrange("(kc p) l -> p kc l", p=128)

    _stg = contextlib.ExitStack()
    def ssb(name, shape, dtype=F32):
        return _stg.enter_context(nc.sbuf_tensor(P.uname(name), list(shape), dtype))
    wa_sb = ssb("f_wa", [128, 8, N1], BF16)
    wg_sb = ssb("f_wg", [128, 8, N1], BF16)
    x_sb = [ssb("f_x", [128, 8, TT], BF16) for _ in range(2)]
    ua_sb = [ssb("f_ua", [128, TT]) for _ in range(2)]
    ug_sb = [ssb("f_ug", [128, TT]) for _ in range(2)]
    at_sb = [ssb("f_at", [128, TT], BF16) for _ in range(4)]
    ba_sb = ssb("f_ba", [128, NCH1])
    bg_sb = ssb("f_bg", [128, NCH1])
    P.dma("gp0", wa_sb.ap(), w1a.ap().rearrange("(kc p) n -> p kc n", p=128))
    P.dma("gp0", wg_sb.ap(), w1g.ap().rearrange("(kc p) n -> p kc n", p=128))
    for i in range(NCH1):
        rows = min(128, N1 - i * 128)
        P.dma("sync", ba_sb.ap()[:rows, i:i + 1],
              b1a.ap()[i * 128:i * 128 + rows].unsqueeze(1))
        P.dma("sync", bg_sb.ap()[:rows, i:i + 1],
              b1g.ap()[i * 128:i * 128 + rows].unsqueeze(1))
    P.barrier()
    bank_ts = [0, 0, 0, 0]
    pe_done_x = [0, 0]
    gm_tv = [0, 0]
    at_tv = [0, 0, 0, 0]
    for tt in range(L // TT):
        xb = tt % 2
        P.tt.wait(nc.sync, pe_done_x[xb])
        P.dma(f"ld{xb}", x_sb[xb].ap(), xa[:, :, tt * TT:(tt + 1) * TT])
        P.qwait(t, f"ld{xb}")
        for nch in range(NCH1):
            rows = min(128, N1 - nch * 128)
            bka = nch % 2
            bkg = 2 + nch % 2
            P.ts.wait(t, max(bank_ts[bka], bank_ts[bkg]))
            for kc in range(8):
                t.matmul(P.psum[bka].ap()[:rows, :],
                         wa_sb.ap()[:, kc, nch * 128:nch * 128 + rows],
                         x_sb[xb].ap()[:, kc, :],
                         start=(kc == 0), stop=(kc == 7))
            for kc in range(8):
                ins = t.matmul(P.psum[bkg].ap()[:rows, :],
                               wg_sb.ap()[:, kc, nch * 128:nch * 128 + rows],
                               x_sb[xb].ap()[:, kc, :],
                               start=(kc == 0), stop=(kc == 7))
            n = P.tt.inc(ins)
            P.tt.wait(s, n)
            jb = nch % 2
            P.tv.wait(s, gm_tv[jb])
            eva = s.activation(ua_sb[jb].ap()[:rows, :], P.psum[bka].ap()[:rows, :],
                               AF.Identity, bias=ba_sb.ap()[:rows, nch:nch + 1])
            bank_ts[bka] = P.ts.inc(eva)
            evg = s.activation(ug_sb[jb].ap()[:rows, :], P.psum[bkg].ap()[:rows, :],
                               AF.Gelu, bias=bg_sb.ap()[:rows, nch:nch + 1])
            bank_ts[bkg] = P.ts.inc(evg)
            j = nch % 4
            P.ts.wait(v, bank_ts[bkg])
            P.qwait(v, f"st{j}")
            gm = v.tensor_tensor(at_sb[j].ap()[:rows, :], ua_sb[jb].ap()[:rows, :],
                                 ug_sb[jb].ap()[:rows, :], ALU.mult)
            gm_tv[jb] = P.tv.inc(gm)
            P.tv.wait(nc.sync, gm_tv[jb])
            P.dma(f"st{j}", act_d.ap()[nch * 128:nch * 128 + rows,
                                       tt * TT:(tt + 1) * TT], at_sb[j].ap()[:rows, :])
        pe_done_x[xb] = P.tt.n
    P.barrier()
    _stg.close()

    _st2 = contextlib.ExitStack()
    def ssb2(name, shape, dtype=F32):
        return _st2.enter_context(nc.sbuf_tensor(P.uname(name), list(shape), dtype))
    w2_sb = ssb2("f_w2", [128, NCH1, DM], BF16)
    c_sb = [ssb2("f_c", [128, NCH1, TT], BF16) for _ in range(2)]
    r2_sb = [ssb2("f_r2", [128, 8, TT]) for _ in range(2)]
    o_sb = [ssb2("f_o", [128, TT]) for _ in range(4)]
    b2_sb = ssb2("f_b2", [128, 8])
    P.dma("gp1", w2_sb.ap(), w2p_d.ap().rearrange("(kc p) n -> p kc n", p=128))
    for i in range(8):
        P.dma("sync", b2_sb.ap()[:, i:i + 1],
              b2v.ap()[i * 128:(i + 1) * 128].unsqueeze(1))
    P.barrier()
    bank2_ts = [0, 0]
    pe_done_c = [0, 0]
    fin_tv = [0, 0, 0, 0]
    r2_done = [0, 0]
    for tt in range(L // TT):
        xb = tt % 2
        P.tt.wait(nc.sync, pe_done_c[xb])
        P.tv.wait(nc.sync, r2_done[xb])
        P.dma(f"ld{xb}", c_sb[xb].ap(), ca[:, :, tt * TT:(tt + 1) * TT])
        P.dma(f"ld{xb}", r2_sb[xb].ap(), ra[:, :, tt * TT:(tt + 1) * TT])
        P.qwait(t, f"ld{xb}")
        for nch2 in range(8):
            bk = 4 + nch2 % 2
            P.ts.wait(t, bank2_ts[nch2 % 2])
            for kc2 in range(NCH1):
                ins = t.matmul(P.psum[bk].ap()[:, 0:TT],
                               w2_sb.ap()[:, kc2, nch2 * 128:(nch2 + 1) * 128],
                               c_sb[xb].ap()[:, kc2, :],
                               start=(kc2 == 0), stop=(kc2 == NCH1 - 1))
            n = P.tt.inc(ins)
            j = nch2 % 4
            P.tt.wait(s, n)
            P.tv.wait(s, fin_tv[j])
            P.qwait(s, f"st{j}")
            ev = s.activation(o_sb[j].ap(), P.psum[bk].ap()[:, 0:TT], AF.Identity,
                              bias=b2_sb.ap()[:, nch2:nch2 + 1])
            bank2_ts[nch2 % 2] = P.ts.inc(ev)
            P.ts.wait(v, bank2_ts[nch2 % 2])
            fin = v.tensor_tensor(o_sb[j].ap(), o_sb[j].ap(),
                                  r2_sb[xb].ap()[:, nch2, :], ALU.add)
            fin_tv[j] = P.tv.inc(fin)
            P.tv.wait(nc.sync, fin_tv[j])
            P.dma(f"st{j}", oa[:, nch2, tt * TT:(tt + 1) * TT], o_sb[j].ap())
        pe_done_c[xb] = P.tt.n
        r2_done[xb] = P.tv.n
    P.barrier()
    _st2.close()


def build_p6():
    P = Prog()
    xfm = P.dram("xfm", [DM, L], F32, kind="ExternalInput")
    sfm = P.dram("sfm", [DM, L], F32, kind="ExternalInput")
    gfm = P.dram("gfm", [DM, L], F32, kind="ExternalInput")
    x1fm = P.dram("x1fm", [DM, L], F32, kind="ExternalOutput")
    xn1fm = P.dram("xn1fm", [DM, L], BF16, kind="ExternalOutput")
    xn1f32 = P.dram("xn1f32", [DM, L], F32, kind="ExternalOutput")
    P.fconst(1e-5)
    ones_row(P)
    with P.scope():
        emit_p6(P, xfm, sfm, gfm, x1fm, xn1fm, xn1f32)
    return P


def emit_p6(P, xfm, sfm, gfm, x1fm, xn1fm, xn1f32):
    nc = P.nc
    v, s = nc.vector, nc.scalar

    def emit(vv, outs, ins):
        zs_n = P.ts.inc(s.activation(outs[0], ins[2], AF.Silu))
        P.ts.wait(vv, zs_n)
        vv.tensor_tensor(outs[0], ins[1], outs[0], ALU.mult)
        vv.drain()
        return vv.tensor_tensor(outs[0], ins[0], outs[0], ALU.add)
    fm_elemwise(P, [xfm, sfm, gfm], [x1fm], R=DM, emit=emit)
    fm_layernorm(P, x1fm, xn1fm, out_dtype=BF16)
    fm_layernorm(P, x1fm, xn1f32, out_dtype=F32)


def build_p7(dbg=()):
    P = Prog()
    xn1f32 = P.dram("xn1f32", [DM, L], F32, kind="ExternalInput")
    g_ln = P.dram("g_ln", [DM], F32, kind="ExternalInput")
    wq = P.dram("wq", [DM, DM], F32, kind="ExternalInput")
    wkv = P.dram("wkv", [DM, 2 * DM], F32, kind="ExternalInput")
    wo_l1 = P.dram("wo_l1", [DM, DM], F32, kind="ExternalInput")
    s1fm = P.dram("s1fm", [DM, L], F32, kind="ExternalOutput")
    P.fconst(1e-5)
    ones_row(P)
    with P.scope():
        emit_p7(P, xn1f32, g_ln, wq, wkv, wo_l1, s1fm, dbg=dbg)
    return P


def emit_p7(P, xn1f32, g_ln, wq, wkv, wo_l1, s1fm, dbg=()):
    nc = P.nc
    v, s, t = nc.vector, nc.scalar, nc.tensor

    def dk(name):
        return "ExternalOutput" if name in dbg else "Internal"

    xln_d = P.dram("l1xln_d", [DM, L], BF16, kind=dk("xln_d"))
    q_d = P.dram("l1q_d", [DM, L], BF16, kind=dk("q_d"))
    k_d = P.dram("l1k_d", [DM, L], BF16, kind=dk("k_d"))
    v_d = P.dram("l1v_d", [L, DM], BF16, kind=dk("v_d"))
    ofm_d = P.dram("l1ofm_d", [DM, L], BF16, kind=dk("ofm_d"))

    fm_layernorm(P, xn1f32, xln_d, lg=g_ln)
    fm_matmul(P, xln_d, wq, q_d, K=DM, N=DM, out_dtype=BF16)
    fm_matmul(P, xln_d, _View(wkv.ap()[:, 0:DM], "wk"), k_d, K=DM, N=DM, out_dtype=BF16)
    tok_matmul(P, xln_d, _View(wkv.ap()[:, DM:2 * DM], "wv1"), v_d, K=DM, N=DM,
               out_dtype=BF16)

    mB = np.where(np.arange(128)[None, :] <= np.arange(128)[:, None], 0.0, NEG
                  ).astype(np.float32)
    _stg = contextlib.ExitStack()
    def ssb(name, shape, dtype=F32):
        return _stg.enter_context(nc.sbuf_tensor(P.uname(name), list(shape), dtype))
    mB_sb = ssb("l1mB", [128, 128])
    P.dma("sync", mB_sb.ap(), P.const("lamB", mB).ap())
    idb = ssb("l1idb", [128, 128], BF16)
    P.dma("gp", idb.ap(), P.const("identf", np.eye(128, dtype=np.float32)).ap())
    k_sb = ssb("l1k", [128, 8, L], BF16)
    v_sb = ssb("l1v", [128, L // 128, DM], BF16)
    P.dma("sync", k_sb.ap(), k_d.ap().rearrange("(rc p) l -> p rc l", p=128))
    for i in range(0, L // 128, 8):
        nbl = min(8, L // 128 - i)
        P.dma("sync", v_sb.ap()[:, i:i + nbl, :],
              v_d.ap().rearrange("(a p) c -> p a c", p=128)[:, i:i + nbl, :])
    qt = [ssb("l1q", [128, 8, 128], BF16) for _ in range(2)]
    Ssb = [ssb("l1S", [128, L]) for _ in range(2)]
    rmax = [ssb("l1m", [128, 2]) for _ in range(2)]
    Pbf = ssb("l1Pb", [128, L], BF16)
    Pt = ssb("l1Pt", [128, L], BF16)
    ob = [ssb("l1o", [64, 128], BF16) for _ in range(4)]
    P.barrier()
    qt_free = [0, 0]
    sevac_guard = [0, 0]
    ssb_free = [0, 0]
    tr_end = 0
    o_end = 0
    o_free = 0
    sy = nc.sync
    for qi in range(L // 128):
        bb = qi % 2
        nkt = qi + 1
        P.tt.wait(sy, qt_free[bb])
        P.dma(f"ld{bb}", qt[bb].ap(),
              q_d.ap().rearrange("(rc p) l -> p rc l", p=128)[:, :, qi * 128:(qi + 1) * 128])
        st = {}
        for hh in range(17):
            if hh < 16:
                h = hh
                p = h % 2
                rc, off = h // 2, (h % 2) * 64
                P.qwait(t, f"ld{bb}")
                first_evac = True
                for g0 in range(0, nkt * 128, 512):
                    cols = min(512, nkt * 128 - g0)
                    bk = (g0 // 512) % 2
                    P.ts.wait(t, sevac_guard[bk])
                    ins = t.matmul(P.psum[bk].ap()[:, 0:cols],
                                   qt[bb].ap()[off:off + 64, rc, :],
                                   k_sb.ap()[off:off + 64, rc, g0:g0 + cols],
                                   start=True, stop=True)
                    smm = P.tt.inc(ins)
                    P.tt.wait(s, smm)
                    if first_evac:
                        P.tv.wait(s, ssb_free[p])
                        first_evac = False
                    se_n = P.ts.inc(s.activation(Ssb[p].ap()[:, g0:g0 + cols],
                                                 P.psum[bk].ap()[:, 0:cols],
                                                 AF.Identity, scale=0.125))
                    sevac_guard[bk] = se_n
                st[h] = dict(evac=se_n)
                if h == 15:
                    qt_free[bb] = P.tt.n
            if hh >= 1:
                h = hh - 1
                p = h % 2
                P.ts.wait(v, st[h]["evac"])
                dg0 = (nkt - 1) * 128
                v.tensor_tensor(Ssb[p].ap()[:, dg0:dg0 + 128],
                                Ssb[p].ap()[:, dg0:dg0 + 128], mB_sb.ap(), ALU.add)
                v.drain()
                v.reduce_max(rmax[p].ap()[:, 0:1], Ssb[p].ap()[:, 0:nkt * 128], axis=AX.X)
                v.drain()
                v.tensor_scalar(out=rmax[p].ap()[:, 1:2], in0=rmax[p].ap()[:, 0:1],
                                scalar1=-1.0, scalar2=None, op0=ALU.mult)
                nm_n = P.tv.inc(v.drain())
                P.tv.wait(s, nm_n)
                P.tt.wait(s, tr_end)
                ex_n = P.ts.inc(s.activation(Pbf.ap()[:, 0:nkt * 128],
                                             Ssb[p].ap()[:, 0:nkt * 128],
                                             AF.Exp, bias=rmax[p].ap()[:, 1:2],
                                             accum_out=rmax[p].ap()[:, 0:1]))
                P.ts.wait(v, ex_n)
                v.reciprocal(rmax[p].ap()[:, 0:1], rmax[p].ap()[:, 0:1])
                v.drain()
                v.tensor_scalar(out=Pbf.ap()[:, 0:nkt * 128], in0=Pbf.ap()[:, 0:nkt * 128],
                                scalar1=rmax[p].ap()[:, 0:1], scalar2=None, op0=ALU.mult)
                pb_n = P.tv.inc(v.drain())
                ssb_free[p] = pb_n
                P.tv.wait(t, pb_n)
                kte_eng = [None, None]
                kte_cnt = [0, 0]
                last_s_copy = 0
                last_v_copy = 0
                for kg in range(0, nkt, 8):
                    kn = min(8, nkt - kg)
                    bfb = (kg // 8) % 2
                    if kte_eng[bfb] == "s":
                        P.ts.wait(t, kte_cnt[bfb])
                    elif kte_eng[bfb] == "v":
                        P.tv.wait(t, kte_cnt[bfb])
                    for kk in range(kn):
                        itr = t.transpose(
                            P.psum_bf[bfb].ap()[:, kk * 128:(kk + 1) * 128],
                            Pbf.ap()[:, (kg + kk) * 128:(kg + kk + 1) * 128], idb.ap())
                    tr_n = P.tt.inc(itr)
                    if (kg // 8) % 2 == 0:
                        P.tt.wait(s, tr_n)
                        kte = P.ts.inc(s.copy(Pt.ap()[:, kg * 128:(kg + kn) * 128],
                                              P.psum_bf[bfb].ap()[:, 0:kn * 128]))
                        kte_eng[bfb] = "s"
                        kte_cnt[bfb] = kte
                        last_s_copy = kte
                    else:
                        P.tt.wait(v, tr_n)
                        kte = P.tv.inc(v.tensor_copy(Pt.ap()[:, kg * 128:(kg + kn) * 128],
                                                     P.psum_bf[bfb].ap()[:, 0:kn * 128]))
                        kte_eng[bfb] = "v"
                        kte_cnt[bfb] = kte
                        last_v_copy = kte
                tr_end = P.tt.n
                if last_s_copy:
                    P.ts.wait(t, last_s_copy)
                if last_v_copy:
                    P.tv.wait(t, last_v_copy)
                P.tv.wait(t, o_free)
                for kk in range(nkt):
                    io = t.matmul(P.psum[2].ap()[:64, 0:128],
                                  v_sb.ap()[:, kk, h * 64:(h + 1) * 64],
                                  Pt.ap()[:, kk * 128:(kk + 1) * 128],
                                  start=(kk == 0), stop=(kk == nkt - 1))
                om = P.tt.inc(io)
                o_end = P.tt.n
                j = h % 4
                P.tt.wait(v, om)
                P.qwait(v, f"st{j}")
                o_n = P.tv.inc(v.tensor_copy(ob[j].ap(), P.psum[2].ap()[:64, 0:128]))
                o_free = o_n
                P.tv.wait(sy, o_n)
                P.dma(f"st{j}", ofm_d.ap()[h * 64:(h + 1) * 64, qi * 128:(qi + 1) * 128],
                      ob[j].ap())
    P.barrier()
    _stg.close()
    fm_matmul(P, ofm_d, wo_l1, s1fm, K=DM, N=DM)


def emit_p8_experts(P, xn1fm, Fp_d, b2p_d, w1, b1, w2, acc_d):
    nc = P.nc
    v, s, t = nc.vector, nc.scalar, nc.tensor
    NQ, LQ, TT = 4, L // 4, 512
    NTT = LQ // TT
    _stg = contextlib.ExitStack()
    def ssb(name, shape, dtype=F32):
        return _stg.enter_context(nc.sbuf_tensor(P.uname(name), list(shape), dtype))
    xq = ssb("e_x", [128, 8, LQ], BF16)
    accq = ssb("e_acc", [128, 8, LQ], F32)
    w1sb = ssb("e_w1", [128, 8, 2048], BF16)
    w2sb = ssb("e_w2", [128, 16, 1024], BF16)
    hsb = [ssb("e_h", [128, 16, TT], BF16) for _ in range(NTT)]
    fqb = ssb("e_fq", [128, LQ], BF16)
    fqr = ssb("e_fqr", [1, 16, LQ], BF16)
    b1sb = ssb("e_b1", [128, 256], F32)
    b2pb = ssb("e_b2", [128, DM], BF16)
    gsb = [ssb("e_g", [128, TT], BF16) for _ in range(NTT)]
    ones_bf = ssb("e_1", [1, 128], BF16)
    P.tg.inc(nc.gpsimd.memset(ones_bf.ap(), 1.0))
    P.dma("gp", b2pb.ap(), b2p_d.ap())
    P.dma("sync", b1sb.ap(), b1.ap())
    P.barrier()
    xa = xn1fm.ap().rearrange("(kc p) l -> p kc l", p=128)
    aa = acc_d.ap().rearrange("(rc p) l -> p rc l", p=128)
    bank_tv = [0, 0]
    bank_ts = [0, 0]
    bank2_tv = [0, 0]
    bank45_ts = [0, 0]
    q_end_tt = 0
    for q in range(NQ):
        q0 = q * LQ
        P.tt.wait(nc.sync, q_end_tt)
        P.dma("ld0", xq.ap(), xa[:, :, q0:q0 + LQ])
        P.tt.wait(nc.gpsimd, q_end_tt)
        P.dma("gp", fqb.ap(), Fp_d.ap()[:, q0:q0 + LQ])
        P.dma("gp", fqr.ap(), Fp_d.ap()[0:16, q0:q0 + LQ].unsqueeze(0))
        P.dma("gp0", w1sb.ap(), w1.ap()[0].rearrange("(kc p) n -> p kc n", p=128))
        P.dma("gp1", w2sb.ap(), w2.ap()[0].rearrange("(kc p) n -> p kc n", p=128))
        w1_cnt = P.qcnt("gp0")
        w2_cnt = P.qcnt("gp1")
        P.qwait(t, "ld0")
        P.qwait(t, "gp")
        for tt2 in range(NTT):
            for ncI in range(8):
                bk = ncI % 2
                P.tv.wait(t, bank_tv[bk])
                P.ts.wait(t, bank_ts[bk])
                mm = t.matmul(P.psum[bk].ap()[:, 0:TT],
                              b2pb.ap()[:, ncI * 128:(ncI + 1) * 128],
                              fqb.ap()[:, tt2 * TT:(tt2 + 1) * TT],
                              start=True, stop=True)
                n = P.tt.inc(mm)
                P.tt.wait(v, n)
                if tt2 == 0 and ncI == 0:
                    P.qwait(v, "st0")
                ev = v.tensor_copy(accq.ap()[:, ncI, tt2 * TT:(tt2 + 1) * TT],
                                   P.psum[bk].ap()[:, 0:TT])
                bank_tv[bk] = P.tv.inc(ev)
        for e in range(16):
            sc_n = [0, 0]
            for tt2 in range(NTT):
                bk45 = 4 + tt2
                P.ts.wait(t, bank45_ts[tt2])
                bmm = t.matmul(P.psum[bk45].ap()[:, 0:TT], ones_bf.ap(),
                               fqr.ap()[0:1, e, tt2 * TT:(tt2 + 1) * TT],
                               start=True, stop=True)
                bn = P.tt.inc(bmm)
                P.tt.wait(s, bn)
                sc = s.copy(gsb[tt2].ap(), P.psum[bk45].ap()[:, 0:TT])
                sc_n[tt2] = P.ts.inc(sc)
                bank45_ts[tt2] = sc_n[tt2]
            gm_n = [0, 0]
            for tt2 in range(NTT):
                for nch in range(16):
                    bk = nch % 2
                    if nch < 2:
                        P.qwait(t, "gp0", w1_cnt)
                        P.tv.wait(t, bank_tv[bk])
                    P.ts.wait(t, bank_ts[bk])
                    for kc in range(8):
                        ins = t.matmul(P.psum[bk].ap()[:, 0:TT],
                                       w1sb.ap()[:, kc, nch * 128:(nch + 1) * 128],
                                       xq.ap()[:, kc, tt2 * TT:(tt2 + 1) * TT],
                                       start=(kc == 0), stop=(kc == 7))
                    n = P.tt.inc(ins)
                    P.tt.wait(s, n)
                    ev = s.activation(hsb[tt2].ap()[:, nch, :],
                                      P.psum[bk].ap()[:, 0:TT], AF.Gelu,
                                      bias=b1sb.ap()[:, e * 16 + nch:e * 16 + nch + 1])
                    bank_ts[bk] = P.ts.inc(ev)
                    P.ts.wait(v, max(bank_ts[bk], sc_n[tt2]))
                    gm = v.tensor_tensor(hsb[tt2].ap()[:, nch, :],
                                         hsb[tt2].ap()[:, nch, :],
                                         gsb[tt2].ap(), ALU.mult)
                    gm_n[tt2] = P.tv.inc(gm)
            mm1_end = P.tt.n
            if e < 15:
                P.tt.wait(nc.gpsimd, mm1_end)
                P.dma("gp0", w1sb.ap(),
                      w1.ap()[e + 1].rearrange("(kc p) n -> p kc n", p=128))
                w1_cnt = P.qcnt("gp0")
            for tt2 in range(NTT):
                P.tv.wait(t, gm_n[tt2])
                for ncI2 in range(8):
                    bk = 2 + ncI2 % 2
                    if ncI2 < 2:
                        P.qwait(t, "gp1", w2_cnt)
                    P.tv.wait(t, bank2_tv[ncI2 % 2])
                    for kc2 in range(16):
                        ins = t.matmul(P.psum[bk].ap()[:, 0:TT],
                                       w2sb.ap()[:, kc2, ncI2 * 128:(ncI2 + 1) * 128],
                                       hsb[tt2].ap()[:, kc2, :],
                                       start=(kc2 == 0), stop=(kc2 == 15))
                    n = P.tt.inc(ins)
                    P.tt.wait(v, n)
                    ev = v.tensor_tensor(accq.ap()[:, ncI2, tt2 * TT:(tt2 + 1) * TT],
                                         accq.ap()[:, ncI2, tt2 * TT:(tt2 + 1) * TT],
                                         P.psum[bk].ap()[:, 0:TT], ALU.add)
                    bank2_tv[ncI2 % 2] = P.tv.inc(ev)
            mm2_end = P.tt.n
            if e < 15:
                P.tt.wait(nc.gpsimd, mm2_end)
                P.dma("gp1", w2sb.ap(),
                      w2.ap()[e + 1].rearrange("(kc p) n -> p kc n", p=128))
                w2_cnt = P.qcnt("gp1")
        P.tv.wait(nc.sync, P.tv.n)
        P.dma("st0", aa[:, :, q0:q0 + LQ], accq.ap())
        q_end_tt = P.tt.n
    P.barrier()
    _stg.close()


def build_p8(dbg=()):
    P = Prog()
    xn1fm = P.dram("xn1fm", [DM, L], BF16, kind="ExternalInput")
    x1fm = P.dram("x1fm", [DM, L], F32, kind="ExternalInput")
    s1fm = P.dram("s1fm", [DM, L], F32, kind="ExternalInput")
    router_w = P.dram("router_w", [DM, 16], F32, kind="ExternalInput")
    w1 = P.dram("w1", [16, DM, 2048], F32, kind="ExternalInput")
    b1 = P.dram("b1", [128, 256], F32, kind="ExternalInput")
    w2 = P.dram("w2moe", [16, 2048, DM], F32, kind="ExternalInput")
    b2all = P.dram("b2all", [16, DM], F32, kind="ExternalInput")
    out = P.dram("out", [L, DM], F32, kind="ExternalOutput")
    P.fconst(1e-5)
    ones_row(P)
    with P.scope():
        emit_p8(P, xn1fm, x1fm, s1fm, router_w, w1, b1, w2, b2all, out, dbg=dbg)
    return P


def emit_p8(P, xn1fm, x1fm, s1fm, router_w, w1, b1, w2, b2all, out, dbg=()):
    nc = P.nc
    v, s, t = nc.vector, nc.scalar, nc.tensor

    def dk(name):
        return "ExternalOutput" if name in dbg else "Internal"

    logits_d = P.dram("logits_d", [16, L], F32, kind=dk("logits_d"))
    Fp_d = P.dram("Fp_d", [128, L], F32, kind=dk("Fp_d"))
    w_d = P.dram("w_d", [1, L], F32, kind=dk("w_d"))
    b2p_d = P.dram("b2p_d", [128, DM], F32)
    he_d = P.dram("he_d", [2048, L], BF16)
    acc_d = P.dram("acc_d", [DM, L], F32, kind=dk("acc_d"))

    fm_matmul(P, xn1fm, router_w, logits_d, K=DM, N=16)

    iota16 = np.broadcast_to(np.arange(16, dtype=np.float32)[None, :], (128, 16)).copy()
    _stg = contextlib.ExitStack()
    def ssb(name, shape, dtype=F32):
        return _stg.enter_context(nc.sbuf_tensor(P.uname(name), list(shape), dtype))
    io_sb = ssb("io16", [128, 16])
    io2_sb = ssb("io16b", [128, 16])
    negio_sb = ssb("io16n", [128, 16])
    P.dma("sync", io_sb.ap(), P.const("iota16", iota16).ap())
    P.dma("sync", io2_sb.ap(), P.const("iota16c", -(iota16 + 1e4)).ap())
    P.dma("sync", negio_sb.ap(), P.const("iota16n", -iota16).ap())
    idf_sb = ssb("g_id", [128, 128])
    P.dma("sync", idf_sb.ap(), P.const("identf", np.eye(128, dtype=np.float32)).ap())
    zp = ssb("g_zp", [112, L // 8])
    P.tg.inc(nc.gpsimd.memset(zp.ap(), 0.0))
    P.tg.wait(nc.sync)
    for i in range(8):
        P.dma("sync", Fp_d.ap()[16:128, i * (L // 8):(i + 1) * (L // 8)], zp.ap())
    zp2 = ssb("g_zp2", [112, DM])
    P.tg.inc(nc.gpsimd.memset(zp2.ap(), 0.0))
    P.tg.wait(nc.sync)
    P.dma("sync", b2p_d.ap()[16:128, :], zp2.ap())
    P.dma("sync", b2p_d.ap()[0:16, :], b2all.ap())
    lg_sb = [ssb("g_l", [16, 128]) for _ in range(2)]
    lt = ssb("g_lt", [128, 16])
    Mc = ssb("g_M", [128, 4])
    eq = ssb("g_eq", [128, 16])
    fh = ssb("g_fh", [128, 16], BF16)
    fT = [ssb("g_fT", [16, 128]) for _ in range(2)]
    wrow = [ssb("g_wr", [1, 128]) for _ in range(2)]
    idb16 = ssb("g_idb", [128, 128], BF16)
    P.dma("gp", idb16.ap(), P.const("identf", np.eye(128, dtype=np.float32)).ap())
    P.barrier()
    done = [0, 0]
    for ti in range(L // 128):
        bb = ti % 2
        sl = slice(ti * 128, (ti + 1) * 128)
        P.tv.wait(nc.sync, done[bb])
        P.dma(f"ld{bb}", lg_sb[bb].ap(), logits_d.ap()[:, sl])
        P.qwait(t, f"ld{bb}")
        P.tv.wait(t)
        tr1 = P.tt.inc(t.transpose(P.psum[0].ap()[:, 0:16], lg_sb[bb].ap(),
                                   idf_sb.ap()[:16, :16]))
        P.tt.wait(v, tr1)
        v.tensor_copy(lt.ap(), P.psum[0].ap()[:, 0:16])
        v.drain()
        v.reduce_max(Mc.ap()[:, 0:1], lt.ap(), axis=AX.X)
        M_n = P.tv.inc(v.drain())
        P.tv.wait(s, M_n)
        sg_n = P.ts.inc(s.activation(Mc.ap()[:, 1:2], Mc.ap()[:, 0:1], AF.Sigmoid))
        v.tensor_scalar(out=eq.ap(), in0=lt.ap(), scalar1=Mc.ap()[:, 0:1], scalar2=None,
                        op0=ALU.is_ge)
        v.drain()
        v.scalar_tensor_tensor(out=eq.ap(), in0=eq.ap(), scalar=1e4, in1=io2_sb.ap(),
                               op0=ALU.mult, op1=ALU.add)
        v.drain()
        v.reduce_max(Mc.ap()[:, 2:3], eq.ap(), axis=AX.X)
        v.drain()
        v.tensor_scalar(out=fh.ap(), in0=negio_sb.ap(), scalar1=Mc.ap()[:, 2:3], scalar2=None,
                        op0=ALU.is_equal)
        fh_n = P.tv.inc(v.drain())
        P.tv.wait(t, fh_n)
        P.ts.wait(t, sg_n)
        tr2 = t.transpose(P.psum_bf[0].ap()[:16, 0:128], fh.ap(), idb16.ap())
        tr3 = t.transpose(P.psum[1].ap()[:1, 0:128], Mc.ap()[:, 1:2], idf_sb.ap())
        tr_n = P.tt.inc(tr3)
        P.tt.wait(v, tr_n)
        P.qwait(v, f"st{bb}")
        v.tensor_copy(fT[bb].ap(), P.psum_bf[0].ap()[:16, 0:128])
        ev_n = P.tv.inc(v.tensor_copy(wrow[bb].ap(), P.psum[1].ap()[:1, 0:128]))
        P.tv.wait(nc.sync, ev_n)
        P.dma(f"st{bb}", Fp_d.ap()[0:16, sl], fT[bb].ap())
        P.dma(f"st{bb}", w_d.ap()[:, sl], wrow[bb].ap())
        done[bb] = P.tv.n
    P.barrier()
    _stg.close()

    emit_p8_experts(P, xn1fm, Fp_d, b2p_d, w1, b1, w2, acc_d)

    _st2 = contextlib.ExitStack()
    def ssb2(name, shape, dtype=F32):
        return _st2.enter_context(nc.sbuf_tensor(P.uname(name), list(shape), dtype))
    ones_row(P)
    acc_sb = [ssb2("f_acc", [128, 8, 128]) for _ in range(2)]
    s1_sb = [ssb2("f_s1", [128, 8, 128]) for _ in range(2)]
    x1_sb = [ssb2("f_x1", [128, 8, 128]) for _ in range(2)]
    wr_sb = ssb2("f_wr", [1, 128])
    gs_sb = ssb2("f_gs", [128, 8, 128])
    t_sb = ssb2("f_t", [128, 8, 128])
    o_sb = [ssb2("f_o", [128, 1024]) for _ in range(2)]
    idf2 = ssb2("f_id", [128, 128])
    P.dma("sync", idf2.ap(), P.const("identf", np.eye(128, dtype=np.float32)).ap())
    P.barrier()
    done2 = [0, 0]
    for ti in range(L // 128):
        bb = ti % 2
        sl = slice(ti * 128, (ti + 1) * 128)
        sy = nc.sync
        P.tv.wait(sy, done2[bb])
        P.dma(f"ld{bb}", acc_sb[bb].ap(),
              acc_d.ap().rearrange("(rc p) l -> p rc l", p=128)[:, :, sl])
        P.dma(f"ld{bb}", s1_sb[bb].ap(),
              s1fm.ap().rearrange("(rc p) l -> p rc l", p=128)[:, :, sl])
        P.dma(f"ld{bb}", x1_sb[bb].ap(),
              x1fm.ap().rearrange("(rc p) l -> p rc l", p=128)[:, :, sl])
        P.dma(f"ld{bb}", wr_sb.ap(), w_d.ap()[:, sl])
        P.qwait(t, f"ld{bb}")
        P.tv.wait(t)
        wb = P.tt.inc(t.matmul(P.psum[2].ap()[:, 0:128], ones_row(P).ap(), wr_sb.ap(),
                               start=True, stop=True))
        P.tt.wait(v, wb)
        for rc in range(8):
            v.scalar_tensor_tensor(out=t_sb.ap()[:, rc, :], in0=acc_sb[bb].ap()[:, rc, :],
                                   scalar=1.0, in1=P.psum[2].ap()[:, 0:128],
                                   op0=ALU.mult, op1=ALU.mult)
        g1_n = P.tv.inc(v.drain())
        P.tv.wait(s, g1_n)
        zs_n = P.ts.inc(s.activation(gs_sb.ap(), t_sb.ap(), AF.Silu))
        P.ts.wait(v, zs_n)
        v.tensor_tensor(t_sb.ap(), s1_sb[bb].ap(), gs_sb.ap(), ALU.mult)
        v.drain()
        v.tensor_tensor(t_sb.ap(), x1_sb[bb].ap(), t_sb.ap(), ALU.add)
        tt_n = P.tv.inc(v.drain())
        P.tv.wait(t, tt_n)
        for rc in range(8):
            half, col = rc // 4, (rc % 4) * 128
            itr = t.transpose(P.psum[3 + half].ap()[:, col:col + 128],
                              t_sb.ap()[:, rc, :], idf2.ap())
        tro = P.tt.inc(itr)
        P.tt.wait(v, tro)
        P.qwait(v, f"st{bb}")
        v.tensor_copy(o_sb[bb].ap()[:, 0:512], P.psum[3].ap())
        oe_n = P.tv.inc(v.tensor_copy(o_sb[bb].ap()[:, 512:1024], P.psum[4].ap()))
        P.tv.wait(sy, oe_n)
        P.dma(f"st{bb}", out.ap()[sl, :], o_sb[bb].ap())
        done2[bb] = P.tv.n
    P.barrier()
    _st2.close()


_PROGS = {}


def _prog(name, builder):
    if name not in _PROGS:
        _PROGS[name] = builder()
    return _PROGS[name]


def kernel_multi(**ins):
    x = np.ascontiguousarray(ins["L0_attn_u"], dtype=np.float32)
    ncore = x.shape[0]
    cores = list(range(ncore))

    def run(name, builder, per_core):
        P = _prog(name, builder)
        res = run_bass_kernel_spmd(P.nc, per_core, core_ids=cores)
        return res.results

    def rep(arrs):
        return [arrs for _ in cores] if isinstance(arrs, dict) else arrs

    r1 = run("p1", build_p1, [dict(x=np.ascontiguousarray(x[i])) for i in cores])
    w_in = ins["L0_attn_W_in"]
    r2 = run("p2", build_p2, [dict(xnfm=r1[i]["xnfm"], w_in=w_in) for i in cores])
    D = ins["L0_attn_D"]
    drep = np.ascontiguousarray(np.repeat(D, 64).reshape(16, 128).T)
    bbT = np.ascontiguousarray(ins["L0_attn_B_bias"].T)
    cbT = np.ascontiguousarray(ins["L0_attn_C_bias"].T)
    r3 = run("p3", build_p3, [dict(
        projfm=r2[i]["projfm"], xs_tok=r2[i]["xs_tok"],
        dt_bias=ins["L0_attn_dt_bias"], b_biasT=bbT, c_biasT=cbT,
        b_norm_w=ins["L0_attn_B_norm_w"], c_norm_w=ins["L0_attn_C_norm_w"],
        d_rep=drep) for i in cores])
    r4 = run("p4", build_p4, [dict(ygfm=r3[i]["ygfm"], w_out=ins["L0_attn_W_out"])
                              for i in cores])
    cur = [r1[i]["xnfm"] for i in cores]
    for l in range(2):
        im = []
        for i in cores:
            im.append(dict(
                rin=cur[i],
                wqkv=np.ascontiguousarray(ins["L0_ffn_wqkv"][l]),
                qs_col=np.ascontiguousarray(np.tile(ins["L0_ffn_q_scale"][l], 2)[:, None]),
                ks_col=np.ascontiguousarray(np.tile(ins["L0_ffn_k_scale"][l], 2)[:, None]),
                wo=np.ascontiguousarray(ins["L0_ffn_wo"][l]),
                lg=np.ascontiguousarray(ins["L0_ffn_ln_g"][l]),
                lbv=np.ascontiguousarray(ins["L0_ffn_ln_b"][l]),
                w1a=np.ascontiguousarray(ins["L0_ffn_w1"][l][:, 0:2730]),
                b1a=np.ascontiguousarray(ins["L0_ffn_b1"][l][0:2730]),
                w1g=np.ascontiguousarray(ins["L0_ffn_w1"][l][:, 2730:5460]),
                b1g=np.ascontiguousarray(ins["L0_ffn_b1"][l][2730:5460]),
                w2=np.ascontiguousarray(ins["L0_ffn_w2"][l]),
                b2v=np.ascontiguousarray(ins["L0_ffn_b2"][l])))
        r5 = run("p5", build_p5, im)
        cur = [r5[i]["rout"] for i in cores]
    r6 = run("p6", build_p6, [dict(xfm=r1[i]["xfm"], sfm=r4[i]["sfm"], gfm=cur[i])
                              for i in cores])
    r7 = run("p7", build_p7, [dict(
        xn1f32=r6[i]["xn1f32"], g_ln=ins["L1_attn_g_ln"], wq=ins["L1_attn_Wq"],
        wkv=ins["L1_attn_Wkv"], wo_l1=ins["L1_attn_Wo"]) for i in cores])
    r8 = run("p8", build_p8, [dict(
        xn1fm=r6[i]["xn1fm"], x1fm=r6[i]["x1fm"], s1fm=r7[i]["s1fm"],
        router_w=ins["L1_ffn_router_w"], w1=ins["L1_ffn_w1"],
        b1=np.ascontiguousarray(
            ins["L1_ffn_b1"].reshape(16, 16, 128).transpose(2, 0, 1).reshape(128, 256)),
        w2moe=ins["L1_ffn_w2"], b2all=ins["L1_ffn_b2"]) for i in cores])
    out = np.stack([np.asarray(r8[i]["out"], dtype=np.float32) for i in cores], axis=0)
    return out


def build_full():
    P = Prog()
    P.fconst(1e-5)
    ones_row(P)
    D = lambda *a, **k: P.dram(*a, **k)
    x = D("x", [L, DM], F32, kind="ExternalInput")
    w_in = D("w_in", [DM, 4480], F32, kind="ExternalInput")
    dt_bias = D("dt_bias", [32], F32, kind="ExternalInput")
    b_biasT = D("b_biasT", [128, 32], F32, kind="ExternalInput")
    c_biasT = D("c_biasT", [128, 32], F32, kind="ExternalInput")
    b_norm_w = D("b_norm_w", [128], F32, kind="ExternalInput")
    c_norm_w = D("c_norm_w", [128], F32, kind="ExternalInput")
    d_rep = D("d_rep", [128, 16], F32, kind="ExternalInput")
    w_out = D("w_out", [2048, DM], F32, kind="ExternalInput")
    wqkv2 = D("wqkv2", [2, DM, 1536], F32, kind="ExternalInput")
    qs2 = D("qs2", [2, 128, 1], F32, kind="ExternalInput")
    ks2 = D("ks2", [2, 128, 1], F32, kind="ExternalInput")
    wo2 = D("wo2", [2, 512, DM], F32, kind="ExternalInput")
    lg2 = D("lg2", [2, DM], F32, kind="ExternalInput")
    lb2 = D("lb2", [2, DM], F32, kind="ExternalInput")
    w1f = D("w1f", [2, DM, 5460], F32, kind="ExternalInput")
    b1f = D("b1f", [2, 5460], F32, kind="ExternalInput")
    w2f = D("w2f", [2, 2730, DM], F32, kind="ExternalInput")
    b2f = D("b2f", [2, DM], F32, kind="ExternalInput")
    g_ln = D("g_ln", [DM], F32, kind="ExternalInput")
    wq = D("wq", [DM, DM], F32, kind="ExternalInput")
    wkv = D("wkv", [DM, 2 * DM], F32, kind="ExternalInput")
    wo_l1 = D("wo_l1", [DM, DM], F32, kind="ExternalInput")
    router_w = D("router_w", [DM, 16], F32, kind="ExternalInput")
    mw1 = D("mw1", [16, DM, 2048], F32, kind="ExternalInput")
    mb1 = D("mb1", [128, 256], F32, kind="ExternalInput")
    mw2 = D("mw2", [16, 2048, DM], F32, kind="ExternalInput")
    mb2 = D("mb2", [16, DM], F32, kind="ExternalInput")
    out = D("out", [L, DM], F32, kind="ExternalOutput")

    xfm = D("xfm", [DM, L], F32)
    xnfm = D("xnfm", [DM, L], F32)
    projfm = D("projfm", [4480, L], F32)
    xs_tok = D("xs_tok", [L, 2048], BF16)
    ygfm = D("ygfm", [2048, L], BF16)
    sfm = D("sfm", [DM, L], F32)
    r5a = D("r5a", [DM, L], F32)
    r5b = D("r5b", [DM, L], F32)
    x1fm = D("x1fm", [DM, L], F32)
    xn1fm = D("xn1fm", [DM, L], BF16)
    xn1f32 = D("xn1f32", [DM, L], F32)
    s1fm = D("s1fm", [DM, L], F32)
    sc5 = p5_scratch(P)

    with P.scope():
        emit_p1(P, x, xfm, xnfm)
    with P.scope():
        emit_p2(P, xnfm, w_in, projfm, xs_tok)
    with P.scope():
        emit_p3(P, projfm, xs_tok, dt_bias, b_biasT, c_biasT, b_norm_w, c_norm_w,
                d_rep, ygfm)
    fm_matmul(P, ygfm, w_out, sfm, K=2048, N=DM)
    rin = xnfm
    for l, rout_l in ((0, r5a), (1, r5b)):
        with P.scope():
            emit_p5(P, rin, rout_l,
                    _View(wqkv2.ap()[l], f"wqkv{l}"),
                    _View(qs2.ap()[l], f"qs{l}"), _View(ks2.ap()[l], f"ks{l}"),
                    _View(wo2.ap()[l], f"wo{l}"),
                    _View(lg2.ap()[l], f"lg{l}"), _View(lb2.ap()[l], f"lb{l}"),
                    _View(w1f.ap()[l][:, 0:2730], f"w1a{l}"),
                    _View(b1f.ap()[l][0:2730], f"b1a{l}"),
                    _View(w1f.ap()[l][:, 2730:5460], f"w1g{l}"),
                    _View(b1f.ap()[l][2730:5460], f"b1g{l}"),
                    _View(w2f.ap()[l], f"w2l{l}"), _View(b2f.ap()[l], f"b2l{l}"),
                    sc5)
        rin = rout_l
    with P.scope():
        emit_p6(P, xfm, sfm, r5b, x1fm, xn1fm, xn1f32)
    with P.scope():
        emit_p7(P, xn1f32, g_ln, wq, wkv, wo_l1, s1fm)
    with P.scope():
        emit_p8(P, xn1fm, x1fm, s1fm, router_w, mw1, mb1, mw2, mb2, out)
    P.barrier()
    return P


def kernel(**ins):
    return kernel_fused(**ins)


def kernel_fused(**ins):
    x = np.ascontiguousarray(ins["L0_attn_u"], dtype=np.float32)
    ncore = x.shape[0]
    cores = list(range(ncore))
    P = _prog("full", build_full)
    D = ins["L0_attn_D"]
    common = dict(
        w_in=ins["L0_attn_W_in"],
        dt_bias=ins["L0_attn_dt_bias"],
        b_biasT=np.ascontiguousarray(ins["L0_attn_B_bias"].T),
        c_biasT=np.ascontiguousarray(ins["L0_attn_C_bias"].T),
        b_norm_w=ins["L0_attn_B_norm_w"], c_norm_w=ins["L0_attn_C_norm_w"],
        d_rep=np.ascontiguousarray(np.repeat(D, 64).reshape(16, 128).T),
        w_out=ins["L0_attn_W_out"],
        wqkv2=ins["L0_ffn_wqkv"],
        qs2=np.ascontiguousarray(np.tile(ins["L0_ffn_q_scale"], (1, 2))[:, :, None]),
        ks2=np.ascontiguousarray(np.tile(ins["L0_ffn_k_scale"], (1, 2))[:, :, None]),
        wo2=ins["L0_ffn_wo"], lg2=ins["L0_ffn_ln_g"], lb2=ins["L0_ffn_ln_b"],
        w1f=ins["L0_ffn_w1"], b1f=ins["L0_ffn_b1"], w2f=ins["L0_ffn_w2"],
        b2f=ins["L0_ffn_b2"],
        g_ln=ins["L1_attn_g_ln"], wq=ins["L1_attn_Wq"], wkv=ins["L1_attn_Wkv"],
        wo_l1=ins["L1_attn_Wo"],
        router_w=ins["L1_ffn_router_w"], mw1=ins["L1_ffn_w1"],
        mb1=np.ascontiguousarray(
            ins["L1_ffn_b1"].reshape(16, 16, 128).transpose(2, 0, 1).reshape(128, 256)),
        mw2=ins["L1_ffn_w2"], mb2=ins["L1_ffn_b2"])
    im = [dict(common, x=np.ascontiguousarray(x[i])) for i in cores]
    res = run_bass_kernel_spmd(P.nc, im, core_ids=cores)
    return np.stack([np.asarray(res.results[i]["out"], dtype=np.float32)
                     for i in cores], axis=0)


# version 07
# speedup vs baseline: 1.4102x; 1.0286x over previous
import contextlib
import numpy as np
import concourse.bass as bass
import concourse.mybir as mybir
from concourse.bass_utils import run_bass_kernel_spmd

F32 = mybir.dt.float32
BF16 = mybir.dt.bfloat16
AF = mybir.ActivationFunctionType
ALU = mybir.AluOpType
AX = mybir.AxisListType

L = 4096
DM = 1024
NCORE = 8
NEG = -1e30


def _ceil(a, b):
    return -(-a // b)


class Tok:
    def __init__(self, prog, name):
        self.sem = prog.stack.enter_context(prog.nc.semaphore(name))
        self.n = 0

    def inc(self, instr):
        instr.then_inc(self.sem, 1)
        self.n += 1
        return self.n

    def wait(self, eng, n=None):
        eng.wait_ge(self.sem, self.n if n is None else n)


class Prog:
    def __init__(self):
        self.nc = bass.Bass("TRN2", target_bir_lowering=False)
        self.stack = contextlib.ExitStack()
        nc = self.nc
        self.q = {
            "sync": [nc.sync, self.stack.enter_context(nc.semaphore("q_sync")), 0],
            "gp": [nc.gpsimd, self.stack.enter_context(nc.semaphore("q_gp")), 0],
            "act": [nc.scalar, self.stack.enter_context(nc.semaphore("q_act")), 0],
            "st": [nc.sync, self.stack.enter_context(nc.semaphore("q_st")), 0],
            "ld0": [nc.sync, self.stack.enter_context(nc.semaphore("q_ld0")), 0],
            "ld1": [nc.sync, self.stack.enter_context(nc.semaphore("q_ld1")), 0],
            "gp0": [nc.gpsimd, self.stack.enter_context(nc.semaphore("q_gp0")), 0],
            "fr": [nc.sync, self.stack.enter_context(nc.semaphore("q_fr")), 0],
            "st0": [nc.sync, self.stack.enter_context(nc.semaphore("q_st0")), 0],
            "st1": [nc.sync, self.stack.enter_context(nc.semaphore("q_st1")), 0],
            "st2": [nc.sync, self.stack.enter_context(nc.semaphore("q_st2")), 0],
            "st3": [nc.sync, self.stack.enter_context(nc.semaphore("q_st3")), 0],
            "gp1": [nc.gpsimd, self.stack.enter_context(nc.semaphore("q_gp1")), 0],
        }
        self.tv = Tok(self, "tok_v")
        self.ts = Tok(self, "tok_s")
        self.tt = Tok(self, "tok_t")
        self.tg = Tok(self, "tok_g")
        self.psum = [nc.alloc_psum_tensor(f"pb{i}", [128, 512], F32) for i in range(6)]
        self.psum_bf = [nc.alloc_psum_tensor(f"pbb{i}", [128, 1024], BF16) for i in range(2)]
        self._consts = {}
        self._names = {}
        self._fconsts = {}

    def eng(self, qn):
        return self.q[qn][0]

    def dma(self, qn, out, in_):
        eng, sem, _ = self.q[qn]
        r = eng.dma_start(out=out, in_=in_).then_inc(sem, 16)
        self.q[qn][2] += 1
        return self.q[qn][2]

    def qcnt(self, qn):
        return self.q[qn][2]

    def qwait(self, eng, qn, cnt=None):
        _, sem, n = self.q[qn]
        eng.wait_ge(sem, 16 * (n if cnt is None else cnt))

    def flush(self):
        for qn, (eng, sem, cnt) in self.q.items():
            if cnt:
                eng.wait_ge(sem, 16 * cnt)

    def barrier(self):
        self.flush()
        self.nc.all_engine_barrier()

    def const(self, name, arr):
        if name not in self._consts:
            self._consts[name] = self.nc.inline_tensor(np.ascontiguousarray(arr), name=name)
        return self._consts[name]

    def fconst(self, val):
        if val not in self._fconsts:
            t = self.nc.alloc_sbuf_tensor(self.uname("fc"), [128, 1], F32)
            self.nc.gpsimd.memset(t.ap(), float(val))
            self._fconsts[val] = t
        return self._fconsts[val].ap()

    @contextlib.contextmanager
    def scope(self):
        stg = contextlib.ExitStack()
        saved = self.sb

        def sb2(name, shape, dtype=F32):
            return stg.enter_context(
                self.nc.sbuf_tensor(self.uname(name), list(shape), dtype))
        self.sb = sb2
        try:
            yield
        finally:
            self.barrier()
            self.sb = saved
            stg.close()

    def uname(self, base):
        k = self._names.get(base, 0)
        self._names[base] = k + 1
        return f"{base}{k}"

    def sb(self, name, shape, dtype=F32):
        return self.nc.alloc_sbuf_tensor(self.uname(name), list(shape), dtype)

    def dram(self, name, shape, dtype=F32, kind="Internal"):
        return self.nc.dram_tensor(name, list(shape), dtype, kind=kind)

    def close(self):
        self.stack.close()


def fm_matmul(P, X, W, out, K, N, Lw=None, bias=None, act=None, scale=1.0,
              out_dtype=F32, TTT=512, gate_row=None, accum=False):
    Lw = L if Lw is None else Lw
    assert not (accum and (bias is not None or act is not None or scale != 1.0))
    nc = P.nc
    if accum:
        ones_row(P)
    kcs = K // 128
    nch = _ceil(N, 128)
    ntt = Lw // TTT
    _stg = contextlib.ExitStack()
    def _sb(name, shape, dtype=F32):
        return _stg.enter_context(nc.sbuf_tensor(P.uname(name), list(shape), dtype))
    P_sb_saved = P.sb
    P.sb = _sb
    wsb = P.sb("mmw", [128, kcs, N], BF16)
    P.dma("gp", wsb.ap(), W.ap().rearrange("(kc p) n -> p kc n", p=128))
    bsb = None
    if bias is not None:
        bsb = P.sb("mmbias", [128, nch], F32)
        for i in range(nch):
            rows = min(128, N - i * 128)
            P.dma("sync", bsb.ap()[:rows, i:i + 1], bias.ap()[i * 128:i * 128 + rows].unsqueeze(1))
    P.barrier()

    xsb = [P.sb("mmx", [128, kcs, TTT], BF16) for _ in range(2)]
    osb = [P.sb("mmo", [128, TTT], out_dtype) for _ in range(4)]
    if accum:
        fb_sb = P.sb("mmfb", [128, TTT], F32)
        tmp_sb = P.sb("mmtmp", [128, TTT], F32)
        rsb = [P.sb("mmr", [128, TTT], F32) for _ in range(2)]
        racc_free = [0, 0]
    xa = X.ap().rearrange("(kc p) l -> p kc l", p=128)
    use_scalar = (act is not None) or (bias is not None) or (scale != 1.0)
    etok = P.ts if use_scalar else P.tv
    eeng = nc.scalar if use_scalar else nc.vector

    pe_done_at_x = [0, 0]
    bank_free = [0] * 2

    frow_ld = 0
    for tt in range(ntt):
        xb = tt % 2
        gp = nc.gpsimd
        P.tt.wait(gp, pe_done_at_x[xb])
        P.dma(f"gp{xb}", xsb[xb].ap(), xa[:, :, tt * TTT:(tt + 1) * TTT])
        if accum and gate_row is not None:
            P.tv.wait(nc.sync)
            P.dma("fr", fb_sb.ap()[0:1, :],
                  gate_row.ap()[:, tt * TTT:(tt + 1) * TTT])
            t = nc.tensor
            P.qwait(t, "fr")
            P.tv.wait(t)
            fmm = P.tt.inc(t.matmul(P.psum[2].ap()[:, 0:TTT], ones_row(P).ap(),
                                    fb_sb.ap()[0:1, :], start=True, stop=True))
            v0 = nc.vector
            P.tt.wait(v0, fmm)
            fb_n = P.tv.inc(v0.tensor_copy(fb_sb.ap(), P.psum[2].ap()[:, 0:TTT]))
            v0.drain()
            P.tv.wait(v0, fb_n)
        for ncI in range(nch):
            bk = ncI % 2
            pb = P.psum[bk]
            rows = min(128, N - ncI * 128)
            t = nc.tensor
            P.qwait(t, f"gp{xb}")
            etok.wait(t, bank_free[bk])
            for kc in range(kcs):
                ins = t.matmul(pb.ap()[:rows, :],
                               wsb.ap()[:, kc, ncI * 128:ncI * 128 + rows],
                               xsb[xb].ap()[:, kc, :],
                               start=(kc == 0), stop=(kc == kcs - 1))
            mm_n = P.tt.inc(ins)
            j = ncI % 4
            ob = osb[j]
            if accum:
                P.tv.wait(nc.sync, racc_free[ncI % 2])
                P.dma(f"ld{xb}", rsb[ncI % 2].ap()[:rows, :],
                      out.ap()[ncI * 128:ncI * 128 + rows, tt * TTT:(tt + 1) * TTT])
            P.tt.wait(eeng, mm_n)
            P.qwait(eeng, f"st{j}")
            if accum:
                v0 = nc.vector
                P.qwait(v0, f"ld{xb}")
                if gate_row is not None:
                    v0.scalar_tensor_tensor(out=tmp_sb.ap()[:rows, :],
                                            in0=fb_sb.ap()[:rows, :], scalar=1.0,
                                            in1=pb.ap()[:rows, :], op0=ALU.mult,
                                            op1=ALU.mult)
                else:
                    v0.tensor_copy(tmp_sb.ap()[:rows, :], pb.ap()[:rows, :])
                v0.drain()
                ev = v0.tensor_tensor(ob.ap()[:rows, :], tmp_sb.ap()[:rows, :],
                                      rsb[ncI % 2].ap()[:rows, :], ALU.add)
            elif use_scalar:
                kw = dict(scale=scale)
                if bias is not None:
                    kw["bias"] = bsb.ap()[:rows, ncI:ncI + 1]
                ev = eeng.activation(ob.ap()[:rows, :], pb.ap()[:rows, :],
                                     act if act is not None else AF.Identity, **kw)
            else:
                ev = eeng.tensor_copy(ob.ap()[:rows, :], pb.ap()[:rows, :])
            bank_free[bk] = etok.inc(ev)
            if accum:
                racc_free[ncI % 2] = bank_free[bk]
            sy = nc.sync
            etok.wait(sy)
            P.dma(f"st{j}", out.ap()[ncI * 128:ncI * 128 + rows, tt * TTT:(tt + 1) * TTT],
                  ob.ap()[:rows, :])
        pe_done_at_x[xb] = P.tt.n
    P.barrier()
    P.sb = P_sb_saved
    _stg.close()


def tok_matmul(P, X, W, out, K, N, Lw=None, out_dtype=F32):
    Lw = L if Lw is None else Lw
    nc = P.nc
    kcs = K // 128
    nns = _ceil(N, 512)
    _stg = contextlib.ExitStack()
    def _sb(name, shape, dtype=F32):
        return _stg.enter_context(nc.sbuf_tensor(P.uname(name), list(shape), dtype))
    P_sb_saved = P.sb
    P.sb = _sb
    wsb = P.sb("wt", [128, kcs, N], BF16)
    P.dma("gp", wsb.ap(), W.ap().rearrange("(kc p) n -> p kc n", p=128))
    P.barrier()
    xsb = [P.sb("xt", [128, kcs, 128], BF16) for _ in range(2)]
    osb = [P.sb("ot", [128, 512], out_dtype) for _ in range(4)]
    xa = X.ap().rearrange("(kc p) l -> p kc l", p=128)
    pe_done_at_x = [0, 0]
    bank_free = [0] * 2
    for ti in range(Lw // 128):
        xb = ti % 2
        gp = nc.gpsimd
        P.tt.wait(gp, pe_done_at_x[xb])
        P.dma(f"gp{xb}", xsb[xb].ap(), xa[:, :, ti * 128:(ti + 1) * 128])
        for ns in range(nns):
            cols = min(512, N - ns * 512)
            bk = ns % 2
            pb = P.psum[bk]
            t = nc.tensor
            P.qwait(t, f"gp{xb}")
            P.tv.wait(t, bank_free[bk])
            for kc in range(kcs):
                ins = t.matmul(pb.ap()[:, :cols],
                               xsb[xb].ap()[:, kc, :],
                               wsb.ap()[:, kc, ns * 512:ns * 512 + cols],
                               start=(kc == 0), stop=(kc == kcs - 1))
            mm_n = P.tt.inc(ins)
            j = (ti * nns + ns) % 4
            ob = osb[j]
            v = nc.vector
            P.tt.wait(v, mm_n)
            P.qwait(v, f"st{j}")
            ev = v.tensor_copy(ob.ap()[:, :cols], pb.ap()[:, :cols])
            bank_free[bk] = P.tv.inc(ev)
            sy = nc.sync
            P.tv.wait(sy)
            P.dma(f"st{j}", out.ap()[ti * 128:(ti + 1) * 128, ns * 512:ns * 512 + cols],
                  ob.ap()[:, :cols])
        pe_done_at_x[xb] = P.tt.n
    P.barrier()
    P.sb = P_sb_saved
    _stg.close()


def fm_elemwise(P, ins, outs, R, emit, Lw=None, TTT=2048, in_dtypes=None, out_dtypes=None):
    Lw = L if Lw is None else Lw
    TTT = min(TTT, Lw)
    nc = P.nc
    rch = _ceil(R, 128)
    ntt = Lw // TTT
    if in_dtypes is None:
        in_dtypes = [F32] * len(ins)
    if out_dtypes is None:
        out_dtypes = [F32] * len(outs)
    _stg = contextlib.ExitStack()
    def _sb(name, shape, dtype=F32):
        return _stg.enter_context(nc.sbuf_tensor(P.uname(name), list(shape), dtype))
    isb = [[_sb("ew_i", [128, TTT], dt) for dt in in_dtypes] for _ in range(2)]
    osb = [[_sb("ew_o", [128, TTT], dt) for dt in out_dtypes] for _ in range(2)]
    done = [0, 0]
    for idx in range(rch * ntt):
        rc, tt = divmod(idx, ntt)
        rows = min(128, R - rc * 128)
        bb = idx % 2
        gp = nc.gpsimd
        P.tv.wait(gp, done[bb])
        for k, x in enumerate(ins):
            P.dma(f"gp{bb}", isb[bb][k].ap()[:rows, :],
                  x.ap()[rc * 128:rc * 128 + rows, tt * TTT:(tt + 1) * TTT])
        v = nc.vector
        P.qwait(v, f"gp{bb}")
        P.qwait(v, f"st{bb}")
        P.qwait(nc.scalar, f"gp{bb}")
        P.qwait(nc.scalar, f"st{bb}")
        last = emit(v, [o.ap()[:rows, :] for o in osb[bb]], [i.ap()[:rows, :] for i in isb[bb]])
        done[bb] = P.tv.inc(last)
        sy = nc.sync
        P.tv.wait(sy)
        for k, o in enumerate(outs):
            P.dma(f"st{bb}", o.ap()[rc * 128:rc * 128 + rows, tt * TTT:(tt + 1) * TTT],
                  osb[bb][k].ap()[:rows, :])
    P.barrier()
    _stg.close()


def build_p1():
    P = Prog()
    x = P.dram("x", [L, DM], F32, kind="ExternalInput")
    xfm = P.dram("xfm", [DM, L], F32, kind="ExternalOutput")
    xnfm = P.dram("xnfm", [DM, L], F32, kind="ExternalOutput")
    emit_p1(P, x, xfm, xnfm)
    return P


def emit_p1(P, x, xfm, xnfm):
    nc = P.nc
    idf = P.const("identf", np.eye(128, dtype=np.float32))
    id_s = P.sb("id_s", [128, 128], F32)
    P.dma("sync", id_s.ap(), idf.ap())
    eps_c = P.fconst(1e-5)

    xt = [P.sb("xt", [128, DM], F32) for _ in range(2)]
    xnb = [P.sb("xnb", [128, DM], F32) for _ in range(2)]
    stat = P.sb("stat", [128, 8], F32)
    ofm = [P.sb("ofm", [128, DM], F32) for _ in range(2)]
    onb = [P.sb("onb", [128, DM], F32) for _ in range(2)]
    P.barrier()

    cons = [0, 0]
    prev_ev = 0
    prev_xn = 0
    for ti in range(L // 128):
        bb = ti % 2
        gp = nc.gpsimd
        P.tt.wait(gp, cons[bb])
        P.dma(f"gp{bb}", xt[bb].ap(), x.ap()[ti * 128:(ti + 1) * 128, :])
        v, s, t = nc.vector, nc.scalar, nc.tensor
        P.qwait(v, f"gp{bb}")
        P.ts.wait(v, prev_xn)
        v.reduce_sum(stat.ap()[:, 0:1], xt[bb].ap(), axis=AX.X)
        v.drain()
        P.qwait(s, f"gp{bb}")
        P.tt.wait(s, cons[bb])
        sq_n = P.ts.inc(s.activation(xnb[bb].ap(), xt[bb].ap(), AF.Square))
        P.ts.wait(v, sq_n)
        v.reduce_sum(stat.ap()[:, 1:2], xnb[bb].ap(), axis=AX.X)
        v.drain()
        v.tensor_scalar(out=stat.ap()[:, 0:1], in0=stat.ap()[:, 0:1], scalar1=1.0 / DM,
                        scalar2=None, op0=ALU.mult)
        v.drain()
        v.tensor_tensor(stat.ap()[:, 2:3], stat.ap()[:, 0:1], stat.ap()[:, 0:1], ALU.mult)
        v.drain()
        var_n = P.tv.inc(v.scalar_tensor_tensor(out=stat.ap()[:, 2:3], in0=stat.ap()[:, 1:2],
                               scalar=1.0 / DM, in1=stat.ap()[:, 2:3],
                               op0=ALU.mult, op1=ALU.subtract))
        v.drain()
        P.tv.wait(s, var_n)
        sd_n = P.ts.inc(s.activation(stat.ap()[:, 3:4], stat.ap()[:, 2:3], AF.Sqrt, bias=eps_c))
        P.ts.wait(v, sd_n)
        v.reciprocal(stat.ap()[:, 4:5], stat.ap()[:, 3:4])
        v.drain()
        r_n = P.tv.inc(v.scalar_tensor_tensor(out=stat.ap()[:, 5:6], in0=stat.ap()[:, 0:1],
                               scalar=-1.0, in1=stat.ap()[:, 4:5],
                               op0=ALU.mult, op1=ALU.mult))
        v.drain()
        P.tv.wait(s, r_n)
        xn_n = P.ts.inc(s.activation(xnb[bb].ap(), xt[bb].ap(), AF.Identity,
                     scale=stat.ap()[:, 4:5], bias=stat.ap()[:, 5:6]))
        prev_xn = xn_n
        P.ts.wait(t, xn_n)
        P.tv.wait(t, prev_ev)
        tr_n = 0
        for half in range(2):
            for c in range(4):
                cc = half * 4 + c
                pbn = P.psum[half]
                pbx = P.psum[2 + half]
                i1 = t.transpose(pbn.ap()[:, c * 128:(c + 1) * 128],
                                 xnb[bb].ap()[:, cc * 128:(cc + 1) * 128], id_s.ap())
                i2 = t.transpose(pbx.ap()[:, c * 128:(c + 1) * 128],
                                 xt[bb].ap()[:, cc * 128:(cc + 1) * 128], id_s.ap())
        cons[bb] = P.tt.inc(i2)
        P.tt.wait(v)
        P.qwait(v, f"st{bb}")
        v.tensor_copy(onb[bb].ap()[:, 0:512], P.psum[0].ap())
        v.tensor_copy(onb[bb].ap()[:, 512:1024], P.psum[1].ap())
        v.tensor_copy(ofm[bb].ap()[:, 0:512], P.psum[2].ap())
        ev_n = P.tv.inc(v.tensor_copy(ofm[bb].ap()[:, 512:1024], P.psum[3].ap()))
        prev_ev = ev_n
        sy = nc.sync
        P.tv.wait(sy)
        P.dma("st", xnfm.ap().rearrange("(c p) l -> p c l", p=128)[:, :, ti * 128:(ti + 1) * 128],
              onb[bb].ap().rearrange("p (c t) -> p c t", t=128))
        P.dma("st", xfm.ap().rearrange("(c p) l -> p c l", p=128)[:, :, ti * 128:(ti + 1) * 128],
              ofm[bb].ap().rearrange("p (c t) -> p c t", t=128))
    P.barrier()


def build_p2():
    P = Prog()
    xnfm = P.dram("xnfm", [DM, L], F32, kind="ExternalInput")
    w_in = P.dram("w_in", [DM, 4480], F32, kind="ExternalInput")
    projfm = P.dram("projfm", [4480, L], F32, kind="ExternalOutput")
    xs_tok = P.dram("xs_tok", [L, 2048], BF16, kind="ExternalOutput")
    emit_p2(P, xnfm, w_in, projfm, xs_tok)
    return P


def emit_p2(P, xnfm, w_in, projfm, xs_tok):
    nc = P.nc
    fm_matmul(P, xnfm, w_in, projfm, K=DM, N=4480)
    kcs = 8
    wsb = P.sb("wsm", [128, kcs, 128], F32)
    P.dma("sync", wsb.ap(), w_in.ap().rearrange("(kc p) n -> p kc n", p=128)[:, :, 4352:4480])
    P.barrier()
    xsb = [P.sb("xsm", [128, kcs, 512], F32) for _ in range(2)]
    osb = [P.sb("osm", [128, 512], F32) for _ in range(2)]
    pe_done = [0, 0]; bank_free = [0, 0]
    xa = xnfm.ap().rearrange("(kc p) l -> p kc l", p=128)
    for tt in range(L // 512):
        bb = tt % 2
        gp = nc.gpsimd
        P.tt.wait(gp, pe_done[bb])
        P.dma(f"gp{bb}", xsb[bb].ap(), xa[:, :, tt * 512:(tt + 1) * 512])
        t = nc.tensor
        P.qwait(t, f"gp{bb}")
        P.tv.wait(t, bank_free[bb])
        pb = P.psum[bb]
        for kc in range(kcs):
            ins = t.matmul(pb.ap(), wsb.ap()[:, kc, :], xsb[bb].ap()[:, kc, :],
                           start=(kc == 0), stop=(kc == kcs - 1))
        mm_n = P.tt.inc(ins)
        v = nc.vector
        P.tt.wait(v, mm_n)
        P.qwait(v, f"st{bb}")
        bank_free[bb] = P.tv.inc(v.tensor_copy(osb[bb].ap(), pb.ap()))
        sy = nc.sync
        P.tv.wait(sy)
        P.dma(f"st{bb}", projfm.ap()[4352:4480, tt * 512:(tt + 1) * 512], osb[bb].ap())
        pe_done[bb] = P.tt.n
    P.barrier()

    class _View:
        def __init__(self, ap, name):
            self._ap = ap; self.name = name
        def ap(self):
            return self._ap
    wv = _View(w_in.ap()[:, 2048:4096], "w_xs")
    tok_matmul(P, xnfm, wv, xs_tok, K=DM, N=2048, out_dtype=BF16)


def kernel(**inputs):
    raise NotImplementedError("under construction")


NH = 32
HD = 64
NST = 128
CH = 128
NCH = L // CH
PI = float(np.pi)


def build_p3(dbg=(), serial=0):
    P = Prog()
    projfm = P.dram("projfm", [4480, L], F32, kind="ExternalInput")
    xs_tok = P.dram("xs_tok", [L, 2048], BF16, kind="ExternalInput")
    dt_bias = P.dram("dt_bias", [32], F32, kind="ExternalInput")
    b_biasT = P.dram("b_biasT", [128, 32], F32, kind="ExternalInput")
    c_biasT = P.dram("c_biasT", [128, 32], F32, kind="ExternalInput")
    b_norm_w = P.dram("b_norm_w", [128], F32, kind="ExternalInput")
    c_norm_w = P.dram("c_norm_w", [128], F32, kind="ExternalInput")
    d_rep = P.dram("d_rep", [128, 16], F32, kind="ExternalInput")
    ygfm = P.dram("ygfm", [2048, L], BF16, kind="ExternalOutput")
    P.fconst(1e-5)
    with P.scope():
        emit_p3(P, projfm, xs_tok, dt_bias, b_biasT, c_biasT, b_norm_w, c_norm_w,
                d_rep, ygfm, dbg=dbg, serial=serial)
    return P


def emit_p3(P, projfm, xs_tok, dt_bias, b_biasT, c_biasT, b_norm_w, c_norm_w,
            d_rep, ygfm, dbg=(), serial=0):
    nc = P.nc
    v, s, t, g = nc.vector, nc.scalar, nc.tensor, nc.gpsimd

    def dk(name):
        return "ExternalOutput" if name in dbg else "Internal"

    ell_d = P.dram("ell_d", [32, L], F32, kind=dk("ell_d"))
    dt_d = P.dram("dt_d", [32, L], F32, kind=dk("dt_d"))
    c_d = P.dram("c_d", [NH * 32, L], F32, kind=dk("c_d"))
    s_d = P.dram("s_d", [NH * 32, L], F32, kind=dk("s_d"))
    K_d = P.dram("K_d", [128, NH, L + 1], BF16, kind=dk("K_d"))
    Q_d = P.dram("Q_d", [128, NH, L], BF16, kind=dk("Q_d"))
    Qe_d = P.dram("Qe_d", [128, NH, L], BF16, kind=dk("Qe_d"))

    oh = np.zeros((32, 32 * 128), dtype=np.float32)
    for h in range(32):
        oh[h, h * 128:(h + 1) * 128] = 1.0
    ohc = P.const("ohc", oh)
    rep4 = np.zeros((32, 128), dtype=np.float32)
    for r in range(128):
        rep4[r % 32, r] = 1.0
    rep4c = P.const("rep4c", rep4)
    rdt = np.zeros((32, 8 * 128), dtype=np.float32)
    for b in range(8):
        for r in range(128):
            rdt[4 * b + r // 32, b * 128 + r] = 1.0
    rdtc = P.const("rdtc", rdt)
    msk = np.where(np.arange(128)[:, None] <= np.arange(128)[None, :], 0.0, NEG
                   ).astype(np.float32)
    mskc = P.const("mskc", msk)
    onesr = P.const("onesr128", np.ones((1, 128), dtype=np.float32))

    ell_sb = P.sb("ell_sb", [32, L], F32)
    elltok = P.sb("elltok", [128, NCH * 32], F32)
    g1tok = P.sb("g1tok", [128, NCH * 32], F32)
    g2tok = P.sb("g2tok", [128, NCH * 32], F32)
    w1full = P.sb("w1full", [128, NCH * 32], F32)
    w2full = P.sb("w2full", [128, NCH * 32], F32)
    cdb = P.sb("cdb", [128, NCH * 32], F32)
    ones_sb = P.sb("ones_sb", [1, 128], F32)
    id_sb = P.sb("id_sb", [128, 128], F32)
    idb_sb = P.sb("idb_sb", [128, 128], BF16)
    drep_sb = P.sb("drep_sb", [128, 16], F32)
    P.dma("sync", ones_sb.ap(), onesr.ap())
    P.dma("sync", id_sb.ap(), P.const("identf", np.eye(128, dtype=np.float32)).ap())
    P.dma("gp", idb_sb.ap(), P.const("identf", np.eye(128, dtype=np.float32)).ap())
    P.dma("sync", drep_sb.ap(), d_rep.ap())
    eps_c3 = P.fconst(1e-5)
    st_dt = contextlib.ExitStack()
    st_bn = contextlib.ExitStack()

    dt_sb = st_dt.enter_context(nc.sbuf_tensor("dt_sb", [32, L + 128], F32))
    with contextlib.ExitStack() as stg:
        def ssb(name, shape, dtype=F32):
            return stg.enter_context(nc.sbuf_tensor(P.uname(name), list(shape), dtype))
        dd_dt = ssb("dd_dt", [32, L]); dd_A = ssb("dd_A", [32, L])
        trap = ssb("trap", [32, L])
        tmp1 = ssb("tmp1", [32, L]); tmp2 = ssb("tmp2", [32, L])
        g1f = dd_dt; g2f = tmp1; adt = dd_A; beta = trap
        dtb_sb = ssb("dtb_sb", [32, 1])
        P.dma("sync", dd_dt.ap(), projfm.ap()[4352:4384, :])
        P.dma("sync", dd_A.ap(), projfm.ap()[4384:4416, :])
        P.dma("sync", trap.ap(), projfm.ap()[4416:4448, :])
        P.dma("sync", dtb_sb.ap(), dt_bias.ap().unsqueeze(1))
        g.memset(dt_sb.ap()[:, 0:128], 0.0)
        segm = ssb("segm", [32, L])
        P.tg.inc(g.memset(segm.ap(), 1.0))
        P.tg.wait(g)
        for c in range(NCH):
            g.memset(segm.ap()[:, c * CH:c * CH + 1], 0.0)
        P.barrier()

        v.tensor_scalar(out=dd_dt.ap(), in0=dd_dt.ap(), scalar1=dtb_sb.ap(), scalar2=None,
                        op0=ALU.add)
        u_n = P.tv.inc(v.drain())
        P.tv.wait(s, u_n)
        P.ts.wait(s, P.ts.inc(s.activation(tmp1.ap(), dd_dt.ap(), AF.Abs)))
        P.ts.wait(s, P.ts.inc(s.activation(tmp1.ap(), tmp1.ap(), AF.Exp, scale=-1.0)))
        ln_n = P.ts.inc(s.activation(tmp1.ap(), tmp1.ap(), AF.Ln, bias=1.0))
        v.tensor_scalar(out=tmp2.ap(), in0=dd_dt.ap(), scalar1=0.0, scalar2=None, op0=ALU.max)
        v.drain()
        P.ts.wait(v, ln_n)
        v.tensor_tensor(dt_sb.ap()[:, 128:], tmp2.ap(), tmp1.ap(), ALU.add)
        dt_n = P.tv.inc(v.drain())
        P.tv.wait(s, dt_n)
        P.ts.wait(s, P.ts.inc(s.activation(tmp1.ap(), dd_A.ap(), AF.Abs)))
        P.ts.wait(s, P.ts.inc(s.activation(tmp1.ap(), tmp1.ap(), AF.Exp, scale=-1.0)))
        ln2_n = P.ts.inc(s.activation(tmp1.ap(), tmp1.ap(), AF.Ln, bias=1.0))
        v.tensor_scalar(out=tmp2.ap(), in0=dd_A.ap(), scalar1=0.0, scalar2=None, op0=ALU.max)
        v.drain()
        P.ts.wait(v, ln2_n)
        v.tensor_tensor(tmp2.ap(), tmp2.ap(), tmp1.ap(), ALU.add)
        v.drain()
        v.tensor_scalar(out=tmp2.ap(), in0=tmp2.ap(), scalar1=1e-4, scalar2=None, op0=ALU.max)
        v.drain()
        v.scalar_tensor_tensor(out=adt.ap(), in0=tmp2.ap(), scalar=-1.0,
                               in1=dt_sb.ap()[:, 128:], op0=ALU.mult, op1=ALU.mult)
        adt_n = P.tv.inc(v.drain())
        bt_n = P.ts.inc(s.activation(beta.ap(), trap.ap(), AF.Sigmoid))
        P.tv.wait(s, adt_n)
        ea_n = P.ts.inc(s.activation(tmp1.ap(), adt.ap(), AF.Exp))
        P.ts.wait(v, bt_n)
        v.tensor_tensor(g1f.ap(), beta.ap(), dt_sb.ap()[:, 128:], ALU.mult)
        v.tensor_scalar(out=tmp2.ap(), in0=beta.ap(), scalar1=-1.0, scalar2=1.0,
                        op0=ALU.mult, op1=ALU.add)
        v.drain()
        P.ts.wait(v, ea_n)
        v.tensor_tensor(tmp2.ap(), tmp2.ap(), tmp1.ap(), ALU.mult)
        v.drain()
        v.tensor_tensor(g2f.ap(), tmp2.ap(), dt_sb.ap()[:, 127:127 + L], ALU.mult)
        g2_n = P.tv.inc(v.drain())
        v.tensor_tensor_scan(ell_sb.ap(), segm.ap(), adt.ap(), 0.0, ALU.mult, ALU.add)
        ell_n = P.tv.inc(v.drain())
        P.tv.wait(nc.sync, ell_n)
        P.dma("sync", ell_d.ap(), ell_sb.ap())
        P.dma("sync", dt_d.ap(), dt_sb.ap()[:, 128:])

        for c in range(NCH):
            bk = c % 2
            pb = P.psum[bk]
            P.tv.wait(t)
            i1 = t.transpose(pb.ap()[:, 0:32], ell_sb.ap()[:, c * CH:(c + 1) * CH],
                             id_sb.ap()[:32, :32])
            i2 = t.transpose(pb.ap()[:, 32:64], g1f.ap()[:, c * CH:(c + 1) * CH],
                             id_sb.ap()[:32, :32])
            i3 = t.transpose(pb.ap()[:, 64:96], g2f.ap()[:, c * CH:(c + 1) * CH],
                             id_sb.ap()[:32, :32])
            tr_n = P.tt.inc(i3)
            P.tt.wait(v, tr_n)
            v.tensor_copy(elltok.ap()[:, c * 32:(c + 1) * 32], pb.ap()[:, 0:32])
            v.tensor_copy(g1tok.ap()[:, c * 32:(c + 1) * 32], pb.ap()[:, 32:64])
            P.tv.inc(v.tensor_copy(g2tok.ap()[:, c * 32:(c + 1) * 32], pb.ap()[:, 64:96]))
        P.barrier()

        lastell_sb = ssb("lastell", [32, NCH])
        v.tensor_copy(lastell_sb.ap().unsqueeze(2),
                      ell_sb.ap().rearrange("h (c t) -> h c t", t=CH)[:, :, CH - 1:CH])
        le_n = P.tv.inc(v.drain())
        lastellT = ssb("lastellT", [NCH, 32])
        P.tv.wait(t, le_n)
        ltr_n = P.tt.inc(t.transpose(P.psum[0].ap()[:NCH, 0:32], lastell_sb.ap(),
                                     id_sb.ap()[:32, :32]))
        P.tt.wait(v, ltr_n)
        le2_n = P.tv.inc(v.tensor_copy(lastellT.ap(), P.psum[0].ap()[:NCH, 0:32]))
        lastell_d = P.dram("lastell_d", [NCH, 32], F32)
        P.tv.wait(nc.sync, le2_n)
        P.dma("sync", lastell_d.ap(), lastellT.ap())
        P.barrier()
        lerow = ssb("lerow", [1, NCH * 32])
        P.dma("sync", lerow.ap(), lastell_d.ap().rearrange("c h -> (c h)").unsqueeze(0))
        P.barrier()
        for sl in range(_ceil(NCH * 32, 512)):
            c0 = sl * 512
            cols = min(512, NCH * 32 - c0)
            pb = P.psum[sl % 2]
            P.tv.wait(t)
            mm_n = P.tt.inc(t.matmul(pb.ap()[:, :cols], ones_sb.ap(),
                                     lerow.ap()[:, c0:c0 + cols], start=True, stop=True))
            P.tt.wait(v, mm_n)
            v.scalar_tensor_tensor(out=w1full.ap()[:, c0:c0 + cols],
                                   in0=elltok.ap()[:, c0:c0 + cols], scalar=-1.0,
                                   in1=pb.ap()[:, :cols], op0=ALU.mult, op1=ALU.add)
            ee_n = P.tv.inc(v.tensor_copy(cdb.ap()[:, c0:c0 + cols], pb.ap()[:, :cols]))
            P.tv.wait(s, ee_n)
            s.activation(w1full.ap()[:, c0:c0 + cols], w1full.ap()[:, c0:c0 + cols], AF.Exp)
            P.ts.inc(s.activation(cdb.ap()[:, c0:c0 + cols], cdb.ap()[:, c0:c0 + cols], AF.Exp))
        P.barrier()
        v.tensor_tensor(w2full.ap(), g2tok.ap(), w1full.ap(), ALU.mult)
        v.drain()
        v.tensor_tensor(w1full.ap(), g1tok.ap(), w1full.ap(), ALU.mult)
        v.drain()
        P.barrier()
        if "dbg3_d" in dbg:
            dbg3_d = P.dram("dbg3_d", [128, 6 * NCH * 32], F32, kind="ExternalOutput")
            for i, tsr in enumerate([cdb, w1full, w2full, elltok, g1tok, g2tok]):
                P.dma("sync", dbg3_d.ap()[:, i * NCH * 32:(i + 1) * NCH * 32], tsr.ap())
            P.barrier()

    with contextlib.ExitStack() as stg:
        def ssb(name, shape, dtype=F32):
            return stg.enter_context(nc.sbuf_tensor(P.uname(name), list(shape), dtype))
        rep4_sb = ssb("rep4_sb", [32, 128])
        rdt_sb = ssb("rdt_sb", [32, 8 * 128])
        ang4 = ssb("ang4", [128, L])
        angb = ssb("angb", [128, L])
        thb = ssb("thb", [128, L])
        ones128 = ssb("ones128", [128, L])
        wq = ssb("wq", [128, L])
        wqi = ssb("wqi", [128, L], mybir.dt.int32)
        P.dma("sync", rep4_sb.ap(), rep4c.ap())
        P.dma("sync", rdt_sb.ap(), rdtc.ap())
        g.memset(ones128.ap(), 1.0)
        P.barrier()
        ang_sb = ssb("ang_keep", [32, L])
        P.dma("sync", ang_sb.ap(), projfm.ap()[4448:4480, :])
        P.barrier()
        for sl in range(L // 512):
            pb = P.psum[sl % 2]
            P.tv.wait(t)
            mm_n = P.tt.inc(t.matmul(pb.ap(), rep4_sb.ap(),
                                     ang_sb.ap()[:, sl * 512:(sl + 1) * 512],
                                     start=True, stop=True))
            P.tt.wait(v, mm_n)
            P.tv.inc(v.tensor_copy(ang4.ap()[:, sl * 512:(sl + 1) * 512], pb.ap()))
        P.barrier()
        for b in range(8):
            for sl in range(L // 512):
                pb = P.psum[sl % 2]
                P.tv.wait(t)
                mm_n = P.tt.inc(t.matmul(pb.ap(), rdt_sb.ap()[:, b * 128:(b + 1) * 128],
                                         dt_sb.ap()[:, 128 + sl * 512:128 + (sl + 1) * 512],
                                         start=True, stop=True))
                P.tt.wait(v, mm_n)
                P.tv.inc(v.tensor_tensor(angb.ap()[:, sl * 512:(sl + 1) * 512],
                                         ang4.ap()[:, sl * 512:(sl + 1) * 512],
                                         pb.ap(), ALU.mult))
            v.drain()
            v.tensor_tensor_scan(thb.ap(), ones128.ap(), angb.ap(), 0.0, ALU.mult, ALU.add)
            v.drain()
            for trig in range(2):
                off = PI / 2 if trig == 0 else 0.0
                c1 = off + 2 * PI * 64 - PI
                v.tensor_scalar(out=wq.ap(), in0=thb.ap(), scalar1=c1,
                                scalar2=1.0 / (2 * PI), op0=ALU.add, op1=ALU.mult)
                v.drain()
                v.tensor_copy(wqi.ap(), wq.ap())
                v.drain()
                v.scalar_tensor_tensor(out=wq.ap(), in0=wqi.ap(), scalar=-1.0,
                                       in1=wq.ap(), op0=ALU.mult, op1=ALU.add)
                v.drain()
                v.tensor_scalar(out=wq.ap(), in0=wq.ap(), scalar1=-0.499999,
                                scalar2=0.499999, op0=ALU.max, op1=ALU.min)
                rr_n = P.tv.inc(v.drain())
                P.tv.wait(s, rr_n)
                tn = P.ts.inc(s.activation(angb.ap(), wq.ap(), AF.Sin, scale=2 * PI))
                P.ts.wait(nc.sync, tn)
                dd = c_d if trig == 0 else s_d
                P.dma("sync", dd.ap()[b * 128:(b + 1) * 128, :], angb.ap())
                P.qwait(v, "sync")
        P.barrier()
    st_dt.close()

    def _bn_sb(name, shape, dtype=F32):
        return st_bn.enter_context(nc.sbuf_tensor(P.uname(name), list(shape), dtype))
    Bn = _bn_sb("Bn", [128, L])
    Cn = _bn_sb("Cn", [128, L])
    B12n = _bn_sb("B12n", [64, L])
    Bswn = _bn_sb("Bswn", [128, L])
    Cswn = _bn_sb("Cswn", [128, L])
    bb_sb = _bn_sb("bb_sb", [128, 32])
    cb_sb = _bn_sb("cb_sb", [128, 32])
    bbn_sb = _bn_sb("bbn_sb", [128, 32])
    cbn_sb = _bn_sb("cbn_sb", [128, 32])
    bbsw_sb = _bn_sb("bbsw_sb", [64, 32])
    cbsw_sb = _bn_sb("cbsw_sb", [64, 32])
    with contextlib.ExitStack() as stg:
        def ssb(name, shape, dtype=F32):
            return stg.enter_context(nc.sbuf_tensor(P.uname(name), list(shape), dtype))
        Bg = ssb("Bg", [128, L]); sq = ssb("sq", [128, L])
        nw_sb = ssb("nw_sb", [128, 2])
        rrow = ssb("rrow", [1, L])
        onescol = ssb("onescol", [128, 1])
        g.memset(onescol.ap(), 1.0)
        eps_b = eps_c3
        P.dma("sync", Bg.ap(), projfm.ap()[4096:4224, :])
        P.dma("sync", nw_sb.ap()[:, 0:1], b_norm_w.ap().unsqueeze(1))
        P.dma("sync", nw_sb.ap()[:, 1:2], c_norm_w.ap().unsqueeze(1))
        P.dma("sync", bb_sb.ap(), b_biasT.ap())
        P.dma("sync", cb_sb.ap(), c_biasT.ap())
        P.barrier()
        for which in range(2):
            dst = Bn if which == 0 else Cn
            sq_n = P.ts.inc(s.activation(sq.ap(), Bg.ap(), AF.Square))
            P.ts.wait(t, sq_n)
            for sl in range(L // 512):
                pb = P.psum[sl % 2]
                P.ts.wait(t)
                P.tv.wait(t)
                mm_n = P.tt.inc(t.matmul(pb.ap()[:1, :], onescol.ap(),
                                         sq.ap()[:, sl * 512:(sl + 1) * 512],
                                         start=True, stop=True))
                P.tt.wait(s, mm_n)
                sd_n = P.ts.inc(s.activation(rrow.ap()[:, sl * 512:(sl + 1) * 512],
                                             pb.ap()[:1, :], AF.Sqrt,
                                             scale=1.0 / 128, bias=eps_b[0:1, :]))
                P.ts.wait(v, sd_n)
                v.reciprocal(rrow.ap()[:, sl * 512:(sl + 1) * 512],
                             rrow.ap()[:, sl * 512:(sl + 1) * 512])
                rc_n = P.tv.inc(v.drain())
                P.tv.wait(t, rc_n)
                mm2_n = P.tt.inc(t.matmul(pb.ap()[:, :], ones_sb.ap(),
                                          rrow.ap()[:, sl * 512:(sl + 1) * 512],
                                          start=True, stop=True))
                P.tt.wait(v, mm2_n)
                P.tv.inc(v.scalar_tensor_tensor(
                    out=dst.ap()[:, sl * 512:(sl + 1) * 512],
                    in0=Bg.ap()[:, sl * 512:(sl + 1) * 512],
                    scalar=nw_sb.ap()[:, which:which + 1],
                    in1=pb.ap()[:, :], op0=ALU.mult, op1=ALU.mult))
                P.tv.inc(v.drain())
            P.barrier()
            if which == 0:
                P.dma("sync", Bg.ap(), projfm.ap()[4224:4352, :])
                P.barrier()
        v.tensor_scalar(out=B12n.ap(), in0=Bn.ap()[0:64, :], scalar1=-1.0,
                        scalar2=None, op0=ALU.mult)
        v.tensor_copy(Bswn.ap()[0:32, :], Bn.ap()[32:64, :])
        v.tensor_scalar(out=Bswn.ap()[32:64, :], in0=Bn.ap()[0:32, :], scalar1=-1.0,
                        scalar2=None, op0=ALU.mult)
        v.tensor_copy(Cswn.ap()[0:32, :], Cn.ap()[32:64, :])
        v.tensor_scalar(out=Cswn.ap()[32:64, :], in0=Cn.ap()[0:32, :], scalar1=-1.0,
                        scalar2=None, op0=ALU.mult)
        v.drain()
        v.tensor_scalar(out=Bn.ap()[0:64, :], in0=Cn.ap()[0:64, :], scalar1=-1.0,
                        scalar2=None, op0=ALU.mult)
        v.tensor_scalar(out=bbn_sb.ap(), in0=bb_sb.ap(), scalar1=-1.0, scalar2=None, op0=ALU.mult)
        v.tensor_scalar(out=cbn_sb.ap(), in0=cb_sb.ap(), scalar1=-1.0, scalar2=None, op0=ALU.mult)
        v.tensor_copy(bbsw_sb.ap()[0:32, :], bb_sb.ap()[32:64, :])
        v.tensor_scalar(out=bbsw_sb.ap()[32:64, :], in0=bb_sb.ap()[0:32, :], scalar1=-1.0,
                        scalar2=None, op0=ALU.mult)
        v.tensor_copy(cbsw_sb.ap()[0:32, :], cb_sb.ap()[32:64, :])
        v.tensor_scalar(out=cbsw_sb.ap()[32:64, :], in0=cb_sb.ap()[0:32, :], scalar1=-1.0,
                        scalar2=None, op0=ALU.mult)
        v.drain()
        P.barrier()

    with contextlib.ExitStack() as stg:
        def ssb(name, shape, dtype=F32):
            return stg.enter_context(nc.sbuf_tensor(P.uname(name), list(shape), dtype))
        cdup = ssb("cdup", [64, L]); sdup = ssb("sdup", [64, L])
        Kf = ssb("Kf", [128, L]); Qf = ssb("Qf", [128, L])
        lrow = [ssb("lrow", [1, 512]) for _ in range(2)]
        qe_bf = ssb("qe_bf", [128, L], BF16)
        zcol_sb = ssb("zcol", [128, 1], BF16)
        P.tg.inc(g.memset(zcol_sb.ap(), 0.0))
        P.tg.wait(nc.scalar)
        cons_v = 0
        st_gp = 0
        st_sync = 0
        for h in range(NH):
            P.tv.wait(nc.sync, cons_v)
            P.dma("sync", cdup.ap()[0:32, :], c_d.ap()[h * 32:(h + 1) * 32, :])
            P.dma("sync", cdup.ap()[32:64, :], c_d.ap()[h * 32:(h + 1) * 32, :])
            P.dma("sync", sdup.ap()[0:32, :], s_d.ap()[h * 32:(h + 1) * 32, :])
            P.dma("sync", sdup.ap()[32:64, :], s_d.ap()[h * 32:(h + 1) * 32, :])

            P.qwait(v, "sync")
            P.qwait(v, "st")
            P.qwait(v, "gp")
            v.scalar_tensor_tensor(out=Kf.ap()[0:64, :], in0=B12n.ap(),
                                   scalar=bbn_sb.ap()[0:64, h:h + 1], in1=cdup.ap(),
                                   op0=ALU.add, op1=ALU.mult)
            v.scalar_tensor_tensor(out=Qf.ap()[0:64, :], in0=Bswn.ap()[0:64, :],
                                   scalar=bbsw_sb.ap()[:, h:h + 1], in1=sdup.ap(),
                                   op0=ALU.add, op1=ALU.mult)
            v.drain()
            v.tensor_tensor(Kf.ap()[0:64, :], Kf.ap()[0:64, :], Qf.ap()[0:64, :], ALU.add)
            v.tensor_scalar(out=Kf.ap()[64:128, :], in0=Bn.ap()[64:128, :],
                            scalar1=bb_sb.ap()[64:128, h:h + 1], scalar2=None, op0=ALU.add)
            k_n = P.tv.inc(v.drain())
            P.tv.wait(g, k_n)
            kst = P.dma("gp", K_d.ap()[:, h:h + 1, 1:L + 1], Kf.ap().unsqueeze(1))
            nc.scalar.dma_start(out=K_d.ap()[:, h:h + 1, 0:1], in_=zcol_sb.ap().unsqueeze(1),
                                allow_slow_non_contiguous=True).then_inc(P.q["act"][1], 16)
            P.q["act"][2] += 1
            v.scalar_tensor_tensor(out=Qf.ap()[0:64, :], in0=Bn.ap()[0:64, :],
                                   scalar=cbn_sb.ap()[0:64, h:h + 1], in1=cdup.ap(),
                                   op0=ALU.add, op1=ALU.mult)
            v.drain()
            v.scalar_tensor_tensor(out=cdup.ap(), in0=Cswn.ap()[0:64, :],
                                   scalar=cbsw_sb.ap()[:, h:h + 1], in1=sdup.ap(),
                                   op0=ALU.add, op1=ALU.mult)
            v.drain()
            v.tensor_tensor(Qf.ap()[0:64, :], Qf.ap()[0:64, :], cdup.ap(), ALU.add)
            v.tensor_scalar(out=Qf.ap()[64:128, :], in0=Cn.ap()[64:128, :],
                            scalar1=cb_sb.ap()[64:128, h:h + 1], scalar2=None, op0=ALU.add)
            q_n = P.tv.inc(v.drain())
            P.tv.wait(g, q_n)
            st_gp = P.dma("gp", Q_d.ap()[:, h:h + 1, :], Qf.ap().unsqueeze(1))
            for sl in range(L // 512):
                pb = P.psum[sl % 2]
                P.tt.wait(nc.sync)
                P.dma(f"ld{sl % 2}", lrow[sl % 2].ap(),
                      ell_d.ap()[h:h + 1, sl * 512:(sl + 1) * 512])
                P.qwait(t, f"ld{sl % 2}")
                P.tv.wait(t)
                mm_n = P.tt.inc(t.matmul(pb.ap(), ones_sb.ap(), lrow[sl % 2].ap(),
                                         start=True, stop=True))
                P.tt.wait(s, mm_n)
                ex_n = P.ts.inc(s.activation(pb.ap(), pb.ap(), AF.Exp))
                P.ts.wait(v, ex_n)
                P.tv.inc(v.tensor_tensor(qe_bf.ap()[:, sl * 512:(sl + 1) * 512],
                                         Qf.ap()[:, sl * 512:(sl + 1) * 512],
                                         pb.ap(), ALU.mult))
            qe_n = P.tv.n
            P.tv.wait(nc.sync, qe_n)
            st_sync = P.dma("st", Qe_d.ap()[:, h:h + 1, :], qe_bf.ap().unsqueeze(1))
            cons_v = qe_n
        P.barrier()
    st_bn.close()

    oh_sb = P.sb("oh_sb", [32, 32 * 128], F32)
    msk_sb = P.sb("msk_sb", [128, 128], F32)
    P.dma("sync", oh_sb.ap(), ohc.ap())
    P.dma("sync", msk_sb.ap(), mskc.ap())
    H = P.sb("H", [128, NH * HD], F32)
    Hb = P.sb("Hb", [128, NH * HD], BF16)
    g.memset(H.ap(), 0.0)
    g.memset(Hb.ap(), 0.0)
    Esb = P.sb("Esb", [128, 128], F32)
    Dsb = P.sb("Dsb", [128, 128], F32)
    Ssb1 = P.sb("Ssb1", [128, 128], BF16)
    Ssb2 = P.sb("Ssb2", [128, 128], BF16)
    Ktsb = P.sb("Ktsb", [128, 128], BF16)
    Kptsb = P.sb("Kptsb", [128, 128], BF16)
    xw1 = P.sb("xw1", [128, HD], BF16)
    xw2 = P.sb("xw2", [128, HD], BF16)
    yg1 = P.sb("yg1", [64, 128], F32)
    zsil = P.sb("zsil", [64, 128], F32)
    Kc = [P.sb("Kc", [128, NH, 128], BF16) for _ in range(2)]
    Kpc = [P.sb("Kpc", [128, NH, 128], BF16) for _ in range(2)]
    Qc = [P.sb("Qc", [128, NH, 128], BF16) for _ in range(2)]
    Qec = [P.sb("Qec", [128, NH, 128], BF16) for _ in range(2)]
    xsc = [P.sb("xsc", [128, 2048], BF16) for _ in range(2)]
    xpc = [P.sb("xpc", [128, 2048], BF16) for _ in range(2)]
    zc = [P.sb("zc", [128, 16, 128], BF16) for _ in range(2)]
    xfc = [P.sb("xfc", [128, 16, 128], BF16) for _ in range(2)]
    ygc = [P.sb("ygc", [128, 16, 128], BF16) for _ in range(2)]
    g.memset(xpc[0].ap()[0:1, :], 0.0)
    P.barrier()

    buf_done = [0, 0]
    yg_done = [0, 0]
    bank_free = {0: 0, 1: 0, 2: 0, 3: 0, 4: 0}
    bankbf_free = [0, 0]
    e_free = 0
    d_free = 0
    s_free = 0
    kt_free = 0
    xw_free = 0
    hb_last = 0
    yg1_free = 0
    zs_free = 0

    projz = projfm.ap().rearrange("(rc p) l -> p rc l", p=128)
    DBG2 = "dbg2_d" in dbg
    if DBG2:
        dbg2_d = P.dram("dbg2_d", [128, 16 * 128], F32, kind="ExternalOutput")
        dbgslot = [0]
        dbg_sb = P.sb("dbg_sb", [128, 16 * 128], F32)

    def dbg_cap(src_ap, rows=128, cols=128):
        if not DBG2 or dbgslot[0] >= 16:
            return
        sl = dbgslot[0]; dbgslot[0] += 1
        cp_n = P.tv.inc(v.tensor_copy(dbg_sb.ap()[:rows, sl * 128:sl * 128 + cols], src_ap))
        P.tv.wait(nc.sync, cp_n)
        P.dma("sync", dbg2_d.ap()[:rows, sl * 128:sl * 128 + cols],
              dbg_sb.ap()[:rows, sl * 128:sl * 128 + cols])

    def load_chunk(c, bi):
        ch_ld = f"ld{bi}"
        for hs in range(0, NH, 8):
            P.dma(ch_ld, Kc[bi].ap()[:, hs:hs + 8, :],
                  K_d.ap()[:, hs:hs + 8, 1 + c * CH:1 + (c + 1) * CH])
            P.dma(ch_ld, Qc[bi].ap()[:, hs:hs + 8, :],
                  Q_d.ap()[:, hs:hs + 8, c * CH:(c + 1) * CH])
            P.dma(ch_ld, Qec[bi].ap()[:, hs:hs + 8, :],
                  Qe_d.ap()[:, hs:hs + 8, c * CH:(c + 1) * CH])
            P.dma(ch_ld, Kpc[bi].ap()[:, hs:hs + 8, :],
                  K_d.ap()[:, hs:hs + 8, c * CH:(c + 1) * CH])
        if c == 0:
            P.dma(ch_ld, xpc[bi].ap()[1:128, :], xs_tok.ap()[0:CH - 1, :])
        else:
            P.dma(ch_ld, xpc[bi].ap(), xs_tok.ap()[c * CH - 1:(c + 1) * CH - 1, :])
        P.dma(ch_ld, xsc[bi].ap(), xs_tok.ap()[c * CH:(c + 1) * CH, :])
        P.dma(f"gp{bi}", zc[bi].ap(), projz[:, 0:16, c * CH:(c + 1) * CH])
        P.dma(f"gp{bi}", xfc[bi].ap(), projz[:, 16:32, c * CH:(c + 1) * CH])

    load_chunk(0, 0)
    for c in range(NCH):
        bi = c % 2
        if c + 1 < NCH:
            P.tt.wait(nc.sync, buf_done[1 - bi])
            P.tt.wait(g, buf_done[1 - bi])
            P.tv.wait(nc.sync, yg_done[1 - bi])
            P.tv.wait(g, yg_done[1 - bi])
            load_chunk(c + 1, 1 - bi)
        if serial == 2:
            P.barrier()
        for h in range(NH):
            if serial == 1:
                P.barrier()
            col = c * 32 + h
            hc, off = h // 2, (h % 2) * 64
            P.qwait(t, f"ld{bi}")
            P.tv.wait(t, max(bank_free[0], bank_free[1], bank_free[2]))
            t.matmul(P.psum[0].ap()[:, 0:128], Kc[bi].ap()[:, h, :], Qc[bi].ap()[:, h, :],
                     start=True, stop=True)
            t.matmul(P.psum[1].ap()[:, 0:128], Kpc[bi].ap()[:, h, :], Qc[bi].ap()[:, h, :],
                     start=True, stop=True)
            a_n = P.tt.inc(t.matmul(P.psum[2].ap()[:, 0:128],
                                    oh_sb.ap()[:, h * 128:(h + 1) * 128],
                                    ell_sb.ap()[:, c * CH:(c + 1) * CH],
                                    start=True, stop=True))
            P.tt.wait(v, a_n)
            P.ts.wait(v, e_free)
            e_n = P.tv.inc(v.scalar_tensor_tensor(
                out=Esb.ap(), in0=P.psum[2].ap()[:, 0:128], scalar=elltok.ap()[:, col:col + 1],
                op0=ALU.subtract, in1=msk_sb.ap(), op1=ALU.add))

            P.tv.wait(s, max(e_n, d_free))
            d_n = P.ts.inc(s.activation(Dsb.ap(), Esb.ap(), AF.Exp))
            if (c, h) == (0, 0):
                P.ts.wait(v, d_n)
                dbg_cap(Dsb.ap())
            P.ts.wait(t, max(bankbf_free[0], bankbf_free[1]))
            t.transpose(P.psum_bf[0].ap()[:, 0:128], Kc[bi].ap()[:, h, :], idb_sb.ap())
            tr_n = P.tt.inc(t.transpose(P.psum_bf[1].ap()[:, 0:128], Kpc[bi].ap()[:, h, :],
                                        idb_sb.ap()))
            P.tt.wait(s, max(tr_n, kt_free))
            s.copy(Ktsb.ap(), P.psum_bf[0].ap()[:, 0:128])
            kte_n = P.ts.inc(s.copy(Kptsb.ap(), P.psum_bf[1].ap()[:, 0:128]))
            e_free = kte_n
            P.ts.wait(v, d_n)
            P.tt.wait(v, s_free)
            v.scalar_tensor_tensor(out=Ssb1.ap(), in0=Dsb.ap(), scalar=g1tok.ap()[:, col:col + 1],
                                   in1=P.psum[0].ap()[:, 0:128], op0=ALU.mult, op1=ALU.mult)
            ss_n = P.tv.inc(v.scalar_tensor_tensor(
                out=Ssb2.ap(), in0=Dsb.ap(), scalar=g2tok.ap()[:, col:col + 1],
                in1=P.psum[1].ap()[:, 0:128], op0=ALU.mult, op1=ALU.mult))
            bank_free[2] = ss_n
            P.qwait(v, f"ld{bi}")
            P.tt.wait(v, xw_free)
            v.tensor_scalar(out=xw1.ap(), in0=xsc[bi].ap()[:, h * HD:(h + 1) * HD],
                            scalar1=w1full.ap()[:, col:col + 1], scalar2=None, op0=ALU.mult)
            xw_n = P.tv.inc(v.tensor_scalar(
                out=xw2.ap(), in0=xpc[bi].ap()[:, h * HD:(h + 1) * HD],
                scalar1=w2full.ap()[:, col:col + 1], scalar2=None, op0=ALU.mult))
            bank_free[0] = xw_n
            bank_free[1] = xw_n
            d_free = xw_n
            P.ts.wait(t, kte_n)
            P.tv.wait(t, max(xw_n, bank_free[3]))
            t.matmul(P.psum[3].ap()[:, 0:HD], Ktsb.ap(), xw1.ap(), start=True, stop=False)
            ws_n = P.tt.inc(t.matmul(P.psum[3].ap()[:, 0:HD], Kptsb.ap(), xw2.ap(),
                                     start=False, stop=True))
            kt_free = ws_n
            xw_free = ws_n
            P.tv.wait(t, max(ss_n, bank_free[4], hb_last))
            t.matmul(P.psum[4].ap()[:HD, 0:128], xsc[bi].ap()[:, h * HD:(h + 1) * HD],
                     Ssb1.ap(), start=True, stop=False)
            t.matmul(P.psum[4].ap()[:HD, 0:128], xpc[bi].ap()[:, h * HD:(h + 1) * HD],
                     Ssb2.ap(), start=False, stop=False)
            y_n = P.tt.inc(t.matmul(P.psum[4].ap()[:HD, 0:128], Hb.ap()[:, h * HD:(h + 1) * HD],
                                    Qec[bi].ap()[:, h, :], start=False, stop=True))
            s_free = y_n
            if (c, h) == (0, 0):
                P.tt.wait(v, y_n)
                dbg_cap(P.psum[4].ap()[:HD, 0:128], HD, 128)
                dbg_cap(Ssb1.ap())
                dbg_cap(Ssb2.ap())
                dbg_cap(Qec[bi].ap()[:, 0, :])
                dbg_cap(xsc[bi].ap()[:, 0:128])
            P.tt.wait(v, y_n)
            v.scalar_tensor_tensor(out=Hb.ap()[:, h * HD:(h + 1) * HD],
                                   in0=H.ap()[:, h * HD:(h + 1) * HD],
                                   scalar=cdb.ap()[:, col:col + 1],
                                   in1=P.psum[3].ap()[:, 0:HD], op0=ALU.mult, op1=ALU.add)
            v.drain()
            hb_last = P.tv.n
            hf_n = P.tv.inc(v.scalar_tensor_tensor(
                out=H.ap()[:, h * HD:(h + 1) * HD], in0=H.ap()[:, h * HD:(h + 1) * HD],
                scalar=cdb.ap()[:, col:col + 1], in1=P.psum[3].ap()[:, 0:HD],
                op0=ALU.mult, op1=ALU.add))
            P.qwait(s, f"gp{bi}")
            P.tv.wait(s, zs_free)
            zs_n = P.ts.inc(s.activation(zsil.ap(), zc[bi].ap()[off:off + 64, hc, :], AF.Silu))
            bankbf_free[0] = zs_n
            bankbf_free[1] = zs_n
            P.tt.wait(v, y_n)
            P.qwait(v, f"gp{bi}")
            yg1_n = P.tv.inc(v.scalar_tensor_tensor(
                out=yg1.ap(), in0=xfc[bi].ap()[off:off + 64, hc, :],
                scalar=drep_sb.ap()[off:off + 64, hc:hc + 1],
                in1=P.psum[4].ap()[:HD, 0:128], op0=ALU.mult, op1=ALU.add))
            bank_free[3] = yg1_n
            v.drain()
            P.ts.wait(v, zs_n)
            P.qwait(v, f"st{bi}")
            ygtt_n = P.tv.inc(v.tensor_tensor(ygc[bi].ap()[off:off + 64, hc, :],
                                              yg1.ap(), zsil.ap(), ALU.mult))
            bank_free[4] = ygtt_n
            yg1_free = ygtt_n
            zs_free = P.tv.inc(v.drain())
        buf_done[bi] = P.tt.n
        yg_done[bi] = P.tv.n

        P.tv.wait(nc.sync, yg_done[bi])
        P.dma(f"st{bi}", ygfm.ap().rearrange("(rc p) l -> p rc l", p=128)[:, :, c * CH:(c + 1) * CH],
              ygc[bi].ap())
    P.barrier()


def build_p4():
    P = Prog()
    ygfm = P.dram("ygfm", [2048, L], BF16, kind="ExternalInput")
    w_out = P.dram("w_out", [2048, DM], F32, kind="ExternalInput")
    sfm = P.dram("sfm", [DM, L], F32, kind="ExternalOutput")
    fm_matmul(P, ygfm, w_out, sfm, K=2048, N=DM)
    return P


class _View:
    def __init__(self, ap, name):
        self._ap = ap
        self.name = name

    def ap(self):
        return self._ap


def fm_layernorm(P, X, out, lg=None, lb=None, eps=1e-5, Lw=None, R=DM, out_dtype=BF16):
    Lw = L if Lw is None else Lw
    nc = P.nc
    v, s, t = nc.vector, nc.scalar, nc.tensor
    kcs = R // 128
    eps_c = P.fconst(eps)
    ones_row(P)
    _stg = contextlib.ExitStack()
    def ssb(name, shape, dtype=F32):
        return _stg.enter_context(nc.sbuf_tensor(P.uname(name), list(shape), dtype))
    onescol = ssb("ln_one", [128, 1])
    P.tg.inc(nc.gpsimd.memset(onescol.ap(), 1.0))
    gcol = lcol = None
    if lg is not None:
        gcol = ssb("ln_g", [128, kcs])
        for i in range(kcs):
            P.dma("sync", gcol.ap()[:, i:i + 1], lg.ap()[i * 128:(i + 1) * 128].unsqueeze(1))
    if lb is not None:
        lcol = ssb("ln_b", [128, kcs])
        for i in range(kcs):
            P.dma("sync", lcol.ap()[:, i:i + 1], lb.ap()[i * 128:(i + 1) * 128].unsqueeze(1))
    xsb = [ssb("ln_x", [128, kcs, 512]) for _ in range(2)]
    sq = ssb("ln_sq", [128, kcs, 512])
    row = ssb("ln_row", [1, 512])
    row2 = ssb("ln_row2", [1, 512])
    usb = [ssb("ln_u", [128, 512]) for _ in range(2)]
    osb = [ssb("ln_o", [128, 512], out_dtype) for _ in range(4)]
    xa = X.ap().rearrange("(kc p) l -> p kc l", p=128)
    P.barrier()
    done = [0, 0]
    for tt in range(Lw // 512):
        bb = tt % 2
        gp = nc.gpsimd
        P.tv.wait(gp, done[bb])
        P.dma(f"gp{bb}", xsb[bb].ap(), xa[:, :, tt * 512:(tt + 1) * 512])
        P.qwait(t, f"gp{bb}")
        P.ts.wait(t)
        P.tv.wait(t)
        for kc in range(kcs):
            i1 = t.matmul(P.psum[0].ap()[:1, :], onescol.ap(), xsb[bb].ap()[:, kc, :],
                          start=(kc == 0), stop=(kc == kcs - 1))
        mm1 = P.tt.inc(i1)
        P.qwait(s, f"gp{bb}")
        P.tt.wait(s)
        sq_n = P.ts.inc(s.activation(sq.ap(), xsb[bb].ap(), AF.Square))
        P.ts.wait(t, sq_n)
        for kc in range(kcs):
            i2 = t.matmul(P.psum[1].ap()[:1, :], onescol.ap(), sq.ap()[:, kc, :],
                          start=(kc == 0), stop=(kc == kcs - 1))
        mm2 = P.tt.inc(i2)
        P.tt.wait(v, mm2)
        v.tensor_scalar(out=row.ap(), in0=P.psum[0].ap()[:1, :], scalar1=1.0 / R,
                        scalar2=None, op0=ALU.mult)
        v.drain()
        v.tensor_tensor(row2.ap(), row.ap(), row.ap(), ALU.mult)
        v.drain()
        v.scalar_tensor_tensor(out=row2.ap(), in0=P.psum[1].ap()[:1, :], scalar=1.0 / R,
                               in1=row2.ap(), op0=ALU.mult, op1=ALU.subtract)
        rv_n = P.tv.inc(v.drain())
        P.tv.wait(s, rv_n)
        sd_n = P.ts.inc(s.activation(row2.ap(), row2.ap(), AF.Sqrt, bias=eps_c[0:1, :]))
        P.ts.wait(v, sd_n)
        v.reciprocal(row2.ap(), row2.ap())
        v.drain()
        v.scalar_tensor_tensor(out=row.ap(), in0=row.ap(), scalar=-1.0, in1=row2.ap(),
                               op0=ALU.mult, op1=ALU.mult)
        nm_n = P.tv.inc(v.drain())
        P.tv.wait(t, nm_n)
        t.matmul(P.psum[2].ap(), ones_row(P).ap(), row2.ap(), start=True, stop=True)
        b2 = t.matmul(P.psum[3].ap(), ones_row(P).ap(), row.ap(), start=True, stop=True)
        bc_n = P.tt.inc(b2)
        P.tt.wait(v, bc_n)
        for kc in range(kcs):
            j = kc % 4
            P.qwait(v, f"st{j}")
            v.drain()
            v.scalar_tensor_tensor(out=usb[0].ap(), in0=xsb[bb].ap()[:, kc, :],
                                   scalar=(gcol.ap()[:, kc:kc + 1] if lg is not None else 1.0),
                                   in1=P.psum[2].ap(), op0=ALU.mult, op1=ALU.mult)
            if lb is not None:
                v.scalar_tensor_tensor(out=usb[1].ap(), in0=P.psum[3].ap(),
                                       scalar=(gcol.ap()[:, kc:kc + 1] if lg is not None else 1.0),
                                       in1=lcol.ap()[:, kc:kc + 1].broadcast_to([128, 512]),
                                       op0=ALU.mult, op1=ALU.add)
            elif lg is not None:
                v.tensor_scalar(out=usb[1].ap(), in0=P.psum[3].ap(),
                                scalar1=gcol.ap()[:, kc:kc + 1], scalar2=None, op0=ALU.mult)
            else:
                v.tensor_copy(usb[1].ap(), P.psum[3].ap())
            v.drain()
            ev = v.tensor_tensor(osb[j].ap(), usb[0].ap(), usb[1].ap(), ALU.add)
            ev_n = P.tv.inc(ev)
            P.tv.wait(nc.sync, ev_n)
            P.dma(f"st{j}", out.ap()[kc * 128:(kc + 1) * 128, tt * 512:(tt + 1) * 512],
                  osb[j].ap())
        done[bb] = P.tv.n
    P.barrier()
    _stg.close()


def ones_row(P):
    if not hasattr(P, "_ones_row"):
        t = P.nc.alloc_sbuf_tensor(P.uname("ones_row"), [1, 128], F32)
        P.tg.inc(P.nc.gpsimd.memset(t.ap(), 1.0))
        P._ones_row = t
    return P._ones_row


def ew_add(v, outs, ins):
    return v.tensor_tensor(outs[0], ins[0], ins[1], ALU.add)


def ew_mult(v, outs, ins):
    return v.tensor_tensor(outs[0], ins[0], ins[1], ALU.mult)


def shift_fm(P, X, out, half=512):
    R = 2 * half
    zc = P.const("zcol_h", np.zeros((half, 1), dtype=np.float32))
    P.dma("gp", out.ap()[0:half, :], X.ap()[0:half, :])
    P.dma("gp", out.ap()[half:R, 1:L], X.ap()[half:R, 0:L - 1])
    P.nc.gpsimd.dma_start(out=out.ap()[half:R, 0:1], in_=zc.ap(),
                          allow_slow_non_contiguous=True).then_inc(P.q["gp"][1], 16)
    P.q["gp"][2] += 1
    P.barrier()


def _rope_tables():
    DH = 64
    hi = np.arange(0, DH, 2, dtype=np.float64)
    inv = 1.0 / 10000.0 ** (hi / DH)
    tt = np.arange(L, dtype=np.float64)
    fr = np.concatenate([tt[:, None] * inv, tt[:, None] * inv], axis=-1)
    base = (hi + 0.4 * DH) / (1.4 * DH)
    power = (tt - L // 2) / 256.0
    sc = base[None, :] ** power[:, None]
    sc = np.concatenate([sc, sc], axis=-1)
    Cq = (np.cos(fr) * sc).T.astype(np.float32)
    Sq = (np.sin(fr) * sc).T.astype(np.float32)
    Ck = (np.cos(fr) / sc).T.astype(np.float32)
    Sk = (np.sin(fr) / sc).T.astype(np.float32)
    def two(x):
        return np.concatenate([x, x], axis=0)
    return two(Cq), two(Sq), two(Ck), two(Sk)


def rope_pass(P, qk_d, out_d, qs_col, ks_col):
    nc = P.nc
    v, s, t = nc.vector, nc.scalar, nc.tensor
    Cq, Sq, Ck, Sk = _rope_tables()
    rr = np.zeros((128, 128), dtype=np.float32)
    for blk in range(2):
        for d in range(32):
            rr[blk * 64 + 32 + d, blk * 64 + d] = -1.0
            rr[blk * 64 + d, blk * 64 + 32 + d] = 1.0
    ind2 = np.zeros((128, 2), dtype=np.float32)
    ind2[0:64, 0] = 1.0
    ind2[64:128, 1] = 1.0
    indT = np.zeros((2, 128), dtype=np.float32)
    indT[0, 0:64] = 1.0
    indT[1, 64:128] = 1.0
    _stg = contextlib.ExitStack()
    def ssb(name, shape, dtype=F32):
        return _stg.enter_context(nc.sbuf_tensor(P.uname(name), list(shape), dtype))
    tabs = []
    for nm, arr in [("Cq", Cq), ("Sq", Sq), ("Ck", Ck), ("Sk", Sk)]:
        tsb = ssb("tab" + nm, [128, L])
        P.dma("sync", tsb.ap(), P.const("rope" + nm, arr).ap())
        tabs.append(tsb)
    rr_sb = ssb("rr_sb", [128, 128], BF16)
    P.dma("gp", rr_sb.ap(), P.const("ropeRR", rr).ap())
    ind2_sb = ssb("ind2", [128, 2])
    P.dma("sync", ind2_sb.ap(), P.const("ropeI2", ind2).ap())
    indT_sb = ssb("indT", [2, 128])
    P.dma("sync", indT_sb.ap(), P.const("ropeIT", indT).ap())
    sc_sb = ssb("sc_sb", [128, 2])
    P.dma("sync", sc_sb.ap()[:, 0:1], qs_col.ap())
    P.dma("sync", sc_sb.ap()[:, 1:2], ks_col.ap())
    q_bf = [ssb("rp_qbf", [128, 512], BF16) for _ in range(2)]
    q_f = ssb("rp_qf", [128, 512])
    sq = ssb("rp_sq", [128, 512])
    nrm = ssb("rp_nrm", [2, 512])
    qhs = ssb("rp_qhs", [128, 512])
    qhs_bf = ssb("rp_qhsb", [128, 512], BF16)
    t1 = ssb("rp_t1", [128, 512])
    t2 = ssb("rp_t2", [128, 512])
    osb = [ssb("rp_o", [128, 512], BF16) for _ in range(2)]
    P.barrier()
    done = [0, 0]
    for qk in range(2):
        tabC, tabS = tabs[2 * qk], tabs[2 * qk + 1]
        for rc in range(4):
            rows = qk * 512 + rc * 128
            for tt in range(L // 512):
                bb = tt % 2
                sl = slice(tt * 512, (tt + 1) * 512)
                gp = nc.sync
                P.ts.wait(gp, done[bb])
                P.dma(f"ld{bb}", q_bf[bb].ap(), qk_d.ap()[rows:rows + 128, sl])
                P.qwait(s, f"ld{bb}")
                P.tt.wait(s)
                cp_n = P.ts.inc(s.copy(q_f.ap(), q_bf[bb].ap()))
                sq_n = P.ts.inc(s.activation(sq.ap(), q_bf[bb].ap(), AF.Square))
                P.ts.wait(t, sq_n)
                P.tv.wait(t)
                mm1 = P.tt.inc(t.matmul(P.psum[0].ap()[:2, :], ind2_sb.ap(), sq.ap(),
                                        start=True, stop=True))
                P.tt.wait(s, mm1)
                P.tv.wait(s)
                nr_n = P.ts.inc(s.activation(nrm.ap(), P.psum[0].ap()[:2, :], AF.Sqrt))
                P.ts.wait(v, nr_n)
                v.tensor_scalar(out=nrm.ap(), in0=nrm.ap(), scalar1=1e-12, scalar2=None,
                                op0=ALU.max)
                v.drain()
                v.reciprocal(nrm.ap(), nrm.ap())
                ni_n = P.tv.inc(v.drain())
                P.tv.wait(t, ni_n)
                mm2 = P.tt.inc(t.matmul(P.psum[1].ap(), indT_sb.ap(), nrm.ap(),
                                        start=True, stop=True))
                P.tt.wait(v, mm2)
                P.ts.wait(v, cp_n)
                v.scalar_tensor_tensor(out=qhs.ap(), in0=q_f.ap(),
                                       scalar=sc_sb.ap()[:, qk:qk + 1],
                                       in1=P.psum[1].ap(), op0=ALU.mult, op1=ALU.mult)
                v.drain()
                qb_n = P.tv.inc(v.tensor_copy(qhs_bf.ap(), qhs.ap()))
                P.tv.wait(t, qb_n)
                mm3 = P.tt.inc(t.matmul(P.psum[2].ap(), rr_sb.ap(), qhs_bf.ap(),
                                        start=True, stop=True))
                v.tensor_tensor(t1.ap(), qhs.ap(), tabC.ap()[:, sl], ALU.mult)
                P.tt.wait(v, mm3)
                v.scalar_tensor_tensor(out=t2.ap(), in0=tabS.ap()[:, sl], scalar=1.0,
                                       in1=P.psum[2].ap(), op0=ALU.mult, op1=ALU.mult)
                v.drain()
                P.qwait(v, f"st{bb}")
                ev_n = P.tv.inc(v.tensor_tensor(osb[bb].ap(), t1.ap(), t2.ap(), ALU.add))
                P.tv.wait(nc.sync, ev_n)
                P.dma(f"st{bb}", out_d.ap()[rows:rows + 128, sl], osb[bb].ap())
                done[bb] = P.ts.n
    P.barrier()
    _stg.close()


def local_attn(P, qk_d, v_d, o_d):
    nc = P.nc
    v, s, t = nc.vector, nc.scalar, nc.tensor
    mA = np.where(np.arange(128)[None, :] >= np.arange(128)[:, None], 0.0, NEG
                  ).astype(np.float32)
    mB = np.where(np.arange(128)[None, :] <= np.arange(128)[:, None], 0.0, NEG
                  ).astype(np.float32)
    _stg = contextlib.ExitStack()
    def ssb(name, shape, dtype=F32):
        return _stg.enter_context(nc.sbuf_tensor(P.uname(name), list(shape), dtype))
    mA_sb = ssb("mA", [128, 128])
    mB_sb = ssb("mB", [128, 128])
    P.dma("sync", mA_sb.ap(), P.const("lamA", mA).ap())
    P.dma("sync", mB_sb.ap(), P.const("lamB", mB).ap())
    idb = ssb("la_idb", [128, 128], BF16)
    P.dma("gp", idb.ap(), P.const("identf", np.eye(128, dtype=np.float32)).ap())
    qt = [ssb("la_q", [128, 4, 128], BF16) for _ in range(2)]
    kt = [ssb("la_k", [128, 4, 640], BF16) for _ in range(2)]
    vw = [ssb("la_v", [128, 5, 512], BF16) for _ in range(2)]
    Ssb = [ssb("la_S", [128, 640]) for _ in range(2)]
    rmax = [ssb("la_m", [128, 2]) for _ in range(2)]
    Pbf = ssb("la_Pb", [128, 640], BF16)
    Pt = ssb("la_Pt", [128, 640], BF16)
    ob = [ssb("la_o", [64, 128], BF16) for _ in range(4)]
    P.barrier()
    qkv_free = [0, 0]
    sevac_guard = [0, 0]
    ssb_free = [0, 0]
    tr_end = 0
    o_free = 0
    sy = nc.sync
    for qi in range(L // 128):
        bb = qi % 2
        kw0 = max(0, qi * 128 - 512)
        nkt = (qi * 128 + 128 - kw0) // 128
        P.tt.wait(sy, qkv_free[bb])
        P.dma(f"ld{bb}", qt[bb].ap(),
              qk_d.ap().rearrange("(rc p) l -> p rc l", p=128)[:, 0:4, qi * 128:(qi + 1) * 128])
        P.dma(f"ld{bb}", kt[bb].ap()[:, :, 0:nkt * 128],
              qk_d.ap().rearrange("(rc p) l -> p rc l", p=128)[:, 4:8, kw0:kw0 + nkt * 128])
        P.dma(f"ld{bb}", vw[bb].ap()[:, 0:nkt, :],
              v_d.ap().rearrange("(a p) c -> p a c", p=128)[:, kw0 // 128:kw0 // 128 + nkt, :])
        st = {}
        for hh in range(9):
            if hh < 8:
                h = hh
                p = h % 2
                rc, off = h // 2, (h % 2) * 64
                n1 = min(nkt * 128, 512)
                P.qwait(t, f"ld{bb}")
                P.ts.wait(t, max(sevac_guard[0], sevac_guard[1]))
                ins = t.matmul(P.psum[0].ap()[:, 0:n1], qt[bb].ap()[off:off + 64, rc, :],
                               kt[bb].ap()[off:off + 64, rc, 0:n1], start=True, stop=True)
                if nkt == 5:
                    ins = t.matmul(P.psum[1].ap()[:, 0:128], qt[bb].ap()[off:off + 64, rc, :],
                                   kt[bb].ap()[off:off + 64, rc, 512:640],
                                   start=True, stop=True)
                smm = P.tt.inc(ins)
                P.tt.wait(s, smm)
                P.tv.wait(s, ssb_free[p])
                s.activation(Ssb[p].ap()[:, 0:n1], P.psum[0].ap()[:, 0:n1],
                             AF.Identity, scale=8.0)
                if nkt == 5:
                    s.activation(Ssb[p].ap()[:, 512:640], P.psum[1].ap()[:, 0:128],
                                 AF.Identity, scale=8.0)
                se_n = P.ts.inc(s.drain())
                sevac_guard[0] = se_n
                sevac_guard[1] = se_n
                st[h] = se_n
            if hh >= 1:
                h = hh - 1
                p = h % 2
                P.ts.wait(v, st[h])
                if qi >= 4:
                    v.tensor_tensor(Ssb[p].ap()[:, 0:128], Ssb[p].ap()[:, 0:128],
                                    mA_sb.ap(), ALU.add)
                dg0 = (nkt - 1) * 128
                v.tensor_tensor(Ssb[p].ap()[:, dg0:dg0 + 128], Ssb[p].ap()[:, dg0:dg0 + 128],
                                mB_sb.ap(), ALU.add)
                v.drain()
                v.reduce_max(rmax[p].ap()[:, 0:1], Ssb[p].ap()[:, 0:nkt * 128], axis=AX.X)
                v.drain()
                v.tensor_scalar(out=rmax[p].ap()[:, 1:2], in0=rmax[p].ap()[:, 0:1],
                                scalar1=-1.0, scalar2=None, op0=ALU.mult)
                nm_n = P.tv.inc(v.drain())
                P.tv.wait(s, nm_n)
                P.tt.wait(s, tr_end)
                ex_n = P.ts.inc(s.activation(Pbf.ap()[:, 0:nkt * 128],
                                             Ssb[p].ap()[:, 0:nkt * 128],
                                             AF.Exp, bias=rmax[p].ap()[:, 1:2],
                                             accum_out=rmax[p].ap()[:, 0:1]))
                P.ts.wait(v, ex_n)
                v.reciprocal(rmax[p].ap()[:, 0:1], rmax[p].ap()[:, 0:1])
                v.drain()
                v.tensor_scalar(out=Pbf.ap()[:, 0:nkt * 128], in0=Pbf.ap()[:, 0:nkt * 128],
                                scalar1=rmax[p].ap()[:, 0:1], scalar2=None, op0=ALU.mult)
                pb_n = P.tv.inc(v.drain())
                ssb_free[p] = pb_n
                P.tv.wait(t, pb_n)
                for kti in range(nkt):
                    itr = t.transpose(P.psum_bf[0].ap()[:, kti * 128:(kti + 1) * 128],
                                      Pbf.ap()[:, kti * 128:(kti + 1) * 128], idb.ap())
                tr_n = P.tt.inc(itr)
                tr_end = P.tt.n
                P.tt.wait(s, tr_n)
                kte = P.ts.inc(s.copy(Pt.ap()[:, 0:nkt * 128],
                                      P.psum_bf[0].ap()[:, 0:nkt * 128]))
                P.ts.wait(t, kte)
                P.tv.wait(t, o_free)
                for kti in range(nkt):
                    io = t.matmul(P.psum[2].ap()[:64, 0:128],
                                  vw[bb].ap()[:, kti, h * 64:(h + 1) * 64],
                                  Pt.ap()[:, kti * 128:(kti + 1) * 128],
                                  start=(kti == 0), stop=(kti == nkt - 1))
                om = P.tt.inc(io)
                j = h % 4
                P.tt.wait(v, om)
                P.qwait(v, f"st{j}")
                o_n = P.tv.inc(v.tensor_copy(ob[j].ap(), P.psum[2].ap()[:64, 0:128]))
                o_free = o_n
                P.tv.wait(sy, o_n)
                P.dma(f"st{j}", o_d.ap()[h * 64:(h + 1) * 64, qi * 128:(qi + 1) * 128],
                      ob[j].ap())
        qkv_free[bb] = P.tt.n
    P.barrier()
    _stg.close()


def p5_scratch(P, tag=""):
    return dict(
        h_d=P.dram(f"h_d{tag}", [DM, L], BF16),
        qk_d=P.dram(f"qk_d{tag}", [DM, L], BF16),
        rope_d=P.dram(f"rope_d{tag}", [DM, L], BF16),
        v_d=P.dram(f"v_d{tag}", [L, 512], BF16),
        o_d=P.dram(f"o_d{tag}", [512, L], BF16),
        t_d=P.dram(f"t_d{tag}", [DM, L], F32),
        r2_d=P.dram(f"r2_d{tag}", [DM, L], F32),
        h2_d=P.dram(f"h2_d{tag}", [DM, L], F32),
        h2n_d=P.dram(f"h2n_d{tag}", [DM, L], BF16),
        ua_d=P.dram(f"ua_d{tag}", [2730, L], BF16),
        ug_d=P.dram(f"ug_d{tag}", [2730, L], BF16),
        act_d=P.dram(f"act_d{tag}", [2816, L], BF16),
        w2p_d=P.dram(f"w2p_d{tag}", [2816, DM], F32),
        t2_d=P.dram(f"t2_d{tag}", [DM, L], F32))


def build_p5(dbg=()):
    P = Prog()
    rin = P.dram("rin", [DM, L], F32, kind="ExternalInput")
    wqkv = P.dram("wqkv", [DM, 1536], F32, kind="ExternalInput")
    qs_col = P.dram("qs_col", [128, 1], F32, kind="ExternalInput")
    ks_col = P.dram("ks_col", [128, 1], F32, kind="ExternalInput")
    wo = P.dram("wo", [512, DM], F32, kind="ExternalInput")
    lg = P.dram("lg", [DM], F32, kind="ExternalInput")
    lbv = P.dram("lbv", [DM], F32, kind="ExternalInput")
    w1a = P.dram("w1a", [DM, 2730], F32, kind="ExternalInput")
    b1a = P.dram("b1a", [2730], F32, kind="ExternalInput")
    w1g = P.dram("w1g", [DM, 2730], F32, kind="ExternalInput")
    b1g = P.dram("b1g", [2730], F32, kind="ExternalInput")
    w2 = P.dram("w2", [2730, DM], F32, kind="ExternalInput")
    b2v = P.dram("b2v", [DM], F32, kind="ExternalInput")
    rout = P.dram("rout", [DM, L], F32, kind="ExternalOutput")
    P.fconst(1e-5)
    ones_row(P)
    with P.scope():
        emit_p5(P, rin, rout, wqkv, qs_col, ks_col, wo, lg, lbv, w1a, b1a, w1g,
                b1g, w2, b2v, p5_scratch(P))
    return P


def emit_p5(P, rin, rout, wqkv, qs_col, ks_col, wo, lg, lbv, w1a, b1a, w1g, b1g,
            w2, b2v, sc):
    nc = P.nc
    h_d = sc["h_d"]; qk_d = sc["qk_d"]; rope_d = sc["rope_d"]; v_d = sc["v_d"]
    o_d = sc["o_d"]; t_d = sc["t_d"]; r2_d = sc["r2_d"]; h2_d = sc["h2_d"]
    h2n_d = sc["h2n_d"]; ua_d = sc["ua_d"]; ug_d = sc["ug_d"]; act_d = sc["act_d"]
    w2p_d = sc["w2p_d"]; t2_d = sc["t2_d"]

    zpad = P.sb("zpad", [86, L], BF16)
    P.tg.inc(nc.gpsimd.memset(zpad.ap(), 0.0))
    P.tg.wait(nc.sync)
    P.dma("sync", act_d.ap()[2730:2816, :], zpad.ap())
    zpad2 = P.sb("zpad2", [86, 1024], F32)
    P.tg.inc(nc.gpsimd.memset(zpad2.ap(), 0.0))
    P.tg.wait(nc.sync)
    P.dma("sync", w2p_d.ap()[2730:2816, :], zpad2.ap())
    P.dma("sync", w2p_d.ap()[0:2730, :], w2.ap())
    P.barrier()

    shift_fm(P, rin, h_d)
    fm_matmul(P, h_d, _View(wqkv.ap()[:, 0:1024], "wqk"), qk_d, K=DM, N=1024,
              out_dtype=BF16)
    tok_matmul(P, h_d, _View(wqkv.ap()[:, 1024:1536], "wv"), v_d, K=DM, N=512,
               out_dtype=BF16)
    rope_pass(P, qk_d, rope_d, qs_col, ks_col)
    local_attn(P, rope_d, v_d, o_d)
    fm_matmul(P, o_d, wo, t_d, K=512, N=DM)
    fm_elemwise(P, [rin, t_d], [r2_d], R=DM, emit=ew_add)
    shift_fm(P, r2_d, h2_d)
    fm_layernorm(P, h2_d, h2n_d, lg, lbv)
    emit_p5_ffn(P, h2n_d, w1a, b1a, w1g, b1g, w2p_d, b2v, r2_d, rout, act_d)


def emit_p5_ffn(P, h2n_d, w1a, b1a, w1g, b1g, w2p_d, b2v, r2_d, rout, act_d):
    nc = P.nc
    v, s, t = nc.vector, nc.scalar, nc.tensor
    TT = 512
    N1, NCH1 = 2730, 22
    xa = h2n_d.ap().rearrange("(kc p) l -> p kc l", p=128)
    ra = r2_d.ap().rearrange("(kc p) l -> p kc l", p=128)
    oa = rout.ap().rearrange("(kc p) l -> p kc l", p=128)
    ca = act_d.ap().rearrange("(kc p) l -> p kc l", p=128)

    _stg = contextlib.ExitStack()
    def ssb(name, shape, dtype=F32):
        return _stg.enter_context(nc.sbuf_tensor(P.uname(name), list(shape), dtype))
    wa_sb = ssb("f_wa", [128, 8, N1], BF16)
    wg_sb = ssb("f_wg", [128, 8, N1], BF16)
    x_sb = [ssb("f_x", [128, 8, TT], BF16) for _ in range(2)]
    ua_sb = [ssb("f_ua", [128, TT]) for _ in range(2)]
    ug_sb = [ssb("f_ug", [128, TT]) for _ in range(2)]
    at_sb = [ssb("f_at", [128, TT], BF16) for _ in range(4)]
    ba_sb = ssb("f_ba", [128, NCH1])
    bg_sb = ssb("f_bg", [128, NCH1])
    P.dma("gp0", wa_sb.ap(), w1a.ap().rearrange("(kc p) n -> p kc n", p=128))
    P.dma("gp0", wg_sb.ap(), w1g.ap().rearrange("(kc p) n -> p kc n", p=128))
    for i in range(NCH1):
        rows = min(128, N1 - i * 128)
        P.dma("sync", ba_sb.ap()[:rows, i:i + 1],
              b1a.ap()[i * 128:i * 128 + rows].unsqueeze(1))
        P.dma("sync", bg_sb.ap()[:rows, i:i + 1],
              b1g.ap()[i * 128:i * 128 + rows].unsqueeze(1))
    P.barrier()
    bank_ts = [0, 0, 0, 0]
    pe_done_x = [0, 0]
    gm_tv = [0, 0]
    at_tv = [0, 0, 0, 0]
    for tt in range(L // TT):
        xb = tt % 2
        P.tt.wait(nc.sync, pe_done_x[xb])
        P.dma(f"ld{xb}", x_sb[xb].ap(), xa[:, :, tt * TT:(tt + 1) * TT])
        P.qwait(t, f"ld{xb}")
        for nch in range(NCH1):
            rows = min(128, N1 - nch * 128)
            bka = nch % 2
            bkg = 2 + nch % 2
            P.ts.wait(t, max(bank_ts[bka], bank_ts[bkg]))
            for kc in range(8):
                t.matmul(P.psum[bka].ap()[:rows, :],
                         wa_sb.ap()[:, kc, nch * 128:nch * 128 + rows],
                         x_sb[xb].ap()[:, kc, :],
                         start=(kc == 0), stop=(kc == 7))
            for kc in range(8):
                ins = t.matmul(P.psum[bkg].ap()[:rows, :],
                               wg_sb.ap()[:, kc, nch * 128:nch * 128 + rows],
                               x_sb[xb].ap()[:, kc, :],
                               start=(kc == 0), stop=(kc == 7))
            n = P.tt.inc(ins)
            P.tt.wait(s, n)
            jb = nch % 2
            P.tv.wait(s, gm_tv[jb])
            eva = s.activation(ua_sb[jb].ap()[:rows, :], P.psum[bka].ap()[:rows, :],
                               AF.Identity, bias=ba_sb.ap()[:rows, nch:nch + 1])
            bank_ts[bka] = P.ts.inc(eva)
            evg = s.activation(ug_sb[jb].ap()[:rows, :], P.psum[bkg].ap()[:rows, :],
                               AF.Gelu, bias=bg_sb.ap()[:rows, nch:nch + 1])
            bank_ts[bkg] = P.ts.inc(evg)
            j = nch % 4
            P.ts.wait(v, bank_ts[bkg])
            P.qwait(v, f"st{j}")
            gm = v.tensor_tensor(at_sb[j].ap()[:rows, :], ua_sb[jb].ap()[:rows, :],
                                 ug_sb[jb].ap()[:rows, :], ALU.mult)
            gm_tv[jb] = P.tv.inc(gm)
            P.tv.wait(nc.sync, gm_tv[jb])
            P.dma(f"st{j}", act_d.ap()[nch * 128:nch * 128 + rows,
                                       tt * TT:(tt + 1) * TT], at_sb[j].ap()[:rows, :])
        pe_done_x[xb] = P.tt.n
    P.barrier()
    _stg.close()

    _st2 = contextlib.ExitStack()
    def ssb2(name, shape, dtype=F32):
        return _st2.enter_context(nc.sbuf_tensor(P.uname(name), list(shape), dtype))
    w2_sb = ssb2("f_w2", [128, NCH1, DM], BF16)
    c_sb = [ssb2("f_c", [128, NCH1, TT], BF16) for _ in range(2)]
    r2_sb = [ssb2("f_r2", [128, 8, TT]) for _ in range(2)]
    o_sb = [ssb2("f_o", [128, TT]) for _ in range(4)]
    b2_sb = ssb2("f_b2", [128, 8])
    P.dma("gp1", w2_sb.ap(), w2p_d.ap().rearrange("(kc p) n -> p kc n", p=128))
    for i in range(8):
        P.dma("sync", b2_sb.ap()[:, i:i + 1],
              b2v.ap()[i * 128:(i + 1) * 128].unsqueeze(1))
    P.barrier()
    bank2_ts = [0, 0]
    pe_done_c = [0, 0]
    fin_tv = [0, 0, 0, 0]
    r2_done = [0, 0]
    for tt in range(L // TT):
        xb = tt % 2
        P.tt.wait(nc.sync, pe_done_c[xb])
        P.tv.wait(nc.sync, r2_done[xb])
        P.dma(f"ld{xb}", c_sb[xb].ap(), ca[:, :, tt * TT:(tt + 1) * TT])
        P.dma(f"ld{xb}", r2_sb[xb].ap(), ra[:, :, tt * TT:(tt + 1) * TT])
        P.qwait(t, f"ld{xb}")
        for nch2 in range(8):
            bk = 4 + nch2 % 2
            P.ts.wait(t, bank2_ts[nch2 % 2])
            for kc2 in range(NCH1):
                ins = t.matmul(P.psum[bk].ap()[:, 0:TT],
                               w2_sb.ap()[:, kc2, nch2 * 128:(nch2 + 1) * 128],
                               c_sb[xb].ap()[:, kc2, :],
                               start=(kc2 == 0), stop=(kc2 == NCH1 - 1))
            n = P.tt.inc(ins)
            j = nch2 % 4
            P.tt.wait(s, n)
            P.tv.wait(s, fin_tv[j])
            P.qwait(s, f"st{j}")
            ev = s.activation(o_sb[j].ap(), P.psum[bk].ap()[:, 0:TT], AF.Identity,
                              bias=b2_sb.ap()[:, nch2:nch2 + 1])
            bank2_ts[nch2 % 2] = P.ts.inc(ev)
            P.ts.wait(v, bank2_ts[nch2 % 2])
            fin = v.tensor_tensor(o_sb[j].ap(), o_sb[j].ap(),
                                  r2_sb[xb].ap()[:, nch2, :], ALU.add)
            fin_tv[j] = P.tv.inc(fin)
            P.tv.wait(nc.sync, fin_tv[j])
            P.dma(f"st{j}", oa[:, nch2, tt * TT:(tt + 1) * TT], o_sb[j].ap())
        pe_done_c[xb] = P.tt.n
        r2_done[xb] = P.tv.n
    P.barrier()
    _st2.close()


def build_p6():
    P = Prog()
    xfm = P.dram("xfm", [DM, L], F32, kind="ExternalInput")
    sfm = P.dram("sfm", [DM, L], F32, kind="ExternalInput")
    gfm = P.dram("gfm", [DM, L], F32, kind="ExternalInput")
    x1fm = P.dram("x1fm", [DM, L], F32, kind="ExternalOutput")
    xn1fm = P.dram("xn1fm", [DM, L], BF16, kind="ExternalOutput")
    xn1f32 = P.dram("xn1f32", [DM, L], F32, kind="ExternalOutput")
    P.fconst(1e-5)
    ones_row(P)
    with P.scope():
        emit_p6(P, xfm, sfm, gfm, x1fm, xn1fm, xn1f32)
    return P


def emit_p6(P, xfm, sfm, gfm, x1fm, xn1fm, xn1f32):
    nc = P.nc
    v, s = nc.vector, nc.scalar

    def emit(vv, outs, ins):
        zs_n = P.ts.inc(s.activation(outs[0], ins[2], AF.Silu))
        P.ts.wait(vv, zs_n)
        vv.tensor_tensor(outs[0], ins[1], outs[0], ALU.mult)
        vv.drain()
        return vv.tensor_tensor(outs[0], ins[0], outs[0], ALU.add)
    fm_elemwise(P, [xfm, sfm, gfm], [x1fm], R=DM, emit=emit)
    fm_layernorm(P, x1fm, xn1fm, out_dtype=BF16)
    fm_layernorm(P, x1fm, xn1f32, out_dtype=F32)


def build_p7(dbg=()):
    P = Prog()
    xn1f32 = P.dram("xn1f32", [DM, L], F32, kind="ExternalInput")
    g_ln = P.dram("g_ln", [DM], F32, kind="ExternalInput")
    wq = P.dram("wq", [DM, DM], F32, kind="ExternalInput")
    wkv = P.dram("wkv", [DM, 2 * DM], F32, kind="ExternalInput")
    wo_l1 = P.dram("wo_l1", [DM, DM], F32, kind="ExternalInput")
    s1fm = P.dram("s1fm", [DM, L], F32, kind="ExternalOutput")
    P.fconst(1e-5)
    ones_row(P)
    with P.scope():
        emit_p7(P, xn1f32, g_ln, wq, wkv, wo_l1, s1fm, dbg=dbg)
    return P


def emit_p7(P, xn1f32, g_ln, wq, wkv, wo_l1, s1fm, dbg=()):
    nc = P.nc
    v, s, t = nc.vector, nc.scalar, nc.tensor

    def dk(name):
        return "ExternalOutput" if name in dbg else "Internal"

    xln_d = P.dram("l1xln_d", [DM, L], BF16, kind=dk("xln_d"))
    q_d = P.dram("l1q_d", [DM, L], BF16, kind=dk("q_d"))
    k_d = P.dram("l1k_d", [DM, L], BF16, kind=dk("k_d"))
    v_d = P.dram("l1v_d", [L, DM], BF16, kind=dk("v_d"))
    ofm_d = P.dram("l1ofm_d", [DM, L], BF16, kind=dk("ofm_d"))

    fm_layernorm(P, xn1f32, xln_d, lg=g_ln)
    fm_matmul(P, xln_d, wq, q_d, K=DM, N=DM, out_dtype=BF16)
    fm_matmul(P, xln_d, _View(wkv.ap()[:, 0:DM], "wk"), k_d, K=DM, N=DM, out_dtype=BF16)
    tok_matmul(P, xln_d, _View(wkv.ap()[:, DM:2 * DM], "wv1"), v_d, K=DM, N=DM,
               out_dtype=BF16)

    mB = np.where(np.arange(128)[None, :] <= np.arange(128)[:, None], 0.0, NEG
                  ).astype(np.float32)
    _stg = contextlib.ExitStack()
    def ssb(name, shape, dtype=F32):
        return _stg.enter_context(nc.sbuf_tensor(P.uname(name), list(shape), dtype))
    mB_sb = ssb("l1mB", [128, 128])
    P.dma("sync", mB_sb.ap(), P.const("lamB", mB).ap())
    idb = ssb("l1idb", [128, 128], BF16)
    P.dma("gp", idb.ap(), P.const("identf", np.eye(128, dtype=np.float32)).ap())
    k_sb = ssb("l1k", [128, 8, L], BF16)
    v_sb = ssb("l1v", [128, L // 128, DM], BF16)
    P.dma("sync", k_sb.ap(), k_d.ap().rearrange("(rc p) l -> p rc l", p=128))
    for i in range(0, L // 128, 8):
        nbl = min(8, L // 128 - i)
        P.dma("sync", v_sb.ap()[:, i:i + nbl, :],
              v_d.ap().rearrange("(a p) c -> p a c", p=128)[:, i:i + nbl, :])
    qt = [ssb("l1q", [128, 8, 128], BF16) for _ in range(2)]
    Ssb = [ssb("l1S", [128, L]) for _ in range(2)]
    rmax = [ssb("l1m", [128, 2]) for _ in range(2)]
    Pbf = ssb("l1Pb", [128, L], BF16)
    Pt = ssb("l1Pt", [128, L], BF16)
    ob = [ssb("l1o", [64, 128], BF16) for _ in range(4)]
    P.barrier()
    qt_free = [0, 0]
    sevac_guard = [0, 0]
    ssb_free = [0, 0]
    tr_end = 0
    o_end = 0
    o_free = 0
    sy = nc.sync
    for qi in range(L // 128):
        bb = qi % 2
        nkt = qi + 1
        P.tt.wait(sy, qt_free[bb])
        P.dma(f"ld{bb}", qt[bb].ap(),
              q_d.ap().rearrange("(rc p) l -> p rc l", p=128)[:, :, qi * 128:(qi + 1) * 128])
        st = {}
        for hh in range(17):
            if hh < 16:
                h = hh
                p = h % 2
                rc, off = h // 2, (h % 2) * 64
                P.qwait(t, f"ld{bb}")
                first_evac = True
                for g0 in range(0, nkt * 128, 512):
                    cols = min(512, nkt * 128 - g0)
                    bk = (g0 // 512) % 2
                    P.ts.wait(t, sevac_guard[bk])
                    ins = t.matmul(P.psum[bk].ap()[:, 0:cols],
                                   qt[bb].ap()[off:off + 64, rc, :],
                                   k_sb.ap()[off:off + 64, rc, g0:g0 + cols],
                                   start=True, stop=True)
                    smm = P.tt.inc(ins)
                    P.tt.wait(s, smm)
                    if first_evac:
                        P.tv.wait(s, ssb_free[p])
                        first_evac = False
                    se_n = P.ts.inc(s.activation(Ssb[p].ap()[:, g0:g0 + cols],
                                                 P.psum[bk].ap()[:, 0:cols],
                                                 AF.Identity, scale=0.125))
                    sevac_guard[bk] = se_n
                st[h] = dict(evac=se_n)
                if h == 15:
                    qt_free[bb] = P.tt.n
            if hh >= 1:
                h = hh - 1
                p = h % 2
                P.ts.wait(v, st[h]["evac"])
                dg0 = (nkt - 1) * 128
                v.tensor_tensor(Ssb[p].ap()[:, dg0:dg0 + 128],
                                Ssb[p].ap()[:, dg0:dg0 + 128], mB_sb.ap(), ALU.add)
                v.drain()
                v.reduce_max(rmax[p].ap()[:, 0:1], Ssb[p].ap()[:, 0:nkt * 128], axis=AX.X)
                v.drain()
                v.tensor_scalar(out=rmax[p].ap()[:, 1:2], in0=rmax[p].ap()[:, 0:1],
                                scalar1=-1.0, scalar2=None, op0=ALU.mult)
                nm_n = P.tv.inc(v.drain())
                P.tv.wait(s, nm_n)
                P.tt.wait(s, tr_end)
                ex_n = P.ts.inc(s.activation(Pbf.ap()[:, 0:nkt * 128],
                                             Ssb[p].ap()[:, 0:nkt * 128],
                                             AF.Exp, bias=rmax[p].ap()[:, 1:2],
                                             accum_out=rmax[p].ap()[:, 0:1]))
                P.ts.wait(v, ex_n)
                v.reciprocal(rmax[p].ap()[:, 0:1], rmax[p].ap()[:, 0:1])
                v.drain()
                v.tensor_scalar(out=Pbf.ap()[:, 0:nkt * 128], in0=Pbf.ap()[:, 0:nkt * 128],
                                scalar1=rmax[p].ap()[:, 0:1], scalar2=None, op0=ALU.mult)
                pb_n = P.tv.inc(v.drain())
                ssb_free[p] = pb_n
                P.tv.wait(t, pb_n)
                kte_eng = [None, None]
                kte_cnt = [0, 0]
                last_s_copy = 0
                last_v_copy = 0
                for kg in range(0, nkt, 8):
                    kn = min(8, nkt - kg)
                    bfb = (kg // 8) % 2
                    if kte_eng[bfb] == "s":
                        P.ts.wait(t, kte_cnt[bfb])
                    elif kte_eng[bfb] == "v":
                        P.tv.wait(t, kte_cnt[bfb])
                    for kk in range(kn):
                        itr = t.transpose(
                            P.psum_bf[bfb].ap()[:, kk * 128:(kk + 1) * 128],
                            Pbf.ap()[:, (kg + kk) * 128:(kg + kk + 1) * 128], idb.ap())
                    tr_n = P.tt.inc(itr)
                    if (kg // 8) % 2 == 0:
                        P.tt.wait(s, tr_n)
                        kte = P.ts.inc(s.copy(Pt.ap()[:, kg * 128:(kg + kn) * 128],
                                              P.psum_bf[bfb].ap()[:, 0:kn * 128]))
                        kte_eng[bfb] = "s"
                        kte_cnt[bfb] = kte
                        last_s_copy = kte
                    else:
                        P.tt.wait(v, tr_n)
                        kte = P.tv.inc(v.tensor_copy(Pt.ap()[:, kg * 128:(kg + kn) * 128],
                                                     P.psum_bf[bfb].ap()[:, 0:kn * 128]))
                        kte_eng[bfb] = "v"
                        kte_cnt[bfb] = kte
                        last_v_copy = kte
                tr_end = P.tt.n
                if last_s_copy:
                    P.ts.wait(t, last_s_copy)
                if last_v_copy:
                    P.tv.wait(t, last_v_copy)
                P.tv.wait(t, o_free)
                for kk in range(nkt):
                    io = t.matmul(P.psum[2].ap()[:64, 0:128],
                                  v_sb.ap()[:, kk, h * 64:(h + 1) * 64],
                                  Pt.ap()[:, kk * 128:(kk + 1) * 128],
                                  start=(kk == 0), stop=(kk == nkt - 1))
                om = P.tt.inc(io)
                o_end = P.tt.n
                j = h % 4
                P.tt.wait(v, om)
                P.qwait(v, f"st{j}")
                o_n = P.tv.inc(v.tensor_copy(ob[j].ap(), P.psum[2].ap()[:64, 0:128]))
                o_free = o_n
                P.tv.wait(sy, o_n)
                P.dma(f"st{j}", ofm_d.ap()[h * 64:(h + 1) * 64, qi * 128:(qi + 1) * 128],
                      ob[j].ap())
    P.barrier()
    _stg.close()
    fm_matmul(P, ofm_d, wo_l1, s1fm, K=DM, N=DM)


def emit_p8_experts(P, xn1fm, Fp_d, b2p_d, w1, b1, w2, acc_d):
    nc = P.nc
    v, s, t = nc.vector, nc.scalar, nc.tensor
    NQ, LQ, TT = 4, L // 4, 512
    NTT = LQ // TT
    _stg = contextlib.ExitStack()
    def ssb(name, shape, dtype=F32):
        return _stg.enter_context(nc.sbuf_tensor(P.uname(name), list(shape), dtype))
    xq = ssb("e_x", [128, 8, LQ], BF16)
    accq = ssb("e_acc", [128, 8, LQ], F32)
    w1sb = ssb("e_w1", [128, 8, 2048], BF16)
    w2sb = ssb("e_w2", [128, 16, 1024], BF16)
    hsb = [ssb("e_h", [128, 16, TT], BF16) for _ in range(NTT)]
    fqb = ssb("e_fq", [128, LQ], BF16)
    fqr = ssb("e_fqr", [1, 16, LQ], BF16)
    b1sb = ssb("e_b1", [128, 256], F32)
    b2pb = ssb("e_b2", [128, DM], BF16)
    gsb = [ssb("e_g", [128, TT], BF16) for _ in range(NTT)]
    ones_bf = ssb("e_1", [1, 128], BF16)
    P.tg.inc(nc.gpsimd.memset(ones_bf.ap(), 1.0))
    P.dma("gp", b2pb.ap(), b2p_d.ap())
    P.dma("sync", b1sb.ap(), b1.ap())
    P.barrier()
    xa = xn1fm.ap().rearrange("(kc p) l -> p kc l", p=128)
    aa = acc_d.ap().rearrange("(rc p) l -> p rc l", p=128)
    bank_tv = [0, 0]
    bank_ts = [0, 0]
    bank2_tv = [0, 0]
    bank45_ts = [0, 0]
    q_end_tt = 0
    for q in range(NQ):
        q0 = q * LQ
        P.tt.wait(nc.sync, q_end_tt)
        P.dma("ld0", xq.ap(), xa[:, :, q0:q0 + LQ])
        P.tt.wait(nc.gpsimd, q_end_tt)
        P.dma("gp", fqb.ap(), Fp_d.ap()[:, q0:q0 + LQ])
        P.dma("gp", fqr.ap(), Fp_d.ap()[0:16, q0:q0 + LQ].unsqueeze(0))
        P.dma("gp0", w1sb.ap(), w1.ap()[0].rearrange("(kc p) n -> p kc n", p=128))
        P.dma("gp1", w2sb.ap(), w2.ap()[0].rearrange("(kc p) n -> p kc n", p=128))
        w1_cnt = P.qcnt("gp0")
        w2_cnt = P.qcnt("gp1")
        P.qwait(t, "ld0")
        P.qwait(t, "gp")
        for tt2 in range(NTT):
            for ncI in range(8):
                bk = ncI % 2
                P.tv.wait(t, bank_tv[bk])
                P.ts.wait(t, bank_ts[bk])
                mm = t.matmul(P.psum[bk].ap()[:, 0:TT],
                              b2pb.ap()[:, ncI * 128:(ncI + 1) * 128],
                              fqb.ap()[:, tt2 * TT:(tt2 + 1) * TT],
                              start=True, stop=True)
                n = P.tt.inc(mm)
                P.tt.wait(v, n)
                if tt2 == 0 and ncI == 0:
                    P.qwait(v, "st0")
                ev = v.tensor_copy(accq.ap()[:, ncI, tt2 * TT:(tt2 + 1) * TT],
                                   P.psum[bk].ap()[:, 0:TT])
                bank_tv[bk] = P.tv.inc(ev)
        for e in range(16):
            sc_n = [0, 0]
            for tt2 in range(NTT):
                bk45 = 4 + tt2
                P.ts.wait(t, bank45_ts[tt2])
                bmm = t.matmul(P.psum[bk45].ap()[:, 0:TT], ones_bf.ap(),
                               fqr.ap()[0:1, e, tt2 * TT:(tt2 + 1) * TT],
                               start=True, stop=True)
                bn = P.tt.inc(bmm)
                P.tt.wait(s, bn)
                sc = s.copy(gsb[tt2].ap(), P.psum[bk45].ap()[:, 0:TT])
                sc_n[tt2] = P.ts.inc(sc)
                bank45_ts[tt2] = sc_n[tt2]
            gm_n = [0, 0]
            for tt2 in range(NTT):
                for nch in range(16):
                    bk = nch % 2
                    if nch < 2:
                        P.qwait(t, "gp0", w1_cnt)
                        P.tv.wait(t, bank_tv[bk])
                    P.ts.wait(t, bank_ts[bk])
                    for kc in range(8):
                        ins = t.matmul(P.psum[bk].ap()[:, 0:TT],
                                       w1sb.ap()[:, kc, nch * 128:(nch + 1) * 128],
                                       xq.ap()[:, kc, tt2 * TT:(tt2 + 1) * TT],
                                       start=(kc == 0), stop=(kc == 7))
                    n = P.tt.inc(ins)
                    P.tt.wait(s, n)
                    ev = s.activation(hsb[tt2].ap()[:, nch, :],
                                      P.psum[bk].ap()[:, 0:TT], AF.Gelu,
                                      bias=b1sb.ap()[:, e * 16 + nch:e * 16 + nch + 1])
                    bank_ts[bk] = P.ts.inc(ev)
                    P.ts.wait(v, max(bank_ts[bk], sc_n[tt2]))
                    gm = v.tensor_tensor(hsb[tt2].ap()[:, nch, :],
                                         hsb[tt2].ap()[:, nch, :],
                                         gsb[tt2].ap(), ALU.mult)
                    gm_n[tt2] = P.tv.inc(gm)
            mm1_end = P.tt.n
            if e < 15:
                P.tt.wait(nc.gpsimd, mm1_end)
                P.dma("gp0", w1sb.ap(),
                      w1.ap()[e + 1].rearrange("(kc p) n -> p kc n", p=128))
                w1_cnt = P.qcnt("gp0")
            for tt2 in range(NTT):
                P.tv.wait(t, gm_n[tt2])
                for ncI2 in range(8):
                    bk = 2 + ncI2 % 2
                    if ncI2 < 2:
                        P.qwait(t, "gp1", w2_cnt)
                    P.tv.wait(t, bank2_tv[ncI2 % 2])
                    for kc2 in range(16):
                        ins = t.matmul(P.psum[bk].ap()[:, 0:TT],
                                       w2sb.ap()[:, kc2, ncI2 * 128:(ncI2 + 1) * 128],
                                       hsb[tt2].ap()[:, kc2, :],
                                       start=(kc2 == 0), stop=(kc2 == 15))
                    n = P.tt.inc(ins)
                    P.tt.wait(v, n)
                    ev = v.tensor_tensor(accq.ap()[:, ncI2, tt2 * TT:(tt2 + 1) * TT],
                                         accq.ap()[:, ncI2, tt2 * TT:(tt2 + 1) * TT],
                                         P.psum[bk].ap()[:, 0:TT], ALU.add)
                    bank2_tv[ncI2 % 2] = P.tv.inc(ev)
            mm2_end = P.tt.n
            if e < 15:
                P.tt.wait(nc.gpsimd, mm2_end)
                P.dma("gp1", w2sb.ap(),
                      w2.ap()[e + 1].rearrange("(kc p) n -> p kc n", p=128))
                w2_cnt = P.qcnt("gp1")
        P.tv.wait(nc.sync, P.tv.n)
        P.dma("st0", aa[:, :, q0:q0 + LQ], accq.ap())
        q_end_tt = P.tt.n
    P.barrier()
    _stg.close()


def build_p8(dbg=()):
    P = Prog()
    xn1fm = P.dram("xn1fm", [DM, L], BF16, kind="ExternalInput")
    x1fm = P.dram("x1fm", [DM, L], F32, kind="ExternalInput")
    s1fm = P.dram("s1fm", [DM, L], F32, kind="ExternalInput")
    router_w = P.dram("router_w", [DM, 16], F32, kind="ExternalInput")
    w1 = P.dram("w1", [16, DM, 2048], F32, kind="ExternalInput")
    b1 = P.dram("b1", [128, 256], F32, kind="ExternalInput")
    w2 = P.dram("w2moe", [16, 2048, DM], F32, kind="ExternalInput")
    b2all = P.dram("b2all", [16, DM], F32, kind="ExternalInput")
    out = P.dram("out", [L, DM], F32, kind="ExternalOutput")
    P.fconst(1e-5)
    ones_row(P)
    with P.scope():
        emit_p8(P, xn1fm, x1fm, s1fm, router_w, w1, b1, w2, b2all, out, dbg=dbg)
    return P


def emit_p8(P, xn1fm, x1fm, s1fm, router_w, w1, b1, w2, b2all, out, dbg=()):
    nc = P.nc
    v, s, t = nc.vector, nc.scalar, nc.tensor

    def dk(name):
        return "ExternalOutput" if name in dbg else "Internal"

    logits_d = P.dram("logits_d", [16, L], F32, kind=dk("logits_d"))
    Fp_d = P.dram("Fp_d", [128, L], F32, kind=dk("Fp_d"))
    w_d = P.dram("w_d", [1, L], F32, kind=dk("w_d"))
    b2p_d = P.dram("b2p_d", [128, DM], F32)
    he_d = P.dram("he_d", [2048, L], BF16)
    acc_d = P.dram("acc_d", [DM, L], F32, kind=dk("acc_d"))

    fm_matmul(P, xn1fm, router_w, logits_d, K=DM, N=16)

    iota16 = np.broadcast_to(np.arange(16, dtype=np.float32)[None, :], (128, 16)).copy()
    _stg = contextlib.ExitStack()
    def ssb(name, shape, dtype=F32):
        return _stg.enter_context(nc.sbuf_tensor(P.uname(name), list(shape), dtype))
    io_sb = ssb("io16", [128, 16])
    io2_sb = ssb("io16b", [128, 16])
    negio_sb = ssb("io16n", [128, 16])
    P.dma("sync", io_sb.ap(), P.const("iota16", iota16).ap())
    P.dma("sync", io2_sb.ap(), P.const("iota16c", -(iota16 + 1e4)).ap())
    P.dma("sync", negio_sb.ap(), P.const("iota16n", -iota16).ap())
    idf_sb = ssb("g_id", [128, 128])
    P.dma("sync", idf_sb.ap(), P.const("identf", np.eye(128, dtype=np.float32)).ap())
    zp = ssb("g_zp", [112, L // 8])
    P.tg.inc(nc.gpsimd.memset(zp.ap(), 0.0))
    P.tg.wait(nc.sync)
    for i in range(8):
        P.dma("sync", Fp_d.ap()[16:128, i * (L // 8):(i + 1) * (L // 8)], zp.ap())
    zp2 = ssb("g_zp2", [112, DM])
    P.tg.inc(nc.gpsimd.memset(zp2.ap(), 0.0))
    P.tg.wait(nc.sync)
    P.dma("sync", b2p_d.ap()[16:128, :], zp2.ap())
    P.dma("sync", b2p_d.ap()[0:16, :], b2all.ap())
    lg_sb = [ssb("g_l", [16, 128]) for _ in range(2)]
    lt = ssb("g_lt", [128, 16])
    Mc = ssb("g_M", [128, 4])
    eq = ssb("g_eq", [128, 16])
    fh = ssb("g_fh", [128, 16], BF16)
    fT = [ssb("g_fT", [16, 128]) for _ in range(2)]
    wrow = [ssb("g_wr", [1, 128]) for _ in range(2)]
    idb16 = ssb("g_idb", [128, 128], BF16)
    P.dma("gp", idb16.ap(), P.const("identf", np.eye(128, dtype=np.float32)).ap())
    P.barrier()
    done = [0, 0]
    for ti in range(L // 128):
        bb = ti % 2
        sl = slice(ti * 128, (ti + 1) * 128)
        P.tv.wait(nc.sync, done[bb])
        P.dma(f"ld{bb}", lg_sb[bb].ap(), logits_d.ap()[:, sl])
        P.qwait(t, f"ld{bb}")
        P.tv.wait(t)
        tr1 = P.tt.inc(t.transpose(P.psum[0].ap()[:, 0:16], lg_sb[bb].ap(),
                                   idf_sb.ap()[:16, :16]))
        P.tt.wait(v, tr1)
        v.tensor_copy(lt.ap(), P.psum[0].ap()[:, 0:16])
        v.drain()
        v.reduce_max(Mc.ap()[:, 0:1], lt.ap(), axis=AX.X)
        M_n = P.tv.inc(v.drain())
        P.tv.wait(s, M_n)
        sg_n = P.ts.inc(s.activation(Mc.ap()[:, 1:2], Mc.ap()[:, 0:1], AF.Sigmoid))
        v.tensor_scalar(out=eq.ap(), in0=lt.ap(), scalar1=Mc.ap()[:, 0:1], scalar2=None,
                        op0=ALU.is_ge)
        v.drain()
        v.scalar_tensor_tensor(out=eq.ap(), in0=eq.ap(), scalar=1e4, in1=io2_sb.ap(),
                               op0=ALU.mult, op1=ALU.add)
        v.drain()
        v.reduce_max(Mc.ap()[:, 2:3], eq.ap(), axis=AX.X)
        v.drain()
        v.tensor_scalar(out=fh.ap(), in0=negio_sb.ap(), scalar1=Mc.ap()[:, 2:3], scalar2=None,
                        op0=ALU.is_equal)
        fh_n = P.tv.inc(v.drain())
        P.tv.wait(t, fh_n)
        P.ts.wait(t, sg_n)
        tr2 = t.transpose(P.psum_bf[0].ap()[:16, 0:128], fh.ap(), idb16.ap())
        tr3 = t.transpose(P.psum[1].ap()[:1, 0:128], Mc.ap()[:, 1:2], idf_sb.ap())
        tr_n = P.tt.inc(tr3)
        P.tt.wait(v, tr_n)
        P.qwait(v, f"st{bb}")
        v.tensor_copy(fT[bb].ap(), P.psum_bf[0].ap()[:16, 0:128])
        ev_n = P.tv.inc(v.tensor_copy(wrow[bb].ap(), P.psum[1].ap()[:1, 0:128]))
        P.tv.wait(nc.sync, ev_n)
        P.dma(f"st{bb}", Fp_d.ap()[0:16, sl], fT[bb].ap())
        P.dma(f"st{bb}", w_d.ap()[:, sl], wrow[bb].ap())
        done[bb] = P.tv.n
    P.barrier()
    _stg.close()

    emit_p8_experts(P, xn1fm, Fp_d, b2p_d, w1, b1, w2, acc_d)

    _st2 = contextlib.ExitStack()
    def ssb2(name, shape, dtype=F32):
        return _st2.enter_context(nc.sbuf_tensor(P.uname(name), list(shape), dtype))
    ones_row(P)
    acc_sb = [ssb2("f_acc", [128, 8, 128]) for _ in range(2)]
    s1_sb = [ssb2("f_s1", [128, 8, 128]) for _ in range(2)]
    x1_sb = [ssb2("f_x1", [128, 8, 128]) for _ in range(2)]
    wr_sb = ssb2("f_wr", [1, 128])
    gs_sb = ssb2("f_gs", [128, 8, 128])
    t_sb = ssb2("f_t", [128, 8, 128])
    o_sb = [ssb2("f_o", [128, 1024]) for _ in range(2)]
    idf2 = ssb2("f_id", [128, 128])
    P.dma("sync", idf2.ap(), P.const("identf", np.eye(128, dtype=np.float32)).ap())
    P.barrier()
    done2 = [0, 0]
    for ti in range(L // 128):
        bb = ti % 2
        sl = slice(ti * 128, (ti + 1) * 128)
        sy = nc.sync
        P.tv.wait(sy, done2[bb])
        P.dma(f"ld{bb}", acc_sb[bb].ap(),
              acc_d.ap().rearrange("(rc p) l -> p rc l", p=128)[:, :, sl])
        P.dma(f"ld{bb}", s1_sb[bb].ap(),
              s1fm.ap().rearrange("(rc p) l -> p rc l", p=128)[:, :, sl])
        P.dma(f"ld{bb}", x1_sb[bb].ap(),
              x1fm.ap().rearrange("(rc p) l -> p rc l", p=128)[:, :, sl])
        P.dma(f"ld{bb}", wr_sb.ap(), w_d.ap()[:, sl])
        P.qwait(t, f"ld{bb}")
        P.tv.wait(t)
        wb = P.tt.inc(t.matmul(P.psum[2].ap()[:, 0:128], ones_row(P).ap(), wr_sb.ap(),
                               start=True, stop=True))
        P.tt.wait(v, wb)
        for rc in range(8):
            v.scalar_tensor_tensor(out=t_sb.ap()[:, rc, :], in0=acc_sb[bb].ap()[:, rc, :],
                                   scalar=1.0, in1=P.psum[2].ap()[:, 0:128],
                                   op0=ALU.mult, op1=ALU.mult)
        g1_n = P.tv.inc(v.drain())
        P.tv.wait(s, g1_n)
        zs_n = P.ts.inc(s.activation(gs_sb.ap(), t_sb.ap(), AF.Silu))
        P.ts.wait(v, zs_n)
        v.tensor_tensor(t_sb.ap(), s1_sb[bb].ap(), gs_sb.ap(), ALU.mult)
        v.drain()
        v.tensor_tensor(t_sb.ap(), x1_sb[bb].ap(), t_sb.ap(), ALU.add)
        tt_n = P.tv.inc(v.drain())
        P.tv.wait(t, tt_n)
        for rc in range(8):
            half, col = rc // 4, (rc % 4) * 128
            itr = t.transpose(P.psum[3 + half].ap()[:, col:col + 128],
                              t_sb.ap()[:, rc, :], idf2.ap())
        tro = P.tt.inc(itr)
        P.tt.wait(v, tro)
        P.qwait(v, f"st{bb}")
        v.tensor_copy(o_sb[bb].ap()[:, 0:512], P.psum[3].ap())
        oe_n = P.tv.inc(v.tensor_copy(o_sb[bb].ap()[:, 512:1024], P.psum[4].ap()))
        P.tv.wait(sy, oe_n)
        P.dma(f"st{bb}", out.ap()[sl, :], o_sb[bb].ap())
        done2[bb] = P.tv.n
    P.barrier()
    _st2.close()


_PROGS = {}


def _prog(name, builder):
    if name not in _PROGS:
        _PROGS[name] = builder()
    return _PROGS[name]


def kernel_multi(**ins):
    x = np.ascontiguousarray(ins["L0_attn_u"], dtype=np.float32)
    ncore = x.shape[0]
    cores = list(range(ncore))

    def run(name, builder, per_core):
        P = _prog(name, builder)
        res = run_bass_kernel_spmd(P.nc, per_core, core_ids=cores)
        return res.results

    def rep(arrs):
        return [arrs for _ in cores] if isinstance(arrs, dict) else arrs

    r1 = run("p1", build_p1, [dict(x=np.ascontiguousarray(x[i])) for i in cores])
    w_in = ins["L0_attn_W_in"]
    r2 = run("p2", build_p2, [dict(xnfm=r1[i]["xnfm"], w_in=w_in) for i in cores])
    D = ins["L0_attn_D"]
    drep = np.ascontiguousarray(np.repeat(D, 64).reshape(16, 128).T)
    bbT = np.ascontiguousarray(ins["L0_attn_B_bias"].T)
    cbT = np.ascontiguousarray(ins["L0_attn_C_bias"].T)
    r3 = run("p3", build_p3, [dict(
        projfm=r2[i]["projfm"], xs_tok=r2[i]["xs_tok"],
        dt_bias=ins["L0_attn_dt_bias"], b_biasT=bbT, c_biasT=cbT,
        b_norm_w=ins["L0_attn_B_norm_w"], c_norm_w=ins["L0_attn_C_norm_w"],
        d_rep=drep) for i in cores])
    r4 = run("p4", build_p4, [dict(ygfm=r3[i]["ygfm"], w_out=ins["L0_attn_W_out"])
                              for i in cores])
    cur = [r1[i]["xnfm"] for i in cores]
    for l in range(2):
        im = []
        for i in cores:
            im.append(dict(
                rin=cur[i],
                wqkv=np.ascontiguousarray(ins["L0_ffn_wqkv"][l]),
                qs_col=np.ascontiguousarray(np.tile(ins["L0_ffn_q_scale"][l], 2)[:, None]),
                ks_col=np.ascontiguousarray(np.tile(ins["L0_ffn_k_scale"][l], 2)[:, None]),
                wo=np.ascontiguousarray(ins["L0_ffn_wo"][l]),
                lg=np.ascontiguousarray(ins["L0_ffn_ln_g"][l]),
                lbv=np.ascontiguousarray(ins["L0_ffn_ln_b"][l]),
                w1a=np.ascontiguousarray(ins["L0_ffn_w1"][l][:, 0:2730]),
                b1a=np.ascontiguousarray(ins["L0_ffn_b1"][l][0:2730]),
                w1g=np.ascontiguousarray(ins["L0_ffn_w1"][l][:, 2730:5460]),
                b1g=np.ascontiguousarray(ins["L0_ffn_b1"][l][2730:5460]),
                w2=np.ascontiguousarray(ins["L0_ffn_w2"][l]),
                b2v=np.ascontiguousarray(ins["L0_ffn_b2"][l])))
        r5 = run("p5", build_p5, im)
        cur = [r5[i]["rout"] for i in cores]
    r6 = run("p6", build_p6, [dict(xfm=r1[i]["xfm"], sfm=r4[i]["sfm"], gfm=cur[i])
                              for i in cores])
    r7 = run("p7", build_p7, [dict(
        xn1f32=r6[i]["xn1f32"], g_ln=ins["L1_attn_g_ln"], wq=ins["L1_attn_Wq"],
        wkv=ins["L1_attn_Wkv"], wo_l1=ins["L1_attn_Wo"]) for i in cores])
    r8 = run("p8", build_p8, [dict(
        xn1fm=r6[i]["xn1fm"], x1fm=r6[i]["x1fm"], s1fm=r7[i]["s1fm"],
        router_w=ins["L1_ffn_router_w"], w1=ins["L1_ffn_w1"],
        b1=np.ascontiguousarray(
            ins["L1_ffn_b1"].reshape(16, 16, 128).transpose(2, 0, 1).reshape(128, 256)),
        w2moe=ins["L1_ffn_w2"], b2all=ins["L1_ffn_b2"]) for i in cores])
    out = np.stack([np.asarray(r8[i]["out"], dtype=np.float32) for i in cores], axis=0)
    return out


def build_full():
    P = Prog()
    P.fconst(1e-5)
    ones_row(P)
    D = lambda *a, **k: P.dram(*a, **k)
    x = D("x", [L, DM], F32, kind="ExternalInput")
    w_in = D("w_in", [DM, 4480], F32, kind="ExternalInput")
    dt_bias = D("dt_bias", [32], F32, kind="ExternalInput")
    b_biasT = D("b_biasT", [128, 32], F32, kind="ExternalInput")
    c_biasT = D("c_biasT", [128, 32], F32, kind="ExternalInput")
    b_norm_w = D("b_norm_w", [128], F32, kind="ExternalInput")
    c_norm_w = D("c_norm_w", [128], F32, kind="ExternalInput")
    d_rep = D("d_rep", [128, 16], F32, kind="ExternalInput")
    w_out = D("w_out", [2048, DM], F32, kind="ExternalInput")
    wqkv2 = D("wqkv2", [2, DM, 1536], F32, kind="ExternalInput")
    qs2 = D("qs2", [2, 128, 1], F32, kind="ExternalInput")
    ks2 = D("ks2", [2, 128, 1], F32, kind="ExternalInput")
    wo2 = D("wo2", [2, 512, DM], F32, kind="ExternalInput")
    lg2 = D("lg2", [2, DM], F32, kind="ExternalInput")
    lb2 = D("lb2", [2, DM], F32, kind="ExternalInput")
    w1f = D("w1f", [2, DM, 5460], F32, kind="ExternalInput")
    b1f = D("b1f", [2, 5460], F32, kind="ExternalInput")
    w2f = D("w2f", [2, 2730, DM], F32, kind="ExternalInput")
    b2f = D("b2f", [2, DM], F32, kind="ExternalInput")
    g_ln = D("g_ln", [DM], F32, kind="ExternalInput")
    wq = D("wq", [DM, DM], F32, kind="ExternalInput")
    wkv = D("wkv", [DM, 2 * DM], F32, kind="ExternalInput")
    wo_l1 = D("wo_l1", [DM, DM], F32, kind="ExternalInput")
    router_w = D("router_w", [DM, 16], F32, kind="ExternalInput")
    mw1 = D("mw1", [16, DM, 2048], F32, kind="ExternalInput")
    mb1 = D("mb1", [128, 256], F32, kind="ExternalInput")
    mw2 = D("mw2", [16, 2048, DM], F32, kind="ExternalInput")
    mb2 = D("mb2", [16, DM], F32, kind="ExternalInput")
    out = D("out", [L, DM], F32, kind="ExternalOutput")

    xfm = D("xfm", [DM, L], F32)
    xnfm = D("xnfm", [DM, L], F32)
    projfm = D("projfm", [4480, L], F32)
    xs_tok = D("xs_tok", [L, 2048], BF16)
    ygfm = D("ygfm", [2048, L], BF16)
    sfm = D("sfm", [DM, L], F32)
    r5a = D("r5a", [DM, L], F32)
    r5b = D("r5b", [DM, L], F32)
    x1fm = D("x1fm", [DM, L], F32)
    xn1fm = D("xn1fm", [DM, L], BF16)
    xn1f32 = D("xn1f32", [DM, L], F32)
    s1fm = D("s1fm", [DM, L], F32)
    sc5 = p5_scratch(P)

    with P.scope():
        emit_p1(P, x, xfm, xnfm)
    with P.scope():
        emit_p2(P, xnfm, w_in, projfm, xs_tok)
    with P.scope():
        emit_p3(P, projfm, xs_tok, dt_bias, b_biasT, c_biasT, b_norm_w, c_norm_w,
                d_rep, ygfm)
    fm_matmul(P, ygfm, w_out, sfm, K=2048, N=DM)
    rin = xnfm
    for l, rout_l in ((0, r5a), (1, r5b)):
        with P.scope():
            emit_p5(P, rin, rout_l,
                    _View(wqkv2.ap()[l], f"wqkv{l}"),
                    _View(qs2.ap()[l], f"qs{l}"), _View(ks2.ap()[l], f"ks{l}"),
                    _View(wo2.ap()[l], f"wo{l}"),
                    _View(lg2.ap()[l], f"lg{l}"), _View(lb2.ap()[l], f"lb{l}"),
                    _View(w1f.ap()[l][:, 0:2730], f"w1a{l}"),
                    _View(b1f.ap()[l][0:2730], f"b1a{l}"),
                    _View(w1f.ap()[l][:, 2730:5460], f"w1g{l}"),
                    _View(b1f.ap()[l][2730:5460], f"b1g{l}"),
                    _View(w2f.ap()[l], f"w2l{l}"), _View(b2f.ap()[l], f"b2l{l}"),
                    sc5)
        rin = rout_l
    with P.scope():
        emit_p6(P, xfm, sfm, r5b, x1fm, xn1fm, xn1f32)
    with P.scope():
        emit_p7(P, xn1f32, g_ln, wq, wkv, wo_l1, s1fm)
    with P.scope():
        emit_p8(P, xn1fm, x1fm, s1fm, router_w, mw1, mb1, mw2, mb2, out)
    P.barrier()
    return P


def kernel(**ins):
    return kernel_fused(**ins)


def kernel_fused(**ins):
    x = np.ascontiguousarray(ins["L0_attn_u"], dtype=np.float32)
    ncore = x.shape[0]
    cores = list(range(ncore))
    P = _prog("full", build_full)
    D = ins["L0_attn_D"]
    common = dict(
        w_in=ins["L0_attn_W_in"],
        dt_bias=ins["L0_attn_dt_bias"],
        b_biasT=np.ascontiguousarray(ins["L0_attn_B_bias"].T),
        c_biasT=np.ascontiguousarray(ins["L0_attn_C_bias"].T),
        b_norm_w=ins["L0_attn_B_norm_w"], c_norm_w=ins["L0_attn_C_norm_w"],
        d_rep=np.ascontiguousarray(np.repeat(D, 64).reshape(16, 128).T),
        w_out=ins["L0_attn_W_out"],
        wqkv2=ins["L0_ffn_wqkv"],
        qs2=np.ascontiguousarray(np.tile(ins["L0_ffn_q_scale"], (1, 2))[:, :, None]),
        ks2=np.ascontiguousarray(np.tile(ins["L0_ffn_k_scale"], (1, 2))[:, :, None]),
        wo2=ins["L0_ffn_wo"], lg2=ins["L0_ffn_ln_g"], lb2=ins["L0_ffn_ln_b"],
        w1f=ins["L0_ffn_w1"], b1f=ins["L0_ffn_b1"], w2f=ins["L0_ffn_w2"],
        b2f=ins["L0_ffn_b2"],
        g_ln=ins["L1_attn_g_ln"], wq=ins["L1_attn_Wq"], wkv=ins["L1_attn_Wkv"],
        wo_l1=ins["L1_attn_Wo"],
        router_w=ins["L1_ffn_router_w"], mw1=ins["L1_ffn_w1"],
        mb1=np.ascontiguousarray(
            ins["L1_ffn_b1"].reshape(16, 16, 128).transpose(2, 0, 1).reshape(128, 256)),
        mw2=ins["L1_ffn_w2"], mb2=ins["L1_ffn_b2"])
    im = [dict(common, x=np.ascontiguousarray(x[i])) for i in cores]
    res = run_bass_kernel_spmd(P.nc, im, core_ids=cores)
    return np.stack([np.asarray(res.results[i]["out"], dtype=np.float32)
                     for i in cores], axis=0)
